# version 04
# baseline (speedup 1.0000x reference)
import numpy as np
from contextlib import ExitStack
import concourse.bass as bass
import concourse.mybir as mybir

F32 = mybir.dt.float32
BF16 = mybir.dt.bfloat16
AF = mybir.ActivationFunctionType
ALU = mybir.AluOpType
AX = mybir.AxisListType

T_CORE = 2048
FT = 512
NFT = T_CORE // FT


class Sem:
    __slots__ = ("h", "n", "name")

    def __init__(self, es, nc, name):
        self.h = es.enter_context(nc.semaphore(name))
        self.n = 0
        self.name = name

    def inc(self, instr, k=1):
        instr.then_inc(self.h, k)
        self.n += k
        return self.n

    def wait(self, eng, n=None):
        eng.wait_ge(self.h, self.n if n is None else n)


class Builder:

    def __init__(self, name="prog"):
        self.nc = bass.Bass("TRN2", target_bir_lowering=False)
        self.es = ExitStack()
        self.es.__enter__()
        self._sbuf_bytes = 0
        self._max_sbuf = 0
        self.name = name
        self.pfx = ""
        self._phase_es = None

    def begin_phase(self, pfx):
        assert self._phase_es is None
        self.pfx = pfx
        self._phase_es = ExitStack()
        self._phase_es.__enter__()

    def end_phase(self):
        self._phase_es.__exit__(None, None, None)
        self._phase_es = None
        self.pfx = ""

    def _alloc_es(self):
        return self._phase_es if self._phase_es is not None else self.es

    def sem(self, name):
        return Sem(self.es, self.nc, self.pfx + name)

    def sb(self, name, shape, dt=F32):
        t = self._alloc_es().enter_context(self.nc.sbuf_tensor(self.pfx + name, shape, dt))
        return t

    def psum(self, name, shape=(128, 512), dt=F32):
        return self._alloc_es().enter_context(self.nc.psum_tensor(self.pfx + name, list(shape), dt))

    def dram(self, name, shape, dt=F32, kind=None):
        if kind is None:
            return self.nc.dram_tensor(name, list(shape), dt)
        return self.nc.dram_tensor(name, list(shape), dt, kind=kind)

    def block(self):
        return self._alloc_es().enter_context(self.nc.Block())

    def done(self):
        self.es.__exit__(None, None, None)
        return self.nc




from concourse.bass_utils import run_bass_kernel_spmd


def host_hawk_weights(Win, Wg, Wout, cw, cb, bg, fb):
    winT = np.ascontiguousarray(Win.T)
    winT_hg = np.ascontiguousarray(np.concatenate([winT[:, HID:], winT[:, :HID]], axis=1))
    wgT = np.ascontiguousarray(Wg.T)
    order = []
    for fg in range(NH):
        order += list(range(fg * 128, (fg + 1) * 128)) + \
                 list(range(HID + fg * 128, HID + (fg + 1) * 128))
    wgT_i = np.ascontiguousarray(wgT[:, order])
    woutT = np.ascontiguousarray(Wout.T)
    cwc = np.ascontiguousarray(cw.reshape(NH, 128, 4).transpose(1, 0, 2))
    cbc = np.ascontiguousarray(cb.reshape(NH, 128).T)
    bgc = np.ascontiguousarray(bg.reshape(2 * NH, 128).T)
    fbc = np.ascontiguousarray(fb.reshape(NH, 128).T)
    return winT_hg, wgT_i, woutT, cwc, cbc, bgc, fbc


def host_mlp_weights(wgrow, wshrink):
    GH = wgrow.shape[0] // 2
    wt = np.ascontiguousarray(wgrow.T)
    NG = GH // 128
    order = []
    for fg in range(NG):
        order += list(range(fg * 128, (fg + 1) * 128)) + \
                 list(range(GH + fg * 128, GH + (fg + 1) * 128))
    wt_i = np.ascontiguousarray(wt[:, order])
    ws_t = np.ascontiguousarray(wshrink.T)
    return wt_i, ws_t


def host_col(v):
    return np.ascontiguousarray(v.reshape(-1, 128).T)


def host_smqa_weights(Wq, Wkv, Wo):
    return (np.ascontiguousarray(Wq.T), np.ascontiguousarray(Wkv.T[:, 0:HD]),
            np.ascontiguousarray(Wkv.T[:, HD:2 * HD]), np.ascontiguousarray(Wo.T))


def host_rotary(tok_off, T=T_CORE):
    d = np.arange(0, HD, 2)
    inv = 1.0 / 10000.0 ** (d / HD)
    t = np.arange(tok_off, tok_off + T, dtype=np.float64)
    fr2 = np.repeat(t[:, None] * inv[None, :], 2, axis=1)
    return (np.ascontiguousarray(np.cos(fr2).T).astype(np.float32),
            np.ascontiguousarray(np.sin(fr2).T).astype(np.float32))


def host_rot_mat():
    P = np.zeros((HD, HD), np.float32)
    for i in range(0, HD, 2):
        P[i, i + 1] = -1.0
        P[i + 1, i] = 1.0
    return np.ascontiguousarray(P.T)


def host_masks():
    k = np.arange(128)[:, None]
    q = np.arange(512)[None, :]
    mL = np.stack([(128 * m + k >= q).astype(np.float32) for m in range(4)])
    mR = np.stack([(128 * m + k <= q).astype(np.float32) for m in range(4)])
    return mL, mR


def host_moe_weights(gate_w, b1, b2):
    gwT = np.ascontiguousarray(gate_w.T)
    b1c = np.ascontiguousarray(b1.reshape(E, NF, 128).transpose(2, 0, 1))
    b2c = np.ascontiguousarray(b2.reshape(E, 8, 128).transpose(2, 0, 1))
    sel = np.zeros((8, E * 128), np.float32)
    for e in range(E):
        sel[e, e * 128:(e + 1) * 128] = 1.0
    return gwT, b1c, b2c, np.ascontiguousarray(sel)


def host_l1_weights(Wq, bq, Wk, bk, Wv, bv, Wf, A1, A2, fb):
    col = lambda v: np.ascontiguousarray(v.reshape(8, 128).T)
    return (np.ascontiguousarray(Wq), col(bq), np.ascontiguousarray(Wk), col(bk),
            np.ascontiguousarray(Wv), col(bv), np.ascontiguousarray(Wf),
            np.ascontiguousarray(A1), np.ascontiguousarray(A2), col(fb))


def host_l1_masks(j):
    k = np.arange(128)[:, None]
    q = np.arange(512)[None, :]
    qg = 512 * (q // 256) + (q % 256)
    out = []
    for r0 in range(8):
        m = r0 - 2 * j
        out.append((128 * m + k <= qg).astype(np.float32))
    return np.stack(out)


def host_interleave_tokens(j, T2=4096):
    g = np.arange(T2)
    return g[(g // 256) % 2 == j]



NC = 8
HID = 1536
NH = HID // 128
RG = [[0, 1], [2, 3], [4, 5], [6, 7]]
QH = 8
HD = 128
W = 1024
ISQ = 1.0 / np.sqrt(128.0)
E = 8
F = 4096
NF = F // 128
TH = T_CORE // 2
BIG = 1e30
NHD = 16
HD1 = 64


def subtiles(qt):
    if qt == 0:
        return [(ki, ("L", ki)) for ki in range(4)] + \
               [(ki, None) for ki in range(4, 8)] + \
               [(ki, ("R", ki - 8)) for ki in range(8, 12)]
    return [(ki, ("L", ki - 4)) for ki in range(4, 8)] + \
           [(ki, None) for ki in range(8, 12)] + \
           [(ki, ("R", ki - 12)) for ki in range(12, 16)]


def emit_prep(b, D, T=T_CORE):
    nc = b.nc
    nft = T // FT
    ntc = T // 128

    x_in = D["x_in"]
    identf = D["identf"]
    xT_o = D["xT"]
    xa0_o = D["xa0"]
    xnb_o = D["xnb"]

    xT = b.sb("xTs", [128, NC, T])
    xb = b.sb("xbs", [128, NC, T], BF16)
    sq = b.sb("sqs", [128, NC, T], BF16)
    xtok = b.sb("xtok", [128, 4, 1024])
    idt = b.sb("idt", [128, 128])
    arow = b.sb("arow", [1, T])
    brow = b.sb("brow", [1, T])
    onescol_bf = b.sb("onescol_bf", [128, 1], BF16)
    onesrow_f = b.sb("onesrow_f", [1, 128])
    epsrow = b.sb("epsrow", [1, 1])
    xac = b.sb("xac", [128, 2, T])
    xnc = b.sb("xnc", [128, 2, T], BF16)

    pA = b.psum("pA", (128, 2048))
    pB = b.psum("pB", (128, 2048))

    s_ld4 = [b.sem(f"s_ld{i}") for i in range(4)]
    s_sy = b.sem("s_sy")
    s_st2 = [b.sem(f"s_st{i}") for i in range(2)]
    s_pe = b.sem("s_pe")
    s_ac = b.sem("s_ac")
    s_dv = b.sem("s_dv")

    L = {}
    pe = 0
    L["tp"] = {}
    for tc in range(ntc):
        pe += 1; L["tp"][(tc, 0)] = pe
        pe += 1; L["tp"][(tc, 1)] = pe
    L["sum_done"] = {}
    for tc in range(ntc):
        pe += 1; L["sum_done"][tc] = pe
        pe += 1
    L["stats_done"] = pe
    pe += 8; L["bcast_done"] = pe

    dv = 0
    L["evac"] = {}
    for tc in range(ntc):
        dv += 1; L["evac"][(tc, 0)] = dv
        dv += 1; L["evac"][(tc, 1)] = dv
        dv += 1; L["cast"] = dv
        L["cast", tc] = dv
    L["rows_done"] = dv + 1

    block = b.block()

    @block.sync
    def _(sync):
        sync.dma_start(idt[:], identf).then_inc(s_sy.h, 16)
        s_sy.n += 16
        for tc in range(ntc):
            s = tc % 4
            if tc >= 4:
                sync.wait_ge(s_pe.h, L["tp"][(tc - 4, 1)])
            sync.dma_start(xtok[:, s, :], x_in[tc * 128:(tc + 1) * 128, :]
                           ).then_inc(s_ld4[s].h, 16)
            s_ld4[s].n += 16

    @block.tensor
    def _(t):
        t.wait_ge(s_sy.h, 16)
        for tc in range(ntc):
            s = tc % 4
            t.wait_ge(s_ld4[s].h, 16 * (tc // 4 + 1))
            for bk in range(2):
                if tc >= 1 or bk >= 1:
                    pass
                prev = (tc * 2 + bk) - 2
                if prev >= 0:
                    t.wait_ge(s_dv.h, L["evac"][(prev // 2, prev % 2)])
                for k in range(4):
                    kk = bk * 4 + k
                    tp = t.transpose(pA[:, (bk * 4 + k - bk * 4) * 128:
                                        (bk * 4 + k - bk * 4 + 1) * 128] if False else
                                     (pA if bk == 0 else pB)[:, k * 128:(k + 1) * 128],
                                     xtok[:, s, kk * 128:(kk + 1) * 128], idt[:])
                tp.then_inc(s_pe.h, 1)
                s_pe.n += 1

    @block.vector
    def _(v):
        for tc in range(ntc):
            for bk in range(2):
                v.wait_ge(s_pe.h, L["tp"][(tc, bk)])
                if bk == 0 and tc >= 1:
                    v.wait_ge(s_dv.h, L["cast", tc - 1])
                v.tensor_copy(xT[:, bk * 4:(bk + 1) * 4, tc * 128:(tc + 1) * 128],
                              (pA if bk == 0 else pB)[:, 0:512].rearrange("p (c t) -> p c t", c=4)
                              ).then_inc(s_dv.h, 1)
                s_dv.n += 1
            v.wait_ge(s_dv.h, s_dv.n)
            v.tensor_copy(xb[:, :, tc * 128:(tc + 1) * 128], xT[:, :, tc * 128:(tc + 1) * 128]
                          ).then_inc(s_dv.h, 1)
            s_dv.n += 1

    @block.scalar
    def _(sc):
        for tc in range(ntc):
            sc.wait_ge(s_dv.h, L["cast", tc])
            sc.activation(sq[:, :, tc * 128:(tc + 1) * 128], xb[:, :, tc * 128:(tc + 1) * 128],
                          AF.Square).then_inc(s_ac.h, 1)
            s_ac.n += 1

    @block.vector
    def _(v):
        v.memset(onescol_bf[:], 1.0)
        v.memset(epsrow[:], 1e-5)
        v.memset(onesrow_f[:], 1.0).then_inc(s_dv.h, 1)
        s_dv.n += 1
    ones_done = s_dv.n

    @block.tensor
    def _(t):
        t.wait_ge(s_dv.h, ones_done)
        t.wait_ge(s_dv.h, L["cast", ntc - 1])
        for tc in range(ntc):
            t.wait_ge(s_ac.h, tc + 1)
            for k in range(NC):
                mm = t.matmul(pA[0:1, tc * 128:(tc + 1) * 128], onescol_bf[:],
                              xb[:, k, tc * 128:(tc + 1) * 128], start=(k == 0), stop=(k == NC - 1))
            mm.then_inc(s_pe.h, 1)
            s_pe.n += 1
            for k in range(NC):
                mm = t.matmul(pB[0:1, tc * 128:(tc + 1) * 128], onescol_bf[:],
                              sq[:, k, tc * 128:(tc + 1) * 128], start=(k == 0), stop=(k == NC - 1))
            mm.then_inc(s_pe.h, 1)
            s_pe.n += 1
    assert s_pe.n == L["stats_done"]

    @block.vector
    def _(v):
        v.wait_ge(s_pe.h, L["stats_done"])
        v.tensor_scalar(arow[:], pA[0:1, :], 1.0 / 1024.0, None, ALU.mult).then_inc(s_dv.h, 1)
        s_dv.n += 1
        v.wait_ge(s_dv.h, s_dv.n)
        v.tensor_tensor(brow[:], arow[:], arow[:], ALU.mult).then_inc(s_dv.h, 1)
        s_dv.n += 1
        v.wait_ge(s_dv.h, s_dv.n)
        v.scalar_tensor_tensor(brow[:], pB[0:1, :], 1.0 / 1024.0, brow[:],
                               ALU.mult, ALU.subtract).then_inc(s_dv.h, 1)
        s_dv.n += 1
    var_done = s_dv.n

    @block.scalar
    def _(sc):
        sc.wait_ge(s_dv.h, var_done)
        sc.activation(brow[:], brow[:], AF.Sqrt, bias=epsrow[0:1, 0:1]).then_inc(s_ac.h, 1)
        s_ac.n += 1
    sqrt_done = s_ac.n

    @block.vector
    def _(v):
        v.wait_ge(s_ac.h, sqrt_done)
        v.reciprocal(brow[:], brow[:]).then_inc(s_dv.h, 1)
        s_dv.n += 1
        v.wait_ge(s_dv.h, s_dv.n)
        v.tensor_tensor(arow[:], arow[:], brow[:], ALU.mult).then_inc(s_dv.h, 1)
        s_dv.n += 1
    rows_done = s_dv.n

    @block.tensor
    def _(t):
        t.wait_ge(s_dv.h, rows_done)
        for f in range(nft):
            t.matmul(pA[:, f * FT:(f + 1) * FT], onesrow_f[:], brow[0:1, f * FT:(f + 1) * FT],
                     start=True, stop=True).then_inc(s_pe.h, 1)
            s_pe.n += 1
        for f in range(nft):
            t.matmul(pB[:, f * FT:(f + 1) * FT], onesrow_f[:], arow[0:1, f * FT:(f + 1) * FT],
                     start=True, stop=True).then_inc(s_pe.h, 1)
            s_pe.n += 1
    assert s_pe.n == L["bcast_done"]

    @block.vector
    def _(v):
        v.wait_ge(s_pe.h, L["bcast_done"])
        for k in range(NC):
            sl = k % 2
            if k >= 2:
                v.wait_ge(s_st2[sl].h, 32 * (k // 2))
            v.tensor_tensor(xac[:, sl, :], xT[:, k, :], pA[:], ALU.mult).then_inc(s_dv.h, 1)
            s_dv.n += 1
            v.wait_ge(s_dv.h, s_dv.n)
            v.tensor_tensor(xac[:, sl, :], xac[:, sl, :], pB[:], ALU.subtract).then_inc(s_dv.h, 1)
            s_dv.n += 1
            v.wait_ge(s_dv.h, s_dv.n)
            v.tensor_copy(xnc[:, sl, :], xac[:, sl, :]).then_inc(s_dv.h, 1)
            s_dv.n += 1
    apply_base = s_dv.n - 3 * NC

    @block.sync
    def _(sync):
        sync.wait_ge(s_dv.h, L["cast", ntc - 1])
        sync.dma_start(xT_o.rearrange("(c p) t -> p c t", p=128), xT[:]).then_inc(s_sy.h, 16)
        s_sy.n += 16
        for k in range(NC):
            sl = k % 2
            sync.wait_ge(s_dv.h, apply_base + 3 * k + 3)
            sync.dma_start(xa0_o.rearrange("(c p) t -> p c t", p=128)[:, k, :], xac[:, sl, :]
                           ).then_inc(s_st2[sl].h, 16)
            sync.dma_start(xnb_o.rearrange("(c p) t -> p c t", p=128)[:, k, :], xnc[:, sl, :]
                           ).then_inc(s_st2[sl].h, 16)
            s_st2[sl].n += 32
        for sl in range(2):
            sync.wait_ge(s_st2[sl].h, s_st2[sl].n)
        sync.wait_ge(s_sy.h, s_sy.n)





def emit_hawk(b, D, T=T_CORE):
    nc = b.nc
    nft = T // FT
    TS = T + 4

    xa_in = D["xa"]
    winT = D["winT"]
    wgTd = D["wgT"]
    woutTd = D["woutT"]
    cwc_d = D["cwc"]
    cbc_d = D["cbc"]
    bgc_d = D["bgc"]
    fbc_d = D["fbc"]
    gnc_d = D["gnc"]
    iscol_d = D["iscol"]
    xa_out = D["xa_out"]

    tail_in = nc.dram_tensor(b.pfx + "tail_in", [HID, 3], F32)
    tail_out = nc.dram_tensor(b.pfx + "tail_out", [2 * HID, 3], F32)
    st_in = [nc.dram_tensor(f"st_in{fg}", [128, 1], F32) for fg in range(NH)]
    st_out = [nc.dram_tensor(f"st_out{fg}", [256, 1], F32) for fg in range(NH)]

    ARQ = b.sb("ARQ", [128, 16384])
    AR1 = b.sb("AR1", [128, 16384], BF16)
    WRB = b.sb("WRB", [128, 8192], BF16)
    gg = b.sb("gg", [128, NH, T], BF16)
    hring = b.sb("hring", [128, 2, TS], BF16)
    A = b.sb("A", [128, T])
    convacc = b.sb("convacc", [128, T])
    C = b.sb("C", [128, T])
    wgr = b.sb("wgr", [128, 2, NH, 256], BF16)
    tailb = b.sb("tailb", [128, NH, 3], BF16)
    hp = b.sb("hp", [128, NH, 6], BF16)
    gcol = b.sb("gcol", [128, NC])
    cwcol = b.sb("cwcol", [128, NH, 4])
    cbcol = b.sb("cbcol", [128, NH])
    bgcol = b.sb("bgcol", [128, 2 * NH])
    spcol = b.sb("spcol", [128, NH])
    stcol = b.sb("stcol", [128, NH])
    strecv = b.sb("strecv", [128, NH])
    iscol = b.sb("iscl", [128, 1])
    onescol_bf = b.sb("onescol_bf", [128, 1], BF16)
    onescol_f = b.sb("onescol_f", [128, 1])
    epscol = b.sb("epscol", [128, 1])
    onesrow_f = b.sb("onesrow_f", [1, 128])
    zcol = b.sb("zcol", [128, 1])

    xa = ARQ[:].rearrange("p (c t) -> p c t", c=NC)
    hc = ARQ[:, 0:12288].bitcast(BF16).rearrange("p (c t) -> p c t", c=NH)
    Dv = ARQ[:, 12288:14336].bitcast(BF16).rearrange("p (s t) -> p s t", s=2)
    xar0 = ARQ[:, 14336:16384]
    xar1 = ARQ[:, 12288:14336]
    xh = AR1[:].rearrange("p (c t) -> p c t", c=NC)
    winr = WRB[:].rearrange("p (s c o) -> p s c o", s=2, c=NC)

    def u_sl(fg, lo=0, hi=None):
        hi = T if hi is None else hi
        if fg < 8:
            return AR1[:, fg * T + lo:fg * T + hi]
        return WRB[:, (fg - 8) * T + lo:(fg - 8) * T + hi]

    B = convacc
    srow = convacc[0:1, :]

    pA = b.psum("pA", (128, 2048))
    pB = b.psum("pB", (128, 2048))
    pb = [pA[:, f * FT:(f + 1) * FT] for f in range(4)] + \
         [pB[:, f * FT:(f + 1) * FT] for f in range(4)]

    s_sy = b.sem("s_sy")
    s_wc = [b.sem(f"s_wc{i}") for i in range(2)]
    s_wg2 = [b.sem(f"s_wg{i}") for i in range(2)]
    s_xr2 = [b.sem(f"s_xr{i}") for i in range(2)]
    s_so2 = [b.sem(f"s_so{i}") for i in range(2)]
    s_cc = b.sem("s_cc")
    s_ag = b.sem("s_ag")
    s_pe = b.sem("s_pe")
    s_ac = b.sem("s_ac")
    s_dv = b.sem("s_dv")

    L = {}
    pe = 0
    pe += nft; L["stats"] = pe
    pe += nft; L["bcast"] = pe
    L["h_base"] = pe
    L["h_grp"] = [pe + j + 1 for j in range(NH * nft)]
    pe += NH * nft
    L["gate_base"] = pe
    L["gate_grp"] = [pe + j + 1 for j in range(NH * nft)]
    pe += NH * nft
    L["wg_base"] = pe
    L["wg_F"] = [pe + fg * 8 + 7 for fg in range(NH)]
    L["wg_I"] = [pe + fg * 8 + 8 for fg in range(NH)]
    pe += NH * 8
    L["wout_base"] = pe
    L["wout_grp"] = [pe + j + 1 for j in range(8 * nft)]
    pe += 8 * nft

    ac = 0
    ac += 1; L["sq"] = ac
    ac += 2
    ac += 1; L["srow"] = ac
    L["gelu"] = [ac + j + 1 for j in range(NH * nft)]
    ac += NH * nft
    L["a1"] = [ac + fg * 5 + 1 for fg in range(NH)]
    L["a2"] = [ac + fg * 5 + 2 for fg in range(NH)]
    L["a3"] = [ac + fg * 5 + 3 for fg in range(NH)]
    L["a4"] = [ac + fg * 5 + 4 for fg in range(NH)]
    L["a5"] = [ac + fg * 5 + 5 for fg in range(NH)]
    ac += NH * 5

    dv = 0
    dv += 1; L["cast"] = dv
    dv += 1; L["srowcp"] = dv
    dv += 1; L["recip"] = dv
    dv += 1; L["norm"] = dv
    L["h_evac"] = {}
    L["conv_main"] = {}
    for oc in range(NH):
        for f in range(nft):
            dv += 1; L["h_evac"][(oc, f)] = dv
        dv += 1
        dv += 1
        dv += 5; L["conv_main"][oc] = dv
    dv += 1; L["tailmask"] = dv
    dv += NH * 5; L["convfix"] = dv
    D9 = {}
    Dd = {}
    for fg in range(NH):
        if fg >= 2:
            dv += 1; D9[("a", fg - 2)] = dv
            dv += 1; D9[("b", fg - 2)] = dv
        for k in range(1, 9):
            dv += 1; Dd[(k, fg)] = dv
    for fg in (NH - 2, NH - 1):
        dv += 1; D9[("a", fg)] = dv
        dv += 1; D9[("b", fg)] = dv
    L["scan_end"] = dv
    L["wout_add_i"] = [dv + i + 1 for i in range(NC)]
    dv += NC

    block = b.block()

    @block.sync
    def _(sync):
        for dst, src in [(xa[:], xa_in.rearrange("(c p) t -> p c t", p=128)),
                         (gcol[:], gnc_d), (cwcol[:], cwc_d), (cbcol[:], cbc_d),
                         (bgcol[:], bgc_d), (spcol[:], fbc_d), (iscol[:], iscol_d)]:
            sync.dma_start(dst, src).then_inc(s_sy.h, 16)
            s_sy.n += 16
    ld_done = s_sy.n

    @block.vector
    def _(v):
        v.memset(onescol_bf[:], 1.0)
        v.memset(onescol_f[:], 1.0)
        v.memset(onesrow_f[:], 1.0)
        v.memset(epscol[:], 1.000001)
        v.memset(zcol[:], 0.0)
        v.wait_ge(s_sy.h, ld_done)
        v.tensor_copy(xh[:], xa[:]).then_inc(s_dv.h, 1)
        s_dv.n += 1

    @block.scalar
    def _(sc):
        sc.wait_ge(s_dv.h, L["cast"])
        sc.activation(gg[:, 0:NC, :], xh[:], AF.Square).then_inc(s_ac.h, 1)
        s_ac.n += 1
        sc.activation(spcol[:], spcol[:], AF.Exp).then_inc(s_ac.h, 1)
        s_ac.n += 1
        sc.wait_ge(s_ac.h, s_ac.n)
        sc.activation(spcol[:], spcol[:], AF.Ln, bias=onescol_f[:, 0:1]).then_inc(s_ac.h, 1)
        s_ac.n += 1

    @block.tensor
    def _(t):
        t.wait_ge(s_ac.h, L["sq"])
        for f in range(nft):
            for k in range(NC):
                mm = t.matmul(pb[f][0:1, :], onescol_bf[:], gg[:, k, f * FT:(f + 1) * FT],
                              start=(k == 0), stop=(k == NC - 1))
            mm.then_inc(s_pe.h, 1)
            s_pe.n += 1

    @block.vector
    def _(v):
        v.wait_ge(s_pe.h, L["stats"])
        for f in range(nft):
            cp = v.tensor_copy(srow[0:1, f * FT:(f + 1) * FT], pb[f][0:1, :])
        cp.then_inc(s_dv.h, 1)
        s_dv.n += 1
        v.wait_ge(s_dv.h, s_dv.n)
        v.reciprocal(srow[:], srow[:]).then_inc(s_dv.h, 1)
        s_dv.n += 1

    @block.scalar
    def _(sc):
        sc.wait_ge(s_dv.h, L["recip"])
        sc.activation(srow[:], srow[:], AF.Sqrt, scale=1024.0).then_inc(s_ac.h, 1)
        s_ac.n += 1

    @block.tensor
    def _(t):
        t.wait_ge(s_ac.h, L["srow"])
        for f in range(nft):
            t.matmul(pA[:, f * FT:(f + 1) * FT], onesrow_f[:], srow[0:1, f * FT:(f + 1) * FT],
                     start=True, stop=True).then_inc(s_pe.h, 1)
            s_pe.n += 1

    @block.vector
    def _(v):
        v.wait_ge(s_pe.h, L["bcast"])
        for k in range(NC):
            ins = v.scalar_tensor_tensor(xh[:, k, :], xa[:, k, :], gcol[:, k:k + 1],
                                         pA[:], ALU.mult, ALU.mult)
        ins.then_inc(s_dv.h, 1)
        s_dv.n += 1
    assert s_dv.n == L["norm"]

    @block.gpsimd
    def _(g):
        for q in range(6):
            s = q % 2
            if q >= 2:
                if q - 2 < 3:
                    g.wait_ge(s_pe.h, L["h_grp"][(q - 2 + 1) * 4 * nft - 1])
                else:
                    g.wait_ge(s_pe.h, L["gate_grp"][(q - 5 + 1) * 4 * nft - 1])
            g.dma_start(winr[:, s, :, :],
                        winT.rearrange("(c p) o -> p c o", p=128)[:, :, q * 512:(q + 1) * 512]
                        ).then_inc(s_wc[s].h, 16)
            s_wc[s].n += 16

    @block.tensor
    def _(t):
        t.wait_ge(s_dv.h, L["norm"])
        for oc in range(NH):
            q, ol = divmod(oc, 4)
            if ol == 0:
                t.wait_ge(s_wc[q % 2].h, 16 * (q // 2 + 1))
            if oc >= 2:
                t.wait_ge(s_dv.h, L["conv_main"][oc - 2])
            for f in range(nft):
                j = oc * nft + f
                jj = j % 8
                if j >= 8:
                    t.wait_ge(s_dv.h, L["h_evac"][(oc - 2, f)])
                for k in range(NC):
                    mm = t.matmul(pb[jj][:], winr[:, q % 2, k, ol * 128:(ol + 1) * 128],
                                  xh[:, k, f * FT:(f + 1) * FT], start=(k == 0), stop=(k == NC - 1))
                mm.then_inc(s_pe.h, 1)
                s_pe.n += 1

    @block.vector
    def _(v):
        for oc in range(NH):
            sl = oc % 2
            for f in range(nft):
                j = oc * nft + f
                v.wait_ge(s_pe.h, L["h_grp"][j])
                v.tensor_copy(hring[:, sl, 4 + f * FT:4 + (f + 1) * FT], pb[j % 8][:]
                              ).then_inc(s_dv.h, 1)
                s_dv.n += 1
            v.wait_ge(s_dv.h, s_dv.n)
            v.tensor_copy(tailb[:, oc, :], hring[:, sl, TS - 3:TS]).then_inc(s_dv.h, 1)
            s_dv.n += 1
            v.wait_ge(s_dv.h, s_dv.n)
            v.tensor_copy(hp[:, oc, 3:6], hring[:, sl, 4:7]).then_inc(s_dv.h, 1)
            s_dv.n += 1
            v.wait_ge(s_dv.h, s_dv.n)
            v.scalar_tensor_tensor(convacc[0:128, 0:T - 3], hring[:, sl, 4:4 + T - 3],
                                   cwcol[:, oc, 0:1], zcol[:, 0:1].broadcast_to([128, T - 3]),
                                   ALU.mult, ALU.add).then_inc(s_dv.h, 1)
            s_dv.n += 1
            for kk in range(1, 4):
                v.wait_ge(s_dv.h, s_dv.n)
                v.scalar_tensor_tensor(convacc[0:128, 0:T - 3], hring[:, sl, 4 + kk:4 + kk + T - 3],
                                       cwcol[:, oc, kk:kk + 1], convacc[0:128, 0:T - 3],
                                       ALU.mult, ALU.add).then_inc(s_dv.h, 1)
                s_dv.n += 1
            v.wait_ge(s_dv.h, s_dv.n)
            v.tensor_scalar(hc[:, oc, 3:T], convacc[0:128, 0:T - 3], cbcol[:, oc:oc + 1],
                            None, ALU.add).then_inc(s_dv.h, 1)
            s_dv.n += 1
            assert s_dv.n == L["conv_main"][oc]

    @block.gpsimd
    def _(g):
        g.wait_ge(s_dv.h, L["conv_main"][NH - 1])
        g.dma_start(tail_in.ap().rearrange("(c p) k -> p c k", p=128), tailb[:]
                    ).then_inc(s_ag.h, 16)
        s_ag.n += 16
        g.wait_ge(s_ag.h, s_ag.n)
        g.collective_compute("AllGather", ALU.bypass, replica_groups=RG,
                             ins=[tail_in.ap().opt()], outs=[tail_out.ap().opt()]
                             ).then_inc(s_cc.h, 1)
        s_cc.n += 1
        g.wait_ge(s_cc.h, s_cc.n)
        g.dma_start(hp[:, :, 0:3],
                    tail_out.ap()[0:HID, 0:3].rearrange("(c p) k -> p c k", p=128)
                    ).then_inc(s_ag.h, 16)
        s_ag.n += 16
    tail_ag = s_ag.n
    cc_tail = s_cc.n

    @block.vector
    def _(v):
        v.wait_ge(s_ag.h, tail_ag)
        v.tensor_scalar(hp[:, :, 0:3], hp[:, :, 0:3], iscol[:, 0:1], None, ALU.mult
                        ).then_inc(s_dv.h, 1)
        s_dv.n += 1
        assert s_dv.n == L["tailmask"]
        for oc in range(NH):
            v.wait_ge(s_dv.h, s_dv.n)
            v.scalar_tensor_tensor(C[0:128, 0:3], hp[:, oc, 0:3], cwcol[:, oc, 0:1],
                                   zcol[:, 0:1].broadcast_to([128, 3]), ALU.mult, ALU.add
                                   ).then_inc(s_dv.h, 1)
            s_dv.n += 1
            for kk in range(1, 4):
                v.wait_ge(s_dv.h, s_dv.n)
                v.scalar_tensor_tensor(C[0:128, 0:3], hp[:, oc, kk:kk + 3], cwcol[:, oc, kk:kk + 1],
                                       C[0:128, 0:3], ALU.mult, ALU.add).then_inc(s_dv.h, 1)
                s_dv.n += 1
            v.wait_ge(s_dv.h, s_dv.n)
            v.tensor_scalar(hc[:, oc, 0:3], C[0:128, 0:3], cbcol[:, oc:oc + 1], None, ALU.add
                            ).then_inc(s_dv.h, 1)
            s_dv.n += 1
        assert s_dv.n == L["convfix"]

    @block.tensor
    def _(t):
        for oc in range(NH):
            q, ol = divmod(oc + NH, 4)
            if ol == 0:
                t.wait_ge(s_wc[q % 2].h, 16 * (q // 2 + 1))
            for f in range(nft):
                j = oc * nft + f
                jj = j % 8
                if j >= 8:
                    t.wait_ge(s_ac.h, L["gelu"][j - 8])
                else:
                    t.wait_ge(s_dv.h, L["h_evac"][(10 + j // 4, j % 4)])
                for k in range(NC):
                    mm = t.matmul(pb[jj][:], winr[:, q % 2, k, ol * 128:(ol + 1) * 128],
                                  xh[:, k, f * FT:(f + 1) * FT], start=(k == 0), stop=(k == NC - 1))
                mm.then_inc(s_pe.h, 1)
                s_pe.n += 1

    @block.scalar
    def _(sc):
        for j in range(NH * nft):
            oc, f = divmod(j, nft)
            jj = j % 8
            sc.wait_ge(s_pe.h, L["gate_grp"][j])
            sc.activation(gg[:, oc, f * FT:(f + 1) * FT], pb[jj][:], AF.Gelu
                          ).then_inc(s_ac.h, 1)
            s_ac.n += 1

    def emit_wg_load(g, fg):
        s = fg % 2
        if fg >= 2:
            g.wait_ge(s_pe.h, L["wg_I"][fg - 2])
        g.dma_start(wgr[:, s, :, :],
                    wgTd.rearrange("(c p) o -> p c o", p=128)[:, :, fg * 256:(fg + 1) * 256]
                    ).then_inc(s_wg2[s].h, 16)
        s_wg2[s].n += 16

    def emit_state_ag(g, fg):
        g.wait_ge(s_dv.h, Dd[(5, fg)])
        g.dma_start(st_in[fg].ap(), stcol[:, fg:fg + 1]).then_inc(s_ag.h, 16)
        s_ag.n += 16
        g.wait_ge(s_ag.h, s_ag.n)
        g.collective_compute("AllGather", ALU.bypass, replica_groups=RG,
                             ins=[st_in[fg].ap().opt()], outs=[st_out[fg].ap().opt()]
                             ).then_inc(s_cc.h, 1)
        s_cc.n += 1
        g.wait_ge(s_cc.h, s_cc.n)
        g.dma_start(strecv[:, fg:fg + 1], st_out[fg].ap()[0:128, :]).then_inc(s_ag.h, 16)
        s_ag.n += 16
        g.wait_ge(s_ag.h, s_ag.n)

    @block.gpsimd
    def _(g):
        emit_wg_load(g, 0)
        emit_wg_load(g, 1)
        for fg in range(NH):
            if fg + 2 < NH:
                emit_wg_load(g, fg + 2)
            emit_state_ag(g, fg)

    @block.tensor
    def _(t):
        t.wait_ge(s_dv.h, L["convfix"])
        for fg in range(NH):
            s = fg % 2
            t.wait_ge(s_wg2[s].h, 16 * (fg // 2 + 1))
            if fg == 0:
                t.wait_ge(s_ac.h, L["gelu"][NH * nft - 1])
            else:
                t.wait_ge(s_ac.h, L["a2"][fg - 1])
            for f in range(nft):
                for k in range(NH):
                    mm = t.matmul(pA[:, f * FT:(f + 1) * FT], wgr[:, s, k, 0:128],
                                  hc[:, k, f * FT:(f + 1) * FT], start=(k == 0), stop=(k == NH - 1))
                mm.then_inc(s_pe.h, 1)
                s_pe.n += 1
                for k in range(NH):
                    mm = t.matmul(pB[:, f * FT:(f + 1) * FT], wgr[:, s, k, 128:256],
                                  hc[:, k, f * FT:(f + 1) * FT], start=(k == 0), stop=(k == NH - 1))
                mm.then_inc(s_pe.h, 1)
                s_pe.n += 1
    assert s_pe.n == L["wout_base"]

    @block.scalar
    def _(sc):
        for fg in range(NH):
            sc.wait_ge(s_pe.h, L["wg_F"][fg])
            if fg >= 1:
                sc.wait_ge(s_dv.h, Dd[(6, fg - 1)])
            sc.activation(A[:], pA[:], AF.Sigmoid, bias=bgcol[:, fg:fg + 1]).then_inc(s_ac.h, 1)
            s_ac.n += 1
            sc.wait_ge(s_pe.h, L["wg_I"][fg])
            if fg >= 1:
                sc.wait_ge(s_dv.h, Dd[(4, fg - 1)])
            sc.activation(C[:], pB[:], AF.Sigmoid, bias=bgcol[:, NH + fg:NH + fg + 1]
                          ).then_inc(s_ac.h, 1)
            s_ac.n += 1
            sc.wait_ge(s_dv.h, Dd[(1, fg)])
            sc.activation(A[:], A[:], AF.Exp).then_inc(s_ac.h, 1)
            s_ac.n += 1
            if fg >= 1:
                sc.wait_ge(s_dv.h, Dd[(8, fg - 1)])
            sc.wait_ge(s_ac.h, s_ac.n)
            sc.activation(B[:], A[:], AF.Square).then_inc(s_ac.h, 1)
            s_ac.n += 1
            sc.wait_ge(s_ac.h, s_ac.n)
            sc.activation(B[:], B[:], AF.Sqrt, scale=-1.0, bias=epscol[:, 0:1]).then_inc(s_ac.h, 1)
            s_ac.n += 1

    @block.vector
    def _(v):
        def d9(fgx):
            v.wait_ge(s_ag.h, tail_ag + 16 * (2 * fgx + 2))
            v.tensor_scalar(strecv[:, fgx:fgx + 1], strecv[:, fgx:fgx + 1], iscol[:, 0:1],
                            None, ALU.mult).then_inc(s_dv.h, 1)
            s_dv.n += 1
            assert s_dv.n == D9[("a", fgx)]
            v.wait_ge(s_dv.h, s_dv.n)
            v.scalar_tensor_tensor(u_sl(fgx), Dv[:, fgx % 2, :], strecv[:, fgx:fgx + 1],
                                   u_sl(fgx), ALU.mult, ALU.add).then_inc(s_dv.h, 1)
            s_dv.n += 1
            assert s_dv.n == D9[("b", fgx)]

        for fg in range(NH):
            if fg >= 2:
                d9(fg - 2)
            v.wait_ge(s_ac.h, L["a1"][fg])
            v.tensor_scalar(A[:], A[:], spcol[:, fg:fg + 1], -8.0, ALU.mult, ALU.mult
                            ).then_inc(s_dv.h, 1)
            s_dv.n += 1
            assert s_dv.n == Dd[(1, fg)]
            v.wait_ge(s_ac.h, L["a5"][fg])
            v.tensor_tensor(C[:], B[:], C[:], ALU.mult).then_inc(s_dv.h, 1)
            s_dv.n += 1
            v.wait_ge(s_dv.h, s_dv.n)
            v.tensor_tensor(C[:], C[:], hc[:, fg, :], ALU.mult).then_inc(s_dv.h, 1)
            s_dv.n += 1
            v.wait_ge(s_dv.h, s_dv.n)
            v.tensor_tensor_scan(u_sl(fg), A[:], C[:], zcol[:, 0:1], ALU.mult, ALU.add
                                 ).then_inc(s_dv.h, 1)
            s_dv.n += 1
            assert s_dv.n == Dd[(4, fg)]
            v.wait_ge(s_dv.h, s_dv.n)
            v.tensor_copy(stcol[:, fg:fg + 1], u_sl(fg, T - 1, T)).then_inc(s_dv.h, 1)
            s_dv.n += 1
            v.wait_ge(s_dv.h, s_dv.n)
            v.tensor_tensor_scan(B[:], A[:], zcol[:, 0:1].broadcast_to([128, T]),
                                 onescol_f[:, 0:1], ALU.mult, ALU.add).then_inc(s_dv.h, 1)
            s_dv.n += 1
            v.wait_ge(s_dv.h, s_dv.n)
            v.tensor_tensor(u_sl(fg), gg[:, fg, :], u_sl(fg), ALU.mult).then_inc(s_dv.h, 1)
            s_dv.n += 1
            v.wait_ge(s_dv.h, s_dv.n)
            v.tensor_tensor(Dv[:, fg % 2, :], B[:], gg[:, fg, :], ALU.mult).then_inc(s_dv.h, 1)
            s_dv.n += 1
            assert s_dv.n == Dd[(8, fg)]
        d9(NH - 2)
        d9(NH - 1)
    assert s_dv.n == L["scan_end"]

    @block.gpsimd
    def _(g):
        for r in range(4):
            s = r % 2
            if r < 2:
                g.wait_ge(s_pe.h, L["wg_I"][10 + r])
            else:
                g.wait_ge(s_pe.h, L["wout_grp"][(r - 2 + 1) * 2 * nft - 1])
            g.dma_start(wgr[:, s, :, :],
                        woutTd.rearrange("(c p) o -> p c o", p=128)[:, :, r * 256:(r + 1) * 256]
                        ).then_inc(s_wg2[s].h, 16)
            s_wg2[s].n += 16

    def emit_xa_load(sync, i):
        sl = i % 2
        tgt = xar0 if sl == 0 else xar1
        if i == 0:
            sync.wait_ge(s_dv.h, L["norm"])
        if i == 1:
            sync.wait_ge(s_dv.h, L["scan_end"])
        if i >= 2:
            sync.wait_ge(s_so2[sl].h, 16 * (i // 2))
        sync.dma_start(tgt[0:128, 0:T],
                       xa_in.rearrange("(c p) t -> p c t", p=128)[:, i, :]
                       ).then_inc(s_xr2[sl].h, 16)
        s_xr2[sl].n += 16

    def emit_xa_store(sync, i):
        sl = i % 2
        tgt = xar0 if sl == 0 else xar1
        sync.wait_ge(s_dv.h, L["wout_add_i"][i])
        sync.dma_start(xa_out.rearrange("(c p) t -> p c t", p=128)[:, i, :],
                       tgt[0:128, 0:T]).then_inc(s_so2[sl].h, 16)
        s_so2[sl].n += 16

    @block.sync
    def _(sync):
        for i in range(NC):
            if i >= 2:
                emit_xa_store(sync, i - 2)
            emit_xa_load(sync, i)

    @block.tensor
    def _(t):
        t.wait_ge(s_dv.h, L["scan_end"])
        for i in range(NC):
            wq, wl = divmod(i, 2)
            t.wait_ge(s_wg2[wq % 2].h, 16 * (NH // 2) + 16 * (wq // 2 + 1))
            for f in range(nft):
                j = i * nft + f
                jj = j % 8
                if j >= 8:
                    t.wait_ge(s_dv.h, L["wout_add_i"][(j - 8) // nft])
                for k in range(NH):
                    mm = t.matmul(pb[jj][:], wgr[:, wq % 2, k, wl * 128:(wl + 1) * 128],
                                  u_sl(k, f * FT, (f + 1) * FT), start=(k == 0), stop=(k == NH - 1))
                mm.then_inc(s_pe.h, 1)
                s_pe.n += 1

    @block.vector
    def _(v):
        for i in range(NC):
            sl = i % 2
            tgt = xar0 if sl == 0 else xar1
            v.wait_ge(s_xr2[sl].h, 16 * (i // 2 + 1))
            for f in range(nft):
                j = i * nft + f
                v.wait_ge(s_pe.h, L["wout_grp"][j])
                ins = v.tensor_tensor(tgt[0:128, f * FT:(f + 1) * FT],
                                      tgt[0:128, f * FT:(f + 1) * FT], pb[j % 8][:], ALU.add)
            ins.then_inc(s_dv.h, 1)
            s_dv.n += 1
            assert s_dv.n == L["wout_add_i"][i]

    @block.sync
    def _(sync):
        emit_xa_store(sync, NC - 2)
        emit_xa_store(sync, NC - 1)
        for sl in range(2):
            sync.wait_ge(s_so2[sl].h, s_so2[sl].n)



def emit_mlp(b, D, T=T_CORE):
    nc = b.nc
    NC = 8
    GH = 2048
    NG = GH // 128
    nft = T // FT

    xa_in = D["xa"]
    wgt = D["wgt"]
    wst = D["wst"]
    gnc = D["gnc"]
    xa_out = D["xa_out"]

    xa = b.sb("xa", [128, NC, T])
    xg = b.sb("xg", [128, NC, T], BF16)
    u = b.sb("u", [128, NG, T], BF16)
    gncol = b.sb("gncol", [128, NC])
    onescol_bf = b.sb("onescol_bf", [128, 1], BF16)
    onesrow_f = b.sb("onesrow_f", [1, 128])
    srow = b.sb("srow", [1, T])
    gw_ring = b.sb("gw_ring", [128, 2, NC, 512], BF16)
    sw_ring = b.sb("sw_ring", [128, 2, NG, 256], BF16)
    gg_ring = b.sb("gg_ring", [128, 4, FT], BF16)

    p = [b.psum(f"p{i}") for i in range(4)]
    bc = b.psum("bc", (128, 2048))
    pb = p + [bc[:, f * FT:(f + 1) * FT] for f in range(4)]

    s_sy = b.sem("s_sy")
    s_gc2 = [b.sem(f"s_gc{i}") for i in range(2)]
    s_sc2 = [b.sem(f"s_sc{i}") for i in range(2)]
    s_pe = b.sem("s_pe")
    s_ac = b.sem("s_ac")
    s_dv = b.sem("s_dv")

    block = b.block()

    @block.sync
    def _(sync):
        sync.dma_start(xa[:], xa_in.rearrange("(c p) t -> p c t", p=128)).then_inc(s_sy.h, 16)
        s_sy.n += 16
        sync.dma_start(gncol[:], gnc).then_inc(s_sy.h, 16)
        s_sy.n += 16
    ld_done = s_sy.n

    bcast_done_c = 8
    shrink_base_c = bcast_done_c + 2 * NG * nft

    @block.gpsimd
    def _(g):
        for q in range(8):
            s = q % 2
            if q >= 2:
                fg_done = 2 * (q - 2) + 2
                g.wait_ge(s_pe.h, bcast_done_c + fg_done * 2 * nft)
            g.dma_start(gw_ring[:, s, :, :],
                        wgt.rearrange("(c p) o -> p c o", p=128)[:, :, q * 512:(q + 1) * 512]
                        ).then_inc(s_gc2[s].h, 16)
            s_gc2[s].n += 16
        for r in range(4):
            s = r % 2
            if r >= 2:
                i_done = 2 * (r - 2) + 2
                g.wait_ge(s_pe.h, shrink_base_c + i_done * nft)
            g.dma_start(sw_ring[:, s, :, :],
                        wst.rearrange("(c p) o -> p c o", p=128)[:, :, r * 256:(r + 1) * 256]
                        ).then_inc(s_sc2[s].h, 16)
            s_sc2[s].n += 16

    @block.vector
    def _(v):
        v.memset(onescol_bf[:], 1.0)
        v.memset(onesrow_f[:], 1.0)
        v.wait_ge(s_sy.h, ld_done)
        v.tensor_copy(xg[:], xa[:]).then_inc(s_dv.h, 1)
        s_dv.n += 1
    cast_done = s_dv.n

    @block.scalar
    def _(sc):
        sc.wait_ge(s_dv.h, cast_done)
        sc.activation(u[:, 0:NC, :], xg[:], AF.Square).then_inc(s_ac.h, 1)
        s_ac.n += 1
    sq_done = s_ac.n

    @block.tensor
    def _(t):
        t.wait_ge(s_ac.h, sq_done)
        for f in range(nft):
            for k in range(NC):
                mm = t.matmul(p[f][0:1, :], onescol_bf[:], u[:, k, f * FT:(f + 1) * FT],
                              start=(k == 0), stop=(k == NC - 1))
            mm.then_inc(s_pe.h, 1)
            s_pe.n += 1
    stats_done = s_pe.n

    @block.vector
    def _(v):
        v.wait_ge(s_pe.h, stats_done)
        for f in range(nft):
            cp = v.tensor_copy(srow[0:1, f * FT:(f + 1) * FT], p[f][0:1, :])
        cp.then_inc(s_dv.h, 1)
        s_dv.n += 1
        v.wait_ge(s_dv.h, s_dv.n)
        v.reciprocal(srow[:], srow[:]).then_inc(s_dv.h, 1)
        s_dv.n += 1
    recip_done = s_dv.n

    @block.scalar
    def _(sc):
        sc.wait_ge(s_dv.h, recip_done)
        sc.activation(srow[:], srow[:], AF.Sqrt, scale=1024.0).then_inc(s_ac.h, 1)
        s_ac.n += 1
    srow_done = s_ac.n

    @block.tensor
    def _(t):
        t.wait_ge(s_ac.h, srow_done)
        for f in range(nft):
            t.matmul(bc[:, f * FT:(f + 1) * FT], onesrow_f[:], srow[0:1, f * FT:(f + 1) * FT],
                     start=True, stop=True).then_inc(s_pe.h, 1)
            s_pe.n += 1
    bcast_done = s_pe.n
    assert bcast_done == bcast_done_c

    @block.vector
    def _(v):
        v.wait_ge(s_pe.h, bcast_done)
        for k in range(NC):
            ins = v.scalar_tensor_tensor(xg[:, k, :], xa[:, k, :], gncol[:, k:k + 1],
                                         bc[:], ALU.mult, ALU.mult)
        ins.then_inc(s_dv.h, 1)
        s_dv.n += 1
    norm_done = s_dv.n

    @block.tensor
    def _(t):
        t.wait_ge(s_dv.h, norm_done)
        for fg in range(NG):
            q = fg // 2
            fl = fg % 2
            if fl == 0:
                t.wait_ge(s_gc2[q % 2].h, 16 * (q // 2 + 1))
            for f in range(nft):
                j = fg * nft + f
                jj = j % 4
                if j >= 4:
                    t.wait_ge(s_dv.h, norm_done + (j - 4) + 1)
                for k in range(NC):
                    mm = t.matmul(pb[2 * jj][:], gw_ring[:, q % 2, k, (2 * fl) * 128:(2 * fl + 1) * 128],
                                  xg[:, k, f * FT:(f + 1) * FT], start=(k == 0), stop=(k == NC - 1))
                mm.then_inc(s_pe.h, 1)
                s_pe.n += 1
                for k in range(NC):
                    mm = t.matmul(pb[2 * jj + 1][:], gw_ring[:, q % 2, k, (2 * fl + 1) * 128:(2 * fl + 2) * 128],
                                  xg[:, k, f * FT:(f + 1) * FT], start=(k == 0), stop=(k == NC - 1))
                mm.then_inc(s_pe.h, 1)
                s_pe.n += 1

    @block.scalar
    def _(sc):
        for j in range(NG * nft):
            jj = j % 4
            if j >= 4:
                sc.wait_ge(s_dv.h, norm_done + (j - 4) + 1)
            sc.wait_ge(s_pe.h, bcast_done + 2 * j + 1)
            sc.activation(gg_ring[:, jj, :], pb[2 * jj][:], AF.Gelu).then_inc(s_ac.h, 1)
            s_ac.n += 1

    @block.vector
    def _(v):
        for j in range(NG * nft):
            fg, f = divmod(j, nft)
            jj = j % 4
            v.wait_ge(s_ac.h, srow_done + j + 1)
            v.wait_ge(s_pe.h, bcast_done + 2 * j + 2)
            v.tensor_tensor(u[:, fg, f * FT:(f + 1) * FT], gg_ring[:, jj, :],
                            pb[2 * jj + 1][:], ALU.mult).then_inc(s_dv.h, 1)
            s_dv.n += 1
    grow_done = s_dv.n

    shrink_base = bcast_done + 2 * NG * nft
    assert shrink_base == shrink_base_c

    @block.tensor
    def _(t):
        for i in range(8):
            r = i // 2
            il = i % 2
            if il == 0:
                t.wait_ge(s_sc2[r % 2].h, 16 * (r // 2 + 1))
            for f in range(nft):
                j = i * nft + f
                jj = j % 8
                if j == 0:
                    t.wait_ge(s_dv.h, grow_done)
                if j >= 8:
                    t.wait_ge(s_dv.h, grow_done + (j - 8) + 1)
                for k in range(NG):
                    mm = t.matmul(pb[jj][:], sw_ring[:, r % 2, k, il * 128:(il + 1) * 128],
                                  u[:, k, f * FT:(f + 1) * FT], start=(k == 0), stop=(k == NG - 1))
                mm.then_inc(s_pe.h, 1)
                s_pe.n += 1

    @block.vector
    def _(v):
        for i in range(8):
            for f in range(nft):
                j = i * nft + f
                jj = j % 8
                v.wait_ge(s_pe.h, shrink_base + j + 1)
                v.tensor_tensor(xa[:, i, f * FT:(f + 1) * FT], xa[:, i, f * FT:(f + 1) * FT],
                                pb[jj][:], ALU.add).then_inc(s_dv.h, 1)
                s_dv.n += 1
    add_done = s_dv.n

    @block.sync
    def _(sync):
        sync.wait_ge(s_dv.h, add_done)
        sync.dma_start(xa_out.rearrange("(c p) t -> p c t", p=128), xa[:]).then_inc(s_sy.h, 16)
        s_sy.n += 16
        sync.wait_ge(s_sy.h, s_sy.n)





def emit_smqa(b, D, T=T_CORE):
    nc = b.nc
    nft = T // FT

    xa_in = D["xa"]
    wqTd = D["wqT"]
    wkTd = D["wkT"]
    wvTd = D["wvT"]
    woTd = D["woT"]
    gnc_d = D["gnc"]
    cos_d = D["cosT"]
    sin_d = D["sinT"]
    rot_d = D["rotP"]
    mL_d = D["mL"]
    mR_d = D["mR"]
    iscol_d = D["iscol"]
    xa_out = D["xa_out"]

    kh_in = nc.dram_tensor(b.pfx + "kh_in", [128, W], BF16)
    kh_out = nc.dram_tensor(b.pfx + "kh_out", [256, W], BF16)
    vh_in = nc.dram_tensor(b.pfx + "vh_in", [W, HD], BF16)
    vh_out = nc.dram_tensor(b.pfx + "vh_out", [2 * W, HD], BF16)

    xa = b.sb("xa", [128, NC, T])
    xn = b.sb("xn", [128, NC, T], BF16)
    qb = b.sb("qb", [128, QH, T], BF16)
    kext = b.sb("kext", [128, 3 * W], BF16)
    vext = b.sb("vext", [128, 24, HD], BF16)
    Pbuf = b.sb("Pbuf", [128, 12, FT], BF16)
    cosb = b.sb("cosb", [128, T], BF16)
    sinb = b.sb("sinb", [128, T], BF16)
    rotb = b.sb("rotb", [128, 128], BF16)
    mLb = b.sb("mLb", [128, 4, 512], BF16)
    mRb = b.sb("mRb", [128, 4, 512], BF16)
    tmp1 = b.sb("tmp1", [128, FT])
    tmp2 = b.sb("tmp2", [128, FT])
    wpc = b.sb("wpc", [128, 2, NC, 256], BF16)
    wkb = b.sb("wkb", [128, NC, HD], BF16)
    wvb = b.sb("wvb", [128, NC, HD], BF16)
    brec = b.sb("brec", [128, FT], BF16)
    lrow = b.sb("lrow", [1, FT])
    lrowb = b.sb("lrowb", [1, FT], BF16)
    gcol = b.sb("gcol", [128, NC])
    iscl = b.sb("iscl", [128, 1])
    onescol_bf = b.sb("onescol_bf", [128, 1], BF16)
    onesrow_f = b.sb("onesrow_f", [1, 128])
    onesrow_bf = b.sb("onesrow_bf", [1, 128], BF16)
    srow = b.sb("srow", [1, T])
    mcol = b.sb("mcol", [128, QH])
    mrow = b.sb("mrow", [1, 16])
    Obuf = xn

    pA = b.psum("pA", (128, 2048))
    pB = b.psum("pB", (128, 2048))
    pb = [pA[:, f * FT:(f + 1) * FT] for f in range(4)] + \
         [pB[:, f * FT:(f + 1) * FT] for f in range(4)]

    s_sy = b.sem("s_sy")
    s_tab = b.sem("s_tab")
    s_wp = [b.sem(f"s_wp{i}") for i in range(2)]
    s_cc = b.sem("s_cc")
    s_ag = b.sem("s_ag")
    s_pe = b.sem("s_pe")
    s_ac = b.sem("s_ac")
    s_dv = b.sem("s_dv")

    groups = [(w, h, qt) for w in (1, 0) for h in range(QH) for qt in range(2)]

    L = {"S": {}, "Lo": {}, "PV": {}, "rb": {}, "ev": {}, "msk": {}, "oev": {},
         "rc": {}, "brev": {}, "bev": {}}
    pe = 0
    pe += nft; L["stats"] = pe
    pe += nft; L["bcast"] = pe
    L["qproj"] = [pe + j + 1 for j in range(QH * nft)]
    pe += QH * nft
    pe += 1; L["kproj"] = pe
    L["vproj"] = [pe + i + 1 for i in range(16)]
    pe += 16
    L["rotq"] = [pe + j + 1 for j in range(QH * nft)]
    pe += QH * nft
    pe += 1; L["rotk"] = pe
    L["qsq"] = [pe + h + 1 for h in range(QH)]
    pe += QH
    pe += 1; L["ksqa"] = pe
    pe += 1; L["ksqb"] = pe
    pe += 1; L["mb"] = pe
    for g in groups:
        L["S"][g] = [pe + si + 1 for si in range(12)]
        pe += 12
        pe += 1; L["Lo"][g] = pe
        pe += 1; L["PV"][g] = pe
        pe += 1; L["rb"][g] = pe
    L["wout"] = [pe + j + 1 for j in range(NC * nft)]
    pe += NC * nft

    ac = 0
    ac += 1; L["sq"] = ac
    ac += 1; L["srow"] = ac
    ac += 1; L["qsqact"] = ac
    ac += 1; L["ksqact"] = ac
    ac += 1; L["msqrt"] = ac
    ac += 1; L["mneg"] = ac
    for g in groups:
        L["ev"][g] = [ac + si + 1 for si in range(12)]
        ac += 12

    dv = 0
    dv += 1; L["cast"] = dv
    dv += 1; L["srowcp"] = dv
    dv += 1; L["recipn"] = dv
    dv += 1; L["norm"] = dv
    L["qev"] = [dv + j + 1 for j in range(QH * nft)]
    dv += QH * nft
    dv += 1; L["kev"] = dv
    L["vev"] = [dv + i + 1 for i in range(16)]
    dv += 16
    L["qcmb"] = [dv + 3 * j + 3 for j in range(QH * nft)]
    dv += 3 * QH * nft
    L["kcmb"] = [dv + 3 * f + 3 for f in range(nft)]
    dv += 3 * nft
    L["qmax"] = [dv + h + 1 for h in range(QH)]
    dv += QH
    dv += 1; L["kmaxa"] = dv
    dv += 1; L["kmaxb"] = dv
    dv += 1; L["kmaxc"] = dv
    dv += 1; L["mmul"] = dv
    dv += 1; L["mcolev"] = dv
    for g in groups:
        w, h, qt = g
        nmask = sum(1 for _, mk in subtiles(qt) if mk is not None)
        nisc = (8 if qt == 0 else 4) if w == 0 else 0
        L["msk"][g] = dv + nmask + nisc
        dv += nmask + nisc
        dv += 1; L["rc"][g] = dv
        dv += 1; L["brev"][g] = dv
        dv += 1; L["bev"][g] = dv
        dv += 1; L["oev"][g] = dv
    L["wadd"] = [dv + j + 1 for j in range(NC * nft)]
    dv += NC * nft

    block = b.block()

    @block.sync
    def _(sync):
        for dst, src in [(xa[:], xa_in.rearrange("(c p) t -> p c t", p=128)),
                         (gcol[:], gnc_d), (iscl[:], iscol_d)]:
            sync.dma_start(dst, src).then_inc(s_sy.h, 16)
            s_sy.n += 16
    ld_done = s_sy.n

    @block.gpsimd
    def _(g):
        g.dma_start(cosb[:], cos_d).then_inc(s_tab.h, 16)
        s_tab.n += 16
        g.dma_start(sinb[:], sin_d).then_inc(s_tab.h, 16)
        s_tab.n += 16
        g.dma_start(mLb[:], mL_d.rearrange("m p q -> p m q")).then_inc(s_tab.h, 16)
        s_tab.n += 16
        g.dma_start(mRb[:], mR_d.rearrange("m p q -> p m q")).then_inc(s_tab.h, 16)
        s_tab.n += 16
        g.dma_start(rotb[:], rot_d).then_inc(s_tab.h, 16)
        s_tab.n += 16
        g.dma_start(wkb[:], wkTd.rearrange("(c p) o -> p c o", p=128)).then_inc(s_wp[0].h, 16)
        s_wp[0].n += 16
        g.dma_start(wvb[:], wvTd.rearrange("(c p) o -> p c o", p=128)).then_inc(s_wp[1].h, 16)
        s_wp[1].n += 16
        for q in range(4):
            s = q % 2
            if q >= 2:
                g.wait_ge(s_pe.h, L["qproj"][(q - 2 + 1) * 2 * nft - 1])
            g.dma_start(wpc[:, s, :, :],
                        wqTd.rearrange("(c p) o -> p c o", p=128)[:, :, q * 256:(q + 1) * 256]
                        ).then_inc(s_wp[s].h, 16)
            s_wp[s].n += 16

    @block.vector
    def _(v):
        v.memset(onescol_bf[:], 1.0)
        v.memset(onesrow_f[:], 1.0)
        v.memset(onesrow_bf[:], 1.0)
        v.wait_ge(s_sy.h, ld_done)
        v.tensor_copy(xn[:], xa[:]).then_inc(s_dv.h, 1)
        s_dv.n += 1
    assert s_dv.n == L["cast"]

    @block.scalar
    def _(sc):
        sc.wait_ge(s_dv.h, L["cast"])
        sc.activation(qb[:], xn[:], AF.Square).then_inc(s_ac.h, 1)
        s_ac.n += 1

    @block.tensor
    def _(t):
        t.wait_ge(s_ac.h, L["sq"])
        for f in range(nft):
            for k in range(NC):
                mm = t.matmul(pb[f][0:1, :], onescol_bf[:], qb[:, k, f * FT:(f + 1) * FT],
                              start=(k == 0), stop=(k == NC - 1))
            mm.then_inc(s_pe.h, 1)
            s_pe.n += 1

    @block.vector
    def _(v):
        v.wait_ge(s_pe.h, L["stats"])
        for f in range(nft):
            cp = v.tensor_copy(srow[0:1, f * FT:(f + 1) * FT], pb[f][0:1, :])
        cp.then_inc(s_dv.h, 1)
        s_dv.n += 1
        v.wait_ge(s_dv.h, s_dv.n)
        v.reciprocal(srow[:], srow[:]).then_inc(s_dv.h, 1)
        s_dv.n += 1
    assert s_dv.n == L["recipn"]

    @block.scalar
    def _(sc):
        sc.wait_ge(s_dv.h, L["recipn"])
        sc.activation(srow[:], srow[:], AF.Sqrt, scale=1024.0).then_inc(s_ac.h, 1)
        s_ac.n += 1

    @block.tensor
    def _(t):
        t.wait_ge(s_ac.h, L["srow"])
        for f in range(nft):
            t.matmul(pA[:, f * FT:(f + 1) * FT], onesrow_f[:], srow[0:1, f * FT:(f + 1) * FT],
                     start=True, stop=True).then_inc(s_pe.h, 1)
            s_pe.n += 1

    @block.vector
    def _(v):
        v.wait_ge(s_pe.h, L["bcast"])
        for k in range(NC):
            ins = v.scalar_tensor_tensor(xn[:, k, :], xa[:, k, :], gcol[:, k:k + 1],
                                         pA[:], ALU.mult, ALU.mult)
        ins.then_inc(s_dv.h, 1)
        s_dv.n += 1
    assert s_dv.n == L["norm"]

    @block.tensor
    def _(t):
        t.wait_ge(s_dv.h, L["norm"])
        for hh in range(QH):
            q4, ol = divmod(hh, 2)
            if ol == 0:
                t.wait_ge(s_wp[q4 % 2].h, 32 if q4 < 2 else 48)
            for f in range(nft):
                j = hh * nft + f
                jj = j % 8
                if j >= 8:
                    t.wait_ge(s_dv.h, L["qev"][j - 8])
                for k in range(NC):
                    mm = t.matmul(pb[jj][:], wpc[:, q4 % 2, k, ol * 128:(ol + 1) * 128],
                                  xn[:, k, f * FT:(f + 1) * FT], start=(k == 0), stop=(k == NC - 1))
                mm.then_inc(s_pe.h, 1)
                s_pe.n += 1
        t.wait_ge(s_wp[0].h, 48)
        t.wait_ge(s_wp[1].h, 48)
        t.wait_ge(s_dv.h, L["qev"][QH * nft - 1])
        for f in range(nft):
            for k in range(NC):
                mm = t.matmul(pA[:, f * FT:(f + 1) * FT], wkb[:, k, :],
                              xn[:, k, f * FT:(f + 1) * FT], start=(k == 0), stop=(k == NC - 1))
        mm.then_inc(s_pe.h, 1)
        s_pe.n += 1
        assert s_pe.n == L["kproj"]
        for tc in range(16):
            bank = pB[:, (tc % 4) * FT:(tc % 4) * FT + HD]
            if tc >= 4:
                t.wait_ge(s_dv.h, L["vev"][tc - 4])
            for k in range(NC):
                mm = t.matmul(bank, xn[:, k, tc * 128:(tc + 1) * 128], wvb[:, k, :],
                              start=(k == 0), stop=(k == NC - 1))
            mm.then_inc(s_pe.h, 1)
            s_pe.n += 1

    @block.vector
    def _(v):
        for j in range(QH * nft):
            hh, f = divmod(j, nft)
            v.wait_ge(s_pe.h, L["qproj"][j])
            v.tensor_copy(qb[:, hh, f * FT:(f + 1) * FT], pb[j % 8][:]).then_inc(s_dv.h, 1)
            s_dv.n += 1
        v.wait_ge(s_pe.h, L["kproj"])
        v.tensor_copy(kext[:, W:3 * W], pA[:]).then_inc(s_dv.h, 1)
        s_dv.n += 1
        assert s_dv.n == L["kev"]
        for tc in range(16):
            v.wait_ge(s_pe.h, L["vproj"][tc])
            v.tensor_copy(vext[:, 8 + tc, :], pB[:, (tc % 4) * FT:(tc % 4) * FT + HD]
                          ).then_inc(s_dv.h, 1)
            s_dv.n += 1

    @block.tensor
    def _(t):
        t.wait_ge(s_tab.h, 80)
        for j in range(QH * nft):
            hh, f = divmod(j, nft)
            if j >= 8:
                t.wait_ge(s_dv.h, L["qcmb"][j - 8])
            else:
                t.wait_ge(s_dv.h, L["vev"][15])
            t.matmul(pb[j % 8][:], rotb[:], qb[:, hh, f * FT:(f + 1) * FT],
                     start=True, stop=True).then_inc(s_pe.h, 1)
            s_pe.n += 1
        t.wait_ge(s_dv.h, L["qcmb"][QH * nft - 1])
        for f in range(nft):
            mm = t.matmul(pb[f][:], rotb[:], kext[:, W + f * FT:W + (f + 1) * FT],
                          start=True, stop=True)
        mm.then_inc(s_pe.h, 1)
        s_pe.n += 1
        assert s_pe.n == L["rotk"]

    @block.vector
    def _(v):
        v.wait_ge(s_tab.h, 80)
        for j in range(QH * nft):
            hh, f = divmod(j, nft)
            fs = slice(f * FT, (f + 1) * FT)
            v.wait_ge(s_pe.h, L["rotq"][j])
            v.wait_ge(s_dv.h, s_dv.n)
            v.tensor_tensor(tmp1[:, 0:FT], qb[:, hh, fs], cosb[:, fs], ALU.mult
                            ).then_inc(s_dv.h, 1)
            s_dv.n += 1
            v.wait_ge(s_dv.h, s_dv.n)
            v.tensor_tensor(tmp2[:, 0:FT], sinb[:, fs], pb[j % 8][:], ALU.mult
                            ).then_inc(s_dv.h, 1)
            s_dv.n += 1
            v.wait_ge(s_dv.h, s_dv.n)
            v.tensor_tensor(qb[:, hh, fs], tmp1[:, 0:FT], tmp2[:, 0:FT], ALU.add
                            ).then_inc(s_dv.h, 1)
            s_dv.n += 1
        for f in range(nft):
            fs = slice(f * FT, (f + 1) * FT)
            ks = slice(W + f * FT, W + (f + 1) * FT)
            v.wait_ge(s_pe.h, L["rotk"])
            v.wait_ge(s_dv.h, s_dv.n)
            v.tensor_tensor(tmp1[:, 0:FT], kext[:, ks], cosb[:, fs], ALU.mult
                            ).then_inc(s_dv.h, 1)
            s_dv.n += 1
            v.wait_ge(s_dv.h, s_dv.n)
            v.tensor_tensor(tmp2[:, 0:FT], sinb[:, fs], pb[f][:], ALU.mult).then_inc(s_dv.h, 1)
            s_dv.n += 1
            v.wait_ge(s_dv.h, s_dv.n)
            v.tensor_tensor(kext[:, ks], tmp1[:, 0:FT], tmp2[:, 0:FT], ALU.add
                            ).then_inc(s_dv.h, 1)
            s_dv.n += 1
        assert s_dv.n == L["kcmb"][nft - 1]

    @block.gpsimd
    def _(g):
        g.wait_ge(s_dv.h, L["kcmb"][nft - 1])
        g.dma_start(kh_in.ap(), kext[:, 2 * W:3 * W]).then_inc(s_ag.h, 16)
        s_ag.n += 16
        g.wait_ge(s_ag.h, s_ag.n)
        g.dma_start(vh_in.ap().rearrange("(c p) h -> p c h", p=128), vext[:, 16:24, :]
                    ).then_inc(s_ag.h, 16)
        s_ag.n += 16
        g.wait_ge(s_ag.h, s_ag.n)
        g.collective_compute("AllGather", ALU.bypass, replica_groups=RG,
                             ins=[kh_in.ap().opt()], outs=[kh_out.ap().opt()]
                             ).then_inc(s_cc.h, 1)
        s_cc.n += 1
        g.collective_compute("AllGather", ALU.bypass, replica_groups=RG,
                             ins=[vh_in.ap().opt()], outs=[vh_out.ap().opt()]
                             ).then_inc(s_cc.h, 1)
        s_cc.n += 1
        g.wait_ge(s_cc.h, s_cc.n)
        g.dma_start(kext[:, 0:W], kh_out.ap()[0:128, :]).then_inc(s_ag.h, 16)
        s_ag.n += 16
        g.wait_ge(s_ag.h, s_ag.n)
        g.dma_start(vext[:, 0:8, :], vh_out.ap()[0:W, :].rearrange("(c p) h -> p c h", p=128)
                    ).then_inc(s_ag.h, 16)
        s_ag.n += 16
        g.wait_ge(s_ag.h, s_ag.n)
    ag_done = s_ag.n

    @block.scalar
    def _(sc):
        sc.wait_ge(s_dv.h, L["kcmb"][nft - 1])
        sc.activation(xn[:], qb[:], AF.Square).then_inc(s_ac.h, 1)
        s_ac.n += 1
        sc.wait_ge(s_ag.h, ag_done)
        sc.activation(Pbuf[:].rearrange("p c t -> p (c t)")[:, 0:3 * W], kext[:], AF.Square
                      ).then_inc(s_ac.h, 1)
        s_ac.n += 1
        sc.wait_ge(s_dv.h, L["mmul"])
        sc.activation(mrow[0:1, 0:QH], mrow[0:1, 0:QH], AF.Sqrt).then_inc(s_ac.h, 1)
        s_ac.n += 1
        sc.wait_ge(s_ac.h, s_ac.n)
        sc.mul(mrow[0:1, 0:QH], mrow[0:1, 0:QH], -ISQ).then_inc(s_ac.h, 1)
        s_ac.n += 1
        assert s_ac.n == L["mneg"]

    @block.tensor
    def _(t):
        t.wait_ge(s_ac.h, L["qsqact"])
        for hh in range(QH):
            if hh >= 1:
                t.wait_ge(s_dv.h, L["qmax"][hh - 1])
            for f in range(nft):
                mm = t.matmul(pA[0:1, f * FT:(f + 1) * FT], onescol_bf[:],
                              xn[:, hh, f * FT:(f + 1) * FT], start=True, stop=True)
            mm.then_inc(s_pe.h, 1)
            s_pe.n += 1
        t.wait_ge(s_ac.h, L["ksqact"])
        t.wait_ge(s_dv.h, L["qmax"][QH - 1])
        ksq = Pbuf[:].rearrange("p c t -> p (c t)")
        for f in range(4):
            mm = t.matmul(pB[0:1, f * FT:(f + 1) * FT], onescol_bf[:],
                          ksq[:, f * FT:(f + 1) * FT], start=True, stop=True)
        mm.then_inc(s_pe.h, 1)
        s_pe.n += 1
        assert s_pe.n == L["ksqa"]
        t.wait_ge(s_dv.h, L["kmaxa"])
        for f in range(2):
            mm = t.matmul(pB[0:1, f * FT:(f + 1) * FT], onescol_bf[:],
                          ksq[:, 2048 + f * FT:2048 + (f + 1) * FT], start=True, stop=True)
        mm.then_inc(s_pe.h, 1)
        s_pe.n += 1
        assert s_pe.n == L["ksqb"]
        t.wait_ge(s_ac.h, L["mneg"])
        t.matmul(pA[:, 0:QH], onesrow_f[:], mrow[0:1, 0:QH], start=True, stop=True
                 ).then_inc(s_pe.h, 1)
        s_pe.n += 1
        assert s_pe.n == L["mb"]

    @block.vector
    def _(v):
        for hh in range(QH):
            v.wait_ge(s_pe.h, L["qsq"][hh])
            v.reduce_max(mrow[0:1, hh:hh + 1], pA[0:1, :], axis=AX.X).then_inc(s_dv.h, 1)
            s_dv.n += 1
        v.wait_ge(s_pe.h, L["ksqa"])
        v.reduce_max(mrow[0:1, 8:9], pB[0:1, :], axis=AX.X).then_inc(s_dv.h, 1)
        s_dv.n += 1
        assert s_dv.n == L["kmaxa"]
        v.wait_ge(s_pe.h, L["ksqb"])
        v.reduce_max(mrow[0:1, 9:10], pB[0:1, 0:2 * FT], axis=AX.X).then_inc(s_dv.h, 1)
        s_dv.n += 1
        v.wait_ge(s_dv.h, s_dv.n)
        v.tensor_tensor(mrow[0:1, 8:9], mrow[0:1, 8:9], mrow[0:1, 9:10], ALU.max
                        ).then_inc(s_dv.h, 1)
        s_dv.n += 1
        assert s_dv.n == L["kmaxc"]
        v.wait_ge(s_dv.h, s_dv.n)
        v.tensor_scalar(mrow[0:1, 0:QH], mrow[0:1, 0:QH], mrow[0:1, 8:9], None, ALU.mult
                        ).then_inc(s_dv.h, 1)
        s_dv.n += 1
        assert s_dv.n == L["mmul"]
        v.wait_ge(s_pe.h, L["mb"])
        v.tensor_copy(mcol[:], pA[:, 0:QH]).then_inc(s_dv.h, 1)
        s_dv.n += 1
        assert s_dv.n == L["mcolev"]

    @block.tensor
    def _(t):
        for gi, g in enumerate(groups):
            w, h, qt = g
            st = subtiles(qt)
            if w == 0:
                t.wait_ge(s_ag.h, ag_done)
            for si, (ki, mk) in enumerate(st):
                bk = si % 6
                if si >= 6:
                    t.wait_ge(s_ac.h, L["ev"][g][si - 6])
                elif gi >= 1:
                    t.wait_ge(s_ac.h, L["ev"][groups[gi - 1]][6 + si])
                else:
                    t.wait_ge(s_dv.h, L["mcolev"])
                mm = t.matmul(pb[bk][:], kext[:, w * W + ki * 128:w * W + (ki + 1) * 128],
                              qb[:, h, w * W + qt * FT:w * W + (qt + 1) * FT],
                              start=True, stop=True)
                mm.then_inc(s_pe.h, 1)
                s_pe.n += 1
            assert s_pe.n == L["S"][g][-1]
            t.wait_ge(s_dv.h, L["msk"][g])
            if gi >= 1:
                t.wait_ge(s_dv.h, L["bev"][groups[gi - 1]])
            for si in range(12):
                mm = t.matmul(pb[6][0:1, :], onescol_bf[:], Pbuf[:, si, :],
                              start=(si == 0), stop=(si == 11))
            mm.then_inc(s_pe.h, 1)
            s_pe.n += 1
            assert s_pe.n == L["Lo"][g]
            if gi >= 1:
                t.wait_ge(s_dv.h, L["oev"][groups[gi - 1]])
            for si, (ki, mk) in enumerate(st):
                mm = t.matmul(pb[7][:], vext[:, w * 8 + ki, :], Pbuf[:, si, :],
                              start=(si == 0), stop=(si == 11))
            mm.then_inc(s_pe.h, 1)
            s_pe.n += 1
            assert s_pe.n == L["PV"][g]
            t.wait_ge(s_dv.h, L["brev"][g])
            t.matmul(pb[6][:], onesrow_bf[:], lrowb[0:1, :], start=True, stop=True
                     ).then_inc(s_pe.h, 1)
            s_pe.n += 1
            assert s_pe.n == L["rb"][g]

    @block.scalar
    def _(sc):
        for gi, g in enumerate(groups):
            w, h, qt = g
            for si in range(12):
                sc.wait_ge(s_pe.h, L["S"][g][si])
                if gi >= 1:
                    sc.wait_ge(s_pe.h, L["PV"][groups[gi - 1]])
                sc.activation(Pbuf[:, si, :], pb[si % 6][:], AF.Exp, scale=ISQ,
                              bias=mcol[:, h:h + 1]).then_inc(s_ac.h, 1)
                s_ac.n += 1
            assert s_ac.n == L["ev"][g][-1]

    @block.vector
    def _(v):
        v.wait_ge(s_tab.h, 80)
        for gi, g in enumerate(groups):
            w, h, qt = g
            st = subtiles(qt)
            for si, (ki, mk) in enumerate(st):
                left = (w * W + ki * 128) < W
                if mk is not None:
                    v.wait_ge(s_ac.h, L["ev"][g][si])
                    mt = mLb if mk[0] == "L" else mRb
                    v.tensor_tensor(Pbuf[:, si, :], Pbuf[:, si, :], mt[:, mk[1], :], ALU.mult
                                    ).then_inc(s_dv.h, 1)
                    s_dv.n += 1
                if w == 0 and left:
                    v.wait_ge(s_ac.h, L["ev"][g][si])
                    v.wait_ge(s_dv.h, s_dv.n)
                    v.tensor_scalar(Pbuf[:, si, :], Pbuf[:, si, :], iscl[:, 0:1], None,
                                    ALU.mult).then_inc(s_dv.h, 1)
                    s_dv.n += 1
            assert s_dv.n == L["msk"][g]
            v.wait_ge(s_pe.h, L["Lo"][g])
            v.reciprocal(lrow[0:1, :], pb[6][0:1, :]).then_inc(s_dv.h, 1)
            s_dv.n += 1
            v.wait_ge(s_dv.h, s_dv.n)
            v.tensor_copy(lrowb[0:1, :], lrow[0:1, :]).then_inc(s_dv.h, 1)
            s_dv.n += 1
            assert s_dv.n == L["brev"][g]
            v.wait_ge(s_pe.h, L["rb"][g])
            v.tensor_copy(brec[:], pb[6][:]).then_inc(s_dv.h, 1)
            s_dv.n += 1
            assert s_dv.n == L["bev"][g]
            v.wait_ge(s_pe.h, L["PV"][g])
            v.wait_ge(s_dv.h, s_dv.n)
            v.tensor_tensor(Obuf[:, h, w * W + qt * FT:w * W + (qt + 1) * FT],
                            pb[7][:], brec[:], ALU.mult).then_inc(s_dv.h, 1)
            s_dv.n += 1
            assert s_dv.n == L["oev"][g]

    @block.tensor
    def _(t):
        t.wait_ge(s_dv.h, L["oev"][groups[-1]])
        for i in range(NC):
            wq4, wl = divmod(i, 2)
            t.wait_ge(s_wp[wq4 % 2].h, 48 + 16 * (wq4 // 2 + 1))
            for f in range(nft):
                j = i * nft + f
                jj = j % 8
                if j >= 8:
                    t.wait_ge(s_dv.h, L["wadd"][j - 8])
                for k in range(NC):
                    mm = t.matmul(pb[jj][:], wpc[:, wq4 % 2, k, wl * 128:(wl + 1) * 128],
                                  Obuf[:, k, f * FT:(f + 1) * FT],
                                  start=(k == 0), stop=(k == NC - 1))
                mm.then_inc(s_pe.h, 1)
                s_pe.n += 1

    @block.gpsimd
    def _(g):
        for q in range(4):
            s = q % 2
            if q < 2:
                g.wait_ge(s_pe.h, L["qproj"][(q + 2 + 1) * 2 * nft - 1])
            else:
                g.wait_ge(s_pe.h, L["wout"][(q - 2 + 1) * 2 * nft - 1])
            g.dma_start(wpc[:, s, :, :],
                        woTd.rearrange("(c p) o -> p c o", p=128)[:, :, q * 256:(q + 1) * 256]
                        ).then_inc(s_wp[s].h, 16)
            s_wp[s].n += 16

    @block.vector
    def _(v):
        for i in range(NC):
            for f in range(nft):
                j = i * nft + f
                v.wait_ge(s_pe.h, L["wout"][j])
                v.tensor_tensor(xa[:, i, f * FT:(f + 1) * FT], xa[:, i, f * FT:(f + 1) * FT],
                                pb[j % 8][:], ALU.add).then_inc(s_dv.h, 1)
                s_dv.n += 1
        assert s_dv.n == L["wadd"][-1]

    @block.sync
    def _(sync):
        sync.wait_ge(s_dv.h, L["wadd"][-1])
        sync.dma_start(xa_out.rearrange("(c p) t -> p c t", p=128), xa[:]).then_inc(s_sy.h, 16)
        s_sy.n += 16
        sync.wait_ge(s_sy.h, s_sy.n)



def emit_moe(b, D, T=T_CORE):
    nc = b.nc
    ntc = T // 128

    xT_d = D["xT"]
    xnb_d = D["xnb"]
    xa_d = D["xa"]
    gwT_d = D["gwT"]
    w1_d = D["w1"]
    b1_d = D["b1c"]
    w2_d = D["w2"]
    b2_d = D["b2c"]
    sel_d = D["sel"]
    ident_d = D["identf"]
    x1_o = D["x1"]

    ACC = b.sb("ACC", [128, NC, T])
    xnh = b.sb("xnh", [128, NC, TH], BF16)
    hid = b.sb("hid", [128, NF, TH], BF16)
    w1r = b.sb("w1r", [128, 2, NC, 512], BF16)
    w2r = b.sb("w2r", [128, 2, NF, 128], BF16)
    gwb = b.sb("gwb", [128, NC, E], BF16)
    selb = b.sb("selb", [8, E * 128])
    wTg = b.sb("wTg", [8, T])
    lg = b.sb("lg", [128, 16, E])
    p2s = b.sb("p2s", [128, E])
    lcol = b.sb("lcol", [128, 4])
    b1c = b.sb("b1cs", [128, E, NF])
    b2c = b.sb("b2cs", [128, E, 8])
    wbc = b.sb("wbc", [128, 2, FT])
    tmpv = b.sb("tmpv", [128, FT])
    xtr = b.sb("xtr", [128, 2, TH])
    identf = b.sb("idf", [128, 128])
    onescol_bf = b.sb("onescol_bf", [128, 1], BF16)

    pA = b.psum("pA", (128, 2048))
    pB = b.psum("pB", (128, 2048))
    pb = [pA[:, f * FT:(f + 1) * FT] for f in range(4)] + \
         [pB[:, f * FT:(f + 1) * FT] for f in range(4)]

    s_sy = b.sem("s_sy")
    s_xh = b.sem("s_xh")
    s_w1 = [b.sem(f"s_w1{i}") for i in range(2)]
    s_w2 = [b.sem(f"s_w2{i}") for i in range(2)]
    s_xr = [b.sem(f"s_xr{i}") for i in range(2)]
    s_pe = b.sem("s_pe")
    s_ac = b.sem("s_ac")
    s_dv = b.sem("s_dv")
    s_gp = b.sem("s_gp")

    L = {}
    pe = 0
    L["glog"] = [0] * ntc
    L["gtr"] = [0] * ntc
    for tc in range(ntc):
        pe += 1; L["glog"][tc] = pe
        if tc >= 1:
            pe += 1; L["gtr"][tc - 1] = pe
    pe += 1; L["gtr"][ntc - 1] = pe
    HG = {}
    YB = {}
    YG = {}
    for th in range(2):
        for e in range(E):
            for fc in range(NF):
                for tt in range(2):
                    pe += 1; HG[(th, e, fc, tt)] = pe
            for tt in range(2):
                pe += 1; YB[(th, e, tt)] = pe
            for i in range(8):
                for tt in range(2):
                    pe += 1; YG[(th, e, i, tt)] = pe
    L["pe_end"] = pe

    ac = 0
    GEX = {}
    for tc in range(ntc):
        ac += 1; GEX[tc] = ac
    REL = {}
    for th in range(2):
        for e in range(E):
            for fc in range(NF):
                for tt in range(2):
                    ac += 1; REL[(th, e, fc, tt)] = ac
    L["ac_end"] = ac

    dv = 0
    GEV = {}
    GSM = {}
    GTV = {}
    for tc in range(ntc):
        dv += 1; GEV[tc] = dv
        dv += 11; GSM[tc] = dv
        dv += 1; GTV[tc] = dv
    WBE = {}
    YEV = {}
    for th in range(2):
        for e in range(E):
            for tt in range(2):
                dv += 1; WBE[(th, e, tt)] = dv
            for i in range(8):
                for tt in range(2):
                    dv += 2; YEV[(th, e, i, tt)] = dv
    XAD = {}
    for k in range(2 * NC):
        dv += 1; XAD[k] = dv
    L["dv_end"] = dv

    block = b.block()

    @block.sync
    def _(sync):
        sync.dma_start(ACC[:], xa_d.rearrange("(c p) t -> p c t", p=128)).then_inc(s_sy.h, 16)
        s_sy.n += 16
        sync.dma_start(b1c[:], b1_d).then_inc(s_sy.h, 16)
        s_sy.n += 16
        sync.dma_start(b2c[:], b2_d).then_inc(s_sy.h, 16)
        s_sy.n += 16
        sync.dma_start(selb[:], sel_d).then_inc(s_sy.h, 16)
        s_sy.n += 16
        sync.dma_start(identf[:], ident_d).then_inc(s_sy.h, 16)
        s_sy.n += 16
    ld_done = s_sy.n

    @block.gpsimd
    def _(g):
        g.dma_start(gwb[:], gwT_d.rearrange("(c p) e -> p c e", p=128)).then_inc(s_gp.h, 16)
        s_gp.n += 16
        def xnh_load(m, hf):
            if m == 1:
                g.wait_ge(s_pe.h, L["glog"][7])
            elif m == 2:
                g.wait_ge(s_pe.h, L["glog"][15])
            elif m == 3:
                g.wait_ge(s_pe.h, YG[(0, E - 1, 7, 1)])
            g.dma_start(xnh[:], xnb_d.rearrange("(c p) t -> p c t", p=128)[:, :, hf * TH:(hf + 1) * TH]
                        ).then_inc(s_xh.h, 16)
            s_xh.n += 16
        xnh_load(0, 0)
        xnh_load(1, 1)
        xnh_load(2, 0)
        for th in range(2):
            if th == 1:
                xnh_load(3, 1)
            for e in range(E):
                for q in range(8):
                    s = q % 2
                    m1 = (th * E + e) * 8 + q
                    if m1 >= 2:
                        qp, ep, thp = (m1 - 2) % 8, ((m1 - 2) // 8) % E, (m1 - 2) // (8 * E)
                        g.wait_ge(s_pe.h, HG[(thp, ep, qp * 4 + 3, 1)])
                    g.dma_start(w1r[:, s, :, :],
                                w1_d[e].rearrange("(c p) f -> p c f", p=128)[:, :, q * 512:(q + 1) * 512]
                                ).then_inc(s_w1[s].h, 16)
                    s_w1[s].n += 16
                for i in range(8):
                    s = i % 2
                    m2 = (th * E + e) * 8 + i
                    if m2 >= 2:
                        ip, ep, thp = (m2 - 2) % 8, ((m2 - 2) // 8) % E, (m2 - 2) // (8 * E)
                        g.wait_ge(s_pe.h, YG[(thp, ep, ip, 1)])
                    g.dma_start(w2r[:, s, :, :],
                                w2_d[e].rearrange("(c p) h -> p c h", p=128)[:, :, i * 128:(i + 1) * 128]
                                ).then_inc(s_w2[s].h, 16)
                    s_w2[s].n += 16

    def emit_logits(t, tc):
        hf = tc // 8
        tcl = tc % 8
        t.wait_ge(s_xh.h, 16 * (hf + 1))
        if tc >= 4:
            t.wait_ge(s_dv.h, GEV[tc - 4])
        bank = pb[tc % 4]
        col = (tc // 4) * 8
        for k in range(NC):
            mm = t.matmul(bank[:, col:col + 8], xnh[:, k, tcl * 128:(tcl + 1) * 128],
                          gwb[:, k, :], start=(k == 0), stop=(k == NC - 1))
        mm.then_inc(s_pe.h, 1)
        s_pe.n += 1
        assert s_pe.n == L["glog"][tc]

    def emit_gtr(t, tc):
        t.wait_ge(s_dv.h, GSM[tc])
        if tc >= 4:
            t.wait_ge(s_dv.h, GTV[tc - 4])
        t.transpose(pb[4 + tc % 4][0:8, 0:128], lg[:, tc, :], identf[:]).then_inc(s_pe.h, 1)
        s_pe.n += 1
        assert s_pe.n == L["gtr"][tc]

    @block.tensor
    def _(t):
        t.wait_ge(s_gp.h, 16)
        t.wait_ge(s_sy.h, ld_done)
        for tc in range(ntc):
            emit_logits(t, tc)
            if tc >= 1:
                emit_gtr(t, tc - 1)
        emit_gtr(t, ntc - 1)

    @block.scalar
    def _(sc):
        for tc in range(ntc):
            sc.wait_ge(s_dv.h, GSM[tc] - 9)
            sc.activation(lg[:, tc, :], lg[:, tc, :], AF.Exp, bias=lcol[:, 0:1]
                          ).then_inc(s_ac.h, 1)
            s_ac.n += 1
            assert s_ac.n == GEX[tc]

    @block.vector
    def _(v):
        v.memset(onescol_bf[:], 1.0)
        for tc in range(ntc):
            v.wait_ge(s_pe.h, L["glog"][tc])
            v.tensor_copy(lg[:, tc, :], pb[tc % 4][:, (tc // 4) * 8:(tc // 4) * 8 + 8]
                          ).then_inc(s_dv.h, 1)
            s_dv.n += 1
            assert s_dv.n == GEV[tc]
            v.wait_ge(s_dv.h, s_dv.n)
            v.reduce_max(lcol[:, 0:1], lg[:, tc, :], axis=AX.X).then_inc(s_dv.h, 1)
            s_dv.n += 1
            v.wait_ge(s_dv.h, s_dv.n)
            v.tensor_scalar(lcol[:, 0:1], lcol[:, 0:1], -1.0, None, ALU.mult).then_inc(s_dv.h, 1)
            s_dv.n += 1
            v.wait_ge(s_ac.h, GEX[tc])
            v.reduce_sum(lcol[:, 1:2], lg[:, tc, :], axis=AX.X).then_inc(s_dv.h, 1)
            s_dv.n += 1
            v.wait_ge(s_dv.h, s_dv.n)
            v.reciprocal(lcol[:, 1:2], lcol[:, 1:2]).then_inc(s_dv.h, 1)
            s_dv.n += 1
            v.wait_ge(s_dv.h, s_dv.n)
            v.tensor_scalar(lg[:, tc, :], lg[:, tc, :], lcol[:, 1:2], None, ALU.mult
                            ).then_inc(s_dv.h, 1)
            s_dv.n += 1
            v.wait_ge(s_dv.h, s_dv.n)
            v.reduce_max(lcol[:, 2:3], lg[:, tc, :], axis=AX.X).then_inc(s_dv.h, 1)
            s_dv.n += 1
            v.wait_ge(s_dv.h, s_dv.n)
            v.tensor_scalar(p2s[:], lg[:, tc, :], lcol[:, 2:3], None, ALU.is_ge
                            ).then_inc(s_dv.h, 1)
            s_dv.n += 1
            v.wait_ge(s_dv.h, s_dv.n)
            v.scalar_tensor_tensor(p2s[:], p2s[:], -BIG, lg[:, tc, :], ALU.mult, ALU.add
                                   ).then_inc(s_dv.h, 1)
            s_dv.n += 1
            v.wait_ge(s_dv.h, s_dv.n)
            v.reduce_max(lcol[:, 3:4], p2s[:], axis=AX.X).then_inc(s_dv.h, 1)
            s_dv.n += 1
            v.wait_ge(s_dv.h, s_dv.n)
            v.tensor_scalar(p2s[:], lg[:, tc, :], lcol[:, 3:4], None, ALU.is_ge
                            ).then_inc(s_dv.h, 1)
            s_dv.n += 1
            v.wait_ge(s_dv.h, s_dv.n)
            v.tensor_tensor(lg[:, tc, :], lg[:, tc, :], p2s[:], ALU.mult).then_inc(s_dv.h, 1)
            s_dv.n += 1
            assert s_dv.n == GSM[tc]
            v.wait_ge(s_pe.h, L["gtr"][tc])
            v.tensor_copy(wTg[0:8, tc * 128:(tc + 1) * 128],
                          pb[4 + tc % 4][0:8, 0:128]).then_inc(s_dv.h, 1)
            s_dv.n += 1
            assert s_dv.n == GTV[tc]

    @block.tensor
    def _(t):
        t.wait_ge(s_dv.h, GTV[ntc - 1])
        for th in range(2):
            t.wait_ge(s_xh.h, 16 * (th + 3))
            for e in range(E):
                for q in range(8):
                    s = q % 2
                    m1 = (th * E + e) * 8 + q
                    t.wait_ge(s_w1[s].h, 16 * (m1 // 2 + 1))
                    for icl in range(4):
                        fc = q * 4 + icl
                        for tt in range(2):
                            gidx = (fc * 2 + tt)
                            bk = gidx % 6
                            prev = gidx - 6
                            if prev >= 0:
                                t.wait_ge(s_ac.h, REL[(th, e, prev // 2, prev % 2)])
                            elif (th, e) != (0, 0):
                                pth, pe_ = (th, e - 1) if e > 0 else (th - 1, E - 1)
                                pgidx = 64 + prev
                                t.wait_ge(s_ac.h, REL[(pth, pe_, pgidx // 2, pgidx % 2)])
                                t.wait_ge(s_dv.h, YEV[(pth, pe_, 7, 1)])
                            for k in range(NC):
                                mm = t.matmul(pb[bk][:], w1r[:, s, k, icl * 128:(icl + 1) * 128],
                                              xnh[:, k, tt * FT:(tt + 1) * FT],
                                              start=(k == 0), stop=(k == NC - 1))
                            mm.then_inc(s_pe.h, 1)
                            s_pe.n += 1
                            assert s_pe.n == HG[(th, e, fc, tt)]
                for tt in range(2):
                    t.wait_ge(s_dv.h, GTV[ntc - 1] if (th, e, tt) == (0, 0, 0) else
                              WBE[(th, e, tt - 1)] if tt > 0 else
                              (WBE[(th, e - 1, 1)] if e > 0 else WBE[(th - 1, E - 1, 1)]))
                    t.matmul(pb[6][:], selb[:, e * 128:(e + 1) * 128],
                             wTg[0:8, th * TH + tt * FT:th * TH + (tt + 1) * FT],
                             start=True, stop=True).then_inc(s_pe.h, 1)
                    s_pe.n += 1
                    assert s_pe.n == YB[(th, e, tt)]
                for i in range(8):
                    s = i % 2
                    m2 = (th * E + e) * 8 + i
                    t.wait_ge(s_w2[s].h, 16 * (m2 // 2 + 1))
                    for tt in range(2):
                        ygi = i * 2 + tt
                        bk = ygi % 6
                        prev = ygi - 6
                        if prev >= 0:
                            t.wait_ge(s_dv.h, YEV[(th, e, prev // 2, prev % 2)])
                        else:
                            pg = 58 + bk
                            t.wait_ge(s_ac.h, REL[(th, e, pg // 2, pg % 2)])
                        t.wait_ge(s_ac.h, REL[(th, e, NF - 1, 1)])
                        for k in range(NF):
                            mm = t.matmul(pb[bk][:], w2r[:, s, k, :],
                                          hid[:, k, tt * FT:(tt + 1) * FT],
                                          start=(k == 0), stop=(k == NF - 1))
                        mm.then_inc(s_pe.h, 1)
                        s_pe.n += 1
                        assert s_pe.n == YG[(th, e, i, tt)]

    @block.scalar
    def _(sc):
        for th in range(2):
            for e in range(E):
                for fc in range(NF):
                    for tt in range(2):
                        gidx = fc * 2 + tt
                        bk = gidx % 6
                        sc.wait_ge(s_pe.h, HG[(th, e, fc, tt)])
                        if (th, e) != (0, 0) and fc == 0 and tt == 0:
                            pth, pe_ = (th, e - 1) if e > 0 else (th - 1, E - 1)
                            sc.wait_ge(s_pe.h, YG[(pth, pe_, 7, 1)])
                        sc.activation(hid[:, fc, tt * FT:(tt + 1) * FT], pb[bk][:], AF.Relu,
                                      bias=b1c[:, e, fc:fc + 1]).then_inc(s_ac.h, 1)
                        s_ac.n += 1
                        assert s_ac.n == REL[(th, e, fc, tt)]

    @block.vector
    def _(v):
        for th in range(2):
            for e in range(E):
                for tt in range(2):
                    v.wait_ge(s_pe.h, YB[(th, e, tt)])
                    v.tensor_copy(wbc[:, tt, :], pb[6][:]).then_inc(s_dv.h, 1)
                    s_dv.n += 1
                    assert s_dv.n == WBE[(th, e, tt)]
                for i in range(8):
                    for tt in range(2):
                        v.wait_ge(s_pe.h, YG[(th, e, i, tt)])
                        ygi = i * 2 + tt
                        bk = ygi % 6
                        v.wait_ge(s_dv.h, s_dv.n)
                        v.scalar_tensor_tensor(tmpv[:], pb[bk][:], b2c[:, e, i:i + 1],
                                               wbc[:, tt, :], ALU.add, ALU.mult).then_inc(s_dv.h, 1)
                        s_dv.n += 1
                        v.wait_ge(s_dv.h, s_dv.n)
                        v.tensor_tensor(ACC[:, i, th * TH + tt * FT:th * TH + (tt + 1) * FT],
                                        ACC[:, i, th * TH + tt * FT:th * TH + (tt + 1) * FT],
                                        tmpv[:], ALU.add).then_inc(s_dv.h, 1)
                        s_dv.n += 1
                        assert s_dv.n == YEV[(th, e, i, tt)]
        for k in range(2 * NC):
            sl = k % 2
            ch, hf2 = divmod(k, 2)
            v.wait_ge(s_xr[sl].h, 16 * (k // 2 + 1))
            v.tensor_tensor(ACC[:, ch, hf2 * TH:(hf2 + 1) * TH],
                            ACC[:, ch, hf2 * TH:(hf2 + 1) * TH], xtr[:, sl, :], ALU.add
                            ).then_inc(s_dv.h, 1)
            s_dv.n += 1
            assert s_dv.n == XAD[k]

    @block.sync
    def _(sync):
        for k in range(2 * NC):
            sl = k % 2
            ch, hf2 = divmod(k, 2)
            if k >= 2:
                sync.wait_ge(s_dv.h, XAD[k - 2])
            sync.dma_start(xtr[:, sl, :],
                           xT_d.rearrange("(c p) t -> p c t", p=128)[:, ch, hf2 * TH:(hf2 + 1) * TH]
                           ).then_inc(s_xr[sl].h, 16)
            s_xr[sl].n += 16
        sync.wait_ge(s_dv.h, XAD[2 * NC - 1])
        sync.dma_start(x1_o.rearrange("(c p) t -> p c t", p=128), ACC[:]).then_inc(s_sy.h, 16)
        s_sy.n += 16
        sync.wait_ge(s_sy.h, s_sy.n)




def emit_l1(b, D, T=T_CORE):
    nc = b.nc
    nft = T // FT

    x_in = D["x_in"]
    wq_d = D["wq"]
    wk_d = D["wk"]
    wv_d = D["wv"]
    bq_d = D["bqc"]
    bk_d = D["bkc"]
    wf_d = D["wf"]
    a1_d = D["a1"]
    a2_d = D["a2"]
    fb_d = D["fbc"]
    mk_d = D["mks"]
    id_d = D["identf"]
    bvrow_d = D["bvrow"]
    km_in = nc.dram_tensor(b.pfx + "km_in", [1, 16], F32)
    km_out = nc.dram_tensor(b.pfx + "km_out", [1, 16], F32)
    out_o = D["out"]

    k_in = nc.dram_tensor(b.pfx + "k_in", [1024, T], BF16)
    k_out = nc.dram_tensor(b.pfx + "k_out", [2048, T], BF16)
    v_in = nc.dram_tensor(b.pfx + "v_in", [T, 16 * 65], BF16)
    v_out = nc.dram_tensor(b.pfx + "v_out", [2 * T, 16 * 65], BF16)

    X = b.sb("X", [128, NC, T])
    xn = b.sb("xn", [128, NC, T], BF16)
    AQ = b.sb("AQ", [128, NC * T], BF16)
    AK = b.sb("AK", [128, NC * T], BF16)
    AV = b.sb("AV", [128, 16 * 16 * 65], BF16)
    wpc = b.sb("wpc", [128, 2, NC, 256], BF16)
    bvb = b.sb("bvb", [128, 1024], BF16)
    ttr = b.sb("ttr", [128, 2, FT])
    mrow = b.sb("mrow", [1, 3 * 16])
    mcol = b.sb("mcol", [128, 16])
    bqc = b.sb("bqcs", [128, 8])
    bkc = b.sb("bkcs", [128, 8])
    fbc = b.sb("fbcs", [128, 8])
    epsc = b.sb("epsc", [1, 1])
    onescol_bf = b.sb("onescol_bf", [128, 1], BF16)
    onesrow_f = b.sb("onesrow_f", [1, 128])
    onesrow_bf = b.sb("onesrow_bf", [1, 128], BF16)

    qb = AQ[:].rearrange("p (c t) -> p c t", c=NC)
    identf = bvb[:, 0:256].bitcast(F32)
    brec = ttr[:, 0, :].bitcast(BF16)[:, 0:FT]
    lrowb = ttr[:, 1, :].bitcast(BF16)[0:1, 0:FT]
    otr = AQ[:, 0:4096].bitcast(F32).rearrange("p (s t) -> p s t", s=2)
    h1 = AK[:].rearrange("p (c t) -> p c t", c=NC)
    kown = h1
    Pbuf = AK[:].rearrange("p (c t) -> p c t", c=32)
    h2 = AV[:, 0:NC * T].rearrange("p (c t) -> p c t", c=NC)
    vt65 = AV[:].rearrange("p (c d) -> p c d", c=16)
    khr = AV[:, 0:2 * 4096].rearrange("p (s t) -> p s t", s=2)
    vhr = AV[:, 2 * 4096:2 * 4096 + 2 * 32 * 65].rearrange("p (s c d) -> p s c d", s=2, c=32)
    mks = AV[:, 2 * 4096 + 2 * 32 * 65:2 * 4096 + 2 * 32 * 65 + 8 * 512].rearrange(
        "p (m q) -> p m q", m=8)
    AKf = AK[:].bitcast(F32)
    arow = AKf[0:1, 0:2048]
    brow = AKf[0:1, 2048:4096]
    bvrow = AKf[0:1, 4096:5120]
    lrow = ttr[0:1, 0, :]
    sqscr = xn[:].rearrange("p c t -> p (c t)")[:, 0:2048]
    O = xn

    pA = b.psum("pA", (128, 2048))
    pB = b.psum("pB", (128, 2048))
    pb = [pA[:, f * FT:(f + 1) * FT] for f in range(4)] + \
         [pB[:, f * FT:(f + 1) * FT] for f in range(4)]

    s_sy = b.sem("s_sy")
    s_wp = [b.sem(f"s_wp{i}") for i in range(2)]
    s_kh = [b.sem(f"s_kh{i}") for i in range(2)]
    s_vh = [b.sem(f"s_vh{i}") for i in range(2)]
    s_cc = b.sem("s_cc")
    s_ag = b.sem("s_ag")
    s_pe = b.sem("s_pe")
    s_ac = b.sem("s_ac")
    s_dv = b.sem("s_dv")
    s_gp = b.sem("s_gp")
    s_ot = [b.sem(f"s_ot{i}") for i in range(2)]

    groups = [(h, it) for h in range(NHD) for it in range(4)]
    nki = lambda it: 8 * it + 8

    L = {"S": {}, "PV": {}, "rb": {}, "ev": {}, "msk": {}, "oev": {}, "rc": {},
         "brv": {}, "bev": {}}
    pe = 0
    pe += nft; L["sum"] = pe
    pe += nft; L["sumsq"] = pe
    pe += 2 * nft; L["lnbc"] = pe
    pe += 1; L["bvb"] = pe
    FF = {}
    for st in range(3):
        for i in range(NC):
            for f in range(nft):
                pe += 1; FF[(st, i, f)] = pe
    PQ = {}
    PK = {}
    PV_ = {}
    for i in range(NC):
        for f in range(nft):
            pe += 1; PQ[(i, f)] = pe
    for i in range(NC):
        for f in range(nft):
            pe += 1; PK[(i, f)] = pe
    for hv in range(2):
        for tc in range(16):
            pe += 1; PV_[(tc, hv)] = pe
    QSQP = {}
    for c in range(NC):
        for par in range(2):
            pe += 1; QSQP[(c, par)] = pe
    L["qsq1"] = pe
    KSQP = {}
    for c in range(NC):
        for par in range(2):
            pe += 1; KSQP[(c, par)] = pe
    L["ksq1"] = pe
    pe += 1; L["mb"] = pe
    for g in groups:
        h, it = g
        L["S"][g] = [pe + s + 1 for s in range(nki(it))]
        pe += nki(it)
        pe += 1; L["PV"][g] = pe
        pe += 1; L["rb"][g] = pe
    L["pe_attn_end"] = pe
    TRO = {}
    for tc in range(16):
        pe += 1; TRO[tc] = pe
    L["pe_end"] = pe

    ac = 0
    ac += 1; L["sq"] = ac
    ac += 1; L["lnsqrt"] = ac
    FTN = {}
    for i in range(NC):
        for f in range(nft):
            ac += 1; FTN[(i, f)] = ac
    PQE = {}
    PKE = {}
    for i in range(NC):
        for f in range(nft):
            ac += 1; PQE[(i, f)] = ac
    for i in range(NC):
        for f in range(nft):
            ac += 1; PKE[(i, f)] = ac
    QSQA = []
    for c in range(NC):
        ac += 1; QSQA.append(ac)
    L["qsqa"] = ac
    KSQA = []
    for c in range(NC):
        ac += 1; KSQA.append(ac)
    L["ksqa"] = ac
    ac += 1; L["msq"] = ac
    ac += 1; L["msq2"] = ac
    for g in groups:
        h, it = g
        L["ev"][g] = [ac + s + 1 for s in range(nki(it))]
        ac += nki(it)
    L["ac_end"] = ac

    dv = 0
    dv += 1; L["cast"] = dv
    dv += 1; L["srowcp"] = dv
    dv += 2; L["rowm"] = dv
    dv += 1; L["recipn"] = dv
    dv += 1; L["recip2"] = dv
    dv += 1; L["mrN"] = dv
    LNA = {}
    for kk in range(NC):
        dv += 2; LNA[kk] = dv
    dv += 1; L["bve"] = dv
    FFE = {}
    for st in range(2):
        for i in range(NC):
            for f in range(nft):
                dv += 1; FFE[(st, i, f)] = dv
    FAD = {}
    for i in range(NC):
        for f in range(nft):
            dv += 1; FAD[(i, f)] = dv
    dv += 1; L["vones"] = dv
    PVE = {}
    for hv in range(2):
        for tc in range(16):
            dv += 1; PVE[(tc, hv)] = dv
    QMX = {}
    for h in range(NHD):
        dv += 1; QMX[h] = dv
    L["qmax"] = dv
    KMX = {}
    for h in range(NHD):
        dv += 1; KMX[h] = dv
    L["kmax1"] = dv
    dv += 1; L["kmax2"] = dv
    dv += 1; L["mmul"] = dv
    dv += 1; L["mcev"] = dv
    for g in groups:
        h, it = g
        nm = min(8, nki(it))
        L["msk"][g] = dv + nm
        dv += nm
        dv += 1; L["rc"][g] = dv
        dv += 1; L["brv"][g] = dv
        dv += 1; L["bev"][g] = dv
        dv += 1; L["oev"][g] = dv
    OAD = {}
    for i in range(NC):
        for f in range(nft):
            dv += 1; OAD[(i, f)] = dv
    TRE = {}
    for tc in range(16):
        dv += 1; TRE[tc] = dv
    L["dv_end"] = dv

    block = b.block()

    @block.sync
    def _(sync):
        for dst, src in [(X[:], x_in.rearrange("(c p) t -> p c t", p=128)),
                         (bqc[:], bq_d), (bkc[:], bk_d),
                         (fbc[:], fb_d), (bvrow[:], bvrow_d)]:
            sync.dma_start(dst, src).then_inc(s_sy.h, 16)
            s_sy.n += 16
    ld_done = s_sy.n


    wnames = [wf_d, a1_d, a2_d, wq_d, wk_d, wv_d]
    WPL = {}

    @block.gpsimd
    def _(g):
        cnt = [0, 0]
        for widx, wd in enumerate(wnames):
            for qp in range(4):
                m = widx * 4 + qp
                s = m % 2
                if m >= 2:
                    pm = m - 2
                    pwidx, pqp = pm // 4, pm % 4
                    if pwidx < 3:
                        L_prev = FF[(pwidx, pqp * 2 + 1, nft - 1)]
                    elif pwidx == 3:
                        L_prev = PQ[(pqp * 2 + 1, nft - 1)]
                    elif pwidx == 4:
                        L_prev = PK[(pqp * 2 + 1, nft - 1)]
                    else:
                        L_prev = PV_[(15, 0 if pqp < 2 else 1)]
                    g.wait_ge(s_pe.h, L_prev)
                g.dma_start(wpc[:, s, :, :],
                            wd.rearrange("(c p) o -> p c o", p=128)[:, :, qp * 256:(qp + 1) * 256]
                            ).then_inc(s_wp[s].h, 16)
                cnt[s] += 16
                s_wp[s].n += 16
                WPL[(widx, qp)] = cnt[s]

    @block.vector
    def _(v):
        v.memset(onescol_bf[:], 1.0)
        v.memset(onesrow_f[:], 1.0)
        v.memset(onesrow_bf[:], 1.0)
        v.memset(epsc[:], 1e-5)
        v.wait_ge(s_sy.h, ld_done)
        v.tensor_copy(xn[:], X[:]).then_inc(s_dv.h, 1)
        s_dv.n += 1
    assert s_dv.n == L["cast"]

    @block.scalar
    def _(sc):
        sc.wait_ge(s_dv.h, L["cast"])
        sc.activation(qb[:], xn[:], AF.Square).then_inc(s_ac.h, 1)
        s_ac.n += 1

    @block.tensor
    def _(t):
        t.wait_ge(s_dv.h, L["cast"])
        for f in range(nft):
            for k in range(NC):
                mm = t.matmul(pb[f][0:1, :], onescol_bf[:], xn[:, k, f * FT:(f + 1) * FT],
                              start=(k == 0), stop=(k == NC - 1))
            mm.then_inc(s_pe.h, 1)
            s_pe.n += 1
        t.wait_ge(s_ac.h, L["sq"])
        for f in range(nft):
            for k in range(NC):
                mm = t.matmul(pb[4 + f][0:1, :], onescol_bf[:], qb[:, k, f * FT:(f + 1) * FT],
                              start=(k == 0), stop=(k == NC - 1))
            mm.then_inc(s_pe.h, 1)
            s_pe.n += 1

    @block.vector
    def _(v):
        v.wait_ge(s_pe.h, L["sum"])
        for f in range(nft):
            cp = v.tensor_copy(arow[0:1, f * FT:(f + 1) * FT], pb[f][0:1, :])
        cp.then_inc(s_dv.h, 1)
        s_dv.n += 1
        assert s_dv.n == L["srowcp"]
        v.wait_ge(s_pe.h, L["sumsq"])
        v.wait_ge(s_dv.h, s_dv.n)
        v.tensor_scalar(arow[:], arow[:], 1.0 / 1024.0, None, ALU.mult).then_inc(s_dv.h, 1)
        s_dv.n += 1
        v.wait_ge(s_dv.h, s_dv.n)
        v.tensor_tensor(brow[:], arow[:], arow[:], ALU.mult).then_inc(s_dv.h, 1)
        s_dv.n += 1
        assert s_dv.n == L["rowm"]
        sscat = [pb[4 + f][0:1, :] for f in range(nft)]
        for f in range(nft):
            v.wait_ge(s_dv.h, s_dv.n)
            ins = v.scalar_tensor_tensor(brow[0:1, f * FT:(f + 1) * FT], sscat[f],
                                         1.0 / 1024.0, brow[0:1, f * FT:(f + 1) * FT],
                                         ALU.mult, ALU.subtract)
        ins.then_inc(s_dv.h, 1)
        s_dv.n += 1
        assert s_dv.n == L["recipn"]

    @block.scalar
    def _(sc):
        sc.wait_ge(s_dv.h, L["recipn"])
        sc.activation(brow[:], brow[:], AF.Sqrt, bias=epsc[0:1, 0:1]).then_inc(s_ac.h, 1)
        s_ac.n += 1
    assert True

    @block.vector
    def _(v):
        v.wait_ge(s_ac.h, L["lnsqrt"])
        v.reciprocal(brow[:], brow[:]).then_inc(s_dv.h, 1)
        s_dv.n += 1
        assert s_dv.n == L["recip2"]
        v.wait_ge(s_dv.h, s_dv.n)
        v.tensor_tensor(arow[:], arow[:], brow[:], ALU.mult).then_inc(s_dv.h, 1)
        s_dv.n += 1
        assert s_dv.n == L["mrN"]

    @block.tensor
    def _(t):
        t.wait_ge(s_dv.h, L["mrN"])
        for f in range(nft):
            t.matmul(pb[f][:], onesrow_f[:], brow[0:1, f * FT:(f + 1) * FT],
                     start=True, stop=True).then_inc(s_pe.h, 1)
            s_pe.n += 1
        for f in range(nft):
            t.matmul(pb[4 + f][:], onesrow_f[:], arow[0:1, f * FT:(f + 1) * FT],
                     start=True, stop=True).then_inc(s_pe.h, 1)
            s_pe.n += 1
        t.wait_ge(s_dv.h, LNA[NC - 1])
        for hvv in range(2):
            mm = t.matmul(pb[hvv][:], onesrow_f[:], bvrow[0:1, hvv * 512:(hvv + 1) * 512],
                          start=True, stop=True)
        mm.then_inc(s_pe.h, 1)
        s_pe.n += 1
        assert s_pe.n == L["bvb"]
    assert True

    @block.vector
    def _(v):
        v.wait_ge(s_pe.h, L["lnbc"])
        for kk in range(NC):
            v.wait_ge(s_dv.h, s_dv.n)
            v.tensor_tensor(xn[:, kk, :], X[:, kk, :], pA[:], ALU.mult).then_inc(s_dv.h, 1)
            s_dv.n += 1
            v.wait_ge(s_dv.h, s_dv.n)
            v.tensor_tensor(xn[:, kk, :], xn[:, kk, :], pB[:], ALU.subtract).then_inc(s_dv.h, 1)
            s_dv.n += 1
            assert s_dv.n == LNA[kk]
        v.wait_ge(s_pe.h, L["bvb"])
        v.tensor_copy(bvb[:, 0:512], pb[0][:])
        v.tensor_copy(bvb[:, 512:1024], pb[1][:]).then_inc(s_dv.h, 1)
        s_dv.n += 1
        assert s_dv.n == L["bve"]

    @block.tensor
    def _(t):
        for st in range(3):
            rhs_buf = [xn, h1, h2][st]
            for i in range(NC):
                qp = i // 2
                widx = st
                t.wait_ge(s_wp[(widx * 4 + qp) % 2].h, WPL[(widx, qp)])
                for f in range(nft):
                    j = (st * NC + i) * nft + f
                    jj = j % 8
                    if st == 0 and j < 8:
                        t.wait_ge(s_dv.h, L["bve"])
                    if st >= 1 and i == 0 and f == 0:
                        t.wait_ge(s_dv.h, FFE[(st - 1, NC - 1, nft - 1)])
                    prev = j - 8
                    if prev >= 0:
                        pst, pi = divmod(prev // nft, NC)
                        pf = prev % nft
                        if pst < 2:
                            t.wait_ge(s_dv.h, FFE[(pst, pi, pf)])
                        else:
                            t.wait_ge(s_ac.h, FTN[(pi, pf)])
                    for k in range(NC):
                        mm = t.matmul(pb[jj][:], wpc[:, (st * 4 + qp) % 2, k, (i % 2) * 128:(i % 2 + 1) * 128],
                                      rhs_buf[:, k, f * FT:(f + 1) * FT],
                                      start=(k == 0), stop=(k == NC - 1))
                    mm.then_inc(s_pe.h, 1)
                    s_pe.n += 1
                    assert s_pe.n == FF[(st, i, f)]

    @block.vector
    def _(v):
        for st in range(2):
            dst = [h1, h2][st]
            for i in range(NC):
                for f in range(nft):
                    j = (st * NC + i) * nft + f
                    v.wait_ge(s_pe.h, FF[(st, i, f)])
                    v.wait_ge(s_dv.h, s_dv.n)
                    v.tensor_copy(dst[:, i, f * FT:(f + 1) * FT], pb[j % 8][:]).then_inc(s_dv.h, 1)
                    s_dv.n += 1
                    assert s_dv.n == FFE[(st, i, f)]

    @block.scalar
    def _(sc):
        for i in range(NC):
            for f in range(nft):
                j = (2 * NC + i) * nft + f
                sc.wait_ge(s_pe.h, FF[(2, i, f)])
                jf = i * nft + f
                if jf >= 2:
                    pj = jf - 2
                    sc.wait_ge(s_dv.h, FAD[(pj // nft, pj % nft)])
                sc.activation(ttr[:, jf % 2, :], pb[j % 8][:], AF.Tanh, bias=fbc[:, i:i + 1]
                              ).then_inc(s_ac.h, 1)
                s_ac.n += 1
                assert s_ac.n == FTN[(i, f)]

    @block.vector
    def _(v):
        for i in range(NC):
            for f in range(nft):
                v.wait_ge(s_ac.h, FTN[(i, f)])
                v.wait_ge(s_dv.h, s_dv.n)
                jf = i * nft + f
                v.tensor_tensor(X[:, i, f * FT:(f + 1) * FT], X[:, i, f * FT:(f + 1) * FT],
                                ttr[:, jf % 2, :], ALU.add).then_inc(s_dv.h, 1)
                s_dv.n += 1
                assert s_dv.n == FAD[(i, f)]

    @block.tensor
    def _(t):
        for pi, (PX, widx) in enumerate([(PQ, 3), (PK, 4)]):
            for i in range(NC):
                qp = i // 2
                t.wait_ge(s_wp[(widx * 4 + qp) % 2].h, WPL[(widx, qp)])
                for f in range(nft):
                    j = i * nft + f
                    jj = j % 8
                    if pi == 0 and j < 8:
                        t.wait_ge(s_ac.h, FTN[(NC - 1, nft - 1)])
                        t.wait_ge(s_dv.h, FAD[(NC - 1, nft - 1)])
                    elif pi == 0:
                        t.wait_ge(s_ac.h, PQE[((j - 8) // nft, (j - 8) % nft)])
                    elif pi == 1 and j < 8:
                        t.wait_ge(s_ac.h, PQE[(NC - 1, nft - 1)])
                    else:
                        t.wait_ge(s_ac.h, PKE[((j - 8) // nft, (j - 8) % nft)])
                    for k in range(NC):
                        mm = t.matmul(pb[jj][:], wpc[:, (widx * 4 + qp) % 2, k, (i % 2) * 128:(i % 2 + 1) * 128],
                                      xn[:, k, f * FT:(f + 1) * FT], start=(k == 0), stop=(k == NC - 1))
                    mm.then_inc(s_pe.h, 1)
                    s_pe.n += 1
                    assert s_pe.n == (PQ if pi == 0 else PK)[(i, f)]
        widx = 5
        for hv in range(2):
            for tc in range(16):
                qpa, qpb_ = 2 * hv, 2 * hv + 1
                t.wait_ge(s_wp[(widx * 4 + qpa) % 2].h, WPL[(widx, qpa)])
                t.wait_ge(s_wp[(widx * 4 + qpb_) % 2].h, WPL[(widx, qpb_)])
                m = hv * 16 + tc
                bank = pb[m % 8]
                if m < 8:
                    t.wait_ge(s_ac.h, PKE[(NC - 1, nft - 1)])
                else:
                    pm = m - 8
                    t.wait_ge(s_dv.h, PVE[(pm % 16, pm // 16)])
                for k in range(NC):
                    t.matmul(bank[:, 0:256], xn[:, k, tc * 128:(tc + 1) * 128],
                             wpc[:, (widx * 4 + qpa) % 2, k, :],
                             start=(k == 0), stop=(k == NC - 1))
                for k in range(NC):
                    mm = t.matmul(bank[:, 256:512], xn[:, k, tc * 128:(tc + 1) * 128],
                                  wpc[:, (widx * 4 + qpb_) % 2, k, :],
                                  start=(k == 0), stop=(k == NC - 1))
                mm.then_inc(s_pe.h, 1)
                s_pe.n += 1
                assert s_pe.n == PV_[(tc, hv)]

    @block.scalar
    def _(sc):
        for pi, (PX, PXE, dst, bc) in enumerate([(PQ, PQE, qb, bqc), (PK, PKE, kown, bkc)]):
            for i in range(NC):
                for f in range(nft):
                    j = i * nft + f
                    sc.wait_ge(s_pe.h, PX[(i, f)])
                    sc.add(dst[:, i, f * FT:(f + 1) * FT], pb[j % 8][:],
                           bc[:, i:i + 1]).then_inc(s_ac.h, 1)
                    s_ac.n += 1
                    assert s_ac.n == PXE[(i, f)]

    @block.vector
    def _(v):
        v.wait_ge(s_pe.h, FF[(2, NC - 1, nft - 1)])
        v.memset(vt65[:], 1.0).then_inc(s_dv.h, 1)
        s_dv.n += 1
        assert s_dv.n == L["vones"]
        for hv in range(2):
            for tc in range(16):
                m = hv * 16 + tc
                v.wait_ge(s_pe.h, PV_[(tc, hv)])
                v.wait_ge(s_dv.h, s_dv.n)
                v.tensor_tensor(
                    vt65[:, tc, hv * 8 * 65:(hv * 8 + 8) * 65].rearrange("p (h d) -> p h d", h=8)[:, :, 0:64],
                    pb[m % 8][:].rearrange("p (h d) -> p h d", h=8),
                    bvb[:, hv * 512:(hv + 1) * 512].rearrange("p (h d) -> p h d", h=8),
                    ALU.add).then_inc(s_dv.h, 1)
                s_dv.n += 1
                assert s_dv.n == PVE[(tc, hv)]

    @block.scalar
    def _(sc):
        sc.wait_ge(s_ac.h, PQE[(NC - 1, nft - 1)])
        sc.wait_ge(s_dv.h, PVE[(15, 1)])
        for c in range(NC):
            if c >= 1:
                sc.wait_ge(s_pe.h, QSQP[(c - 1, 1)])
            sc.activation(sqscr, qb[:, c, :], AF.Square).then_inc(s_ac.h, 1)
            s_ac.n += 1
        assert s_ac.n == L["qsqa"]
        for c in range(NC):
            sc.wait_ge(s_pe.h, QSQP[(NC - 1, 1)] if c == 0 else KSQP[(c - 1, 1)])
            sc.activation(sqscr, kown[:, c, :], AF.Square).then_inc(s_ac.h, 1)
            s_ac.n += 1
        assert s_ac.n == L["ksqa"]
        sc.wait_ge(s_dv.h, L["mmul"])
        sc.activation(mrow[0:1, 0:16], mrow[0:1, 0:16], AF.Sqrt).then_inc(s_ac.h, 1)
        s_ac.n += 1
        sc.wait_ge(s_ac.h, s_ac.n)
        sc.mul(mrow[0:1, 0:16], mrow[0:1, 0:16], -0.125).then_inc(s_ac.h, 1)
        s_ac.n += 1
        assert s_ac.n == L["msq2"]

    @block.scalar
    def _(sc):
        for gi, g in enumerate(groups):
            h, it = g
            for si in range(nki(it)):
                sc.wait_ge(s_pe.h, L["S"][g][si])
                if gi >= 1:
                    sc.wait_ge(s_pe.h, L["PV"][groups[gi - 1]])
                sc.activation(Pbuf[:, si, :], pb[si % 6][:], AF.Exp, scale=0.125,
                              bias=mcol[:, h:h + 1]).then_inc(s_ac.h, 1)
                s_ac.n += 1
            assert s_ac.n == L["ev"][g][-1]

    @block.tensor
    def _(t):
        t.wait_ge(s_dv.h, PVE[(15, 1)])
        for c in range(NC):
            t.wait_ge(s_ac.h, QSQA[c])
            for par in range(2):
                hb = par * 64
                buf = pA if par == 0 else pB
                if c * 2 + par >= 2:
                    ph = c * 2 + par - 2
                    t.wait_ge(s_dv.h, QMX[ph])
                for f in range(nft):
                    mm = t.matmul(buf[0:1, f * FT:(f + 1) * FT], onescol_bf[hb:hb + 64, :],
                                  sqscr[hb:hb + 64, f * FT:(f + 1) * FT],
                                  start=True, stop=True)
                mm.then_inc(s_pe.h, 1)
                s_pe.n += 1
                assert s_pe.n == QSQP[(c, par)]
        for c in range(NC):
            t.wait_ge(s_ac.h, KSQA[c])
            for par in range(2):
                hb = par * 64
                buf = pA if par == 0 else pB
                if c * 2 + par >= 2:
                    t.wait_ge(s_dv.h, KMX[c * 2 + par - 2])
                else:
                    t.wait_ge(s_dv.h, QMX[NHD - 1 - (1 - par)])
                for f in range(nft):
                    mm = t.matmul(buf[0:1, f * FT:(f + 1) * FT], onescol_bf[hb:hb + 64, :],
                                  sqscr[hb:hb + 64, f * FT:(f + 1) * FT],
                                  start=True, stop=True)
                mm.then_inc(s_pe.h, 1)
                s_pe.n += 1
                assert s_pe.n == KSQP[(c, par)]
        t.wait_ge(s_ac.h, L["msq2"])
        t.matmul(pb[2][:, 0:16], onesrow_f[:], mrow[0:1, 0:16], start=True, stop=True
                 ).then_inc(s_pe.h, 1)
        s_pe.n += 1
        assert s_pe.n == L["mb"]

    @block.vector
    def _(v):
        for h in range(NHD):
            buf = pA if h % 2 == 0 else pB
            v.wait_ge(s_pe.h, QSQP[(h // 2, h % 2)])
            v.reduce_max(mrow[0:1, 16 + h:17 + h], buf[0:1, :], axis=AX.X).then_inc(s_dv.h, 1)
            s_dv.n += 1
            assert s_dv.n == QMX[h]
        for h in range(NHD):
            buf = pA if h % 2 == 0 else pB
            v.wait_ge(s_pe.h, KSQP[(h // 2, h % 2)])
            v.reduce_max(mrow[0:1, 32 + h:33 + h], buf[0:1, :], axis=AX.X).then_inc(s_dv.h, 1)
            s_dv.n += 1
            assert s_dv.n == KMX[h]
        v.wait_ge(s_ag.h, 16 * 5)
        v.tensor_copy(mrow[0:1, 0:16], mrow[0:1, 0:16]).then_inc(s_dv.h, 1)
        s_dv.n += 1
        assert s_dv.n == L["kmax2"]
        v.wait_ge(s_dv.h, s_dv.n)
        v.tensor_tensor(mrow[0:1, 0:16], mrow[0:1, 16:32], mrow[0:1, 0:16], ALU.mult
                        ).then_inc(s_dv.h, 1)
        s_dv.n += 1
        assert s_dv.n == L["mmul"]
        v.wait_ge(s_pe.h, L["mb"])
        v.tensor_copy(mcol[:], pb[2][:, 0:16]).then_inc(s_dv.h, 1)
        s_dv.n += 1
        assert s_dv.n == L["mcev"]

    @block.gpsimd
    def _(g):
        g.wait_ge(s_ac.h, PKE[(NC - 1, nft - 1)])
        g.dma_start(k_in.ap().rearrange("(c p) t -> p c t", p=128), kown[:]).then_inc(s_ag.h, 16)
        s_ag.n += 16
        g.wait_ge(s_ag.h, s_ag.n)
        g.wait_ge(s_dv.h, PVE[(15, 1)])
        g.dma_start(v_in.ap().rearrange("(c p) d -> p c d", p=128), vt65[:]).then_inc(s_ag.h, 16)
        s_ag.n += 16
        g.wait_ge(s_ag.h, s_ag.n)
        g.dma_start(mks[:], mk_d.rearrange("m p q -> p m q")).then_inc(s_gp.h, 16)
        s_gp.n += 16
        g.collective_compute("AllGather", ALU.bypass, replica_groups=RG,
                             ins=[k_in.ap().opt()], outs=[k_out.ap().opt()]).then_inc(s_cc.h, 1)
        s_cc.n += 1
        g.collective_compute("AllGather", ALU.bypass, replica_groups=RG,
                             ins=[v_in.ap().opt()], outs=[v_out.ap().opt()]).then_inc(s_cc.h, 1)
        s_cc.n += 1
        g.wait_ge(s_dv.h, L["kmax1"])
        g.dma_start(km_in.ap(), mrow[0:1, 32:48]).then_inc(s_ag.h, 16)
        s_ag.n += 16
        g.wait_ge(s_ag.h, s_ag.n)
        g.collective_compute("AllReduce", ALU.max, replica_groups=RG,
                             ins=[km_in.ap().opt()], outs=[km_out.ap().opt()]).then_inc(s_cc.h, 1)
        s_cc.n += 1
        g.wait_ge(s_cc.h, s_cc.n)
        g.dma_start(mrow[0:1, 0:16], km_out.ap()).then_inc(s_ag.h, 16)
        s_ag.n += 16
        g.wait_ge(s_ag.h, s_ag.n)
        g.dma_start(km_in.ap(), km_out.ap()).then_inc(s_ag.h, 16)
        s_ag.n += 16
        g.wait_ge(s_ag.h, s_ag.n)

    kh_done = {}
    vh_done = {}

    @block.sync
    def _(sync):
        sync.wait_ge(s_cc.h, 2)
        for h in range(NHD):
            sl = h % 2
            if h >= 2:
                sync.wait_ge(s_pe.h, L["rb"][(h - 2, 3)])
            for o in range(2):
                hb = (h % 2) * 64
                sync.dma_start(
                    khr[hb:hb + 64, sl, :].rearrange("p (a t) -> p a t", t=256)[:, o::2, :],
                    k_out.ap()[o * 1024 + h * HD1:o * 1024 + (h + 1) * HD1, :].rearrange(
                        "p (a t) -> p a t", t=256)).then_inc(s_kh[sl].h, 16)
                s_kh[sl].n += 16
            kh_done[h] = 16 * (h // 2 + 1) * 2
            for o in range(2):
                for ci in range(2):
                    sync.dma_start(
                        vhr[:, sl, 2 * o + ci::4, :],
                        v_out.ap()[o * 2048 + ci * 128:(o + 1) * 2048, h * 65:(h + 1) * 65
                                   ].rearrange("(a p) d -> p a d", p=128)[:, 0::2, :]
                        ).then_inc(s_vh[sl].h, 16)
                    s_vh[sl].n += 16
            vh_done[h] = 16 * (h // 2 + 1) * 4

    @block.tensor
    def _(t):
        t.wait_ge(s_dv.h, L["mcev"])
        for gi, g in enumerate(groups):
            h, it = g
            sl = h % 2
            if it == 0:
                t.wait_ge(s_kh[sl].h, kh_done[h])
                t.wait_ge(s_vh[sl].h, vh_done[h])
            for si in range(nki(it)):
                bk = si % 6
                if si >= 6:
                    t.wait_ge(s_ac.h, L["ev"][g][si - 6])
                elif gi >= 1:
                    pg = groups[gi - 1]
                    pn = nki(pg[1])
                    t.wait_ge(s_ac.h, L["ev"][pg][pn - 6 + si] if pn >= 6 - si else 0)
                hb = (h % 2) * 64
                mm = t.matmul(pb[bk][:], khr[hb:hb + 64, sl, si * 128:(si + 1) * 128],
                              qb[hb:hb + 64, h // 2, it * FT:(it + 1) * FT],
                              start=True, stop=True)
                mm.then_inc(s_pe.h, 1)
                s_pe.n += 1
            assert s_pe.n == L["S"][g][-1]
            t.wait_ge(s_dv.h, L["msk"][g])
            if gi >= 1:
                t.wait_ge(s_dv.h, L["oev"][groups[gi - 1]])
            for si in range(nki(it)):
                mm = t.matmul(pb[6][0:65, :], vhr[:, sl, si, :], Pbuf[:, si, :],
                              start=(si == 0), stop=(si == nki(it) - 1))
            mm.then_inc(s_pe.h, 1)
            s_pe.n += 1
            assert s_pe.n == L["PV"][g]
            t.wait_ge(s_dv.h, L["brv"][g])
            t.matmul(pb[7][:], onesrow_bf[:], lrowb[0:1, :], start=True, stop=True
                     ).then_inc(s_pe.h, 1)
            s_pe.n += 1
            assert s_pe.n == L["rb"][g]

    @block.vector
    def _(v):
        v.wait_ge(s_gp.h, 16)
        for gi, g in enumerate(groups):
            h, it = g
            nm = 0
            for si in range(nki(it)):
                r0 = si - 8 * it
                if 0 <= r0 < 8:
                    v.wait_ge(s_ac.h, L["ev"][g][si])
                    v.wait_ge(s_dv.h, s_dv.n)
                    v.tensor_tensor(Pbuf[:, si, :], Pbuf[:, si, :], mks[:, r0, :], ALU.mult
                                    ).then_inc(s_dv.h, 1)
                    s_dv.n += 1
                    nm += 1
            assert s_dv.n == L["msk"][g]
            v.wait_ge(s_pe.h, L["PV"][g])
            v.reciprocal(lrow[0:1, :], pb[6][64:65, :]).then_inc(s_dv.h, 1)
            s_dv.n += 1
            assert s_dv.n == L["rc"][g]
            v.wait_ge(s_dv.h, s_dv.n)
            v.tensor_copy(lrowb[0:1, :], lrow[0:1, :]).then_inc(s_dv.h, 1)
            s_dv.n += 1
            assert s_dv.n == L["brv"][g]
            v.wait_ge(s_pe.h, L["rb"][g])
            v.tensor_copy(brec[:], pb[7][:]).then_inc(s_dv.h, 1)
            s_dv.n += 1
            assert s_dv.n == L["bev"][g]
            v.wait_ge(s_dv.h, s_dv.n)
            v.tensor_tensor(O[(h % 2) * 64:(h % 2) * 64 + 64, h // 2, it * FT:(it + 1) * FT],
                            pb[6][0:64, :], brec[0:64, :], ALU.mult).then_inc(s_dv.h, 1)
            s_dv.n += 1
            assert s_dv.n == L["oev"][g]
        for i in range(NC):
            for f in range(nft):
                v.wait_ge(s_dv.h, s_dv.n)
                v.tensor_tensor(X[:, i, f * FT:(f + 1) * FT], X[:, i, f * FT:(f + 1) * FT],
                                O[:, i, f * FT:(f + 1) * FT], ALU.add).then_inc(s_dv.h, 1)
                s_dv.n += 1
                assert s_dv.n == OAD[(i, f)]
        for tc in range(16):
            v.wait_ge(s_pe.h, TRO[tc])
            v.tensor_copy(otr[:, tc % 2, :], (pA if tc % 2 == 0 else pB)[:, 0:1024]
                          ).then_inc(s_dv.h, 1)
            s_dv.n += 1
            assert s_dv.n == TRE[tc]

    @block.tensor
    def _(t):
        t.wait_ge(s_sy.h, ld_done + 16)
        for tc in range(16):
            if tc >= 2:
                t.wait_ge(s_ot[tc % 2].h, 16 * (tc // 2))
            if tc < 2:
                t.wait_ge(s_dv.h, OAD[(NC - 1, nft - 1)])
            for k in range(NC):
                mm = t.transpose((pA if tc % 2 == 0 else pB)[:, k * 128:(k + 1) * 128],
                                 X[:, k, tc * 128:(tc + 1) * 128], identf[:])
            mm.then_inc(s_pe.h, 1)
            s_pe.n += 1
            assert s_pe.n == TRO[tc]

    @block.sync
    def _(sync):
        sync.wait_ge(s_dv.h, OAD[(NC - 1, nft - 1)])
        sync.dma_start(identf[:], id_d).then_inc(s_sy.h, 16)
        s_sy.n += 16
        for tc in range(16):
            sync.wait_ge(s_dv.h, TRE[tc])
            sync.dma_start(out_o[tc * 128:(tc + 1) * 128, :], otr[:, tc % 2, :]
                           ).then_inc(s_ot[tc % 2].h, 16)
            s_ot[tc % 2].n += 16
        for sl in range(2):
            sync.wait_ge(s_ot[sl].h, s_ot[sl].n)







def emit_reshard(b, D, T=T_CORE):
    nc = b.nc
    x1 = D["x1"]
    dst = D["l1x"]
    iscol_d = D["iscol"]
    rs_in = nc.dram_tensor(b.pfx + "rs_in", [1024, T], F32)
    rs_out = nc.dram_tensor(b.pfx + "rs_out", [2048, T], F32)

    xs = b.sb("xs", [128, 2, 1024])
    xsel = b.sb("xsel", [128, 2, 1024])
    iscl = b.sb("iscl", [128, 1])
    s_sy = b.sem("s_sy")
    s_ag = b.sem("s_ag")
    s_cc = b.sem("s_cc")
    s_dv = b.sem("s_dv")
    s_st2 = [b.sem(f"s_st{i}") for i in range(2)]
    s_vv = b.sem("s_vv")
    block = b.block()

    @block.gpsimd
    def _(g):
        for s in range(2):
            g.dma_start(rs_in.ap().rearrange("d (a u) -> d a u", u=256)[:, s::2, :].rearrange("d a u -> d (a u)")
                        if False else
                        rs_in.ap()[:, s * 1024:(s + 1) * 1024].rearrange("d (a u) -> d a u", u=256),
                        x1.rearrange("d (a u) -> d a u", u=256)[:, s::2, :]).then_inc(s_ag.h, 16)
            s_ag.n += 16
            g.wait_ge(s_ag.h, s_ag.n)
        g.collective_compute("AllGather", ALU.bypass, replica_groups=RG,
                             ins=[rs_in.ap().opt()], outs=[rs_out.ap().opt()]).then_inc(s_cc.h, 1)
        s_cc.n += 1
        g.wait_ge(s_cc.h, s_cc.n)

    @block.sync
    def _(sync):
        sync.dma_start(iscl[:], iscol_d).then_inc(s_sy.h, 16)
        s_sy.n += 16
        n = 0
        for c in range(8):
            for r in range(2):
                m = c * 2 + r
                if m >= 1:
                    sync.wait_ge(s_dv.h, m)
                if m >= 2:
                    sync.wait_ge(s_st2[m % 2].h, 16 * ((m - 2) // 2 + 1))
                sync.wait_ge(s_cc.h, 1)
                sync.dma_start(xs[:, 0, :], rs_out.ap()[r * 1024 + c * 128:r * 1024 + (c + 1) * 128, 0:1024]
                               ).then_inc(s_sy.h, 16)
                s_sy.n += 16
                sync.dma_start(xs[:, 1, :], rs_out.ap()[r * 1024 + c * 128:r * 1024 + (c + 1) * 128, 1024:2048]
                               ).then_inc(s_sy.h, 16)
                s_sy.n += 16

    @block.vector
    def _(v):
        for c in range(8):
            for r in range(2):
                m = c * 2 + r
                v.wait_ge(s_sy.h, 16 + 32 * (m + 1))
                if m >= 2:
                    v.wait_ge(s_st2[m % 2].h, 16 * ((m - 2) // 2 + 1))
                v.tensor_tensor(xsel[:, m % 2, :], xs[:, 1, :], xs[:, 0, :], ALU.subtract).then_inc(s_vv.h, 1)
                s_vv.n += 1
                v.wait_ge(s_vv.h, s_vv.n)
                v.scalar_tensor_tensor(xsel[:, m % 2, :],
                                       xsel[:, m % 2, :], iscl[:, 0:1], xs[:, 0, :],
                                       ALU.mult, ALU.add).then_inc(s_dv.h, 1)
                s_dv.n += 1

    @block.scalar
    def _(sc):
        for c in range(8):
            for r in range(2):
                m = c * 2 + r
                sc.wait_ge(s_dv.h, m + 1)
                sc.dma_start(dst.rearrange("(cc p) t -> p cc t", p=128)[:, c, r * 1024:(r + 1) * 1024],
                             xsel[:, m % 2, :]).then_inc(s_st2[m % 2].h, 16)
                s_st2[m % 2].n += 16
        for sl in range(2):
            sc.wait_ge(s_st2[sl].h, s_st2[sl].n)




def build_fused():
    b = Builder("mega")
    nc = b.nc
    F = F32
    T = T_CORE
    EI = lambda n, s, dt=F32: nc.dram_tensor(n, s, dt, kind="ExternalInput").ap()
    INT = lambda n, s, dt=F32: nc.dram_tensor(n, s, dt).ap()

    io = {}
    io["x_in"] = EI("x_in", [T, 1024])
    io["identf"] = EI("identf", [128, 128])
    io["iscol"] = EI("iscol", [128, 1])
    for n, s in [("winT", [1024, 3072]), ("wgT", [1536, 3072]), ("woutT", [1536, 1024]),
                 ("cwc", [128, 12, 4]), ("cbc", [128, 12]), ("bgc", [128, 24]),
                 ("fbc_h", [128, 12]), ("gnc_h", [128, 8])]:
        io[n] = EI(n, s)
    for n, s in [("wgt1", [1024, 4096]), ("wst1", [2048, 1024]), ("gnc_m1", [128, 8]),
                 ("wgt2", [1024, 4096]), ("wst2", [2048, 1024]), ("gnc_m2", [128, 8])]:
        io[n] = EI(n, s)
    for n, s in [("wqT", [1024, 1024]), ("wkT", [1024, 128]), ("wvT", [1024, 128]),
                 ("woT", [1024, 1024]), ("gnc_sq", [128, 8]), ("cosT", [128, T]),
                 ("sinT", [128, T]), ("rotP", [128, 128]), ("mL", [4, 128, 512]),
                 ("mR", [4, 128, 512])]:
        io[n] = EI(n, s)
    for n, s in [("gwT", [1024, 8]), ("w1", [8, 1024, 4096]), ("b1c", [128, 8, 32]),
                 ("w2", [8, 4096, 1024]), ("b2c", [128, 8, 8]), ("sel", [8, 1024])]:
        io[n] = EI(n, s)
    for n, s in [("wq", [1024, 1024]), ("wk", [1024, 1024]), ("wv", [1024, 1024]),
                 ("bqc", [128, 8]), ("bkc", [128, 8]), ("wf", [1024, 1024]),
                 ("a1", [1024, 1024]), ("a2", [1024, 1024]), ("fbc_l", [128, 8]),
                 ("mks", [8, 128, 512]), ("bvrow", [1, 1024])]:
        io[n] = EI(n, s)
    out_o = nc.dram_tensor("out", [T, 1024], F32, kind="ExternalOutput").ap()

    c_xT = INT("c_xT", [1024, T])
    c_xa = INT("c_xa", [1024, T])
    c_xnb = INT("c_xnb", [1024, T], BF16)
    c_x1 = INT("c_x1", [1024, T])
    c_l1x = INT("c_l1x", [1024, T])

    b.begin_phase("p0_")
    emit_prep(b, {"x_in": io["x_in"], "identf": io["identf"],
                  "xT": c_xT, "xa0": c_xa, "xnb": c_xnb})
    b.end_phase()

    b.begin_phase("p1_")
    emit_hawk(b, {"xa": c_xa, "winT": io["winT"], "wgT": io["wgT"], "woutT": io["woutT"],
                  "cwc": io["cwc"], "cbc": io["cbc"], "bgc": io["bgc"], "fbc": io["fbc_h"],
                  "gnc": io["gnc_h"], "iscol": io["iscol"], "xa_out": c_xa})
    b.end_phase()

    b.begin_phase("p2_")
    emit_mlp(b, {"xa": c_xa, "wgt": io["wgt1"], "wst": io["wst1"], "gnc": io["gnc_m1"],
                 "xa_out": c_xa})
    b.end_phase()

    b.begin_phase("p3_")
    emit_smqa(b, {"xa": c_xa, "wqT": io["wqT"], "wkT": io["wkT"], "wvT": io["wvT"],
                  "woT": io["woT"], "gnc": io["gnc_sq"], "cosT": io["cosT"],
                  "sinT": io["sinT"], "rotP": io["rotP"], "mL": io["mL"], "mR": io["mR"],
                  "iscol": io["iscol"], "xa_out": c_xa})
    b.end_phase()

    b.begin_phase("p4_")
    emit_mlp(b, {"xa": c_xa, "wgt": io["wgt2"], "wst": io["wst2"], "gnc": io["gnc_m2"],
                 "xa_out": c_xa})
    b.end_phase()

    b.begin_phase("p5_")
    emit_moe(b, {"xT": c_xT, "xnb": c_xnb, "xa": c_xa, "gwT": io["gwT"], "w1": io["w1"],
                 "b1c": io["b1c"], "w2": io["w2"], "b2c": io["b2c"], "sel": io["sel"],
                 "identf": io["identf"], "x1": c_x1})
    b.end_phase()

    b.begin_phase("p6_")
    emit_reshard(b, {"x1": c_x1, "l1x": c_l1x, "iscol": io["iscol"]})
    b.end_phase()

    b.begin_phase("p7_")
    emit_l1(b, {"x_in": c_l1x, "wq": io["wq"], "wk": io["wk"], "wv": io["wv"],
                "bqc": io["bqc"], "bkc": io["bkc"], "wf": io["wf"], "a1": io["a1"],
                "a2": io["a2"], "fbc": io["fbc_l"], "mks": io["mks"],
                "identf": io["identf"], "bvrow": io["bvrow"], "out": out_o})
    b.end_phase()

    b.done()
    return b.nc




_PROGS = {}


def _get(name, builder):
    if name not in _PROGS:
        _PROGS[name] = builder()
    return _PROGS[name]


def kernel(**inp):
    f32 = np.float32
    x = np.asarray(inp["L0_attn_x"], f32)
    B, T2, D = x.shape
    cores = list(range(8))
    identf = np.eye(128, dtype=f32)

    winT, wgT, woutT, cwc, cbc, bgc, fbc = host_hawk_weights(
        np.asarray(inp["L0_attn_hawk_Win"], f32), np.asarray(inp["L0_attn_hawk_Wg"], f32),
        np.asarray(inp["L0_attn_hawk_Wout"], f32), np.asarray(inp["L0_attn_hawk_cw"], f32),
        np.asarray(inp["L0_attn_hawk_cb"], f32), np.asarray(inp["L0_attn_hawk_bg"], f32),
        np.asarray(inp["L0_attn_hawk_fb"], f32))
    gnc_hawk = host_col(np.asarray(inp["L0_attn_gn_hawk"], f32))
    wgt1, wst1 = host_mlp_weights(np.asarray(inp["L0_attn_g1_Wgrow"], f32),
                                  np.asarray(inp["L0_attn_g1_Wshrink"], f32))
    gnc_m1 = host_col(np.asarray(inp["L0_attn_gn_hgmlp"], f32))
    wqT, wkT, wvT, woT = host_smqa_weights(np.asarray(inp["L0_attn_smqa_Wq"], f32),
                                           np.asarray(inp["L0_attn_smqa_Wkv"], f32),
                                           np.asarray(inp["L0_attn_smqa_Wout"], f32))
    mL, mR = host_masks()
    rotP = host_rot_mat()
    gnc_sq = host_col(np.asarray(inp["L0_attn_gn_smqa"], f32))
    cos0, sin0 = host_rotary(0)
    cos1, sin1 = host_rotary(2048)
    wgt2, wst2 = host_mlp_weights(np.asarray(inp["L0_attn_g2_Wgrow"], f32),
                                  np.asarray(inp["L0_attn_g2_Wshrink"], f32))
    gnc_m2 = host_col(np.asarray(inp["L0_attn_gn_sgmlp"], f32))
    gwT, b1c, b2c, sel = host_moe_weights(np.asarray(inp["L0_ffn_gate_w"], f32),
                                          np.asarray(inp["L0_ffn_b1"], f32),
                                          np.asarray(inp["L0_ffn_b2"], f32))
    w1 = np.ascontiguousarray(np.asarray(inp["L0_ffn_w1"], f32))
    w2 = np.ascontiguousarray(np.asarray(inp["L0_ffn_w2"], f32))
    wq, bqc, wk, bkc, wv, _bvc, wf, a1, a2, fbcol = host_l1_weights(
        np.asarray(inp["L1_attn_Wq"], f32), np.asarray(inp["L1_attn_bq"], f32),
        np.asarray(inp["L1_attn_Wk"], f32), np.asarray(inp["L1_attn_bk"], f32),
        np.asarray(inp["L1_attn_Wv"], f32), np.asarray(inp["L1_attn_bv"], f32),
        np.asarray(inp["L1_ffn_weight"], f32), np.asarray(inp["L1_ffn_aux_weight1"], f32),
        np.asarray(inp["L1_ffn_aux_weight2"], f32), np.asarray(inp["L1_ffn_bias"], f32))
    bvrow = np.ascontiguousarray(np.asarray(inp["L1_attn_bv"], f32).reshape(1, 1024))
    mks = {j: host_l1_masks(j) for j in range(2)}

    nc = _get("mega", build_fused)
    ins = []
    for c in cores:
        b2, hh = c // 2, c % 2
        ins.append(dict(
            x_in=np.ascontiguousarray(x[b2, hh * 2048:(hh + 1) * 2048]),
            identf=identf, iscol=np.full((128, 1), float(hh), f32),
            winT=winT, wgT=wgT, woutT=woutT, cwc=cwc, cbc=cbc, bgc=bgc,
            fbc_h=fbc, gnc_h=gnc_hawk,
            wgt1=wgt1, wst1=wst1, gnc_m1=gnc_m1,
            wgt2=wgt2, wst2=wst2, gnc_m2=gnc_m2,
            wqT=wqT, wkT=wkT, wvT=wvT, woT=woT, gnc_sq=gnc_sq,
            cosT=cos0 if hh == 0 else cos1, sinT=sin0 if hh == 0 else sin1,
            rotP=rotP, mL=mL, mR=mR,
            gwT=gwT, w1=w1, b1c=b1c, w2=w2, b2c=b2c, sel=sel,
            wq=wq, wk=wk, wv=wv, bqc=bqc, bkc=bkc, wf=wf, a1=a1, a2=a2,
            fbc_l=fbcol, mks=mks[hh], bvrow=bvrow))
    res = run_bass_kernel_spmd(nc, ins, core_ids=cores).results

    toks = {j: host_interleave_tokens(j) for j in range(2)}
    out = np.empty((B, T2, D), f32)
    for c in cores:
        b2, j = c // 2, c % 2
        out[b2, toks[j]] = res[c]["out"]
    return out


# version 05 (final)
# speedup vs baseline: 1.0067x; 1.0067x over previous
import numpy as np
from contextlib import ExitStack
import concourse.bass as bass
import concourse.mybir as mybir

F32 = mybir.dt.float32
BF16 = mybir.dt.bfloat16
AF = mybir.ActivationFunctionType
ALU = mybir.AluOpType
AX = mybir.AxisListType

T_CORE = 2048
FT = 512
NFT = T_CORE // FT


class Sem:
    __slots__ = ("h", "n", "name")

    def __init__(self, es, nc, name):
        self.h = es.enter_context(nc.semaphore(name))
        self.n = 0
        self.name = name

    def inc(self, instr, k=1):
        instr.then_inc(self.h, k)
        self.n += k
        return self.n

    def wait(self, eng, n=None):
        eng.wait_ge(self.h, self.n if n is None else n)


class Builder:

    def __init__(self, name="prog"):
        self.nc = bass.Bass("TRN2", target_bir_lowering=False)
        self.es = ExitStack()
        self.es.__enter__()
        self._sbuf_bytes = 0
        self._max_sbuf = 0
        self.name = name
        self.pfx = ""
        self._phase_es = None

    def begin_phase(self, pfx):
        assert self._phase_es is None
        self.pfx = pfx
        self._phase_es = ExitStack()
        self._phase_es.__enter__()

    def end_phase(self):
        self._phase_es.__exit__(None, None, None)
        self._phase_es = None
        self.pfx = ""

    def _alloc_es(self):
        return self._phase_es if self._phase_es is not None else self.es

    def sem(self, name):
        return Sem(self.es, self.nc, self.pfx + name)

    def sb(self, name, shape, dt=F32):
        t = self._alloc_es().enter_context(self.nc.sbuf_tensor(self.pfx + name, shape, dt))
        return t

    def psum(self, name, shape=(128, 512), dt=F32):
        return self._alloc_es().enter_context(self.nc.psum_tensor(self.pfx + name, list(shape), dt))

    def dram(self, name, shape, dt=F32, kind=None):
        if kind is None:
            return self.nc.dram_tensor(name, list(shape), dt)
        return self.nc.dram_tensor(name, list(shape), dt, kind=kind)

    def block(self):
        return self._alloc_es().enter_context(self.nc.Block())

    def done(self):
        self.es.__exit__(None, None, None)
        return self.nc




from concourse.bass_utils import run_bass_kernel_spmd


def host_hawk_weights(Win, Wg, Wout, cw, cb, bg, fb):
    winT = np.ascontiguousarray(Win.T)
    winT_hg = np.ascontiguousarray(np.concatenate([winT[:, HID:], winT[:, :HID]], axis=1))
    wgT = np.ascontiguousarray(Wg.T)
    order = []
    for fg in range(NH):
        order += list(range(fg * 128, (fg + 1) * 128)) + \
                 list(range(HID + fg * 128, HID + (fg + 1) * 128))
    wgT_i = np.ascontiguousarray(wgT[:, order])
    woutT = np.ascontiguousarray(Wout.T)
    cwc = np.ascontiguousarray(cw.reshape(NH, 128, 4).transpose(1, 0, 2))
    cbc = np.ascontiguousarray(cb.reshape(NH, 128).T)
    bgc = np.ascontiguousarray(bg.reshape(2 * NH, 128).T)
    fbc = np.ascontiguousarray(fb.reshape(NH, 128).T)
    return winT_hg, wgT_i, woutT, cwc, cbc, bgc, fbc


def host_mlp_weights(wgrow, wshrink):
    GH = wgrow.shape[0] // 2
    wt = np.ascontiguousarray(wgrow.T)
    NG = GH // 128
    order = []
    for fg in range(NG):
        order += list(range(fg * 128, (fg + 1) * 128)) + \
                 list(range(GH + fg * 128, GH + (fg + 1) * 128))
    wt_i = np.ascontiguousarray(wt[:, order])
    ws_t = np.ascontiguousarray(wshrink.T)
    return wt_i, ws_t


def host_col(v):
    return np.ascontiguousarray(v.reshape(-1, 128).T)


def host_smqa_weights(Wq, Wkv, Wo):
    return (np.ascontiguousarray(Wq.T), np.ascontiguousarray(Wkv.T[:, 0:HD]),
            np.ascontiguousarray(Wkv.T[:, HD:2 * HD]), np.ascontiguousarray(Wo.T))


def host_rotary(tok_off, T=T_CORE):
    d = np.arange(0, HD, 2)
    inv = 1.0 / 10000.0 ** (d / HD)
    t = np.arange(tok_off, tok_off + T, dtype=np.float64)
    fr2 = np.repeat(t[:, None] * inv[None, :], 2, axis=1)
    return (np.ascontiguousarray(np.cos(fr2).T).astype(np.float32),
            np.ascontiguousarray(np.sin(fr2).T).astype(np.float32))


def host_rot_mat():
    P = np.zeros((HD, HD), np.float32)
    for i in range(0, HD, 2):
        P[i, i + 1] = -1.0
        P[i + 1, i] = 1.0
    return np.ascontiguousarray(P.T)


def host_masks():
    k = np.arange(128)[:, None]
    q = np.arange(512)[None, :]
    mL = np.stack([(128 * m + k >= q).astype(np.float32) for m in range(4)])
    mR = np.stack([(128 * m + k <= q).astype(np.float32) for m in range(4)])
    return mL, mR


def host_moe_weights(gate_w, b1, b2):
    gwT = np.ascontiguousarray(gate_w.T)
    b1c = np.ascontiguousarray(b1.reshape(E, NF, 128).transpose(2, 0, 1))
    b2c = np.ascontiguousarray(b2.reshape(E, 8, 128).transpose(2, 0, 1))
    sel = np.zeros((8, E * 128), np.float32)
    for e in range(E):
        sel[e, e * 128:(e + 1) * 128] = 1.0
    return gwT, b1c, b2c, np.ascontiguousarray(sel)


def host_l1_weights(Wq, bq, Wk, bk, Wv, bv, Wf, A1, A2, fb):
    col = lambda v: np.ascontiguousarray(v.reshape(8, 128).T)
    return (np.ascontiguousarray(Wq), col(bq), np.ascontiguousarray(Wk), col(bk),
            np.ascontiguousarray(Wv), col(bv), np.ascontiguousarray(Wf),
            np.ascontiguousarray(A1), np.ascontiguousarray(A2), col(fb))


def host_l1_masks(j):
    k = np.arange(128)[:, None]
    q = np.arange(512)[None, :]
    qg = 512 * (q // 256) + (q % 256)
    out = []
    for r0 in range(8):
        m = r0 - 2 * j
        out.append((128 * m + k <= qg).astype(np.float32))
    return np.stack(out)


def host_interleave_tokens(j, T2=4096):
    g = np.arange(T2)
    return g[(g // 256) % 2 == j]



NC = 8
HID = 1536
NH = HID // 128
RG = [[0, 1], [2, 3], [4, 5], [6, 7]]
QH = 8
HD = 128
W = 1024
ISQ = 1.0 / np.sqrt(128.0)
E = 8
F = 4096
NF = F // 128
TH = T_CORE // 2
BIG = 1e30
NHD = 16
HD1 = 64


def subtiles(qt):
    if qt == 0:
        return [(ki, ("L", ki)) for ki in range(4)] + \
               [(ki, None) for ki in range(4, 8)] + \
               [(ki, ("R", ki - 8)) for ki in range(8, 12)]
    return [(ki, ("L", ki - 4)) for ki in range(4, 8)] + \
           [(ki, None) for ki in range(8, 12)] + \
           [(ki, ("R", ki - 12)) for ki in range(12, 16)]


def emit_prep(b, D, T=T_CORE):
    nc = b.nc
    nft = T // FT
    ntc = T // 128

    x_in = D["x_in"]
    identf = D["identf"]
    xT_o = D["xT"]
    xa0_o = D["xa0"]
    xnb_o = D["xnb"]

    xT = b.sb("xTs", [128, NC, T])
    xb = b.sb("xbs", [128, NC, T], BF16)
    sq = b.sb("sqs", [128, NC, T], BF16)
    xtok = b.sb("xtok", [128, 4, 1024])
    idt = b.sb("idt", [128, 128])
    arow = b.sb("arow", [1, T])
    brow = b.sb("brow", [1, T])
    onescol_bf = b.sb("onescol_bf", [128, 1], BF16)
    onesrow_f = b.sb("onesrow_f", [1, 128])
    epsrow = b.sb("epsrow", [1, 1])
    xac = b.sb("xac", [128, 2, T])
    xnc = b.sb("xnc", [128, 2, T], BF16)

    pA = b.psum("pA", (128, 2048))
    pB = b.psum("pB", (128, 2048))

    s_ld4 = [b.sem(f"s_ld{i}") for i in range(4)]
    s_sy = b.sem("s_sy")
    s_st2 = [b.sem(f"s_st{i}") for i in range(2)]
    s_pe = b.sem("s_pe")
    s_ac = b.sem("s_ac")
    s_dv = b.sem("s_dv")

    L = {}
    pe = 0
    L["tp"] = {}
    for tc in range(ntc):
        pe += 1; L["tp"][(tc, 0)] = pe
        pe += 1; L["tp"][(tc, 1)] = pe
    L["sum_done"] = {}
    for tc in range(ntc):
        pe += 1; L["sum_done"][tc] = pe
        pe += 1
    L["stats_done"] = pe
    pe += 8; L["bcast_done"] = pe

    dv = 0
    L["evac"] = {}
    for tc in range(ntc):
        dv += 1; L["evac"][(tc, 0)] = dv
        dv += 1; L["evac"][(tc, 1)] = dv
        dv += 1; L["cast"] = dv
        L["cast", tc] = dv
    L["rows_done"] = dv + 1

    block = b.block()

    @block.sync
    def _(sync):
        sync.dma_start(idt[:], identf).then_inc(s_sy.h, 16)
        s_sy.n += 16
        for tc in range(ntc):
            s = tc % 4
            if tc >= 4:
                sync.wait_ge(s_pe.h, L["tp"][(tc - 4, 1)])
            sync.dma_start(xtok[:, s, :], x_in[tc * 128:(tc + 1) * 128, :]
                           ).then_inc(s_ld4[s].h, 16)
            s_ld4[s].n += 16

    @block.tensor
    def _(t):
        t.wait_ge(s_sy.h, 16)
        for tc in range(ntc):
            s = tc % 4
            t.wait_ge(s_ld4[s].h, 16 * (tc // 4 + 1))
            for bk in range(2):
                if tc >= 1 or bk >= 1:
                    pass
                prev = (tc * 2 + bk) - 2
                if prev >= 0:
                    t.wait_ge(s_dv.h, L["evac"][(prev // 2, prev % 2)])
                for k in range(4):
                    kk = bk * 4 + k
                    tp = t.transpose(pA[:, (bk * 4 + k - bk * 4) * 128:
                                        (bk * 4 + k - bk * 4 + 1) * 128] if False else
                                     (pA if bk == 0 else pB)[:, k * 128:(k + 1) * 128],
                                     xtok[:, s, kk * 128:(kk + 1) * 128], idt[:])
                tp.then_inc(s_pe.h, 1)
                s_pe.n += 1

    @block.vector
    def _(v):
        for tc in range(ntc):
            for bk in range(2):
                v.wait_ge(s_pe.h, L["tp"][(tc, bk)])
                if bk == 0 and tc >= 1:
                    v.wait_ge(s_dv.h, L["cast", tc - 1])
                v.tensor_copy(xT[:, bk * 4:(bk + 1) * 4, tc * 128:(tc + 1) * 128],
                              (pA if bk == 0 else pB)[:, 0:512].rearrange("p (c t) -> p c t", c=4)
                              ).then_inc(s_dv.h, 1)
                s_dv.n += 1
            v.wait_ge(s_dv.h, s_dv.n)
            v.tensor_copy(xb[:, :, tc * 128:(tc + 1) * 128], xT[:, :, tc * 128:(tc + 1) * 128]
                          ).then_inc(s_dv.h, 1)
            s_dv.n += 1

    @block.scalar
    def _(sc):
        for tc in range(ntc):
            sc.wait_ge(s_dv.h, L["cast", tc])
            sc.activation(sq[:, :, tc * 128:(tc + 1) * 128], xb[:, :, tc * 128:(tc + 1) * 128],
                          AF.Square).then_inc(s_ac.h, 1)
            s_ac.n += 1

    @block.vector
    def _(v):
        v.memset(onescol_bf[:], 1.0)
        v.memset(epsrow[:], 1e-5)
        v.memset(onesrow_f[:], 1.0).then_inc(s_dv.h, 1)
        s_dv.n += 1
    ones_done = s_dv.n

    @block.tensor
    def _(t):
        t.wait_ge(s_dv.h, ones_done)
        t.wait_ge(s_dv.h, L["cast", ntc - 1])
        for tc in range(ntc):
            t.wait_ge(s_ac.h, tc + 1)
            for k in range(NC):
                mm = t.matmul(pA[0:1, tc * 128:(tc + 1) * 128], onescol_bf[:],
                              xb[:, k, tc * 128:(tc + 1) * 128], start=(k == 0), stop=(k == NC - 1))
            mm.then_inc(s_pe.h, 1)
            s_pe.n += 1
            for k in range(NC):
                mm = t.matmul(pB[0:1, tc * 128:(tc + 1) * 128], onescol_bf[:],
                              sq[:, k, tc * 128:(tc + 1) * 128], start=(k == 0), stop=(k == NC - 1))
            mm.then_inc(s_pe.h, 1)
            s_pe.n += 1
    assert s_pe.n == L["stats_done"]

    @block.vector
    def _(v):
        v.wait_ge(s_pe.h, L["stats_done"])
        v.tensor_scalar(arow[:], pA[0:1, :], 1.0 / 1024.0, None, ALU.mult).then_inc(s_dv.h, 1)
        s_dv.n += 1
        v.wait_ge(s_dv.h, s_dv.n)
        v.tensor_tensor(brow[:], arow[:], arow[:], ALU.mult).then_inc(s_dv.h, 1)
        s_dv.n += 1
        v.wait_ge(s_dv.h, s_dv.n)
        v.scalar_tensor_tensor(brow[:], pB[0:1, :], 1.0 / 1024.0, brow[:],
                               ALU.mult, ALU.subtract).then_inc(s_dv.h, 1)
        s_dv.n += 1
    var_done = s_dv.n

    @block.scalar
    def _(sc):
        sc.wait_ge(s_dv.h, var_done)
        sc.activation(brow[:], brow[:], AF.Sqrt, bias=epsrow[0:1, 0:1]).then_inc(s_ac.h, 1)
        s_ac.n += 1
    sqrt_done = s_ac.n

    @block.vector
    def _(v):
        v.wait_ge(s_ac.h, sqrt_done)
        v.reciprocal(brow[:], brow[:]).then_inc(s_dv.h, 1)
        s_dv.n += 1
        v.wait_ge(s_dv.h, s_dv.n)
        v.tensor_tensor(arow[:], arow[:], brow[:], ALU.mult).then_inc(s_dv.h, 1)
        s_dv.n += 1
    rows_done = s_dv.n

    @block.tensor
    def _(t):
        t.wait_ge(s_dv.h, rows_done)
        for f in range(nft):
            t.matmul(pA[:, f * FT:(f + 1) * FT], onesrow_f[:], brow[0:1, f * FT:(f + 1) * FT],
                     start=True, stop=True).then_inc(s_pe.h, 1)
            s_pe.n += 1
        for f in range(nft):
            t.matmul(pB[:, f * FT:(f + 1) * FT], onesrow_f[:], arow[0:1, f * FT:(f + 1) * FT],
                     start=True, stop=True).then_inc(s_pe.h, 1)
            s_pe.n += 1
    assert s_pe.n == L["bcast_done"]

    @block.vector
    def _(v):
        v.wait_ge(s_pe.h, L["bcast_done"])
        for k in range(NC):
            sl = k % 2
            if k >= 2:
                v.wait_ge(s_st2[sl].h, 32 * (k // 2))
            v.tensor_tensor(xac[:, sl, :], xT[:, k, :], pA[:], ALU.mult).then_inc(s_dv.h, 1)
            s_dv.n += 1
            v.wait_ge(s_dv.h, s_dv.n)
            v.tensor_tensor(xac[:, sl, :], xac[:, sl, :], pB[:], ALU.subtract).then_inc(s_dv.h, 1)
            s_dv.n += 1
            v.wait_ge(s_dv.h, s_dv.n)
            v.tensor_copy(xnc[:, sl, :], xac[:, sl, :]).then_inc(s_dv.h, 1)
            s_dv.n += 1
    apply_base = s_dv.n - 3 * NC

    @block.sync
    def _(sync):
        sync.wait_ge(s_dv.h, L["cast", ntc - 1])
        sync.dma_start(xT_o.rearrange("(c p) t -> p c t", p=128), xT[:]).then_inc(s_sy.h, 16)
        s_sy.n += 16
        for k in range(NC):
            sl = k % 2
            sync.wait_ge(s_dv.h, apply_base + 3 * k + 3)
            sync.dma_start(xa0_o.rearrange("(c p) t -> p c t", p=128)[:, k, :], xac[:, sl, :]
                           ).then_inc(s_st2[sl].h, 16)
            sync.dma_start(xnb_o.rearrange("(c p) t -> p c t", p=128)[:, k, :], xnc[:, sl, :]
                           ).then_inc(s_st2[sl].h, 16)
            s_st2[sl].n += 32
        for sl in range(2):
            sync.wait_ge(s_st2[sl].h, s_st2[sl].n)
        sync.wait_ge(s_sy.h, s_sy.n)





def emit_hawk(b, D, T=T_CORE):
    nc = b.nc
    nft = T // FT
    TS = T + 4

    xa_in = D["xa"]
    winT = D["winT"]
    wgTd = D["wgT"]
    woutTd = D["woutT"]
    cwc_d = D["cwc"]
    cbc_d = D["cbc"]
    bgc_d = D["bgc"]
    fbc_d = D["fbc"]
    gnc_d = D["gnc"]
    iscol_d = D["iscol"]
    xa_out = D["xa_out"]

    tail_in = nc.dram_tensor(b.pfx + "tail_in", [HID, 3], F32)
    tail_out = nc.dram_tensor(b.pfx + "tail_out", [2 * HID, 3], F32)
    st_in = [nc.dram_tensor(f"st_in{fg}", [128, 1], F32) for fg in range(NH)]
    st_out = [nc.dram_tensor(f"st_out{fg}", [256, 1], F32) for fg in range(NH)]

    ARQ = b.sb("ARQ", [128, 16384])
    AR1 = b.sb("AR1", [128, 16384], BF16)
    WRB = b.sb("WRB", [128, 8192], BF16)
    gg = b.sb("gg", [128, NH, T], BF16)
    hring = b.sb("hring", [128, 2, TS], BF16)
    A = b.sb("A", [128, T])
    convacc = b.sb("convacc", [128, T])
    C = b.sb("C", [128, T])
    wgr = b.sb("wgr", [128, 2, NH, 256], BF16)
    tailb = b.sb("tailb", [128, NH, 3], BF16)
    hp = b.sb("hp", [128, NH, 6], BF16)
    gcol = b.sb("gcol", [128, NC])
    cwcol = b.sb("cwcol", [128, NH, 4])
    cbcol = b.sb("cbcol", [128, NH])
    bgcol = b.sb("bgcol", [128, 2 * NH])
    spcol = b.sb("spcol", [128, NH])
    stcol = b.sb("stcol", [128, NH])
    strecv = b.sb("strecv", [128, NH])
    iscol = b.sb("iscl", [128, 1])
    onescol_bf = b.sb("onescol_bf", [128, 1], BF16)
    onescol_f = b.sb("onescol_f", [128, 1])
    epscol = b.sb("epscol", [128, 1])
    onesrow_f = b.sb("onesrow_f", [1, 128])
    zcol = b.sb("zcol", [128, 1])

    xa = ARQ[:].rearrange("p (c t) -> p c t", c=NC)
    hc = ARQ[:, 0:12288].bitcast(BF16).rearrange("p (c t) -> p c t", c=NH)
    Dv = ARQ[:, 12288:14336].bitcast(BF16).rearrange("p (s t) -> p s t", s=2)
    xar0 = ARQ[:, 14336:16384]
    xar1 = ARQ[:, 12288:14336]
    xh = AR1[:].rearrange("p (c t) -> p c t", c=NC)
    winr = WRB[:].rearrange("p (s c o) -> p s c o", s=2, c=NC)

    def u_sl(fg, lo=0, hi=None):
        hi = T if hi is None else hi
        if fg < 8:
            return AR1[:, fg * T + lo:fg * T + hi]
        return WRB[:, (fg - 8) * T + lo:(fg - 8) * T + hi]

    B = convacc
    srow = convacc[0:1, :]

    pA = b.psum("pA", (128, 2048))
    pB = b.psum("pB", (128, 2048))
    pb = [pA[:, f * FT:(f + 1) * FT] for f in range(4)] + \
         [pB[:, f * FT:(f + 1) * FT] for f in range(4)]

    s_sy = b.sem("s_sy")
    s_wc = [b.sem(f"s_wc{i}") for i in range(2)]
    s_wg2 = [b.sem(f"s_wg{i}") for i in range(2)]
    s_xr2 = [b.sem(f"s_xr{i}") for i in range(2)]
    s_so2 = [b.sem(f"s_so{i}") for i in range(2)]
    s_cc = b.sem("s_cc")
    s_ag = b.sem("s_ag")
    s_pe = b.sem("s_pe")
    s_ac = b.sem("s_ac")
    s_dv = b.sem("s_dv")

    L = {}
    pe = 0
    pe += nft; L["stats"] = pe
    pe += nft; L["bcast"] = pe
    L["h_base"] = pe
    L["h_grp"] = [pe + j + 1 for j in range(NH * nft)]
    pe += NH * nft
    L["gate_base"] = pe
    L["gate_grp"] = [pe + j + 1 for j in range(NH * nft)]
    pe += NH * nft
    L["wg_base"] = pe
    L["wg_F"] = [pe + fg * 8 + 7 for fg in range(NH)]
    L["wg_I"] = [pe + fg * 8 + 8 for fg in range(NH)]
    pe += NH * 8
    L["wout_base"] = pe
    L["wout_grp"] = [pe + j + 1 for j in range(8 * nft)]
    pe += 8 * nft

    ac = 0
    ac += 1; L["sq"] = ac
    ac += 2
    ac += 1; L["srow"] = ac
    L["gelu"] = [ac + j + 1 for j in range(NH * nft)]
    ac += NH * nft
    L["a1"] = [ac + fg * 5 + 1 for fg in range(NH)]
    L["a2"] = [ac + fg * 5 + 2 for fg in range(NH)]
    L["a3"] = [ac + fg * 5 + 3 for fg in range(NH)]
    L["a4"] = [ac + fg * 5 + 4 for fg in range(NH)]
    L["a5"] = [ac + fg * 5 + 5 for fg in range(NH)]
    ac += NH * 5

    dv = 0
    dv += 1; L["cast"] = dv
    dv += 1; L["srowcp"] = dv
    dv += 1; L["recip"] = dv
    dv += 1; L["norm"] = dv
    L["h_evac"] = {}
    L["conv_main"] = {}
    for oc in range(NH):
        for f in range(nft):
            dv += 1; L["h_evac"][(oc, f)] = dv
        dv += 1
        dv += 1
        dv += 5; L["conv_main"][oc] = dv
    dv += 1; L["tailmask"] = dv
    dv += NH * 5; L["convfix"] = dv
    D9 = {}
    Dd = {}
    for fg in range(NH):
        if fg >= 2:
            dv += 1; D9[("a", fg - 2)] = dv
            dv += 1; D9[("b", fg - 2)] = dv
        for k in range(1, 9):
            dv += 1; Dd[(k, fg)] = dv
    for fg in (NH - 2, NH - 1):
        dv += 1; D9[("a", fg)] = dv
        dv += 1; D9[("b", fg)] = dv
    L["scan_end"] = dv
    L["wout_add_i"] = [dv + i + 1 for i in range(NC)]
    dv += NC

    block = b.block()

    @block.sync
    def _(sync):
        for dst, src in [(xa[:], xa_in.rearrange("(c p) t -> p c t", p=128)),
                         (gcol[:], gnc_d), (cwcol[:], cwc_d), (cbcol[:], cbc_d),
                         (bgcol[:], bgc_d), (spcol[:], fbc_d), (iscol[:], iscol_d)]:
            sync.dma_start(dst, src).then_inc(s_sy.h, 16)
            s_sy.n += 16
    ld_done = s_sy.n

    @block.vector
    def _(v):
        v.memset(onescol_bf[:], 1.0)
        v.memset(onescol_f[:], 1.0)
        v.memset(onesrow_f[:], 1.0)
        v.memset(epscol[:], 1.000001)
        v.memset(zcol[:], 0.0)
        v.wait_ge(s_sy.h, ld_done)
        v.tensor_copy(xh[:], xa[:]).then_inc(s_dv.h, 1)
        s_dv.n += 1

    @block.scalar
    def _(sc):
        sc.wait_ge(s_dv.h, L["cast"])
        sc.activation(gg[:, 0:NC, :], xh[:], AF.Square).then_inc(s_ac.h, 1)
        s_ac.n += 1
        sc.activation(spcol[:], spcol[:], AF.Exp).then_inc(s_ac.h, 1)
        s_ac.n += 1
        sc.wait_ge(s_ac.h, s_ac.n)
        sc.activation(spcol[:], spcol[:], AF.Ln, bias=onescol_f[:, 0:1]).then_inc(s_ac.h, 1)
        s_ac.n += 1

    @block.tensor
    def _(t):
        t.wait_ge(s_ac.h, L["sq"])
        for f in range(nft):
            for k in range(NC):
                mm = t.matmul(pb[f][0:1, :], onescol_bf[:], gg[:, k, f * FT:(f + 1) * FT],
                              start=(k == 0), stop=(k == NC - 1))
            mm.then_inc(s_pe.h, 1)
            s_pe.n += 1

    @block.vector
    def _(v):
        v.wait_ge(s_pe.h, L["stats"])
        for f in range(nft):
            cp = v.tensor_copy(srow[0:1, f * FT:(f + 1) * FT], pb[f][0:1, :])
        cp.then_inc(s_dv.h, 1)
        s_dv.n += 1
        v.wait_ge(s_dv.h, s_dv.n)
        v.reciprocal(srow[:], srow[:]).then_inc(s_dv.h, 1)
        s_dv.n += 1

    @block.scalar
    def _(sc):
        sc.wait_ge(s_dv.h, L["recip"])
        sc.activation(srow[:], srow[:], AF.Sqrt, scale=1024.0).then_inc(s_ac.h, 1)
        s_ac.n += 1

    @block.tensor
    def _(t):
        t.wait_ge(s_ac.h, L["srow"])
        for f in range(nft):
            t.matmul(pA[:, f * FT:(f + 1) * FT], onesrow_f[:], srow[0:1, f * FT:(f + 1) * FT],
                     start=True, stop=True).then_inc(s_pe.h, 1)
            s_pe.n += 1

    @block.vector
    def _(v):
        v.wait_ge(s_pe.h, L["bcast"])
        for k in range(NC):
            ins = v.scalar_tensor_tensor(xh[:, k, :], xa[:, k, :], gcol[:, k:k + 1],
                                         pA[:], ALU.mult, ALU.mult)
        ins.then_inc(s_dv.h, 1)
        s_dv.n += 1
    assert s_dv.n == L["norm"]

    @block.gpsimd
    def _(g):
        for q in range(6):
            s = q % 2
            if q >= 2:
                if q - 2 < 3:
                    g.wait_ge(s_pe.h, L["h_grp"][(q - 2 + 1) * 4 * nft - 1])
                else:
                    g.wait_ge(s_pe.h, L["gate_grp"][(q - 5 + 1) * 4 * nft - 1])
            g.dma_start(winr[:, s, :, :],
                        winT.rearrange("(c p) o -> p c o", p=128)[:, :, q * 512:(q + 1) * 512]
                        ).then_inc(s_wc[s].h, 16)
            s_wc[s].n += 16

    @block.tensor
    def _(t):
        t.wait_ge(s_dv.h, L["norm"])
        for oc in range(NH):
            q, ol = divmod(oc, 4)
            if ol == 0:
                t.wait_ge(s_wc[q % 2].h, 16 * (q // 2 + 1))
            if oc >= 2:
                t.wait_ge(s_dv.h, L["conv_main"][oc - 2])
            for f in range(nft):
                j = oc * nft + f
                jj = j % 8
                if j >= 8:
                    t.wait_ge(s_dv.h, L["h_evac"][(oc - 2, f)])
                for k in range(NC):
                    mm = t.matmul(pb[jj][:], winr[:, q % 2, k, ol * 128:(ol + 1) * 128],
                                  xh[:, k, f * FT:(f + 1) * FT], start=(k == 0), stop=(k == NC - 1))
                mm.then_inc(s_pe.h, 1)
                s_pe.n += 1

    @block.vector
    def _(v):
        for oc in range(NH):
            sl = oc % 2
            for f in range(nft):
                j = oc * nft + f
                v.wait_ge(s_pe.h, L["h_grp"][j])
                v.tensor_copy(hring[:, sl, 4 + f * FT:4 + (f + 1) * FT], pb[j % 8][:]
                              ).then_inc(s_dv.h, 1)
                s_dv.n += 1
            v.wait_ge(s_dv.h, s_dv.n)
            v.tensor_copy(tailb[:, oc, :], hring[:, sl, TS - 3:TS]).then_inc(s_dv.h, 1)
            s_dv.n += 1
            v.wait_ge(s_dv.h, s_dv.n)
            v.tensor_copy(hp[:, oc, 3:6], hring[:, sl, 4:7]).then_inc(s_dv.h, 1)
            s_dv.n += 1
            v.wait_ge(s_dv.h, s_dv.n)
            v.scalar_tensor_tensor(convacc[0:128, 0:T - 3], hring[:, sl, 4:4 + T - 3],
                                   cwcol[:, oc, 0:1], zcol[:, 0:1].broadcast_to([128, T - 3]),
                                   ALU.mult, ALU.add).then_inc(s_dv.h, 1)
            s_dv.n += 1
            for kk in range(1, 4):
                v.wait_ge(s_dv.h, s_dv.n)
                v.scalar_tensor_tensor(convacc[0:128, 0:T - 3], hring[:, sl, 4 + kk:4 + kk + T - 3],
                                       cwcol[:, oc, kk:kk + 1], convacc[0:128, 0:T - 3],
                                       ALU.mult, ALU.add).then_inc(s_dv.h, 1)
                s_dv.n += 1
            v.wait_ge(s_dv.h, s_dv.n)
            v.tensor_scalar(hc[:, oc, 3:T], convacc[0:128, 0:T - 3], cbcol[:, oc:oc + 1],
                            None, ALU.add).then_inc(s_dv.h, 1)
            s_dv.n += 1
            assert s_dv.n == L["conv_main"][oc]

    @block.gpsimd
    def _(g):
        g.wait_ge(s_dv.h, L["conv_main"][NH - 1])
        g.dma_start(tail_in.ap().rearrange("(c p) k -> p c k", p=128), tailb[:]
                    ).then_inc(s_ag.h, 16)
        s_ag.n += 16
        g.wait_ge(s_ag.h, s_ag.n)
        g.collective_compute("AllGather", ALU.bypass, replica_groups=RG,
                             ins=[tail_in.ap().opt()], outs=[tail_out.ap().opt()]
                             ).then_inc(s_cc.h, 1)
        s_cc.n += 1
        g.wait_ge(s_cc.h, s_cc.n)
        g.dma_start(hp[:, :, 0:3],
                    tail_out.ap()[0:HID, 0:3].rearrange("(c p) k -> p c k", p=128)
                    ).then_inc(s_ag.h, 16)
        s_ag.n += 16
    tail_ag = s_ag.n
    cc_tail = s_cc.n

    @block.vector
    def _(v):
        v.wait_ge(s_ag.h, tail_ag)
        v.tensor_scalar(hp[:, :, 0:3], hp[:, :, 0:3], iscol[:, 0:1], None, ALU.mult
                        ).then_inc(s_dv.h, 1)
        s_dv.n += 1
        assert s_dv.n == L["tailmask"]
        for oc in range(NH):
            v.wait_ge(s_dv.h, s_dv.n)
            v.scalar_tensor_tensor(C[0:128, 0:3], hp[:, oc, 0:3], cwcol[:, oc, 0:1],
                                   zcol[:, 0:1].broadcast_to([128, 3]), ALU.mult, ALU.add
                                   ).then_inc(s_dv.h, 1)
            s_dv.n += 1
            for kk in range(1, 4):
                v.wait_ge(s_dv.h, s_dv.n)
                v.scalar_tensor_tensor(C[0:128, 0:3], hp[:, oc, kk:kk + 3], cwcol[:, oc, kk:kk + 1],
                                       C[0:128, 0:3], ALU.mult, ALU.add).then_inc(s_dv.h, 1)
                s_dv.n += 1
            v.wait_ge(s_dv.h, s_dv.n)
            v.tensor_scalar(hc[:, oc, 0:3], C[0:128, 0:3], cbcol[:, oc:oc + 1], None, ALU.add
                            ).then_inc(s_dv.h, 1)
            s_dv.n += 1
        assert s_dv.n == L["convfix"]

    @block.tensor
    def _(t):
        for oc in range(NH):
            q, ol = divmod(oc + NH, 4)
            if ol == 0:
                t.wait_ge(s_wc[q % 2].h, 16 * (q // 2 + 1))
            for f in range(nft):
                j = oc * nft + f
                jj = j % 8
                if j >= 8:
                    t.wait_ge(s_ac.h, L["gelu"][j - 8])
                else:
                    t.wait_ge(s_dv.h, L["h_evac"][(10 + j // 4, j % 4)])
                for k in range(NC):
                    mm = t.matmul(pb[jj][:], winr[:, q % 2, k, ol * 128:(ol + 1) * 128],
                                  xh[:, k, f * FT:(f + 1) * FT], start=(k == 0), stop=(k == NC - 1))
                mm.then_inc(s_pe.h, 1)
                s_pe.n += 1

    @block.scalar
    def _(sc):
        for j in range(NH * nft):
            oc, f = divmod(j, nft)
            jj = j % 8
            sc.wait_ge(s_pe.h, L["gate_grp"][j])
            sc.activation(gg[:, oc, f * FT:(f + 1) * FT], pb[jj][:], AF.Gelu
                          ).then_inc(s_ac.h, 1)
            s_ac.n += 1

    def emit_wg_load(g, fg):
        s = fg % 2
        if fg >= 2:
            g.wait_ge(s_pe.h, L["wg_I"][fg - 2])
        g.dma_start(wgr[:, s, :, :],
                    wgTd.rearrange("(c p) o -> p c o", p=128)[:, :, fg * 256:(fg + 1) * 256]
                    ).then_inc(s_wg2[s].h, 16)
        s_wg2[s].n += 16

    def emit_state_ag(g, fg):
        g.wait_ge(s_dv.h, Dd[(5, fg)])
        g.dma_start(st_in[fg].ap(), stcol[:, fg:fg + 1]).then_inc(s_ag.h, 16)
        s_ag.n += 16
        g.wait_ge(s_ag.h, s_ag.n)
        g.collective_compute("AllGather", ALU.bypass, replica_groups=RG,
                             ins=[st_in[fg].ap().opt()], outs=[st_out[fg].ap().opt()]
                             ).then_inc(s_cc.h, 1)
        s_cc.n += 1
        g.wait_ge(s_cc.h, s_cc.n)
        g.dma_start(strecv[:, fg:fg + 1], st_out[fg].ap()[0:128, :]).then_inc(s_ag.h, 16)
        s_ag.n += 16
        g.wait_ge(s_ag.h, s_ag.n)

    @block.gpsimd
    def _(g):
        emit_wg_load(g, 0)
        emit_wg_load(g, 1)
        for fg in range(NH):
            if fg + 2 < NH:
                emit_wg_load(g, fg + 2)
            emit_state_ag(g, fg)

    @block.tensor
    def _(t):
        t.wait_ge(s_dv.h, L["convfix"])
        for fg in range(NH):
            s = fg % 2
            t.wait_ge(s_wg2[s].h, 16 * (fg // 2 + 1))
            if fg == 0:
                t.wait_ge(s_ac.h, L["gelu"][NH * nft - 1])
            else:
                t.wait_ge(s_ac.h, L["a2"][fg - 1])
            for f in range(nft):
                for k in range(NH):
                    mm = t.matmul(pA[:, f * FT:(f + 1) * FT], wgr[:, s, k, 0:128],
                                  hc[:, k, f * FT:(f + 1) * FT], start=(k == 0), stop=(k == NH - 1))
                mm.then_inc(s_pe.h, 1)
                s_pe.n += 1
                for k in range(NH):
                    mm = t.matmul(pB[:, f * FT:(f + 1) * FT], wgr[:, s, k, 128:256],
                                  hc[:, k, f * FT:(f + 1) * FT], start=(k == 0), stop=(k == NH - 1))
                mm.then_inc(s_pe.h, 1)
                s_pe.n += 1
    assert s_pe.n == L["wout_base"]

    @block.scalar
    def _(sc):
        for fg in range(NH):
            sc.wait_ge(s_pe.h, L["wg_F"][fg])
            if fg >= 1:
                sc.wait_ge(s_dv.h, Dd[(6, fg - 1)])
            sc.activation(A[:], pA[:], AF.Sigmoid, bias=bgcol[:, fg:fg + 1]).then_inc(s_ac.h, 1)
            s_ac.n += 1
            sc.wait_ge(s_pe.h, L["wg_I"][fg])
            if fg >= 1:
                sc.wait_ge(s_dv.h, Dd[(4, fg - 1)])
            sc.activation(C[:], pB[:], AF.Sigmoid, bias=bgcol[:, NH + fg:NH + fg + 1]
                          ).then_inc(s_ac.h, 1)
            s_ac.n += 1
            sc.wait_ge(s_dv.h, Dd[(1, fg)])
            sc.activation(A[:], A[:], AF.Exp).then_inc(s_ac.h, 1)
            s_ac.n += 1
            if fg >= 1:
                sc.wait_ge(s_dv.h, Dd[(8, fg - 1)])
            sc.wait_ge(s_ac.h, s_ac.n)
            sc.activation(B[:], A[:], AF.Square).then_inc(s_ac.h, 1)
            s_ac.n += 1
            sc.wait_ge(s_ac.h, s_ac.n)
            sc.activation(B[:], B[:], AF.Sqrt, scale=-1.0, bias=epscol[:, 0:1]).then_inc(s_ac.h, 1)
            s_ac.n += 1

    @block.vector
    def _(v):
        def d9(fgx):
            v.wait_ge(s_ag.h, tail_ag + 16 * (2 * fgx + 2))
            v.tensor_scalar(strecv[:, fgx:fgx + 1], strecv[:, fgx:fgx + 1], iscol[:, 0:1],
                            None, ALU.mult).then_inc(s_dv.h, 1)
            s_dv.n += 1
            assert s_dv.n == D9[("a", fgx)]
            v.wait_ge(s_dv.h, s_dv.n)
            v.scalar_tensor_tensor(u_sl(fgx), Dv[:, fgx % 2, :], strecv[:, fgx:fgx + 1],
                                   u_sl(fgx), ALU.mult, ALU.add).then_inc(s_dv.h, 1)
            s_dv.n += 1
            assert s_dv.n == D9[("b", fgx)]

        for fg in range(NH):
            if fg >= 2:
                d9(fg - 2)
            v.wait_ge(s_ac.h, L["a1"][fg])
            v.tensor_scalar(A[:], A[:], spcol[:, fg:fg + 1], -8.0, ALU.mult, ALU.mult
                            ).then_inc(s_dv.h, 1)
            s_dv.n += 1
            assert s_dv.n == Dd[(1, fg)]
            v.wait_ge(s_ac.h, L["a5"][fg])
            v.tensor_tensor(C[:], B[:], C[:], ALU.mult).then_inc(s_dv.h, 1)
            s_dv.n += 1
            v.wait_ge(s_dv.h, s_dv.n)
            v.tensor_tensor(C[:], C[:], hc[:, fg, :], ALU.mult).then_inc(s_dv.h, 1)
            s_dv.n += 1
            v.wait_ge(s_dv.h, s_dv.n)
            v.tensor_tensor_scan(u_sl(fg), A[:], C[:], zcol[:, 0:1], ALU.mult, ALU.add
                                 ).then_inc(s_dv.h, 1)
            s_dv.n += 1
            assert s_dv.n == Dd[(4, fg)]
            v.wait_ge(s_dv.h, s_dv.n)
            v.tensor_copy(stcol[:, fg:fg + 1], u_sl(fg, T - 1, T)).then_inc(s_dv.h, 1)
            s_dv.n += 1
            v.wait_ge(s_dv.h, s_dv.n)
            v.tensor_tensor_scan(B[:], A[:], zcol[:, 0:1].broadcast_to([128, T]),
                                 onescol_f[:, 0:1], ALU.mult, ALU.add).then_inc(s_dv.h, 1)
            s_dv.n += 1
            v.wait_ge(s_dv.h, s_dv.n)
            v.tensor_tensor(u_sl(fg), gg[:, fg, :], u_sl(fg), ALU.mult).then_inc(s_dv.h, 1)
            s_dv.n += 1
            v.wait_ge(s_dv.h, s_dv.n)
            v.tensor_tensor(Dv[:, fg % 2, :], B[:], gg[:, fg, :], ALU.mult).then_inc(s_dv.h, 1)
            s_dv.n += 1
            assert s_dv.n == Dd[(8, fg)]
        d9(NH - 2)
        d9(NH - 1)
    assert s_dv.n == L["scan_end"]

    @block.gpsimd
    def _(g):
        for r in range(4):
            s = r % 2
            if r < 2:
                g.wait_ge(s_pe.h, L["wg_I"][10 + r])
            else:
                g.wait_ge(s_pe.h, L["wout_grp"][(r - 2 + 1) * 2 * nft - 1])
            g.dma_start(wgr[:, s, :, :],
                        woutTd.rearrange("(c p) o -> p c o", p=128)[:, :, r * 256:(r + 1) * 256]
                        ).then_inc(s_wg2[s].h, 16)
            s_wg2[s].n += 16

    def emit_xa_load(sync, i):
        sl = i % 2
        tgt = xar0 if sl == 0 else xar1
        if i == 0:
            sync.wait_ge(s_dv.h, L["norm"])
        if i == 1:
            sync.wait_ge(s_dv.h, L["scan_end"])
        if i >= 2:
            sync.wait_ge(s_so2[sl].h, 16 * (i // 2))
        sync.dma_start(tgt[0:128, 0:T],
                       xa_in.rearrange("(c p) t -> p c t", p=128)[:, i, :]
                       ).then_inc(s_xr2[sl].h, 16)
        s_xr2[sl].n += 16

    def emit_xa_store(sync, i):
        sl = i % 2
        tgt = xar0 if sl == 0 else xar1
        sync.wait_ge(s_dv.h, L["wout_add_i"][i])
        sync.dma_start(xa_out.rearrange("(c p) t -> p c t", p=128)[:, i, :],
                       tgt[0:128, 0:T]).then_inc(s_so2[sl].h, 16)
        s_so2[sl].n += 16

    @block.sync
    def _(sync):
        for i in range(NC):
            if i >= 2:
                emit_xa_store(sync, i - 2)
            emit_xa_load(sync, i)

    @block.tensor
    def _(t):
        t.wait_ge(s_dv.h, L["scan_end"])
        for i in range(NC):
            wq, wl = divmod(i, 2)
            t.wait_ge(s_wg2[wq % 2].h, 16 * (NH // 2) + 16 * (wq // 2 + 1))
            for f in range(nft):
                j = i * nft + f
                jj = j % 8
                if j >= 8:
                    t.wait_ge(s_dv.h, L["wout_add_i"][(j - 8) // nft])
                for k in range(NH):
                    mm = t.matmul(pb[jj][:], wgr[:, wq % 2, k, wl * 128:(wl + 1) * 128],
                                  u_sl(k, f * FT, (f + 1) * FT), start=(k == 0), stop=(k == NH - 1))
                mm.then_inc(s_pe.h, 1)
                s_pe.n += 1

    @block.vector
    def _(v):
        for i in range(NC):
            sl = i % 2
            tgt = xar0 if sl == 0 else xar1
            v.wait_ge(s_xr2[sl].h, 16 * (i // 2 + 1))
            for f in range(nft):
                j = i * nft + f
                v.wait_ge(s_pe.h, L["wout_grp"][j])
                ins = v.tensor_tensor(tgt[0:128, f * FT:(f + 1) * FT],
                                      tgt[0:128, f * FT:(f + 1) * FT], pb[j % 8][:], ALU.add)
            ins.then_inc(s_dv.h, 1)
            s_dv.n += 1
            assert s_dv.n == L["wout_add_i"][i]

    @block.sync
    def _(sync):
        emit_xa_store(sync, NC - 2)
        emit_xa_store(sync, NC - 1)
        for sl in range(2):
            sync.wait_ge(s_so2[sl].h, s_so2[sl].n)



def emit_mlp(b, D, T=T_CORE):
    nc = b.nc
    NC = 8
    GH = 2048
    NG = GH // 128
    nft = T // FT

    xa_in = D["xa"]
    wgt = D["wgt"]
    wst = D["wst"]
    gnc = D["gnc"]
    xa_out = D["xa_out"]

    xa = b.sb("xa", [128, NC, T])
    xg = b.sb("xg", [128, NC, T], BF16)
    u = b.sb("u", [128, NG, T], BF16)
    gncol = b.sb("gncol", [128, NC])
    onescol_bf = b.sb("onescol_bf", [128, 1], BF16)
    onesrow_f = b.sb("onesrow_f", [1, 128])
    srow = b.sb("srow", [1, T])
    gw_ring = b.sb("gw_ring", [128, 2, NC, 512], BF16)
    sw_ring = b.sb("sw_ring", [128, 2, NG, 256], BF16)
    gg_ring = b.sb("gg_ring", [128, 4, FT], BF16)

    p = [b.psum(f"p{i}") for i in range(4)]
    bc = b.psum("bc", (128, 2048))
    pb = p + [bc[:, f * FT:(f + 1) * FT] for f in range(4)]

    s_sy = b.sem("s_sy")
    s_gc2 = [b.sem(f"s_gc{i}") for i in range(2)]
    s_sc2 = [b.sem(f"s_sc{i}") for i in range(2)]
    s_pe = b.sem("s_pe")
    s_ac = b.sem("s_ac")
    s_dv = b.sem("s_dv")

    block = b.block()

    @block.sync
    def _(sync):
        sync.dma_start(xa[:], xa_in.rearrange("(c p) t -> p c t", p=128)).then_inc(s_sy.h, 16)
        s_sy.n += 16
        sync.dma_start(gncol[:], gnc).then_inc(s_sy.h, 16)
        s_sy.n += 16
    ld_done = s_sy.n

    bcast_done_c = 8
    shrink_base_c = bcast_done_c + 2 * NG * nft

    @block.gpsimd
    def _(g):
        for q in range(8):
            s = q % 2
            if q >= 2:
                fg_done = 2 * (q - 2) + 2
                g.wait_ge(s_pe.h, bcast_done_c + fg_done * 2 * nft)
            g.dma_start(gw_ring[:, s, :, :],
                        wgt.rearrange("(c p) o -> p c o", p=128)[:, :, q * 512:(q + 1) * 512]
                        ).then_inc(s_gc2[s].h, 16)
            s_gc2[s].n += 16
        for r in range(4):
            s = r % 2
            if r >= 2:
                i_done = 2 * (r - 2) + 2
                g.wait_ge(s_pe.h, shrink_base_c + i_done * nft)
            g.dma_start(sw_ring[:, s, :, :],
                        wst.rearrange("(c p) o -> p c o", p=128)[:, :, r * 256:(r + 1) * 256]
                        ).then_inc(s_sc2[s].h, 16)
            s_sc2[s].n += 16

    @block.vector
    def _(v):
        v.memset(onescol_bf[:], 1.0)
        v.memset(onesrow_f[:], 1.0)
        v.wait_ge(s_sy.h, ld_done)
        v.tensor_copy(xg[:], xa[:]).then_inc(s_dv.h, 1)
        s_dv.n += 1
    cast_done = s_dv.n

    @block.scalar
    def _(sc):
        sc.wait_ge(s_dv.h, cast_done)
        sc.activation(u[:, 0:NC, :], xg[:], AF.Square).then_inc(s_ac.h, 1)
        s_ac.n += 1
    sq_done = s_ac.n

    @block.tensor
    def _(t):
        t.wait_ge(s_ac.h, sq_done)
        for f in range(nft):
            for k in range(NC):
                mm = t.matmul(p[f][0:1, :], onescol_bf[:], u[:, k, f * FT:(f + 1) * FT],
                              start=(k == 0), stop=(k == NC - 1))
            mm.then_inc(s_pe.h, 1)
            s_pe.n += 1
    stats_done = s_pe.n

    @block.vector
    def _(v):
        v.wait_ge(s_pe.h, stats_done)
        for f in range(nft):
            cp = v.tensor_copy(srow[0:1, f * FT:(f + 1) * FT], p[f][0:1, :])
        cp.then_inc(s_dv.h, 1)
        s_dv.n += 1
        v.wait_ge(s_dv.h, s_dv.n)
        v.reciprocal(srow[:], srow[:]).then_inc(s_dv.h, 1)
        s_dv.n += 1
    recip_done = s_dv.n

    @block.scalar
    def _(sc):
        sc.wait_ge(s_dv.h, recip_done)
        sc.activation(srow[:], srow[:], AF.Sqrt, scale=1024.0).then_inc(s_ac.h, 1)
        s_ac.n += 1
    srow_done = s_ac.n

    @block.tensor
    def _(t):
        t.wait_ge(s_ac.h, srow_done)
        for f in range(nft):
            t.matmul(bc[:, f * FT:(f + 1) * FT], onesrow_f[:], srow[0:1, f * FT:(f + 1) * FT],
                     start=True, stop=True).then_inc(s_pe.h, 1)
            s_pe.n += 1
    bcast_done = s_pe.n
    assert bcast_done == bcast_done_c

    @block.vector
    def _(v):
        v.wait_ge(s_pe.h, bcast_done)
        for k in range(NC):
            ins = v.scalar_tensor_tensor(xg[:, k, :], xa[:, k, :], gncol[:, k:k + 1],
                                         bc[:], ALU.mult, ALU.mult)
        ins.then_inc(s_dv.h, 1)
        s_dv.n += 1
    norm_done = s_dv.n

    @block.tensor
    def _(t):
        t.wait_ge(s_dv.h, norm_done)
        for fg in range(NG):
            q = fg // 2
            fl = fg % 2
            if fl == 0:
                t.wait_ge(s_gc2[q % 2].h, 16 * (q // 2 + 1))
            for f in range(nft):
                j = fg * nft + f
                jj = j % 4
                if j >= 4:
                    t.wait_ge(s_dv.h, norm_done + (j - 4) + 1)
                for k in range(NC):
                    mm = t.matmul(pb[2 * jj][:], gw_ring[:, q % 2, k, (2 * fl) * 128:(2 * fl + 1) * 128],
                                  xg[:, k, f * FT:(f + 1) * FT], start=(k == 0), stop=(k == NC - 1))
                mm.then_inc(s_pe.h, 1)
                s_pe.n += 1
                for k in range(NC):
                    mm = t.matmul(pb[2 * jj + 1][:], gw_ring[:, q % 2, k, (2 * fl + 1) * 128:(2 * fl + 2) * 128],
                                  xg[:, k, f * FT:(f + 1) * FT], start=(k == 0), stop=(k == NC - 1))
                mm.then_inc(s_pe.h, 1)
                s_pe.n += 1

    @block.scalar
    def _(sc):
        for j in range(NG * nft):
            jj = j % 4
            if j >= 4:
                sc.wait_ge(s_dv.h, norm_done + (j - 4) + 1)
            sc.wait_ge(s_pe.h, bcast_done + 2 * j + 1)
            sc.activation(gg_ring[:, jj, :], pb[2 * jj][:], AF.Gelu).then_inc(s_ac.h, 1)
            s_ac.n += 1

    @block.vector
    def _(v):
        for j in range(NG * nft):
            fg, f = divmod(j, nft)
            jj = j % 4
            v.wait_ge(s_ac.h, srow_done + j + 1)
            v.wait_ge(s_pe.h, bcast_done + 2 * j + 2)
            v.tensor_tensor(u[:, fg, f * FT:(f + 1) * FT], gg_ring[:, jj, :],
                            pb[2 * jj + 1][:], ALU.mult).then_inc(s_dv.h, 1)
            s_dv.n += 1
    grow_done = s_dv.n

    shrink_base = bcast_done + 2 * NG * nft
    assert shrink_base == shrink_base_c

    @block.tensor
    def _(t):
        for i in range(8):
            r = i // 2
            il = i % 2
            if il == 0:
                t.wait_ge(s_sc2[r % 2].h, 16 * (r // 2 + 1))
            for f in range(nft):
                j = i * nft + f
                jj = j % 8
                if j == 0:
                    t.wait_ge(s_dv.h, grow_done)
                if j >= 8:
                    t.wait_ge(s_dv.h, grow_done + (j - 8) + 1)
                for k in range(NG):
                    mm = t.matmul(pb[jj][:], sw_ring[:, r % 2, k, il * 128:(il + 1) * 128],
                                  u[:, k, f * FT:(f + 1) * FT], start=(k == 0), stop=(k == NG - 1))
                mm.then_inc(s_pe.h, 1)
                s_pe.n += 1

    @block.vector
    def _(v):
        for i in range(8):
            for f in range(nft):
                j = i * nft + f
                jj = j % 8
                v.wait_ge(s_pe.h, shrink_base + j + 1)
                v.tensor_tensor(xa[:, i, f * FT:(f + 1) * FT], xa[:, i, f * FT:(f + 1) * FT],
                                pb[jj][:], ALU.add).then_inc(s_dv.h, 1)
                s_dv.n += 1
    add_done = s_dv.n

    @block.sync
    def _(sync):
        sync.wait_ge(s_dv.h, add_done)
        sync.dma_start(xa_out.rearrange("(c p) t -> p c t", p=128), xa[:]).then_inc(s_sy.h, 16)
        s_sy.n += 16
        sync.wait_ge(s_sy.h, s_sy.n)





def emit_smqa(b, D, T=T_CORE):
    nc = b.nc
    nft = T // FT

    xa_in = D["xa"]
    wqTd = D["wqT"]
    wkTd = D["wkT"]
    wvTd = D["wvT"]
    woTd = D["woT"]
    gnc_d = D["gnc"]
    cos_d = D["cosT"]
    sin_d = D["sinT"]
    rot_d = D["rotP"]
    mL_d = D["mL"]
    mR_d = D["mR"]
    iscol_d = D["iscol"]
    xa_out = D["xa_out"]

    kh_in = nc.dram_tensor(b.pfx + "kh_in", [128, W], BF16)
    kh_out = nc.dram_tensor(b.pfx + "kh_out", [256, W], BF16)
    vh_in = nc.dram_tensor(b.pfx + "vh_in", [W, HD], BF16)
    vh_out = nc.dram_tensor(b.pfx + "vh_out", [2 * W, HD], BF16)

    xa = b.sb("xa", [128, NC, T])
    xn = b.sb("xn", [128, NC, T], BF16)
    qb = b.sb("qb", [128, QH, T], BF16)
    kext = b.sb("kext", [128, 3 * W], BF16)
    vext = b.sb("vext", [128, 24, HD], BF16)
    Pbuf = b.sb("Pbuf", [128, 12, FT], BF16)
    cosb = b.sb("cosb", [128, T], BF16)
    sinb = b.sb("sinb", [128, T], BF16)
    rotb = b.sb("rotb", [128, 128], BF16)
    mLb = b.sb("mLb", [128, 4, 512], BF16)
    mRb = b.sb("mRb", [128, 4, 512], BF16)
    tmp1 = b.sb("tmp1", [128, FT])
    tmp2 = b.sb("tmp2", [128, FT])
    wpc = b.sb("wpc", [128, 2, NC, 256], BF16)
    wkb = b.sb("wkb", [128, NC, HD], BF16)
    wvb = b.sb("wvb", [128, NC, HD], BF16)
    brec = b.sb("brec", [128, FT], BF16)
    lrow = b.sb("lrow", [1, FT])
    lrowb = b.sb("lrowb", [1, FT], BF16)
    gcol = b.sb("gcol", [128, NC])
    iscl = b.sb("iscl", [128, 1])
    onescol_bf = b.sb("onescol_bf", [128, 1], BF16)
    onesrow_f = b.sb("onesrow_f", [1, 128])
    onesrow_bf = b.sb("onesrow_bf", [1, 128], BF16)
    srow = b.sb("srow", [1, T])
    mcol = b.sb("mcol", [128, QH])
    mrow = b.sb("mrow", [1, 16])
    Obuf = xn

    pA = b.psum("pA", (128, 2048))
    pB = b.psum("pB", (128, 2048))
    pb = [pA[:, f * FT:(f + 1) * FT] for f in range(4)] + \
         [pB[:, f * FT:(f + 1) * FT] for f in range(4)]

    s_sy = b.sem("s_sy")
    s_tab = b.sem("s_tab")
    s_wp = [b.sem(f"s_wp{i}") for i in range(2)]
    s_cc = b.sem("s_cc")
    s_ag = b.sem("s_ag")
    s_pe = b.sem("s_pe")
    s_ac = b.sem("s_ac")
    s_dv = b.sem("s_dv")

    groups = [(w, h, qt) for w in (1, 0) for h in range(QH) for qt in range(2)]

    L = {"S": {}, "Lo": {}, "PV": {}, "rb": {}, "ev": {}, "msk": {}, "oev": {},
         "rc": {}, "brev": {}, "bev": {}}
    pe = 0
    pe += nft; L["stats"] = pe
    pe += nft; L["bcast"] = pe
    L["qproj"] = [pe + j + 1 for j in range(QH * nft)]
    pe += QH * nft
    pe += 1; L["kproj"] = pe
    L["vproj"] = [pe + i + 1 for i in range(16)]
    pe += 16
    L["rotq"] = [pe + j + 1 for j in range(QH * nft)]
    pe += QH * nft
    pe += 1; L["rotk"] = pe
    L["qsq"] = [pe + h + 1 for h in range(QH)]
    pe += QH
    pe += 1; L["ksqa"] = pe
    pe += 1; L["ksqb"] = pe
    pe += 1; L["mb"] = pe
    for g in groups:
        L["S"][g] = [pe + si + 1 for si in range(12)]
        pe += 12
        pe += 1; L["Lo"][g] = pe
        pe += 1; L["PV"][g] = pe
        pe += 1; L["rb"][g] = pe
    L["wout"] = [pe + j + 1 for j in range(NC * nft)]
    pe += NC * nft

    ac = 0
    ac += 1; L["sq"] = ac
    ac += 1; L["srow"] = ac
    ac += 1; L["qsqact"] = ac
    ac += 1; L["ksqact"] = ac
    ac += 1; L["msqrt"] = ac
    ac += 1; L["mneg"] = ac
    for g in groups:
        L["ev"][g] = [ac + si + 1 for si in range(12)]
        ac += 12

    dv = 0
    dv += 1; L["cast"] = dv
    dv += 1; L["srowcp"] = dv
    dv += 1; L["recipn"] = dv
    dv += 1; L["norm"] = dv
    L["qev"] = [dv + j + 1 for j in range(QH * nft)]
    dv += QH * nft
    dv += 1; L["kev"] = dv
    L["vev"] = [dv + i + 1 for i in range(16)]
    dv += 16
    L["qcmb"] = [dv + 3 * j + 3 for j in range(QH * nft)]
    dv += 3 * QH * nft
    L["kcmb"] = [dv + 3 * f + 3 for f in range(nft)]
    dv += 3 * nft
    L["qmax"] = [dv + h + 1 for h in range(QH)]
    dv += QH
    dv += 1; L["kmaxa"] = dv
    dv += 1; L["kmaxb"] = dv
    dv += 1; L["kmaxc"] = dv
    dv += 1; L["mmul"] = dv
    dv += 1; L["mcolev"] = dv
    for g in groups:
        w, h, qt = g
        nmask = sum(1 for _, mk in subtiles(qt) if mk is not None)
        nisc = (8 if qt == 0 else 4) if w == 0 else 0
        L["msk"][g] = dv + nmask + nisc
        dv += nmask + nisc
        dv += 1; L["rc"][g] = dv
        dv += 1; L["brev"][g] = dv
        dv += 1; L["bev"][g] = dv
        dv += 1; L["oev"][g] = dv
    L["wadd"] = [dv + j + 1 for j in range(NC * nft)]
    dv += NC * nft

    block = b.block()

    @block.sync
    def _(sync):
        for dst, src in [(xa[:], xa_in.rearrange("(c p) t -> p c t", p=128)),
                         (gcol[:], gnc_d), (iscl[:], iscol_d)]:
            sync.dma_start(dst, src).then_inc(s_sy.h, 16)
            s_sy.n += 16
    ld_done = s_sy.n

    @block.gpsimd
    def _(g):
        g.dma_start(cosb[:], cos_d).then_inc(s_tab.h, 16)
        s_tab.n += 16
        g.dma_start(sinb[:], sin_d).then_inc(s_tab.h, 16)
        s_tab.n += 16
        g.dma_start(mLb[:], mL_d.rearrange("m p q -> p m q")).then_inc(s_tab.h, 16)
        s_tab.n += 16
        g.dma_start(mRb[:], mR_d.rearrange("m p q -> p m q")).then_inc(s_tab.h, 16)
        s_tab.n += 16
        g.dma_start(rotb[:], rot_d).then_inc(s_tab.h, 16)
        s_tab.n += 16
        g.dma_start(wkb[:], wkTd.rearrange("(c p) o -> p c o", p=128)).then_inc(s_wp[0].h, 16)
        s_wp[0].n += 16
        g.dma_start(wvb[:], wvTd.rearrange("(c p) o -> p c o", p=128)).then_inc(s_wp[1].h, 16)
        s_wp[1].n += 16
        for q in range(4):
            s = q % 2
            if q >= 2:
                g.wait_ge(s_pe.h, L["qproj"][(q - 2 + 1) * 2 * nft - 1])
            g.dma_start(wpc[:, s, :, :],
                        wqTd.rearrange("(c p) o -> p c o", p=128)[:, :, q * 256:(q + 1) * 256]
                        ).then_inc(s_wp[s].h, 16)
            s_wp[s].n += 16

    @block.vector
    def _(v):
        v.memset(onescol_bf[:], 1.0)
        v.memset(onesrow_f[:], 1.0)
        v.memset(onesrow_bf[:], 1.0)
        v.wait_ge(s_sy.h, ld_done)
        v.tensor_copy(xn[:], xa[:]).then_inc(s_dv.h, 1)
        s_dv.n += 1
    assert s_dv.n == L["cast"]

    @block.scalar
    def _(sc):
        sc.wait_ge(s_dv.h, L["cast"])
        sc.activation(qb[:], xn[:], AF.Square).then_inc(s_ac.h, 1)
        s_ac.n += 1

    @block.tensor
    def _(t):
        t.wait_ge(s_ac.h, L["sq"])
        for f in range(nft):
            for k in range(NC):
                mm = t.matmul(pb[f][0:1, :], onescol_bf[:], qb[:, k, f * FT:(f + 1) * FT],
                              start=(k == 0), stop=(k == NC - 1))
            mm.then_inc(s_pe.h, 1)
            s_pe.n += 1

    @block.vector
    def _(v):
        v.wait_ge(s_pe.h, L["stats"])
        for f in range(nft):
            cp = v.tensor_copy(srow[0:1, f * FT:(f + 1) * FT], pb[f][0:1, :])
        cp.then_inc(s_dv.h, 1)
        s_dv.n += 1
        v.wait_ge(s_dv.h, s_dv.n)
        v.reciprocal(srow[:], srow[:]).then_inc(s_dv.h, 1)
        s_dv.n += 1
    assert s_dv.n == L["recipn"]

    @block.scalar
    def _(sc):
        sc.wait_ge(s_dv.h, L["recipn"])
        sc.activation(srow[:], srow[:], AF.Sqrt, scale=1024.0).then_inc(s_ac.h, 1)
        s_ac.n += 1

    @block.tensor
    def _(t):
        t.wait_ge(s_ac.h, L["srow"])
        for f in range(nft):
            t.matmul(pA[:, f * FT:(f + 1) * FT], onesrow_f[:], srow[0:1, f * FT:(f + 1) * FT],
                     start=True, stop=True).then_inc(s_pe.h, 1)
            s_pe.n += 1

    @block.vector
    def _(v):
        v.wait_ge(s_pe.h, L["bcast"])
        for k in range(NC):
            ins = v.scalar_tensor_tensor(xn[:, k, :], xa[:, k, :], gcol[:, k:k + 1],
                                         pA[:], ALU.mult, ALU.mult)
        ins.then_inc(s_dv.h, 1)
        s_dv.n += 1
    assert s_dv.n == L["norm"]

    @block.tensor
    def _(t):
        t.wait_ge(s_dv.h, L["norm"])
        for hh in range(QH):
            q4, ol = divmod(hh, 2)
            if ol == 0:
                t.wait_ge(s_wp[q4 % 2].h, 32 if q4 < 2 else 48)
            for f in range(nft):
                j = hh * nft + f
                jj = j % 8
                if j >= 8:
                    t.wait_ge(s_dv.h, L["qev"][j - 8])
                for k in range(NC):
                    mm = t.matmul(pb[jj][:], wpc[:, q4 % 2, k, ol * 128:(ol + 1) * 128],
                                  xn[:, k, f * FT:(f + 1) * FT], start=(k == 0), stop=(k == NC - 1))
                mm.then_inc(s_pe.h, 1)
                s_pe.n += 1
        t.wait_ge(s_wp[0].h, 48)
        t.wait_ge(s_wp[1].h, 48)
        t.wait_ge(s_dv.h, L["qev"][QH * nft - 1])
        for f in range(nft):
            for k in range(NC):
                mm = t.matmul(pA[:, f * FT:(f + 1) * FT], wkb[:, k, :],
                              xn[:, k, f * FT:(f + 1) * FT], start=(k == 0), stop=(k == NC - 1))
        mm.then_inc(s_pe.h, 1)
        s_pe.n += 1
        assert s_pe.n == L["kproj"]
        for tc in range(16):
            bank = pB[:, (tc % 4) * FT:(tc % 4) * FT + HD]
            if tc >= 4:
                t.wait_ge(s_dv.h, L["vev"][tc - 4])
            for k in range(NC):
                mm = t.matmul(bank, xn[:, k, tc * 128:(tc + 1) * 128], wvb[:, k, :],
                              start=(k == 0), stop=(k == NC - 1))
            mm.then_inc(s_pe.h, 1)
            s_pe.n += 1

    @block.vector
    def _(v):
        for j in range(QH * nft):
            hh, f = divmod(j, nft)
            v.wait_ge(s_pe.h, L["qproj"][j])
            v.tensor_copy(qb[:, hh, f * FT:(f + 1) * FT], pb[j % 8][:]).then_inc(s_dv.h, 1)
            s_dv.n += 1
        v.wait_ge(s_pe.h, L["kproj"])
        v.tensor_copy(kext[:, W:3 * W], pA[:]).then_inc(s_dv.h, 1)
        s_dv.n += 1
        assert s_dv.n == L["kev"]
        for tc in range(16):
            v.wait_ge(s_pe.h, L["vproj"][tc])
            v.tensor_copy(vext[:, 8 + tc, :], pB[:, (tc % 4) * FT:(tc % 4) * FT + HD]
                          ).then_inc(s_dv.h, 1)
            s_dv.n += 1

    @block.tensor
    def _(t):
        t.wait_ge(s_tab.h, 80)
        for j in range(QH * nft):
            hh, f = divmod(j, nft)
            if j >= 8:
                t.wait_ge(s_dv.h, L["qcmb"][j - 8])
            else:
                t.wait_ge(s_dv.h, L["vev"][15])
            t.matmul(pb[j % 8][:], rotb[:], qb[:, hh, f * FT:(f + 1) * FT],
                     start=True, stop=True).then_inc(s_pe.h, 1)
            s_pe.n += 1
        t.wait_ge(s_dv.h, L["qcmb"][QH * nft - 1])
        for f in range(nft):
            mm = t.matmul(pb[f][:], rotb[:], kext[:, W + f * FT:W + (f + 1) * FT],
                          start=True, stop=True)
        mm.then_inc(s_pe.h, 1)
        s_pe.n += 1
        assert s_pe.n == L["rotk"]

    @block.vector
    def _(v):
        v.wait_ge(s_tab.h, 80)
        for j in range(QH * nft):
            hh, f = divmod(j, nft)
            fs = slice(f * FT, (f + 1) * FT)
            v.wait_ge(s_pe.h, L["rotq"][j])
            v.wait_ge(s_dv.h, s_dv.n)
            v.tensor_tensor(tmp1[:, 0:FT], qb[:, hh, fs], cosb[:, fs], ALU.mult
                            ).then_inc(s_dv.h, 1)
            s_dv.n += 1
            v.wait_ge(s_dv.h, s_dv.n)
            v.tensor_tensor(tmp2[:, 0:FT], sinb[:, fs], pb[j % 8][:], ALU.mult
                            ).then_inc(s_dv.h, 1)
            s_dv.n += 1
            v.wait_ge(s_dv.h, s_dv.n)
            v.tensor_tensor(qb[:, hh, fs], tmp1[:, 0:FT], tmp2[:, 0:FT], ALU.add
                            ).then_inc(s_dv.h, 1)
            s_dv.n += 1
        for f in range(nft):
            fs = slice(f * FT, (f + 1) * FT)
            ks = slice(W + f * FT, W + (f + 1) * FT)
            v.wait_ge(s_pe.h, L["rotk"])
            v.wait_ge(s_dv.h, s_dv.n)
            v.tensor_tensor(tmp1[:, 0:FT], kext[:, ks], cosb[:, fs], ALU.mult
                            ).then_inc(s_dv.h, 1)
            s_dv.n += 1
            v.wait_ge(s_dv.h, s_dv.n)
            v.tensor_tensor(tmp2[:, 0:FT], sinb[:, fs], pb[f][:], ALU.mult).then_inc(s_dv.h, 1)
            s_dv.n += 1
            v.wait_ge(s_dv.h, s_dv.n)
            v.tensor_tensor(kext[:, ks], tmp1[:, 0:FT], tmp2[:, 0:FT], ALU.add
                            ).then_inc(s_dv.h, 1)
            s_dv.n += 1
        assert s_dv.n == L["kcmb"][nft - 1]

    @block.gpsimd
    def _(g):
        g.wait_ge(s_dv.h, L["kcmb"][nft - 1])
        g.dma_start(kh_in.ap(), kext[:, 2 * W:3 * W]).then_inc(s_ag.h, 16)
        s_ag.n += 16
        g.wait_ge(s_ag.h, s_ag.n)
        g.dma_start(vh_in.ap().rearrange("(c p) h -> p c h", p=128), vext[:, 16:24, :]
                    ).then_inc(s_ag.h, 16)
        s_ag.n += 16
        g.wait_ge(s_ag.h, s_ag.n)
        g.collective_compute("AllGather", ALU.bypass, replica_groups=RG,
                             ins=[kh_in.ap().opt()], outs=[kh_out.ap().opt()]
                             ).then_inc(s_cc.h, 1)
        s_cc.n += 1
        g.collective_compute("AllGather", ALU.bypass, replica_groups=RG,
                             ins=[vh_in.ap().opt()], outs=[vh_out.ap().opt()]
                             ).then_inc(s_cc.h, 1)
        s_cc.n += 1
        g.wait_ge(s_cc.h, s_cc.n)
        g.dma_start(kext[:, 0:W], kh_out.ap()[0:128, :]).then_inc(s_ag.h, 16)
        s_ag.n += 16
        g.wait_ge(s_ag.h, s_ag.n)
        g.dma_start(vext[:, 0:8, :], vh_out.ap()[0:W, :].rearrange("(c p) h -> p c h", p=128)
                    ).then_inc(s_ag.h, 16)
        s_ag.n += 16
        g.wait_ge(s_ag.h, s_ag.n)
    ag_done = s_ag.n

    @block.scalar
    def _(sc):
        sc.wait_ge(s_dv.h, L["kcmb"][nft - 1])
        sc.activation(xn[:], qb[:], AF.Square).then_inc(s_ac.h, 1)
        s_ac.n += 1
        sc.wait_ge(s_ag.h, ag_done)
        sc.activation(Pbuf[:].rearrange("p c t -> p (c t)")[:, 0:3 * W], kext[:], AF.Square
                      ).then_inc(s_ac.h, 1)
        s_ac.n += 1
        sc.wait_ge(s_dv.h, L["mmul"])
        sc.activation(mrow[0:1, 0:QH], mrow[0:1, 0:QH], AF.Sqrt).then_inc(s_ac.h, 1)
        s_ac.n += 1
        sc.wait_ge(s_ac.h, s_ac.n)
        sc.mul(mrow[0:1, 0:QH], mrow[0:1, 0:QH], -ISQ).then_inc(s_ac.h, 1)
        s_ac.n += 1
        assert s_ac.n == L["mneg"]

    @block.tensor
    def _(t):
        t.wait_ge(s_ac.h, L["qsqact"])
        for hh in range(QH):
            if hh >= 1:
                t.wait_ge(s_dv.h, L["qmax"][hh - 1])
            for f in range(nft):
                mm = t.matmul(pA[0:1, f * FT:(f + 1) * FT], onescol_bf[:],
                              xn[:, hh, f * FT:(f + 1) * FT], start=True, stop=True)
            mm.then_inc(s_pe.h, 1)
            s_pe.n += 1
        t.wait_ge(s_ac.h, L["ksqact"])
        t.wait_ge(s_dv.h, L["qmax"][QH - 1])
        ksq = Pbuf[:].rearrange("p c t -> p (c t)")
        for f in range(4):
            mm = t.matmul(pB[0:1, f * FT:(f + 1) * FT], onescol_bf[:],
                          ksq[:, f * FT:(f + 1) * FT], start=True, stop=True)
        mm.then_inc(s_pe.h, 1)
        s_pe.n += 1
        assert s_pe.n == L["ksqa"]
        t.wait_ge(s_dv.h, L["kmaxa"])
        for f in range(2):
            mm = t.matmul(pB[0:1, f * FT:(f + 1) * FT], onescol_bf[:],
                          ksq[:, 2048 + f * FT:2048 + (f + 1) * FT], start=True, stop=True)
        mm.then_inc(s_pe.h, 1)
        s_pe.n += 1
        assert s_pe.n == L["ksqb"]
        t.wait_ge(s_ac.h, L["mneg"])
        t.matmul(pA[:, 0:QH], onesrow_f[:], mrow[0:1, 0:QH], start=True, stop=True
                 ).then_inc(s_pe.h, 1)
        s_pe.n += 1
        assert s_pe.n == L["mb"]

    @block.vector
    def _(v):
        for hh in range(QH):
            v.wait_ge(s_pe.h, L["qsq"][hh])
            v.reduce_max(mrow[0:1, hh:hh + 1], pA[0:1, :], axis=AX.X).then_inc(s_dv.h, 1)
            s_dv.n += 1
        v.wait_ge(s_pe.h, L["ksqa"])
        v.reduce_max(mrow[0:1, 8:9], pB[0:1, :], axis=AX.X).then_inc(s_dv.h, 1)
        s_dv.n += 1
        assert s_dv.n == L["kmaxa"]
        v.wait_ge(s_pe.h, L["ksqb"])
        v.reduce_max(mrow[0:1, 9:10], pB[0:1, 0:2 * FT], axis=AX.X).then_inc(s_dv.h, 1)
        s_dv.n += 1
        v.wait_ge(s_dv.h, s_dv.n)
        v.tensor_tensor(mrow[0:1, 8:9], mrow[0:1, 8:9], mrow[0:1, 9:10], ALU.max
                        ).then_inc(s_dv.h, 1)
        s_dv.n += 1
        assert s_dv.n == L["kmaxc"]
        v.wait_ge(s_dv.h, s_dv.n)
        v.tensor_scalar(mrow[0:1, 0:QH], mrow[0:1, 0:QH], mrow[0:1, 8:9], None, ALU.mult
                        ).then_inc(s_dv.h, 1)
        s_dv.n += 1
        assert s_dv.n == L["mmul"]
        v.wait_ge(s_pe.h, L["mb"])
        v.tensor_copy(mcol[:], pA[:, 0:QH]).then_inc(s_dv.h, 1)
        s_dv.n += 1
        assert s_dv.n == L["mcolev"]

    @block.tensor
    def _(t):
        for gi, g in enumerate(groups):
            w, h, qt = g
            st = subtiles(qt)
            if w == 0:
                t.wait_ge(s_ag.h, ag_done)
            for si, (ki, mk) in enumerate(st):
                bk = si % 6
                if si >= 6:
                    t.wait_ge(s_ac.h, L["ev"][g][si - 6])
                elif gi >= 1:
                    t.wait_ge(s_ac.h, L["ev"][groups[gi - 1]][6 + si])
                else:
                    t.wait_ge(s_dv.h, L["mcolev"])
                mm = t.matmul(pb[bk][:], kext[:, w * W + ki * 128:w * W + (ki + 1) * 128],
                              qb[:, h, w * W + qt * FT:w * W + (qt + 1) * FT],
                              start=True, stop=True)
                mm.then_inc(s_pe.h, 1)
                s_pe.n += 1
            assert s_pe.n == L["S"][g][-1]
            t.wait_ge(s_dv.h, L["msk"][g])
            if gi >= 1:
                t.wait_ge(s_dv.h, L["bev"][groups[gi - 1]])
            for si in range(12):
                mm = t.matmul(pb[6][0:1, :], onescol_bf[:], Pbuf[:, si, :],
                              start=(si == 0), stop=(si == 11))
            mm.then_inc(s_pe.h, 1)
            s_pe.n += 1
            assert s_pe.n == L["Lo"][g]
            if gi >= 1:
                t.wait_ge(s_dv.h, L["oev"][groups[gi - 1]])
            for si, (ki, mk) in enumerate(st):
                mm = t.matmul(pb[7][:], vext[:, w * 8 + ki, :], Pbuf[:, si, :],
                              start=(si == 0), stop=(si == 11))
            mm.then_inc(s_pe.h, 1)
            s_pe.n += 1
            assert s_pe.n == L["PV"][g]
            t.wait_ge(s_dv.h, L["brev"][g])
            t.matmul(pb[6][:], onesrow_bf[:], lrowb[0:1, :], start=True, stop=True
                     ).then_inc(s_pe.h, 1)
            s_pe.n += 1
            assert s_pe.n == L["rb"][g]

    @block.scalar
    def _(sc):
        for gi, g in enumerate(groups):
            w, h, qt = g
            for si in range(12):
                sc.wait_ge(s_pe.h, L["S"][g][si])
                if gi >= 1:
                    sc.wait_ge(s_pe.h, L["PV"][groups[gi - 1]])
                sc.activation(Pbuf[:, si, :], pb[si % 6][:], AF.Exp, scale=ISQ,
                              bias=mcol[:, h:h + 1]).then_inc(s_ac.h, 1)
                s_ac.n += 1
            assert s_ac.n == L["ev"][g][-1]

    @block.vector
    def _(v):
        v.wait_ge(s_tab.h, 80)
        for gi, g in enumerate(groups):
            w, h, qt = g
            st = subtiles(qt)
            for si, (ki, mk) in enumerate(st):
                left = (w * W + ki * 128) < W
                if mk is not None:
                    v.wait_ge(s_ac.h, L["ev"][g][si])
                    mt = mLb if mk[0] == "L" else mRb
                    v.tensor_tensor(Pbuf[:, si, :], Pbuf[:, si, :], mt[:, mk[1], :], ALU.mult
                                    ).then_inc(s_dv.h, 1)
                    s_dv.n += 1
                if w == 0 and left:
                    v.wait_ge(s_ac.h, L["ev"][g][si])
                    v.wait_ge(s_dv.h, s_dv.n)
                    v.tensor_scalar(Pbuf[:, si, :], Pbuf[:, si, :], iscl[:, 0:1], None,
                                    ALU.mult).then_inc(s_dv.h, 1)
                    s_dv.n += 1
            assert s_dv.n == L["msk"][g]
            v.wait_ge(s_pe.h, L["Lo"][g])
            v.reciprocal(lrow[0:1, :], pb[6][0:1, :]).then_inc(s_dv.h, 1)
            s_dv.n += 1
            v.wait_ge(s_dv.h, s_dv.n)
            v.tensor_copy(lrowb[0:1, :], lrow[0:1, :]).then_inc(s_dv.h, 1)
            s_dv.n += 1
            assert s_dv.n == L["brev"][g]
            v.wait_ge(s_pe.h, L["rb"][g])
            v.tensor_copy(brec[:], pb[6][:]).then_inc(s_dv.h, 1)
            s_dv.n += 1
            assert s_dv.n == L["bev"][g]
            v.wait_ge(s_pe.h, L["PV"][g])
            v.wait_ge(s_dv.h, s_dv.n)
            v.tensor_tensor(Obuf[:, h, w * W + qt * FT:w * W + (qt + 1) * FT],
                            pb[7][:], brec[:], ALU.mult).then_inc(s_dv.h, 1)
            s_dv.n += 1
            assert s_dv.n == L["oev"][g]

    @block.tensor
    def _(t):
        t.wait_ge(s_dv.h, L["oev"][groups[-1]])
        for i in range(NC):
            wq4, wl = divmod(i, 2)
            t.wait_ge(s_wp[wq4 % 2].h, 48 + 16 * (wq4 // 2 + 1))
            for f in range(nft):
                j = i * nft + f
                jj = j % 8
                if j >= 8:
                    t.wait_ge(s_dv.h, L["wadd"][j - 8])
                for k in range(NC):
                    mm = t.matmul(pb[jj][:], wpc[:, wq4 % 2, k, wl * 128:(wl + 1) * 128],
                                  Obuf[:, k, f * FT:(f + 1) * FT],
                                  start=(k == 0), stop=(k == NC - 1))
                mm.then_inc(s_pe.h, 1)
                s_pe.n += 1

    @block.gpsimd
    def _(g):
        for q in range(4):
            s = q % 2
            if q < 2:
                g.wait_ge(s_pe.h, L["qproj"][(q + 2 + 1) * 2 * nft - 1])
            else:
                g.wait_ge(s_pe.h, L["wout"][(q - 2 + 1) * 2 * nft - 1])
            g.dma_start(wpc[:, s, :, :],
                        woTd.rearrange("(c p) o -> p c o", p=128)[:, :, q * 256:(q + 1) * 256]
                        ).then_inc(s_wp[s].h, 16)
            s_wp[s].n += 16

    @block.vector
    def _(v):
        for i in range(NC):
            for f in range(nft):
                j = i * nft + f
                v.wait_ge(s_pe.h, L["wout"][j])
                v.tensor_tensor(xa[:, i, f * FT:(f + 1) * FT], xa[:, i, f * FT:(f + 1) * FT],
                                pb[j % 8][:], ALU.add).then_inc(s_dv.h, 1)
                s_dv.n += 1
        assert s_dv.n == L["wadd"][-1]

    @block.sync
    def _(sync):
        sync.wait_ge(s_dv.h, L["wadd"][-1])
        sync.dma_start(xa_out.rearrange("(c p) t -> p c t", p=128), xa[:]).then_inc(s_sy.h, 16)
        s_sy.n += 16
        sync.wait_ge(s_sy.h, s_sy.n)



def emit_moe(b, D, T=T_CORE):
    nc = b.nc
    ntc = T // 128

    xT_d = D["xT"]
    xnb_d = D["xnb"]
    xa_d = D["xa"]
    gwT_d = D["gwT"]
    w1_d = D["w1"]
    b1_d = D["b1c"]
    w2_d = D["w2"]
    b2_d = D["b2c"]
    sel_d = D["sel"]
    ident_d = D["identf"]
    rs_in = [nc.dram_tensor(b.pfx + f"rs_in{i}", [1024, TH], F32) for i in range(2)]
    rs_out = [nc.dram_tensor(b.pfx + f"rs_out{i}", [2048, TH], F32) for i in range(2)]

    ACC = b.sb("ACC", [128, NC, T])
    xnh = b.sb("xnh", [128, NC, TH], BF16)
    hid = b.sb("hid", [128, NF, TH], BF16)
    w1r = b.sb("w1r", [128, 2, NC, 512], BF16)
    w2r = b.sb("w2r", [128, 2, NF, 128], BF16)
    gwb = b.sb("gwb", [128, NC, E], BF16)
    selb = b.sb("selb", [8, E * 128])
    wTg = b.sb("wTg", [8, T])
    lg = b.sb("lg", [128, 16, E])
    p2s = b.sb("p2s", [128, E])
    lcol = b.sb("lcol", [128, 4])
    b1c = b.sb("b1cs", [128, E, NF])
    b2c = b.sb("b2cs", [128, E, 8])
    wbc = b.sb("wbc", [128, 2, FT])
    tmpv = b.sb("tmpv", [128, FT])
    xtr = b.sb("xtr", [128, 2, TH])
    identf = b.sb("idf", [128, 128])
    onescol_bf = b.sb("onescol_bf", [128, 1], BF16)

    pA = b.psum("pA", (128, 2048))
    pB = b.psum("pB", (128, 2048))
    pb = [pA[:, f * FT:(f + 1) * FT] for f in range(4)] + \
         [pB[:, f * FT:(f + 1) * FT] for f in range(4)]

    s_sy = b.sem("s_sy")
    s_xh = b.sem("s_xh")
    s_w1 = [b.sem(f"s_w1{i}") for i in range(2)]
    s_w2 = [b.sem(f"s_w2{i}") for i in range(2)]
    s_xr = [b.sem(f"s_xr{i}") for i in range(2)]
    s_pe = b.sem("s_pe")
    s_ac = b.sem("s_ac")
    s_dv = b.sem("s_dv")
    s_gp = b.sem("s_gp")
    s_rs = [b.sem(f"s_rs{i}") for i in range(2)]
    s_rsc = [b.sem(f"s_rsc{i}") for i in range(2)]
    b.rs_out = rs_out
    b.rs_cc = s_rsc

    L = {}
    pe = 0
    L["glog"] = [0] * ntc
    L["gtr"] = [0] * ntc
    for tc in range(ntc):
        pe += 1; L["glog"][tc] = pe
        if tc >= 1:
            pe += 1; L["gtr"][tc - 1] = pe
    pe += 1; L["gtr"][ntc - 1] = pe
    HG = {}
    YB = {}
    YG = {}
    for th in range(2):
        for e in range(E):
            for fc in range(NF):
                for tt in range(2):
                    pe += 1; HG[(th, e, fc, tt)] = pe
            for tt in range(2):
                pe += 1; YB[(th, e, tt)] = pe
            for i in range(8):
                for tt in range(2):
                    pe += 1; YG[(th, e, i, tt)] = pe
    L["pe_end"] = pe

    ac = 0
    GEX = {}
    for tc in range(ntc):
        ac += 1; GEX[tc] = ac
    REL = {}
    for th in range(2):
        for e in range(E):
            for fc in range(NF):
                for tt in range(2):
                    ac += 1; REL[(th, e, fc, tt)] = ac
    L["ac_end"] = ac

    dv = 0
    GEV = {}
    GSM = {}
    GTV = {}
    for tc in range(ntc):
        dv += 1; GEV[tc] = dv
        dv += 11; GSM[tc] = dv
        dv += 1; GTV[tc] = dv
    WBE = {}
    YEV = {}
    XAD = {}
    for th in range(2):
        for e in range(E):
            for tt in range(2):
                dv += 1; WBE[(th, e, tt)] = dv
            for i in range(8):
                for tt in range(2):
                    dv += 2; YEV[(th, e, i, tt)] = dv
        for kk in range(NC):
            dv += 1; XAD[th * NC + kk] = dv
    L["dv_end"] = dv

    block = b.block()

    @block.sync
    def _(sync):
        sync.dma_start(ACC[:], xa_d.rearrange("(c p) t -> p c t", p=128)).then_inc(s_sy.h, 16)
        s_sy.n += 16
        sync.dma_start(b1c[:], b1_d).then_inc(s_sy.h, 16)
        s_sy.n += 16
        sync.dma_start(b2c[:], b2_d).then_inc(s_sy.h, 16)
        s_sy.n += 16
        sync.dma_start(selb[:], sel_d).then_inc(s_sy.h, 16)
        s_sy.n += 16
        sync.dma_start(identf[:], ident_d).then_inc(s_sy.h, 16)
        s_sy.n += 16
    ld_done = s_sy.n

    @block.gpsimd
    def _(g):
        g.dma_start(gwb[:], gwT_d.rearrange("(c p) e -> p c e", p=128)).then_inc(s_gp.h, 16)
        s_gp.n += 16
        def xnh_load(m, hf):
            if m == 1:
                g.wait_ge(s_pe.h, L["glog"][7])
            elif m == 2:
                g.wait_ge(s_pe.h, L["glog"][15])
            elif m == 3:
                g.wait_ge(s_pe.h, YG[(0, E - 1, 7, 1)])
            g.dma_start(xnh[:], xnb_d.rearrange("(c p) t -> p c t", p=128)[:, :, hf * TH:(hf + 1) * TH]
                        ).then_inc(s_xh.h, 16)
            s_xh.n += 16
        xnh_load(0, 0)
        xnh_load(1, 1)
        xnh_load(2, 0)
        for th in range(2):
            if th == 1:
                xnh_load(3, 1)
            for e in range(E):
                for q in range(8):
                    s = q % 2
                    m1 = (th * E + e) * 8 + q
                    if m1 >= 2:
                        qp, ep, thp = (m1 - 2) % 8, ((m1 - 2) // 8) % E, (m1 - 2) // (8 * E)
                        g.wait_ge(s_pe.h, HG[(thp, ep, qp * 4 + 3, 1)])
                    g.dma_start(w1r[:, s, :, :],
                                w1_d[e].rearrange("(c p) f -> p c f", p=128)[:, :, q * 512:(q + 1) * 512]
                                ).then_inc(s_w1[s].h, 16)
                    s_w1[s].n += 16
                for i in range(8):
                    s = i % 2
                    m2 = (th * E + e) * 8 + i
                    if m2 >= 2:
                        ip, ep, thp = (m2 - 2) % 8, ((m2 - 2) // 8) % E, (m2 - 2) // (8 * E)
                        g.wait_ge(s_pe.h, YG[(thp, ep, ip, 1)])
                    g.dma_start(w2r[:, s, :, :],
                                w2_d[e].rearrange("(c p) h -> p c h", p=128)[:, :, i * 128:(i + 1) * 128]
                                ).then_inc(s_w2[s].h, 16)
                    s_w2[s].n += 16
        for th in range(2):
            g.wait_ge(s_rs[th].h, 64)
            g.collective_compute("AllGather", ALU.bypass, replica_groups=RG,
                                 ins=[rs_in[th].ap().opt()], outs=[rs_out[th].ap().opt()]
                                 ).then_inc(s_rsc[th].h, 1)
            s_rsc[th].n += 1

    def emit_logits(t, tc):
        hf = tc // 8
        tcl = tc % 8
        t.wait_ge(s_xh.h, 16 * (hf + 1))
        if tc >= 4:
            t.wait_ge(s_dv.h, GEV[tc - 4])
        bank = pb[tc % 4]
        col = (tc // 4) * 8
        for k in range(NC):
            mm = t.matmul(bank[:, col:col + 8], xnh[:, k, tcl * 128:(tcl + 1) * 128],
                          gwb[:, k, :], start=(k == 0), stop=(k == NC - 1))
        mm.then_inc(s_pe.h, 1)
        s_pe.n += 1
        assert s_pe.n == L["glog"][tc]

    def emit_gtr(t, tc):
        t.wait_ge(s_dv.h, GSM[tc])
        if tc >= 4:
            t.wait_ge(s_dv.h, GTV[tc - 4])
        t.transpose(pb[4 + tc % 4][0:8, 0:128], lg[:, tc, :], identf[:]).then_inc(s_pe.h, 1)
        s_pe.n += 1
        assert s_pe.n == L["gtr"][tc]

    @block.tensor
    def _(t):
        t.wait_ge(s_gp.h, 16)
        t.wait_ge(s_sy.h, ld_done)
        for tc in range(ntc):
            emit_logits(t, tc)
            if tc >= 1:
                emit_gtr(t, tc - 1)
        emit_gtr(t, ntc - 1)

    @block.scalar
    def _(sc):
        for tc in range(ntc):
            sc.wait_ge(s_dv.h, GSM[tc] - 9)
            sc.activation(lg[:, tc, :], lg[:, tc, :], AF.Exp, bias=lcol[:, 0:1]
                          ).then_inc(s_ac.h, 1)
            s_ac.n += 1
            assert s_ac.n == GEX[tc]

    @block.vector
    def _(v):
        v.memset(onescol_bf[:], 1.0)
        for tc in range(ntc):
            v.wait_ge(s_pe.h, L["glog"][tc])
            v.tensor_copy(lg[:, tc, :], pb[tc % 4][:, (tc // 4) * 8:(tc // 4) * 8 + 8]
                          ).then_inc(s_dv.h, 1)
            s_dv.n += 1
            assert s_dv.n == GEV[tc]
            v.wait_ge(s_dv.h, s_dv.n)
            v.reduce_max(lcol[:, 0:1], lg[:, tc, :], axis=AX.X).then_inc(s_dv.h, 1)
            s_dv.n += 1
            v.wait_ge(s_dv.h, s_dv.n)
            v.tensor_scalar(lcol[:, 0:1], lcol[:, 0:1], -1.0, None, ALU.mult).then_inc(s_dv.h, 1)
            s_dv.n += 1
            v.wait_ge(s_ac.h, GEX[tc])
            v.reduce_sum(lcol[:, 1:2], lg[:, tc, :], axis=AX.X).then_inc(s_dv.h, 1)
            s_dv.n += 1
            v.wait_ge(s_dv.h, s_dv.n)
            v.reciprocal(lcol[:, 1:2], lcol[:, 1:2]).then_inc(s_dv.h, 1)
            s_dv.n += 1
            v.wait_ge(s_dv.h, s_dv.n)
            v.tensor_scalar(lg[:, tc, :], lg[:, tc, :], lcol[:, 1:2], None, ALU.mult
                            ).then_inc(s_dv.h, 1)
            s_dv.n += 1
            v.wait_ge(s_dv.h, s_dv.n)
            v.reduce_max(lcol[:, 2:3], lg[:, tc, :], axis=AX.X).then_inc(s_dv.h, 1)
            s_dv.n += 1
            v.wait_ge(s_dv.h, s_dv.n)
            v.tensor_scalar(p2s[:], lg[:, tc, :], lcol[:, 2:3], None, ALU.is_ge
                            ).then_inc(s_dv.h, 1)
            s_dv.n += 1
            v.wait_ge(s_dv.h, s_dv.n)
            v.scalar_tensor_tensor(p2s[:], p2s[:], -BIG, lg[:, tc, :], ALU.mult, ALU.add
                                   ).then_inc(s_dv.h, 1)
            s_dv.n += 1
            v.wait_ge(s_dv.h, s_dv.n)
            v.reduce_max(lcol[:, 3:4], p2s[:], axis=AX.X).then_inc(s_dv.h, 1)
            s_dv.n += 1
            v.wait_ge(s_dv.h, s_dv.n)
            v.tensor_scalar(p2s[:], lg[:, tc, :], lcol[:, 3:4], None, ALU.is_ge
                            ).then_inc(s_dv.h, 1)
            s_dv.n += 1
            v.wait_ge(s_dv.h, s_dv.n)
            v.tensor_tensor(lg[:, tc, :], lg[:, tc, :], p2s[:], ALU.mult).then_inc(s_dv.h, 1)
            s_dv.n += 1
            assert s_dv.n == GSM[tc]
            v.wait_ge(s_pe.h, L["gtr"][tc])
            v.tensor_copy(wTg[0:8, tc * 128:(tc + 1) * 128],
                          pb[4 + tc % 4][0:8, 0:128]).then_inc(s_dv.h, 1)
            s_dv.n += 1
            assert s_dv.n == GTV[tc]

    @block.tensor
    def _(t):
        t.wait_ge(s_dv.h, GTV[ntc - 1])
        for th in range(2):
            t.wait_ge(s_xh.h, 16 * (th + 3))
            for e in range(E):
                for q in range(8):
                    s = q % 2
                    m1 = (th * E + e) * 8 + q
                    t.wait_ge(s_w1[s].h, 16 * (m1 // 2 + 1))
                    for icl in range(4):
                        fc = q * 4 + icl
                        for tt in range(2):
                            gidx = (fc * 2 + tt)
                            bk = gidx % 6
                            prev = gidx - 6
                            if prev >= 0:
                                t.wait_ge(s_ac.h, REL[(th, e, prev // 2, prev % 2)])
                            elif (th, e) != (0, 0):
                                pth, pe_ = (th, e - 1) if e > 0 else (th - 1, E - 1)
                                pgidx = 64 + prev
                                t.wait_ge(s_ac.h, REL[(pth, pe_, pgidx // 2, pgidx % 2)])
                                t.wait_ge(s_dv.h, YEV[(pth, pe_, 7, 1)])
                            for k in range(NC):
                                mm = t.matmul(pb[bk][:], w1r[:, s, k, icl * 128:(icl + 1) * 128],
                                              xnh[:, k, tt * FT:(tt + 1) * FT],
                                              start=(k == 0), stop=(k == NC - 1))
                            mm.then_inc(s_pe.h, 1)
                            s_pe.n += 1
                            assert s_pe.n == HG[(th, e, fc, tt)]
                for tt in range(2):
                    t.wait_ge(s_dv.h, GTV[ntc - 1] if (th, e, tt) == (0, 0, 0) else
                              WBE[(th, e, tt - 1)] if tt > 0 else
                              (WBE[(th, e - 1, 1)] if e > 0 else WBE[(th - 1, E - 1, 1)]))
                    t.matmul(pb[6][:], selb[:, e * 128:(e + 1) * 128],
                             wTg[0:8, th * TH + tt * FT:th * TH + (tt + 1) * FT],
                             start=True, stop=True).then_inc(s_pe.h, 1)
                    s_pe.n += 1
                    assert s_pe.n == YB[(th, e, tt)]
                for i in range(8):
                    s = i % 2
                    m2 = (th * E + e) * 8 + i
                    t.wait_ge(s_w2[s].h, 16 * (m2 // 2 + 1))
                    for tt in range(2):
                        ygi = i * 2 + tt
                        bk = ygi % 6
                        prev = ygi - 6
                        if prev >= 0:
                            t.wait_ge(s_dv.h, YEV[(th, e, prev // 2, prev % 2)])
                        else:
                            pg = 58 + bk
                            t.wait_ge(s_ac.h, REL[(th, e, pg // 2, pg % 2)])
                        t.wait_ge(s_ac.h, REL[(th, e, NF - 1, 1)])
                        for k in range(NF):
                            mm = t.matmul(pb[bk][:], w2r[:, s, k, :],
                                          hid[:, k, tt * FT:(tt + 1) * FT],
                                          start=(k == 0), stop=(k == NF - 1))
                        mm.then_inc(s_pe.h, 1)
                        s_pe.n += 1
                        assert s_pe.n == YG[(th, e, i, tt)]

    @block.scalar
    def _(sc):
        for th in range(2):
            for e in range(E):
                for fc in range(NF):
                    for tt in range(2):
                        gidx = fc * 2 + tt
                        bk = gidx % 6
                        sc.wait_ge(s_pe.h, HG[(th, e, fc, tt)])
                        if (th, e) != (0, 0) and fc == 0 and tt == 0:
                            pth, pe_ = (th, e - 1) if e > 0 else (th - 1, E - 1)
                            sc.wait_ge(s_pe.h, YG[(pth, pe_, 7, 1)])
                        sc.activation(hid[:, fc, tt * FT:(tt + 1) * FT], pb[bk][:], AF.Relu,
                                      bias=b1c[:, e, fc:fc + 1]).then_inc(s_ac.h, 1)
                        s_ac.n += 1
                        assert s_ac.n == REL[(th, e, fc, tt)]

    @block.vector
    def _(v):
        for th in range(2):
            for e in range(E):
                for tt in range(2):
                    v.wait_ge(s_pe.h, YB[(th, e, tt)])
                    v.tensor_copy(wbc[:, tt, :], pb[6][:]).then_inc(s_dv.h, 1)
                    s_dv.n += 1
                    assert s_dv.n == WBE[(th, e, tt)]
                for i in range(8):
                    for tt in range(2):
                        v.wait_ge(s_pe.h, YG[(th, e, i, tt)])
                        ygi = i * 2 + tt
                        bk = ygi % 6
                        v.wait_ge(s_dv.h, s_dv.n)
                        v.scalar_tensor_tensor(tmpv[:], pb[bk][:], b2c[:, e, i:i + 1],
                                               wbc[:, tt, :], ALU.add, ALU.mult).then_inc(s_dv.h, 1)
                        s_dv.n += 1
                        v.wait_ge(s_dv.h, s_dv.n)
                        v.tensor_tensor(ACC[:, i, th * TH + tt * FT:th * TH + (tt + 1) * FT],
                                        ACC[:, i, th * TH + tt * FT:th * TH + (tt + 1) * FT],
                                        tmpv[:], ALU.add).then_inc(s_dv.h, 1)
                        s_dv.n += 1
                        assert s_dv.n == YEV[(th, e, i, tt)]
            for kk in range(NC):
                k = th * NC + kk
                sl = k % 2
                v.wait_ge(s_xr[sl].h, 16 * (k // 2 + 1))
                v.tensor_tensor(ACC[:, kk, th * TH:(th + 1) * TH],
                                ACC[:, kk, th * TH:(th + 1) * TH], xtr[:, sl, :], ALU.add
                                ).then_inc(s_dv.h, 1)
                s_dv.n += 1
                assert s_dv.n == XAD[k]

    @block.sync
    def _(sync):
        def rs_store(th):
            sync.wait_ge(s_dv.h, XAD[th * NC + NC - 1])
            for s in range(2):
                for a in range(2):
                    sync.dma_start(
                        rs_in[th].ap().rearrange("(c p) t -> p c t", p=128
                            )[:, :, s * 512 + a * 256:s * 512 + (a + 1) * 256],
                        ACC[:, :, th * TH + (2 * a + s) * 256:th * TH + (2 * a + s + 1) * 256]
                    ).then_inc(s_rs[th].h, 16)
                    s_rs[th].n += 16
        for k in range(2 * NC):
            sl = k % 2
            th, ch = divmod(k, NC)
            if k >= 2:
                sync.wait_ge(s_dv.h, XAD[k - 2])
            sync.dma_start(xtr[:, sl, :],
                           xT_d.rearrange("(c p) t -> p c t", p=128)[:, ch, th * TH:(th + 1) * TH]
                           ).then_inc(s_xr[sl].h, 16)
            s_xr[sl].n += 16
            if k == NC - 1:
                rs_store(0)
        rs_store(1)




def emit_l1(b, D, T=T_CORE):
    nc = b.nc
    nft = T // FT

    iscol_d = D["iscol"]
    wq_d = D["wq"]
    wk_d = D["wk"]
    wv_d = D["wv"]
    bq_d = D["bqc"]
    bk_d = D["bkc"]
    wf_d = D["wf"]
    a1_d = D["a1"]
    a2_d = D["a2"]
    fb_d = D["fbc"]
    mk_d = D["mks"]
    id_d = D["identf"]
    bvrow_d = D["bvrow"]
    km_in = nc.dram_tensor(b.pfx + "km_in", [1, 16], F32)
    km_out = nc.dram_tensor(b.pfx + "km_out", [1, 16], F32)
    out_o = D["out"]

    k_in = nc.dram_tensor(b.pfx + "k_in", [1024, T], BF16)
    k_out = nc.dram_tensor(b.pfx + "k_out", [2048, T], BF16)
    v_in = nc.dram_tensor(b.pfx + "v_in", [T, 16 * 65], BF16)
    v_out = nc.dram_tensor(b.pfx + "v_out", [2 * T, 16 * 65], BF16)

    X = b.sb("X", [128, NC, T])
    xn = b.sb("xn", [128, NC, T], BF16)
    AQ = b.sb("AQ", [128, NC * T], BF16)
    AK = b.sb("AK", [128, NC * T], BF16)
    AV = b.sb("AV", [128, 16 * 16 * 65], BF16)
    wpc = b.sb("wpc", [128, 2, NC, 256], BF16)
    bvb = b.sb("bvb", [128, 1024], BF16)
    ttr = b.sb("ttr", [128, 2, FT])
    mrow = b.sb("mrow", [1, 3 * 16])
    mcol = b.sb("mcol", [128, 16])
    bqc = b.sb("bqcs", [128, 8])
    bkc = b.sb("bkcs", [128, 8])
    fbc = b.sb("fbcs", [128, 8])
    epsc = b.sb("epsc", [1, 1])
    onescol_bf = b.sb("onescol_bf", [128, 1], BF16)
    onesrow_f = b.sb("onesrow_f", [1, 128])
    onesrow_bf = b.sb("onesrow_bf", [1, 128], BF16)

    qb = AQ[:].rearrange("p (c t) -> p c t", c=NC)
    identf = bvb[:, 0:256].bitcast(F32)
    brec = ttr[:, 0, :].bitcast(BF16)[:, 0:FT]
    lrowb = ttr[:, 1, :].bitcast(BF16)[0:1, 0:FT]
    otr = AQ[:, 0:4096].bitcast(F32).rearrange("p (s t) -> p s t", s=2)
    h1 = AK[:].rearrange("p (c t) -> p c t", c=NC)
    kown = h1
    Pbuf = AK[:].rearrange("p (c t) -> p c t", c=32)
    h2 = AV[:, 0:NC * T].rearrange("p (c t) -> p c t", c=NC)
    vt65 = AV[:].rearrange("p (c d) -> p c d", c=16)
    khr = AV[:, 0:2 * 4096].rearrange("p (s t) -> p s t", s=2)
    vhr = AV[:, 2 * 4096:2 * 4096 + 2 * 32 * 65].rearrange("p (s c d) -> p s c d", s=2, c=32)
    mks = AV[:, 2 * 4096 + 2 * 32 * 65:2 * 4096 + 2 * 32 * 65 + 8 * 512].rearrange(
        "p (m q) -> p m q", m=8)
    AKf = AK[:].bitcast(F32)
    arow = AKf[0:1, 0:2048]
    brow = AKf[0:1, 2048:4096]
    bvrow = AKf[0:1, 4096:5120]
    lrow = ttr[0:1, 0, :]
    sqscr = xn[:].rearrange("p c t -> p (c t)")[:, 0:2048]
    O = xn
    xs = AQ[:].bitcast(F32)[:, 0:2048].rearrange("p (a b u) -> p a b u", a=2, b=2)
    iscl = b.sb("iscl", [128, 1])

    pA = b.psum("pA", (128, 2048))
    pB = b.psum("pB", (128, 2048))
    pb = [pA[:, f * FT:(f + 1) * FT] for f in range(4)] + \
         [pB[:, f * FT:(f + 1) * FT] for f in range(4)]

    s_sy = b.sem("s_sy")
    s_wp = [b.sem(f"s_wp{i}") for i in range(2)]
    s_kh = [b.sem(f"s_kh{i}") for i in range(2)]
    s_vh = [b.sem(f"s_vh{i}") for i in range(2)]
    s_cc = b.sem("s_cc")
    s_ag = b.sem("s_ag")
    s_is = b.sem("s_is")
    s_xs = [b.sem(f"s_xs{i}") for i in range(2)]
    s_sel = b.sem("s_sel")
    s_pe = b.sem("s_pe")
    s_ac = b.sem("s_ac")
    s_dv = b.sem("s_dv")
    s_gp = b.sem("s_gp")
    s_ot = [b.sem(f"s_ot{i}") for i in range(2)]

    groups = [(h, it) for h in range(NHD) for it in range(4)]
    nki = lambda it: 8 * it + 8

    L = {"S": {}, "PV": {}, "rb": {}, "ev": {}, "msk": {}, "oev": {}, "rc": {},
         "brv": {}, "bev": {}}
    pe = 0
    pe += nft; L["sum"] = pe
    pe += nft; L["sumsq"] = pe
    pe += 2 * nft; L["lnbc"] = pe
    pe += 1; L["bvb"] = pe
    FF = {}
    for st in range(3):
        for i in range(NC):
            for f in range(nft):
                pe += 1; FF[(st, i, f)] = pe
    PQ = {}
    PK = {}
    PV_ = {}
    for i in range(NC):
        for f in range(nft):
            pe += 1; PQ[(i, f)] = pe
    for i in range(NC):
        for f in range(nft):
            pe += 1; PK[(i, f)] = pe
    for hv in range(2):
        for tc in range(16):
            pe += 1; PV_[(tc, hv)] = pe
    QSQP = {}
    for c in range(NC):
        for par in range(2):
            pe += 1; QSQP[(c, par)] = pe
    L["qsq1"] = pe
    KSQP = {}
    for c in range(NC):
        for par in range(2):
            pe += 1; KSQP[(c, par)] = pe
    L["ksq1"] = pe
    pe += 1; L["mb"] = pe
    for g in groups:
        h, it = g
        L["S"][g] = [pe + s + 1 for s in range(nki(it))]
        pe += nki(it)
        pe += 1; L["PV"][g] = pe
        pe += 1; L["rb"][g] = pe
    L["pe_attn_end"] = pe
    TRO = {}
    for tc in range(16):
        pe += 1; TRO[tc] = pe
    L["pe_end"] = pe

    ac = 0
    ac += 1; L["sq"] = ac
    ac += 1; L["lnsqrt"] = ac
    FTN = {}
    for i in range(NC):
        for f in range(nft):
            ac += 1; FTN[(i, f)] = ac
    PQE = {}
    PKE = {}
    for i in range(NC):
        for f in range(nft):
            ac += 1; PQE[(i, f)] = ac
    for i in range(NC):
        for f in range(nft):
            ac += 1; PKE[(i, f)] = ac
    QSQA = []
    for c in range(NC):
        ac += 1; QSQA.append(ac)
    L["qsqa"] = ac
    KSQA = []
    for c in range(NC):
        ac += 1; KSQA.append(ac)
    L["ksqa"] = ac
    ac += 1; L["msq"] = ac
    ac += 1; L["msq2"] = ac
    for g in groups:
        h, it = g
        L["ev"][g] = [ac + s + 1 for s in range(nki(it))]
        ac += nki(it)
    L["ac_end"] = ac

    dv = 0
    dv += 1; L["cast"] = dv
    dv += 1; L["srowcp"] = dv
    dv += 2; L["rowm"] = dv
    dv += 1; L["recipn"] = dv
    dv += 1; L["recip2"] = dv
    dv += 1; L["mrN"] = dv
    LNA = {}
    for kk in range(NC):
        dv += 2; LNA[kk] = dv
    dv += 1; L["bve"] = dv
    FFE = {}
    for st in range(2):
        for i in range(NC):
            for f in range(nft):
                dv += 1; FFE[(st, i, f)] = dv
    FAD = {}
    for i in range(NC):
        for f in range(nft):
            dv += 1; FAD[(i, f)] = dv
    dv += 1; L["vones"] = dv
    PVE = {}
    for hv in range(2):
        for tc in range(16):
            dv += 1; PVE[(tc, hv)] = dv
    QMX = {}
    for h in range(NHD):
        dv += 1; QMX[h] = dv
    L["qmax"] = dv
    KMX = {}
    for h in range(NHD):
        dv += 1; KMX[h] = dv
    L["kmax1"] = dv
    dv += 1; L["kmax2"] = dv
    dv += 1; L["mmul"] = dv
    dv += 1; L["mcev"] = dv
    for g in groups:
        h, it = g
        nm = min(8, nki(it))
        L["msk"][g] = dv + nm
        dv += nm
        dv += 1; L["rc"][g] = dv
        dv += 1; L["brv"][g] = dv
        dv += 1; L["bev"][g] = dv
        dv += 1; L["oev"][g] = dv
    OAD = {}
    for i in range(NC):
        for f in range(nft):
            dv += 1; OAD[(i, f)] = dv
    TRE = {}
    for tc in range(16):
        dv += 1; TRE[tc] = dv
    L["dv_end"] = dv

    block = b.block()

    @block.sync
    def _(sync):
        sync.dma_start(iscl[:], iscol_d).then_inc(s_is.h, 16)
        s_is.n += 16
        for dst, src in [(bqc[:], bq_d), (bkc[:], bk_d),
                         (fbc[:], fb_d), (bvrow[:], bvrow_d)]:
            sync.dma_start(dst, src).then_inc(s_sy.h, 16)
            s_sy.n += 16
    ld_done = s_sy.n

    @block.sync
    def _(sync):
        for qq in range(2):
            sync.wait_ge(b.rs_cc[qq].h, 1)
            for c in range(NC):
                for r in range(2):
                    m = qq * 16 + c * 2 + r
                    sl = m % 2
                    if m >= 2:
                        sync.wait_ge(s_sel.h, 2 * (m - 1))
                    rows = b.rs_out[qq].ap()[r * 1024 + c * 128:r * 1024 + (c + 1) * 128, :]
                    sync.dma_start(xs[:, sl, 0, :], rows[:, 0:512]).then_inc(s_xs[sl].h, 16)
                    s_xs[sl].n += 16
                    sync.dma_start(xs[:, sl, 1, :], rows[:, 512:1024]).then_inc(s_xs[sl].h, 16)
                    s_xs[sl].n += 16


    wnames = [wf_d, a1_d, a2_d, wq_d, wk_d, wv_d]
    WPL = {}

    @block.gpsimd
    def _(g):
        cnt = [0, 0]
        for widx, wd in enumerate(wnames):
            for qp in range(4):
                m = widx * 4 + qp
                s = m % 2
                if m >= 2:
                    pm = m - 2
                    pwidx, pqp = pm // 4, pm % 4
                    if pwidx < 3:
                        L_prev = FF[(pwidx, pqp * 2 + 1, nft - 1)]
                    elif pwidx == 3:
                        L_prev = PQ[(pqp * 2 + 1, nft - 1)]
                    elif pwidx == 4:
                        L_prev = PK[(pqp * 2 + 1, nft - 1)]
                    else:
                        L_prev = PV_[(15, 0 if pqp < 2 else 1)]
                    g.wait_ge(s_pe.h, L_prev)
                g.dma_start(wpc[:, s, :, :],
                            wd.rearrange("(c p) o -> p c o", p=128)[:, :, qp * 256:(qp + 1) * 256]
                            ).then_inc(s_wp[s].h, 16)
                cnt[s] += 16
                s_wp[s].n += 16
                WPL[(widx, qp)] = cnt[s]

    @block.vector
    def _(v):
        v.memset(onescol_bf[:], 1.0)
        v.memset(onesrow_f[:], 1.0)
        v.memset(onesrow_bf[:], 1.0)
        v.memset(epsc[:], 1e-5)
        v.wait_ge(s_is.h, 16)
        for qq in range(2):
            for c in range(NC):
                for r in range(2):
                    m = qq * 16 + c * 2 + r
                    sl = m % 2
                    ft = 2 * r + qq
                    v.wait_ge(s_xs[sl].h, 32 * (m // 2 + 1))
                    v.tensor_tensor(xs[:, sl, 1, :], xs[:, sl, 1, :], xs[:, sl, 0, :],
                                    ALU.subtract).then_inc(s_sel.h, 1)
                    s_sel.n += 1
                    v.wait_ge(s_sel.h, s_sel.n)
                    v.scalar_tensor_tensor(X[:, c, ft * 512:(ft + 1) * 512], xs[:, sl, 1, :],
                                           iscl[:, 0:1], xs[:, sl, 0, :], ALU.mult, ALU.add
                                           ).then_inc(s_sel.h, 1)
                    s_sel.n += 1
        v.wait_ge(s_sy.h, ld_done)
        v.wait_ge(s_sel.h, s_sel.n)
        v.tensor_copy(xn[:], X[:]).then_inc(s_dv.h, 1)
        s_dv.n += 1
    assert s_dv.n == L["cast"]

    @block.scalar
    def _(sc):
        sc.wait_ge(s_dv.h, L["cast"])
        sc.activation(qb[:], xn[:], AF.Square).then_inc(s_ac.h, 1)
        s_ac.n += 1

    @block.tensor
    def _(t):
        t.wait_ge(s_dv.h, L["cast"])
        for f in range(nft):
            for k in range(NC):
                mm = t.matmul(pb[f][0:1, :], onescol_bf[:], xn[:, k, f * FT:(f + 1) * FT],
                              start=(k == 0), stop=(k == NC - 1))
            mm.then_inc(s_pe.h, 1)
            s_pe.n += 1
        t.wait_ge(s_ac.h, L["sq"])
        for f in range(nft):
            for k in range(NC):
                mm = t.matmul(pb[4 + f][0:1, :], onescol_bf[:], qb[:, k, f * FT:(f + 1) * FT],
                              start=(k == 0), stop=(k == NC - 1))
            mm.then_inc(s_pe.h, 1)
            s_pe.n += 1

    @block.vector
    def _(v):
        v.wait_ge(s_pe.h, L["sum"])
        for f in range(nft):
            cp = v.tensor_copy(arow[0:1, f * FT:(f + 1) * FT], pb[f][0:1, :])
        cp.then_inc(s_dv.h, 1)
        s_dv.n += 1
        assert s_dv.n == L["srowcp"]
        v.wait_ge(s_pe.h, L["sumsq"])
        v.wait_ge(s_dv.h, s_dv.n)
        v.tensor_scalar(arow[:], arow[:], 1.0 / 1024.0, None, ALU.mult).then_inc(s_dv.h, 1)
        s_dv.n += 1
        v.wait_ge(s_dv.h, s_dv.n)
        v.tensor_tensor(brow[:], arow[:], arow[:], ALU.mult).then_inc(s_dv.h, 1)
        s_dv.n += 1
        assert s_dv.n == L["rowm"]
        sscat = [pb[4 + f][0:1, :] for f in range(nft)]
        for f in range(nft):
            v.wait_ge(s_dv.h, s_dv.n)
            ins = v.scalar_tensor_tensor(brow[0:1, f * FT:(f + 1) * FT], sscat[f],
                                         1.0 / 1024.0, brow[0:1, f * FT:(f + 1) * FT],
                                         ALU.mult, ALU.subtract)
        ins.then_inc(s_dv.h, 1)
        s_dv.n += 1
        assert s_dv.n == L["recipn"]

    @block.scalar
    def _(sc):
        sc.wait_ge(s_dv.h, L["recipn"])
        sc.activation(brow[:], brow[:], AF.Sqrt, bias=epsc[0:1, 0:1]).then_inc(s_ac.h, 1)
        s_ac.n += 1
    assert True

    @block.vector
    def _(v):
        v.wait_ge(s_ac.h, L["lnsqrt"])
        v.reciprocal(brow[:], brow[:]).then_inc(s_dv.h, 1)
        s_dv.n += 1
        assert s_dv.n == L["recip2"]
        v.wait_ge(s_dv.h, s_dv.n)
        v.tensor_tensor(arow[:], arow[:], brow[:], ALU.mult).then_inc(s_dv.h, 1)
        s_dv.n += 1
        assert s_dv.n == L["mrN"]

    @block.tensor
    def _(t):
        t.wait_ge(s_dv.h, L["mrN"])
        for f in range(nft):
            t.matmul(pb[f][:], onesrow_f[:], brow[0:1, f * FT:(f + 1) * FT],
                     start=True, stop=True).then_inc(s_pe.h, 1)
            s_pe.n += 1
        for f in range(nft):
            t.matmul(pb[4 + f][:], onesrow_f[:], arow[0:1, f * FT:(f + 1) * FT],
                     start=True, stop=True).then_inc(s_pe.h, 1)
            s_pe.n += 1
        t.wait_ge(s_dv.h, LNA[NC - 1])
        for hvv in range(2):
            mm = t.matmul(pb[hvv][:], onesrow_f[:], bvrow[0:1, hvv * 512:(hvv + 1) * 512],
                          start=True, stop=True)
        mm.then_inc(s_pe.h, 1)
        s_pe.n += 1
        assert s_pe.n == L["bvb"]
    assert True

    @block.vector
    def _(v):
        v.wait_ge(s_pe.h, L["lnbc"])
        for kk in range(NC):
            v.wait_ge(s_dv.h, s_dv.n)
            v.tensor_tensor(xn[:, kk, :], X[:, kk, :], pA[:], ALU.mult).then_inc(s_dv.h, 1)
            s_dv.n += 1
            v.wait_ge(s_dv.h, s_dv.n)
            v.tensor_tensor(xn[:, kk, :], xn[:, kk, :], pB[:], ALU.subtract).then_inc(s_dv.h, 1)
            s_dv.n += 1
            assert s_dv.n == LNA[kk]
        v.wait_ge(s_pe.h, L["bvb"])
        v.tensor_copy(bvb[:, 0:512], pb[0][:])
        v.tensor_copy(bvb[:, 512:1024], pb[1][:]).then_inc(s_dv.h, 1)
        s_dv.n += 1
        assert s_dv.n == L["bve"]

    @block.tensor
    def _(t):
        for st in range(3):
            rhs_buf = [xn, h1, h2][st]
            for i in range(NC):
                qp = i // 2
                widx = st
                t.wait_ge(s_wp[(widx * 4 + qp) % 2].h, WPL[(widx, qp)])
                for f in range(nft):
                    j = (st * NC + i) * nft + f
                    jj = j % 8
                    if st == 0 and j < 8:
                        t.wait_ge(s_dv.h, L["bve"])
                    if st >= 1 and i == 0 and f == 0:
                        t.wait_ge(s_dv.h, FFE[(st - 1, NC - 1, nft - 1)])
                    prev = j - 8
                    if prev >= 0:
                        pst, pi = divmod(prev // nft, NC)
                        pf = prev % nft
                        if pst < 2:
                            t.wait_ge(s_dv.h, FFE[(pst, pi, pf)])
                        else:
                            t.wait_ge(s_ac.h, FTN[(pi, pf)])
                    for k in range(NC):
                        mm = t.matmul(pb[jj][:], wpc[:, (st * 4 + qp) % 2, k, (i % 2) * 128:(i % 2 + 1) * 128],
                                      rhs_buf[:, k, f * FT:(f + 1) * FT],
                                      start=(k == 0), stop=(k == NC - 1))
                    mm.then_inc(s_pe.h, 1)
                    s_pe.n += 1
                    assert s_pe.n == FF[(st, i, f)]

    @block.vector
    def _(v):
        for st in range(2):
            dst = [h1, h2][st]
            for i in range(NC):
                for f in range(nft):
                    j = (st * NC + i) * nft + f
                    v.wait_ge(s_pe.h, FF[(st, i, f)])
                    v.wait_ge(s_dv.h, s_dv.n)
                    v.tensor_copy(dst[:, i, f * FT:(f + 1) * FT], pb[j % 8][:]).then_inc(s_dv.h, 1)
                    s_dv.n += 1
                    assert s_dv.n == FFE[(st, i, f)]

    @block.scalar
    def _(sc):
        for i in range(NC):
            for f in range(nft):
                j = (2 * NC + i) * nft + f
                sc.wait_ge(s_pe.h, FF[(2, i, f)])
                jf = i * nft + f
                if jf >= 2:
                    pj = jf - 2
                    sc.wait_ge(s_dv.h, FAD[(pj // nft, pj % nft)])
                sc.activation(ttr[:, jf % 2, :], pb[j % 8][:], AF.Tanh, bias=fbc[:, i:i + 1]
                              ).then_inc(s_ac.h, 1)
                s_ac.n += 1
                assert s_ac.n == FTN[(i, f)]

    @block.vector
    def _(v):
        for i in range(NC):
            for f in range(nft):
                v.wait_ge(s_ac.h, FTN[(i, f)])
                v.wait_ge(s_dv.h, s_dv.n)
                jf = i * nft + f
                v.tensor_tensor(X[:, i, f * FT:(f + 1) * FT], X[:, i, f * FT:(f + 1) * FT],
                                ttr[:, jf % 2, :], ALU.add).then_inc(s_dv.h, 1)
                s_dv.n += 1
                assert s_dv.n == FAD[(i, f)]

    @block.tensor
    def _(t):
        for pi, (PX, widx) in enumerate([(PQ, 3), (PK, 4)]):
            for i in range(NC):
                qp = i // 2
                t.wait_ge(s_wp[(widx * 4 + qp) % 2].h, WPL[(widx, qp)])
                for f in range(nft):
                    j = i * nft + f
                    jj = j % 8
                    if pi == 0 and j < 8:
                        t.wait_ge(s_ac.h, FTN[(NC - 1, nft - 1)])
                        t.wait_ge(s_dv.h, FAD[(NC - 1, nft - 1)])
                    elif pi == 0:
                        t.wait_ge(s_ac.h, PQE[((j - 8) // nft, (j - 8) % nft)])
                    elif pi == 1 and j < 8:
                        t.wait_ge(s_ac.h, PQE[(NC - 1, nft - 1)])
                    else:
                        t.wait_ge(s_ac.h, PKE[((j - 8) // nft, (j - 8) % nft)])
                    for k in range(NC):
                        mm = t.matmul(pb[jj][:], wpc[:, (widx * 4 + qp) % 2, k, (i % 2) * 128:(i % 2 + 1) * 128],
                                      xn[:, k, f * FT:(f + 1) * FT], start=(k == 0), stop=(k == NC - 1))
                    mm.then_inc(s_pe.h, 1)
                    s_pe.n += 1
                    assert s_pe.n == (PQ if pi == 0 else PK)[(i, f)]
        widx = 5
        for hv in range(2):
            for tc in range(16):
                qpa, qpb_ = 2 * hv, 2 * hv + 1
                t.wait_ge(s_wp[(widx * 4 + qpa) % 2].h, WPL[(widx, qpa)])
                t.wait_ge(s_wp[(widx * 4 + qpb_) % 2].h, WPL[(widx, qpb_)])
                m = hv * 16 + tc
                bank = pb[m % 8]
                if m < 8:
                    t.wait_ge(s_ac.h, PKE[(NC - 1, nft - 1)])
                else:
                    pm = m - 8
                    t.wait_ge(s_dv.h, PVE[(pm % 16, pm // 16)])
                for k in range(NC):
                    t.matmul(bank[:, 0:256], xn[:, k, tc * 128:(tc + 1) * 128],
                             wpc[:, (widx * 4 + qpa) % 2, k, :],
                             start=(k == 0), stop=(k == NC - 1))
                for k in range(NC):
                    mm = t.matmul(bank[:, 256:512], xn[:, k, tc * 128:(tc + 1) * 128],
                                  wpc[:, (widx * 4 + qpb_) % 2, k, :],
                                  start=(k == 0), stop=(k == NC - 1))
                mm.then_inc(s_pe.h, 1)
                s_pe.n += 1
                assert s_pe.n == PV_[(tc, hv)]

    @block.scalar
    def _(sc):
        for pi, (PX, PXE, dst, bc) in enumerate([(PQ, PQE, qb, bqc), (PK, PKE, kown, bkc)]):
            for i in range(NC):
                for f in range(nft):
                    j = i * nft + f
                    sc.wait_ge(s_pe.h, PX[(i, f)])
                    sc.add(dst[:, i, f * FT:(f + 1) * FT], pb[j % 8][:],
                           bc[:, i:i + 1]).then_inc(s_ac.h, 1)
                    s_ac.n += 1
                    assert s_ac.n == PXE[(i, f)]

    @block.vector
    def _(v):
        v.wait_ge(s_pe.h, FF[(2, NC - 1, nft - 1)])
        v.memset(vt65[:], 1.0).then_inc(s_dv.h, 1)
        s_dv.n += 1
        assert s_dv.n == L["vones"]
        for hv in range(2):
            for tc in range(16):
                m = hv * 16 + tc
                v.wait_ge(s_pe.h, PV_[(tc, hv)])
                v.wait_ge(s_dv.h, s_dv.n)
                v.tensor_tensor(
                    vt65[:, tc, hv * 8 * 65:(hv * 8 + 8) * 65].rearrange("p (h d) -> p h d", h=8)[:, :, 0:64],
                    pb[m % 8][:].rearrange("p (h d) -> p h d", h=8),
                    bvb[:, hv * 512:(hv + 1) * 512].rearrange("p (h d) -> p h d", h=8),
                    ALU.add).then_inc(s_dv.h, 1)
                s_dv.n += 1
                assert s_dv.n == PVE[(tc, hv)]

    @block.scalar
    def _(sc):
        sc.wait_ge(s_ac.h, PQE[(NC - 1, nft - 1)])
        sc.wait_ge(s_dv.h, PVE[(15, 1)])
        for c in range(NC):
            if c >= 1:
                sc.wait_ge(s_pe.h, QSQP[(c - 1, 1)])
            sc.activation(sqscr, qb[:, c, :], AF.Square).then_inc(s_ac.h, 1)
            s_ac.n += 1
        assert s_ac.n == L["qsqa"]
        for c in range(NC):
            sc.wait_ge(s_pe.h, QSQP[(NC - 1, 1)] if c == 0 else KSQP[(c - 1, 1)])
            sc.activation(sqscr, kown[:, c, :], AF.Square).then_inc(s_ac.h, 1)
            s_ac.n += 1
        assert s_ac.n == L["ksqa"]
        sc.wait_ge(s_dv.h, L["mmul"])
        sc.activation(mrow[0:1, 0:16], mrow[0:1, 0:16], AF.Sqrt).then_inc(s_ac.h, 1)
        s_ac.n += 1
        sc.wait_ge(s_ac.h, s_ac.n)
        sc.mul(mrow[0:1, 0:16], mrow[0:1, 0:16], -0.125).then_inc(s_ac.h, 1)
        s_ac.n += 1
        assert s_ac.n == L["msq2"]

    @block.scalar
    def _(sc):
        for gi, g in enumerate(groups):
            h, it = g
            for si in range(nki(it)):
                sc.wait_ge(s_pe.h, L["S"][g][si])
                if gi >= 1:
                    sc.wait_ge(s_pe.h, L["PV"][groups[gi - 1]])
                sc.activation(Pbuf[:, si, :], pb[si % 6][:], AF.Exp, scale=0.125,
                              bias=mcol[:, h:h + 1]).then_inc(s_ac.h, 1)
                s_ac.n += 1
            assert s_ac.n == L["ev"][g][-1]

    @block.tensor
    def _(t):
        t.wait_ge(s_dv.h, PVE[(15, 1)])
        for c in range(NC):
            t.wait_ge(s_ac.h, QSQA[c])
            for par in range(2):
                hb = par * 64
                buf = pA if par == 0 else pB
                if c * 2 + par >= 2:
                    ph = c * 2 + par - 2
                    t.wait_ge(s_dv.h, QMX[ph])
                for f in range(nft):
                    mm = t.matmul(buf[0:1, f * FT:(f + 1) * FT], onescol_bf[hb:hb + 64, :],
                                  sqscr[hb:hb + 64, f * FT:(f + 1) * FT],
                                  start=True, stop=True)
                mm.then_inc(s_pe.h, 1)
                s_pe.n += 1
                assert s_pe.n == QSQP[(c, par)]
        for c in range(NC):
            t.wait_ge(s_ac.h, KSQA[c])
            for par in range(2):
                hb = par * 64
                buf = pA if par == 0 else pB
                if c * 2 + par >= 2:
                    t.wait_ge(s_dv.h, KMX[c * 2 + par - 2])
                else:
                    t.wait_ge(s_dv.h, QMX[NHD - 1 - (1 - par)])
                for f in range(nft):
                    mm = t.matmul(buf[0:1, f * FT:(f + 1) * FT], onescol_bf[hb:hb + 64, :],
                                  sqscr[hb:hb + 64, f * FT:(f + 1) * FT],
                                  start=True, stop=True)
                mm.then_inc(s_pe.h, 1)
                s_pe.n += 1
                assert s_pe.n == KSQP[(c, par)]
        t.wait_ge(s_ac.h, L["msq2"])
        t.matmul(pb[2][:, 0:16], onesrow_f[:], mrow[0:1, 0:16], start=True, stop=True
                 ).then_inc(s_pe.h, 1)
        s_pe.n += 1
        assert s_pe.n == L["mb"]

    @block.vector
    def _(v):
        for h in range(NHD):
            buf = pA if h % 2 == 0 else pB
            v.wait_ge(s_pe.h, QSQP[(h // 2, h % 2)])
            v.reduce_max(mrow[0:1, 16 + h:17 + h], buf[0:1, :], axis=AX.X).then_inc(s_dv.h, 1)
            s_dv.n += 1
            assert s_dv.n == QMX[h]
        for h in range(NHD):
            buf = pA if h % 2 == 0 else pB
            v.wait_ge(s_pe.h, KSQP[(h // 2, h % 2)])
            v.reduce_max(mrow[0:1, 32 + h:33 + h], buf[0:1, :], axis=AX.X).then_inc(s_dv.h, 1)
            s_dv.n += 1
            assert s_dv.n == KMX[h]
        v.wait_ge(s_ag.h, 16 * 5)
        v.tensor_copy(mrow[0:1, 0:16], mrow[0:1, 0:16]).then_inc(s_dv.h, 1)
        s_dv.n += 1
        assert s_dv.n == L["kmax2"]
        v.wait_ge(s_dv.h, s_dv.n)
        v.tensor_tensor(mrow[0:1, 0:16], mrow[0:1, 16:32], mrow[0:1, 0:16], ALU.mult
                        ).then_inc(s_dv.h, 1)
        s_dv.n += 1
        assert s_dv.n == L["mmul"]
        v.wait_ge(s_pe.h, L["mb"])
        v.tensor_copy(mcol[:], pb[2][:, 0:16]).then_inc(s_dv.h, 1)
        s_dv.n += 1
        assert s_dv.n == L["mcev"]

    @block.gpsimd
    def _(g):
        g.wait_ge(s_ac.h, PKE[(NC - 1, nft - 1)])
        g.dma_start(k_in.ap().rearrange("(c p) t -> p c t", p=128), kown[:]).then_inc(s_ag.h, 16)
        s_ag.n += 16
        g.wait_ge(s_ag.h, s_ag.n)
        g.collective_compute("AllGather", ALU.bypass, replica_groups=RG,
                             ins=[k_in.ap().opt()], outs=[k_out.ap().opt()]).then_inc(s_cc.h, 1)
        s_cc.n += 1
        g.wait_ge(s_dv.h, L["kmax1"])
        g.dma_start(km_in.ap(), mrow[0:1, 32:48]).then_inc(s_ag.h, 16)
        s_ag.n += 16
        g.wait_ge(s_ag.h, s_ag.n)
        g.collective_compute("AllReduce", ALU.max, replica_groups=RG,
                             ins=[km_in.ap().opt()], outs=[km_out.ap().opt()]).then_inc(s_cc.h, 1)
        s_cc.n += 1
        g.wait_ge(s_dv.h, PVE[(15, 1)])
        g.dma_start(v_in.ap().rearrange("(c p) d -> p c d", p=128), vt65[:]).then_inc(s_ag.h, 16)
        s_ag.n += 16
        g.wait_ge(s_ag.h, s_ag.n)
        g.collective_compute("AllGather", ALU.bypass, replica_groups=RG,
                             ins=[v_in.ap().opt()], outs=[v_out.ap().opt()]).then_inc(s_cc.h, 1)
        s_cc.n += 1
        g.dma_start(mks[:], mk_d.rearrange("m p q -> p m q")).then_inc(s_gp.h, 16)
        s_gp.n += 16
        g.wait_ge(s_cc.h, 2)
        g.dma_start(mrow[0:1, 0:16], km_out.ap()).then_inc(s_ag.h, 16)
        s_ag.n += 16
        g.wait_ge(s_ag.h, s_ag.n)
        g.dma_start(km_in.ap(), km_out.ap()).then_inc(s_ag.h, 16)
        s_ag.n += 16
        g.wait_ge(s_ag.h, s_ag.n)

    kh_done = {}
    vh_done = {}

    @block.sync
    def _(sync):
        def kh(h):
            sl = h % 2
            for o in range(2):
                hb = (h % 2) * 64
                sync.dma_start(
                    khr[hb:hb + 64, sl, :].rearrange("p (a t) -> p a t", t=256)[:, o::2, :],
                    k_out.ap()[o * 1024 + h * HD1:o * 1024 + (h + 1) * HD1, :].rearrange(
                        "p (a t) -> p a t", t=256)).then_inc(s_kh[sl].h, 16)
                s_kh[sl].n += 16
            kh_done[h] = 16 * (h // 2 + 1) * 2
        def vh(h):
            sl = h % 2
            for o in range(2):
                for ci in range(2):
                    sync.dma_start(
                        vhr[:, sl, 2 * o + ci::4, :],
                        v_out.ap()[o * 2048 + ci * 128:(o + 1) * 2048, h * 65:(h + 1) * 65
                                   ].rearrange("(a p) d -> p a d", p=128)[:, 0::2, :]
                        ).then_inc(s_vh[sl].h, 16)
                    s_vh[sl].n += 16
            vh_done[h] = 16 * (h // 2 + 1) * 4
        sync.wait_ge(s_cc.h, 1)
        sync.wait_ge(s_ag.h, 48)
        kh(0)
        kh(1)
        sync.wait_ge(s_cc.h, 3)
        vh(0)
        vh(1)
        for h in range(2, NHD):
            sync.wait_ge(s_pe.h, L["rb"][(h - 2, 3)])
            kh(h)
            vh(h)

    @block.tensor
    def _(t):
        t.wait_ge(s_dv.h, L["mcev"])
        for gi, g in enumerate(groups):
            h, it = g
            sl = h % 2
            if it == 0:
                t.wait_ge(s_kh[sl].h, kh_done[h])
                t.wait_ge(s_vh[sl].h, vh_done[h])
            for si in range(nki(it)):
                bk = si % 6
                if si >= 6:
                    t.wait_ge(s_ac.h, L["ev"][g][si - 6])
                elif gi >= 1:
                    pg = groups[gi - 1]
                    pn = nki(pg[1])
                    t.wait_ge(s_ac.h, L["ev"][pg][pn - 6 + si] if pn >= 6 - si else 0)
                hb = (h % 2) * 64
                mm = t.matmul(pb[bk][:], khr[hb:hb + 64, sl, si * 128:(si + 1) * 128],
                              qb[hb:hb + 64, h // 2, it * FT:(it + 1) * FT],
                              start=True, stop=True)
                mm.then_inc(s_pe.h, 1)
                s_pe.n += 1
            assert s_pe.n == L["S"][g][-1]
            t.wait_ge(s_dv.h, L["msk"][g])
            if gi >= 1:
                t.wait_ge(s_dv.h, L["oev"][groups[gi - 1]])
            for si in range(nki(it)):
                mm = t.matmul(pb[6][0:65, :], vhr[:, sl, si, :], Pbuf[:, si, :],
                              start=(si == 0), stop=(si == nki(it) - 1))
            mm.then_inc(s_pe.h, 1)
            s_pe.n += 1
            assert s_pe.n == L["PV"][g]
            t.wait_ge(s_dv.h, L["brv"][g])
            t.matmul(pb[7][:], onesrow_bf[:], lrowb[0:1, :], start=True, stop=True
                     ).then_inc(s_pe.h, 1)
            s_pe.n += 1
            assert s_pe.n == L["rb"][g]

    @block.vector
    def _(v):
        v.wait_ge(s_gp.h, 16)
        for gi, g in enumerate(groups):
            h, it = g
            nm = 0
            for si in range(nki(it)):
                r0 = si - 8 * it
                if 0 <= r0 < 8:
                    v.wait_ge(s_ac.h, L["ev"][g][si])
                    v.wait_ge(s_dv.h, s_dv.n)
                    v.tensor_tensor(Pbuf[:, si, :], Pbuf[:, si, :], mks[:, r0, :], ALU.mult
                                    ).then_inc(s_dv.h, 1)
                    s_dv.n += 1
                    nm += 1
            assert s_dv.n == L["msk"][g]
            v.wait_ge(s_pe.h, L["PV"][g])
            v.reciprocal(lrow[0:1, :], pb[6][64:65, :]).then_inc(s_dv.h, 1)
            s_dv.n += 1
            assert s_dv.n == L["rc"][g]
            v.wait_ge(s_dv.h, s_dv.n)
            v.tensor_copy(lrowb[0:1, :], lrow[0:1, :]).then_inc(s_dv.h, 1)
            s_dv.n += 1
            assert s_dv.n == L["brv"][g]
            v.wait_ge(s_pe.h, L["rb"][g])
            v.tensor_copy(brec[:], pb[7][:]).then_inc(s_dv.h, 1)
            s_dv.n += 1
            assert s_dv.n == L["bev"][g]
            v.wait_ge(s_dv.h, s_dv.n)
            v.tensor_tensor(O[(h % 2) * 64:(h % 2) * 64 + 64, h // 2, it * FT:(it + 1) * FT],
                            pb[6][0:64, :], brec[0:64, :], ALU.mult).then_inc(s_dv.h, 1)
            s_dv.n += 1
            assert s_dv.n == L["oev"][g]
        for i in range(NC):
            for f in range(nft):
                v.wait_ge(s_dv.h, s_dv.n)
                v.tensor_tensor(X[:, i, f * FT:(f + 1) * FT], X[:, i, f * FT:(f + 1) * FT],
                                O[:, i, f * FT:(f + 1) * FT], ALU.add).then_inc(s_dv.h, 1)
                s_dv.n += 1
                assert s_dv.n == OAD[(i, f)]
        for tc in range(16):
            v.wait_ge(s_pe.h, TRO[tc])
            v.tensor_copy(otr[:, tc % 2, :], (pA if tc % 2 == 0 else pB)[:, 0:1024]
                          ).then_inc(s_dv.h, 1)
            s_dv.n += 1
            assert s_dv.n == TRE[tc]

    @block.tensor
    def _(t):
        t.wait_ge(s_sy.h, ld_done + 16)
        for tc in range(16):
            if tc >= 2:
                t.wait_ge(s_ot[tc % 2].h, 16 * (tc // 2))
            if tc < 2:
                t.wait_ge(s_dv.h, OAD[(NC - 1, nft - 1)])
            for k in range(NC):
                mm = t.transpose((pA if tc % 2 == 0 else pB)[:, k * 128:(k + 1) * 128],
                                 X[:, k, tc * 128:(tc + 1) * 128], identf[:])
            mm.then_inc(s_pe.h, 1)
            s_pe.n += 1
            assert s_pe.n == TRO[tc]

    @block.sync
    def _(sync):
        sync.wait_ge(s_dv.h, OAD[(NC - 1, nft - 1)])
        sync.dma_start(identf[:], id_d).then_inc(s_sy.h, 16)
        s_sy.n += 16
        for tc in range(16):
            sync.wait_ge(s_dv.h, TRE[tc])
            sync.dma_start(out_o[tc * 128:(tc + 1) * 128, :], otr[:, tc % 2, :]
                           ).then_inc(s_ot[tc % 2].h, 16)
            s_ot[tc % 2].n += 16
        for sl in range(2):
            sync.wait_ge(s_ot[sl].h, s_ot[sl].n)







def emit_reshard(b, D, T=T_CORE):
    nc = b.nc
    x1 = D["x1"]
    dst = D["l1x"]
    iscol_d = D["iscol"]
    rs_in = nc.dram_tensor(b.pfx + "rs_in", [1024, T], F32)
    rs_out = nc.dram_tensor(b.pfx + "rs_out", [2048, T], F32)

    xs = b.sb("xs", [128, 2, 1024])
    xsel = b.sb("xsel", [128, 2, 1024])
    iscl = b.sb("iscl", [128, 1])
    s_sy = b.sem("s_sy")
    s_ag = b.sem("s_ag")
    s_cc = b.sem("s_cc")
    s_dv = b.sem("s_dv")
    s_st2 = [b.sem(f"s_st{i}") for i in range(2)]
    s_vv = b.sem("s_vv")
    block = b.block()

    @block.gpsimd
    def _(g):
        for s in range(2):
            g.dma_start(rs_in.ap().rearrange("d (a u) -> d a u", u=256)[:, s::2, :].rearrange("d a u -> d (a u)")
                        if False else
                        rs_in.ap()[:, s * 1024:(s + 1) * 1024].rearrange("d (a u) -> d a u", u=256),
                        x1.rearrange("d (a u) -> d a u", u=256)[:, s::2, :]).then_inc(s_ag.h, 16)
            s_ag.n += 16
            g.wait_ge(s_ag.h, s_ag.n)
        g.collective_compute("AllGather", ALU.bypass, replica_groups=RG,
                             ins=[rs_in.ap().opt()], outs=[rs_out.ap().opt()]).then_inc(s_cc.h, 1)
        s_cc.n += 1
        g.wait_ge(s_cc.h, s_cc.n)

    @block.sync
    def _(sync):
        sync.dma_start(iscl[:], iscol_d).then_inc(s_sy.h, 16)
        s_sy.n += 16
        n = 0
        for c in range(8):
            for r in range(2):
                m = c * 2 + r
                if m >= 1:
                    sync.wait_ge(s_dv.h, m)
                if m >= 2:
                    sync.wait_ge(s_st2[m % 2].h, 16 * ((m - 2) // 2 + 1))
                sync.wait_ge(s_cc.h, 1)
                sync.dma_start(xs[:, 0, :], rs_out.ap()[r * 1024 + c * 128:r * 1024 + (c + 1) * 128, 0:1024]
                               ).then_inc(s_sy.h, 16)
                s_sy.n += 16
                sync.dma_start(xs[:, 1, :], rs_out.ap()[r * 1024 + c * 128:r * 1024 + (c + 1) * 128, 1024:2048]
                               ).then_inc(s_sy.h, 16)
                s_sy.n += 16

    @block.vector
    def _(v):
        for c in range(8):
            for r in range(2):
                m = c * 2 + r
                v.wait_ge(s_sy.h, 16 + 32 * (m + 1))
                if m >= 2:
                    v.wait_ge(s_st2[m % 2].h, 16 * ((m - 2) // 2 + 1))
                v.tensor_tensor(xsel[:, m % 2, :], xs[:, 1, :], xs[:, 0, :], ALU.subtract).then_inc(s_vv.h, 1)
                s_vv.n += 1
                v.wait_ge(s_vv.h, s_vv.n)
                v.scalar_tensor_tensor(xsel[:, m % 2, :],
                                       xsel[:, m % 2, :], iscl[:, 0:1], xs[:, 0, :],
                                       ALU.mult, ALU.add).then_inc(s_dv.h, 1)
                s_dv.n += 1

    @block.scalar
    def _(sc):
        for c in range(8):
            for r in range(2):
                m = c * 2 + r
                sc.wait_ge(s_dv.h, m + 1)
                sc.dma_start(dst.rearrange("(cc p) t -> p cc t", p=128)[:, c, r * 1024:(r + 1) * 1024],
                             xsel[:, m % 2, :]).then_inc(s_st2[m % 2].h, 16)
                s_st2[m % 2].n += 16
        for sl in range(2):
            sc.wait_ge(s_st2[sl].h, s_st2[sl].n)




def build_fused():
    b = Builder("mega")
    nc = b.nc
    F = F32
    T = T_CORE
    EI = lambda n, s, dt=F32: nc.dram_tensor(n, s, dt, kind="ExternalInput").ap()
    INT = lambda n, s, dt=F32: nc.dram_tensor(n, s, dt).ap()

    io = {}
    io["x_in"] = EI("x_in", [T, 1024])
    io["identf"] = EI("identf", [128, 128])
    io["iscol"] = EI("iscol", [128, 1])
    for n, s in [("winT", [1024, 3072]), ("wgT", [1536, 3072]), ("woutT", [1536, 1024]),
                 ("cwc", [128, 12, 4]), ("cbc", [128, 12]), ("bgc", [128, 24]),
                 ("fbc_h", [128, 12]), ("gnc_h", [128, 8])]:
        io[n] = EI(n, s)
    for n, s in [("wgt1", [1024, 4096]), ("wst1", [2048, 1024]), ("gnc_m1", [128, 8]),
                 ("wgt2", [1024, 4096]), ("wst2", [2048, 1024]), ("gnc_m2", [128, 8])]:
        io[n] = EI(n, s)
    for n, s in [("wqT", [1024, 1024]), ("wkT", [1024, 128]), ("wvT", [1024, 128]),
                 ("woT", [1024, 1024]), ("gnc_sq", [128, 8]), ("cosT", [128, T]),
                 ("sinT", [128, T]), ("rotP", [128, 128]), ("mL", [4, 128, 512]),
                 ("mR", [4, 128, 512])]:
        io[n] = EI(n, s)
    for n, s in [("gwT", [1024, 8]), ("w1", [8, 1024, 4096]), ("b1c", [128, 8, 32]),
                 ("w2", [8, 4096, 1024]), ("b2c", [128, 8, 8]), ("sel", [8, 1024])]:
        io[n] = EI(n, s)
    for n, s in [("wq", [1024, 1024]), ("wk", [1024, 1024]), ("wv", [1024, 1024]),
                 ("bqc", [128, 8]), ("bkc", [128, 8]), ("wf", [1024, 1024]),
                 ("a1", [1024, 1024]), ("a2", [1024, 1024]), ("fbc_l", [128, 8]),
                 ("mks", [8, 128, 512]), ("bvrow", [1, 1024])]:
        io[n] = EI(n, s)
    out_o = nc.dram_tensor("out", [T, 1024], F32, kind="ExternalOutput").ap()

    c_xT = INT("c_xT", [1024, T])
    c_xa = INT("c_xa", [1024, T])
    c_xnb = INT("c_xnb", [1024, T], BF16)

    b.begin_phase("p0_")
    emit_prep(b, {"x_in": io["x_in"], "identf": io["identf"],
                  "xT": c_xT, "xa0": c_xa, "xnb": c_xnb})
    b.end_phase()

    b.begin_phase("p1_")
    emit_hawk(b, {"xa": c_xa, "winT": io["winT"], "wgT": io["wgT"], "woutT": io["woutT"],
                  "cwc": io["cwc"], "cbc": io["cbc"], "bgc": io["bgc"], "fbc": io["fbc_h"],
                  "gnc": io["gnc_h"], "iscol": io["iscol"], "xa_out": c_xa})
    b.end_phase()

    b.begin_phase("p2_")
    emit_mlp(b, {"xa": c_xa, "wgt": io["wgt1"], "wst": io["wst1"], "gnc": io["gnc_m1"],
                 "xa_out": c_xa})
    b.end_phase()

    b.begin_phase("p3_")
    emit_smqa(b, {"xa": c_xa, "wqT": io["wqT"], "wkT": io["wkT"], "wvT": io["wvT"],
                  "woT": io["woT"], "gnc": io["gnc_sq"], "cosT": io["cosT"],
                  "sinT": io["sinT"], "rotP": io["rotP"], "mL": io["mL"], "mR": io["mR"],
                  "iscol": io["iscol"], "xa_out": c_xa})
    b.end_phase()

    b.begin_phase("p4_")
    emit_mlp(b, {"xa": c_xa, "wgt": io["wgt2"], "wst": io["wst2"], "gnc": io["gnc_m2"],
                 "xa_out": c_xa})
    b.end_phase()

    b.begin_phase("p5_")
    emit_moe(b, {"xT": c_xT, "xnb": c_xnb, "xa": c_xa, "gwT": io["gwT"], "w1": io["w1"],
                 "b1c": io["b1c"], "w2": io["w2"], "b2c": io["b2c"], "sel": io["sel"],
                 "identf": io["identf"]})
    b.end_phase()

    b.begin_phase("p7_")
    emit_l1(b, {"iscol": io["iscol"], "wq": io["wq"], "wk": io["wk"], "wv": io["wv"],
                "bqc": io["bqc"], "bkc": io["bkc"], "wf": io["wf"], "a1": io["a1"],
                "a2": io["a2"], "fbc": io["fbc_l"], "mks": io["mks"],
                "identf": io["identf"], "bvrow": io["bvrow"], "out": out_o})
    b.end_phase()

    b.done()
    return b.nc




_PROGS = {}


def _get(name, builder):
    if name not in _PROGS:
        _PROGS[name] = builder()
    return _PROGS[name]


def kernel(**inp):
    f32 = np.float32
    x = np.asarray(inp["L0_attn_x"], f32)
    B, T2, D = x.shape
    cores = list(range(8))
    identf = np.eye(128, dtype=f32)

    winT, wgT, woutT, cwc, cbc, bgc, fbc = host_hawk_weights(
        np.asarray(inp["L0_attn_hawk_Win"], f32), np.asarray(inp["L0_attn_hawk_Wg"], f32),
        np.asarray(inp["L0_attn_hawk_Wout"], f32), np.asarray(inp["L0_attn_hawk_cw"], f32),
        np.asarray(inp["L0_attn_hawk_cb"], f32), np.asarray(inp["L0_attn_hawk_bg"], f32),
        np.asarray(inp["L0_attn_hawk_fb"], f32))
    gnc_hawk = host_col(np.asarray(inp["L0_attn_gn_hawk"], f32))
    wgt1, wst1 = host_mlp_weights(np.asarray(inp["L0_attn_g1_Wgrow"], f32),
                                  np.asarray(inp["L0_attn_g1_Wshrink"], f32))
    gnc_m1 = host_col(np.asarray(inp["L0_attn_gn_hgmlp"], f32))
    wqT, wkT, wvT, woT = host_smqa_weights(np.asarray(inp["L0_attn_smqa_Wq"], f32),
                                           np.asarray(inp["L0_attn_smqa_Wkv"], f32),
                                           np.asarray(inp["L0_attn_smqa_Wout"], f32))
    mL, mR = host_masks()
    rotP = host_rot_mat()
    gnc_sq = host_col(np.asarray(inp["L0_attn_gn_smqa"], f32))
    cos0, sin0 = host_rotary(0)
    cos1, sin1 = host_rotary(2048)
    wgt2, wst2 = host_mlp_weights(np.asarray(inp["L0_attn_g2_Wgrow"], f32),
                                  np.asarray(inp["L0_attn_g2_Wshrink"], f32))
    gnc_m2 = host_col(np.asarray(inp["L0_attn_gn_sgmlp"], f32))
    gwT, b1c, b2c, sel = host_moe_weights(np.asarray(inp["L0_ffn_gate_w"], f32),
                                          np.asarray(inp["L0_ffn_b1"], f32),
                                          np.asarray(inp["L0_ffn_b2"], f32))
    w1 = np.ascontiguousarray(np.asarray(inp["L0_ffn_w1"], f32))
    w2 = np.ascontiguousarray(np.asarray(inp["L0_ffn_w2"], f32))
    wq, bqc, wk, bkc, wv, _bvc, wf, a1, a2, fbcol = host_l1_weights(
        np.asarray(inp["L1_attn_Wq"], f32), np.asarray(inp["L1_attn_bq"], f32),
        np.asarray(inp["L1_attn_Wk"], f32), np.asarray(inp["L1_attn_bk"], f32),
        np.asarray(inp["L1_attn_Wv"], f32), np.asarray(inp["L1_attn_bv"], f32),
        np.asarray(inp["L1_ffn_weight"], f32), np.asarray(inp["L1_ffn_aux_weight1"], f32),
        np.asarray(inp["L1_ffn_aux_weight2"], f32), np.asarray(inp["L1_ffn_bias"], f32))
    bvrow = np.ascontiguousarray(np.asarray(inp["L1_attn_bv"], f32).reshape(1, 1024))
    mks = {j: host_l1_masks(j) for j in range(2)}

    nc = _get("mega", build_fused)
    ins = []
    for c in cores:
        b2, hh = c // 2, c % 2
        ins.append(dict(
            x_in=np.ascontiguousarray(x[b2, hh * 2048:(hh + 1) * 2048]),
            identf=identf, iscol=np.full((128, 1), float(hh), f32),
            winT=winT, wgT=wgT, woutT=woutT, cwc=cwc, cbc=cbc, bgc=bgc,
            fbc_h=fbc, gnc_h=gnc_hawk,
            wgt1=wgt1, wst1=wst1, gnc_m1=gnc_m1,
            wgt2=wgt2, wst2=wst2, gnc_m2=gnc_m2,
            wqT=wqT, wkT=wkT, wvT=wvT, woT=woT, gnc_sq=gnc_sq,
            cosT=cos0 if hh == 0 else cos1, sinT=sin0 if hh == 0 else sin1,
            rotP=rotP, mL=mL, mR=mR,
            gwT=gwT, w1=w1, b1c=b1c, w2=w2, b2c=b2c, sel=sel,
            wq=wq, wk=wk, wv=wv, bqc=bqc, bkc=bkc, wf=wf, a1=a1, a2=a2,
            fbc_l=fbcol, mks=mks[hh], bvrow=bvrow))
    res = run_bass_kernel_spmd(nc, ins, core_ids=cores).results

    toks = {j: host_interleave_tokens(j) for j in range(2)}
    out = np.empty((B, T2, D), f32)
    for c in cores:
        b2, j = c // 2, c % 2
        out[b2, toks[j]] = res[c]["out"]
    return out


# version 06
# speedup vs baseline: 1.0416x; 1.0347x over previous
import numpy as np
from contextlib import ExitStack
import concourse.bass as bass
import concourse.mybir as mybir

F32 = mybir.dt.float32
BF16 = mybir.dt.bfloat16
AF = mybir.ActivationFunctionType
ALU = mybir.AluOpType
AX = mybir.AxisListType

T_CORE = 2048
FT = 512
NFT = T_CORE // FT


class Sem:
    __slots__ = ("h", "n", "name")

    def __init__(self, es, nc, name):
        self.h = es.enter_context(nc.semaphore(name))
        self.n = 0
        self.name = name

    def inc(self, instr, k=1):
        instr.then_inc(self.h, k)
        self.n += k
        return self.n

    def wait(self, eng, n=None):
        eng.wait_ge(self.h, self.n if n is None else n)


class Builder:

    def __init__(self, name="prog"):
        self.nc = bass.Bass("TRN2", target_bir_lowering=False)
        self.es = ExitStack()
        self.es.__enter__()
        self._sbuf_bytes = 0
        self._max_sbuf = 0
        self.name = name
        self.pfx = ""
        self._phase_es = None

    def begin_phase(self, pfx):
        assert self._phase_es is None
        self.pfx = pfx
        self._phase_es = ExitStack()
        self._phase_es.__enter__()

    def end_phase(self):
        self._phase_es.__exit__(None, None, None)
        self._phase_es = None
        self.pfx = ""

    def _alloc_es(self):
        return self._phase_es if self._phase_es is not None else self.es

    def sem(self, name):
        return Sem(self.es, self.nc, self.pfx + name)

    def sb(self, name, shape, dt=F32):
        t = self._alloc_es().enter_context(self.nc.sbuf_tensor(self.pfx + name, shape, dt))
        return t

    def psum(self, name, shape=(128, 512), dt=F32):
        return self._alloc_es().enter_context(self.nc.psum_tensor(self.pfx + name, list(shape), dt))

    def dram(self, name, shape, dt=F32, kind=None):
        if kind is None:
            return self.nc.dram_tensor(name, list(shape), dt)
        return self.nc.dram_tensor(name, list(shape), dt, kind=kind)

    def block(self):
        return self._alloc_es().enter_context(self.nc.Block())

    def done(self):
        self.es.__exit__(None, None, None)
        return self.nc




from concourse.bass_utils import run_bass_kernel_spmd


def host_hawk_weights(Win, Wg, Wout, cw, cb, bg, fb):
    winT = np.ascontiguousarray(Win.T)
    winT_hg = np.ascontiguousarray(np.concatenate([winT[:, HID:], winT[:, :HID]], axis=1))
    wgT = np.ascontiguousarray(Wg.T)
    order = []
    for fg in range(NH):
        order += list(range(fg * 128, (fg + 1) * 128)) + \
                 list(range(HID + fg * 128, HID + (fg + 1) * 128))
    wgT_i = np.ascontiguousarray(wgT[:, order])
    woutT = np.ascontiguousarray(Wout.T)
    cwc = np.ascontiguousarray(cw.reshape(NH, 128, 4).transpose(1, 0, 2))
    cbc = np.ascontiguousarray(cb.reshape(NH, 128).T)
    bgc = np.ascontiguousarray(bg.reshape(2 * NH, 128).T)
    fbc = np.ascontiguousarray(fb.reshape(NH, 128).T)
    return winT_hg, wgT_i, woutT, cwc, cbc, bgc, fbc


def host_mlp_weights(wgrow, wshrink):
    GH = wgrow.shape[0] // 2
    wt = np.ascontiguousarray(wgrow.T)
    NG = GH // 128
    order = []
    for fg in range(NG):
        order += list(range(fg * 128, (fg + 1) * 128)) + \
                 list(range(GH + fg * 128, GH + (fg + 1) * 128))
    wt_i = np.ascontiguousarray(wt[:, order])
    ws_t = np.ascontiguousarray(wshrink.T)
    return wt_i, ws_t


def host_col(v):
    return np.ascontiguousarray(v.reshape(-1, 128).T)


def host_smqa_weights(Wq, Wkv, Wo):
    return (np.ascontiguousarray(Wq.T), np.ascontiguousarray(Wkv.T[:, 0:HD]),
            np.ascontiguousarray(Wkv.T[:, HD:2 * HD]), np.ascontiguousarray(Wo.T))


def host_rotary(tok_off, T=T_CORE):
    d = np.arange(0, HD, 2)
    inv = 1.0 / 10000.0 ** (d / HD)
    t = np.arange(tok_off, tok_off + T, dtype=np.float64)
    fr2 = np.repeat(t[:, None] * inv[None, :], 2, axis=1)
    return (np.ascontiguousarray(np.cos(fr2).T).astype(np.float32),
            np.ascontiguousarray(np.sin(fr2).T).astype(np.float32))


def host_rot_mat():
    P = np.zeros((HD, HD), np.float32)
    for i in range(0, HD, 2):
        P[i, i + 1] = -1.0
        P[i + 1, i] = 1.0
    return np.ascontiguousarray(P.T)


def host_masks():
    k = np.arange(128)[:, None]
    q = np.arange(512)[None, :]
    mL = np.stack([(128 * m + k >= q).astype(np.float32) for m in range(4)])
    mR = np.stack([(128 * m + k <= q).astype(np.float32) for m in range(4)])
    return mL, mR


def host_moe_weights(gate_w, b1, b2):
    gwT = np.ascontiguousarray(gate_w.T)
    b1c = np.ascontiguousarray(b1.reshape(E, NF, 128).transpose(2, 0, 1))
    b2c = np.ascontiguousarray(b2.reshape(E, 8, 128).transpose(2, 0, 1))
    sel = np.zeros((8, E * 128), np.float32)
    for e in range(E):
        sel[e, e * 128:(e + 1) * 128] = 1.0
    return gwT, b1c, b2c, np.ascontiguousarray(sel)


def host_l1_weights(Wq, bq, Wk, bk, Wv, bv, Wf, A1, A2, fb):
    col = lambda v: np.ascontiguousarray(v.reshape(8, 128).T)
    return (np.ascontiguousarray(Wq), col(bq), np.ascontiguousarray(Wk), col(bk),
            np.ascontiguousarray(Wv), col(bv), np.ascontiguousarray(Wf),
            np.ascontiguousarray(A1), np.ascontiguousarray(A2), col(fb))


def host_l1_masks(j):
    k = np.arange(128)[:, None]
    q = np.arange(512)[None, :]
    qg = 512 * (q // 256) + (q % 256)
    out = []
    for r0 in range(8):
        m = r0 - 2 * j
        out.append((128 * m + k <= qg).astype(np.float32))
    return np.stack(out)


def host_interleave_tokens(j, T2=4096):
    g = np.arange(T2)
    return g[(g // 256) % 2 == j]



NC = 8
HID = 1536
NH = HID // 128
RG = [[0, 1], [2, 3], [4, 5], [6, 7]]
QH = 8
HD = 128
W = 1024
ISQ = 1.0 / np.sqrt(128.0)
E = 8
F = 4096
NF = F // 128
TH = T_CORE // 2
BIG = 1e30
NHD = 16
HD1 = 64


def subtiles(qt):
    if qt == 0:
        return [(ki, ("L", ki)) for ki in range(4)] + \
               [(ki, None) for ki in range(4, 8)] + \
               [(ki, ("R", ki - 8)) for ki in range(8, 12)]
    return [(ki, ("L", ki - 4)) for ki in range(4, 8)] + \
           [(ki, None) for ki in range(8, 12)] + \
           [(ki, ("R", ki - 12)) for ki in range(12, 16)]


def emit_prep(b, D, T=T_CORE):
    nc = b.nc
    nft = T // FT
    ntc = T // 128

    x_in = D["x_in"]
    identf = D["identf"]
    xT_o = D["xT"]
    xa0_o = D["xa0"]
    xnb_o = D["xnb"]

    xT = b.sb("xTs", [128, NC, T])
    xb = b.sb("xbs", [128, NC, T], BF16)
    sq = b.sb("sqs", [128, NC, T], BF16)
    xtok = b.sb("xtok", [128, 4, 1024])
    idt = b.sb("idt", [128, 128])
    arow = b.sb("arow", [1, T])
    brow = b.sb("brow", [1, T])
    onescol_bf = b.sb("onescol_bf", [128, 1], BF16)
    onesrow_f = b.sb("onesrow_f", [1, 128])
    epsrow = b.sb("epsrow", [1, 1])
    xac = b.sb("xac", [128, 2, T])
    xnc = b.sb("xnc", [128, 2, T], BF16)

    pA = b.psum("pA", (128, 2048))
    pB = b.psum("pB", (128, 2048))

    s_ld4 = [b.sem(f"s_ld{i}") for i in range(4)]
    s_sy = b.sem("s_sy")
    s_st2 = [b.sem(f"s_st{i}") for i in range(2)]
    s_pe = b.sem("s_pe")
    s_ac = b.sem("s_ac")
    s_dv = b.sem("s_dv")

    L = {}
    pe = 0
    L["tp"] = {}
    for tc in range(ntc):
        pe += 1; L["tp"][(tc, 0)] = pe
        pe += 1; L["tp"][(tc, 1)] = pe
    L["sum_done"] = {}
    for tc in range(ntc):
        pe += 1; L["sum_done"][tc] = pe
        pe += 1
    L["stats_done"] = pe
    pe += 8; L["bcast_done"] = pe

    dv = 0
    L["evac"] = {}
    for tc in range(ntc):
        dv += 1; L["evac"][(tc, 0)] = dv
        dv += 1; L["evac"][(tc, 1)] = dv
        dv += 1; L["cast"] = dv
        L["cast", tc] = dv
    L["rows_done"] = dv + 1

    block = b.block()

    @block.sync
    def _(sync):
        sync.dma_start(idt[:], identf).then_inc(s_sy.h, 16)
        s_sy.n += 16
        for tc in range(ntc):
            s = tc % 4
            if tc >= 4:
                sync.wait_ge(s_pe.h, L["tp"][(tc - 4, 1)])
            sync.dma_start(xtok[:, s, :], x_in[tc * 128:(tc + 1) * 128, :]
                           ).then_inc(s_ld4[s].h, 16)
            s_ld4[s].n += 16

    @block.tensor
    def _(t):
        t.wait_ge(s_sy.h, 16)
        for tc in range(ntc):
            s = tc % 4
            t.wait_ge(s_ld4[s].h, 16 * (tc // 4 + 1))
            for bk in range(2):
                if tc >= 1 or bk >= 1:
                    pass
                prev = (tc * 2 + bk) - 2
                if prev >= 0:
                    t.wait_ge(s_dv.h, L["evac"][(prev // 2, prev % 2)])
                for k in range(4):
                    kk = bk * 4 + k
                    tp = t.transpose(pA[:, (bk * 4 + k - bk * 4) * 128:
                                        (bk * 4 + k - bk * 4 + 1) * 128] if False else
                                     (pA if bk == 0 else pB)[:, k * 128:(k + 1) * 128],
                                     xtok[:, s, kk * 128:(kk + 1) * 128], idt[:])
                tp.then_inc(s_pe.h, 1)
                s_pe.n += 1

    @block.vector
    def _(v):
        for tc in range(ntc):
            for bk in range(2):
                v.wait_ge(s_pe.h, L["tp"][(tc, bk)])
                if bk == 0 and tc >= 1:
                    v.wait_ge(s_dv.h, L["cast", tc - 1])
                v.tensor_copy(xT[:, bk * 4:(bk + 1) * 4, tc * 128:(tc + 1) * 128],
                              (pA if bk == 0 else pB)[:, 0:512].rearrange("p (c t) -> p c t", c=4)
                              ).then_inc(s_dv.h, 1)
                s_dv.n += 1
            v.wait_ge(s_dv.h, s_dv.n)
            v.tensor_copy(xb[:, :, tc * 128:(tc + 1) * 128], xT[:, :, tc * 128:(tc + 1) * 128]
                          ).then_inc(s_dv.h, 1)
            s_dv.n += 1

    @block.scalar
    def _(sc):
        for tc in range(ntc):
            sc.wait_ge(s_dv.h, L["cast", tc])
            sc.activation(sq[:, :, tc * 128:(tc + 1) * 128], xb[:, :, tc * 128:(tc + 1) * 128],
                          AF.Square).then_inc(s_ac.h, 1)
            s_ac.n += 1

    @block.vector
    def _(v):
        v.memset(onescol_bf[:], 1.0)
        v.memset(epsrow[:], 1e-5)
        v.memset(onesrow_f[:], 1.0).then_inc(s_dv.h, 1)
        s_dv.n += 1
    ones_done = s_dv.n

    @block.tensor
    def _(t):
        t.wait_ge(s_dv.h, ones_done)
        t.wait_ge(s_dv.h, L["cast", ntc - 1])
        for tc in range(ntc):
            t.wait_ge(s_ac.h, tc + 1)
            for k in range(NC):
                mm = t.matmul(pA[0:1, tc * 128:(tc + 1) * 128], onescol_bf[:],
                              xb[:, k, tc * 128:(tc + 1) * 128], start=(k == 0), stop=(k == NC - 1))
            mm.then_inc(s_pe.h, 1)
            s_pe.n += 1
            for k in range(NC):
                mm = t.matmul(pB[0:1, tc * 128:(tc + 1) * 128], onescol_bf[:],
                              sq[:, k, tc * 128:(tc + 1) * 128], start=(k == 0), stop=(k == NC - 1))
            mm.then_inc(s_pe.h, 1)
            s_pe.n += 1
    assert s_pe.n == L["stats_done"]

    @block.vector
    def _(v):
        v.wait_ge(s_pe.h, L["stats_done"])
        v.tensor_scalar(arow[:], pA[0:1, :], 1.0 / 1024.0, None, ALU.mult).then_inc(s_dv.h, 1)
        s_dv.n += 1
        v.wait_ge(s_dv.h, s_dv.n)
        v.tensor_tensor(brow[:], arow[:], arow[:], ALU.mult).then_inc(s_dv.h, 1)
        s_dv.n += 1
        v.wait_ge(s_dv.h, s_dv.n)
        v.scalar_tensor_tensor(brow[:], pB[0:1, :], 1.0 / 1024.0, brow[:],
                               ALU.mult, ALU.subtract).then_inc(s_dv.h, 1)
        s_dv.n += 1
    var_done = s_dv.n

    @block.scalar
    def _(sc):
        sc.wait_ge(s_dv.h, var_done)
        sc.activation(brow[:], brow[:], AF.Sqrt, bias=epsrow[0:1, 0:1]).then_inc(s_ac.h, 1)
        s_ac.n += 1
    sqrt_done = s_ac.n

    @block.vector
    def _(v):
        v.wait_ge(s_ac.h, sqrt_done)
        v.reciprocal(brow[:], brow[:]).then_inc(s_dv.h, 1)
        s_dv.n += 1
        v.wait_ge(s_dv.h, s_dv.n)
        v.tensor_tensor(arow[:], arow[:], brow[:], ALU.mult).then_inc(s_dv.h, 1)
        s_dv.n += 1
    rows_done = s_dv.n

    @block.tensor
    def _(t):
        t.wait_ge(s_dv.h, rows_done)
        for f in range(nft):
            t.matmul(pA[:, f * FT:(f + 1) * FT], onesrow_f[:], brow[0:1, f * FT:(f + 1) * FT],
                     start=True, stop=True).then_inc(s_pe.h, 1)
            s_pe.n += 1
        for f in range(nft):
            t.matmul(pB[:, f * FT:(f + 1) * FT], onesrow_f[:], arow[0:1, f * FT:(f + 1) * FT],
                     start=True, stop=True).then_inc(s_pe.h, 1)
            s_pe.n += 1
    assert s_pe.n == L["bcast_done"]

    @block.vector
    def _(v):
        v.wait_ge(s_pe.h, L["bcast_done"])
        for k in range(NC):
            sl = k % 2
            if k >= 2:
                v.wait_ge(s_st2[sl].h, 32 * (k // 2))
            v.tensor_tensor(xac[:, sl, :], xT[:, k, :], pA[:], ALU.mult).then_inc(s_dv.h, 1)
            s_dv.n += 1
            v.wait_ge(s_dv.h, s_dv.n)
            v.tensor_tensor(xac[:, sl, :], xac[:, sl, :], pB[:], ALU.subtract).then_inc(s_dv.h, 1)
            s_dv.n += 1
            v.wait_ge(s_dv.h, s_dv.n)
            v.tensor_copy(xnc[:, sl, :], xac[:, sl, :]).then_inc(s_dv.h, 1)
            s_dv.n += 1
    apply_base = s_dv.n - 3 * NC

    @block.sync
    def _(sync):
        sync.wait_ge(s_dv.h, L["cast", ntc - 1])
        sync.dma_start(xT_o.rearrange("(c p) t -> p c t", p=128), xT[:]).then_inc(s_sy.h, 16)
        s_sy.n += 16
        for k in range(NC):
            sl = k % 2
            sync.wait_ge(s_dv.h, apply_base + 3 * k + 3)
            sync.dma_start(xa0_o.rearrange("(c p) t -> p c t", p=128)[:, k, :], xac[:, sl, :]
                           ).then_inc(s_st2[sl].h, 16)
            sync.dma_start(xnb_o.rearrange("(c p) t -> p c t", p=128)[:, k, :], xnc[:, sl, :]
                           ).then_inc(s_st2[sl].h, 16)
            s_st2[sl].n += 32
        for sl in range(2):
            sync.wait_ge(s_st2[sl].h, s_st2[sl].n)
        sync.wait_ge(s_sy.h, s_sy.n)





def emit_hawk(b, D, T=T_CORE):
    nc = b.nc
    nft = T // FT
    TS = T + 4

    xa_in = D["xa"]
    winT = D["winT"]
    wgTd = D["wgT"]
    woutTd = D["woutT"]
    cwc_d = D["cwc"]
    cbc_d = D["cbc"]
    bgc_d = D["bgc"]
    fbc_d = D["fbc"]
    gnc_d = D["gnc"]
    iscol_d = D["iscol"]
    xa_out = D["xa_out"]

    tail_in = nc.dram_tensor(b.pfx + "tail_in", [HID, 3], F32)
    tail_out = nc.dram_tensor(b.pfx + "tail_out", [2 * HID, 3], F32)
    st_in = [nc.dram_tensor(f"st_in{fg}", [128, 1], F32) for fg in range(NH)]
    st_out = [nc.dram_tensor(f"st_out{fg}", [256, 1], F32) for fg in range(NH)]

    ARQ = b.sb("ARQ", [128, 16384])
    AR1 = b.sb("AR1", [128, 16384], BF16)
    WRB = b.sb("WRB", [128, 8192], BF16)
    gg = b.sb("gg", [128, NH, T], BF16)
    hring = b.sb("hring", [128, 2, TS], BF16)
    A = b.sb("A", [128, T])
    convacc = b.sb("convacc", [128, T])
    C = b.sb("C", [128, T])
    wgr = b.sb("wgr", [128, 2, NH, 256], BF16)
    tailb = b.sb("tailb", [128, NH, 3], BF16)
    hp = b.sb("hp", [128, NH, 6], BF16)
    gcol = b.sb("gcol", [128, NC])
    cwcol = b.sb("cwcol", [128, NH, 4])
    cbcol = b.sb("cbcol", [128, NH])
    bgcol = b.sb("bgcol", [128, 2 * NH])
    spcol = b.sb("spcol", [128, NH])
    stcol = b.sb("stcol", [128, NH])
    strecv = b.sb("strecv", [128, NH])
    iscol = b.sb("iscl", [128, 1])
    onescol_bf = b.sb("onescol_bf", [128, 1], BF16)
    onescol_f = b.sb("onescol_f", [128, 1])
    epscol = b.sb("epscol", [128, 1])
    onesrow_f = b.sb("onesrow_f", [1, 128])
    zcol = b.sb("zcol", [128, 1])

    xa = ARQ[:].rearrange("p (c t) -> p c t", c=NC)
    hc = ARQ[:, 0:12288].bitcast(BF16).rearrange("p (c t) -> p c t", c=NH)
    Dv = ARQ[:, 12288:14336].bitcast(BF16).rearrange("p (s t) -> p s t", s=2)
    xar0 = ARQ[:, 14336:16384]
    xar1 = ARQ[:, 12288:14336]
    xh = AR1[:].rearrange("p (c t) -> p c t", c=NC)
    winr = WRB[:].rearrange("p (s c o) -> p s c o", s=2, c=NC)

    def u_sl(fg, lo=0, hi=None):
        hi = T if hi is None else hi
        if fg < 8:
            return AR1[:, fg * T + lo:fg * T + hi]
        return WRB[:, (fg - 8) * T + lo:(fg - 8) * T + hi]

    B = convacc
    srow = convacc[0:1, :]

    pA = b.psum("pA", (128, 2048))
    pB = b.psum("pB", (128, 2048))
    pb = [pA[:, f * FT:(f + 1) * FT] for f in range(4)] + \
         [pB[:, f * FT:(f + 1) * FT] for f in range(4)]

    s_sy = b.sem("s_sy")
    s_wc = [b.sem(f"s_wc{i}") for i in range(2)]
    s_wg2 = [b.sem(f"s_wg{i}") for i in range(2)]
    s_xr2 = [b.sem(f"s_xr{i}") for i in range(2)]
    s_so2 = [b.sem(f"s_so{i}") for i in range(2)]
    s_cc = b.sem("s_cc")
    s_ag = b.sem("s_ag")
    s_pe = b.sem("s_pe")
    s_ac = b.sem("s_ac")
    s_dv = b.sem("s_dv")

    L = {}
    pe = 0
    pe += nft; L["stats"] = pe
    pe += nft; L["bcast"] = pe
    L["h_base"] = pe
    L["h_grp"] = [pe + j + 1 for j in range(NH * nft)]
    pe += NH * nft
    L["gate_base"] = pe
    L["gate_grp"] = [pe + j + 1 for j in range(NH * nft)]
    pe += NH * nft
    L["wg_base"] = pe
    L["wg_F"] = [pe + fg * 8 + 7 for fg in range(NH)]
    L["wg_I"] = [pe + fg * 8 + 8 for fg in range(NH)]
    pe += NH * 8
    L["wout_base"] = pe
    L["wout_grp"] = [pe + j + 1 for j in range(8 * nft)]
    pe += 8 * nft

    ac = 0
    ac += 1; L["sq"] = ac
    ac += 2
    ac += 1; L["srow"] = ac
    L["gelu"] = [ac + j + 1 for j in range(NH * nft)]
    ac += NH * nft
    L["a1"] = [ac + fg * 5 + 1 for fg in range(NH)]
    L["a2"] = [ac + fg * 5 + 2 for fg in range(NH)]
    L["a3"] = [ac + fg * 5 + 3 for fg in range(NH)]
    L["a4"] = [ac + fg * 5 + 4 for fg in range(NH)]
    L["a5"] = [ac + fg * 5 + 5 for fg in range(NH)]
    ac += NH * 5

    dv = 0
    dv += 1; L["cast"] = dv
    dv += 1; L["srowcp"] = dv
    dv += 1; L["recip"] = dv
    dv += 1; L["norm"] = dv
    L["h_evac"] = {}
    L["conv_main"] = {}
    for oc in range(NH):
        for f in range(nft):
            dv += 1; L["h_evac"][(oc, f)] = dv
        dv += 1
        dv += 1
        dv += 5; L["conv_main"][oc] = dv
    dv += 1; L["tailmask"] = dv
    dv += NH * 5; L["convfix"] = dv
    D9 = {}
    Dd = {}
    for fg in range(NH):
        if fg >= 2:
            dv += 1; D9[("a", fg - 2)] = dv
            dv += 1; D9[("b", fg - 2)] = dv
        for k in range(1, 9):
            dv += 1; Dd[(k, fg)] = dv
    for fg in (NH - 2, NH - 1):
        dv += 1; D9[("a", fg)] = dv
        dv += 1; D9[("b", fg)] = dv
    L["scan_end"] = dv
    L["wout_add_i"] = [dv + i + 1 for i in range(NC)]
    dv += NC

    block = b.block()

    @block.sync
    def _(sync):
        for dst, src in [(xa[:, 0:4, :], xa_in.rearrange("(c p) t -> p c t", p=128)[:, 0:4, :]),
                         (gcol[:], gnc_d), (cwcol[:], cwc_d), (cbcol[:], cbc_d),
                         (bgcol[:], bgc_d), (spcol[:], fbc_d), (iscol[:], iscol_d)]:
            sync.dma_start(dst, src).then_inc(s_sy.h, 16)
            s_sy.n += 16

    @block.scalar
    def _(sc):
        sc.dma_start(xa[:, 4:8, :], xa_in.rearrange("(c p) t -> p c t", p=128)[:, 4:8, :]
                     ).then_inc(s_sy.h, 16)
        s_sy.n += 16
    ld_done = s_sy.n

    @block.vector
    def _(v):
        v.memset(onescol_bf[:], 1.0)
        v.memset(onescol_f[:], 1.0)
        v.memset(onesrow_f[:], 1.0)
        v.memset(epscol[:], 1.000001)
        v.memset(zcol[:], 0.0)
        v.wait_ge(s_sy.h, ld_done)
        v.tensor_copy(xh[:], xa[:]).then_inc(s_dv.h, 1)
        s_dv.n += 1

    @block.scalar
    def _(sc):
        sc.wait_ge(s_dv.h, L["cast"])
        sc.activation(gg[:, 0:NC, :], xh[:], AF.Square).then_inc(s_ac.h, 1)
        s_ac.n += 1
        sc.activation(spcol[:], spcol[:], AF.Exp).then_inc(s_ac.h, 1)
        s_ac.n += 1
        sc.wait_ge(s_ac.h, s_ac.n)
        sc.activation(spcol[:], spcol[:], AF.Ln, bias=onescol_f[:, 0:1]).then_inc(s_ac.h, 1)
        s_ac.n += 1

    @block.tensor
    def _(t):
        t.wait_ge(s_ac.h, L["sq"])
        for f in range(nft):
            for k in range(NC):
                mm = t.matmul(pb[f][0:1, :], onescol_bf[:], gg[:, k, f * FT:(f + 1) * FT],
                              start=(k == 0), stop=(k == NC - 1))
            mm.then_inc(s_pe.h, 1)
            s_pe.n += 1

    @block.vector
    def _(v):
        v.wait_ge(s_pe.h, L["stats"])
        for f in range(nft):
            cp = v.tensor_copy(srow[0:1, f * FT:(f + 1) * FT], pb[f][0:1, :])
        cp.then_inc(s_dv.h, 1)
        s_dv.n += 1
        v.wait_ge(s_dv.h, s_dv.n)
        v.reciprocal(srow[:], srow[:]).then_inc(s_dv.h, 1)
        s_dv.n += 1

    @block.scalar
    def _(sc):
        sc.wait_ge(s_dv.h, L["recip"])
        sc.activation(srow[:], srow[:], AF.Sqrt, scale=1024.0).then_inc(s_ac.h, 1)
        s_ac.n += 1

    @block.tensor
    def _(t):
        t.wait_ge(s_ac.h, L["srow"])
        for f in range(nft):
            t.matmul(pA[:, f * FT:(f + 1) * FT], onesrow_f[:], srow[0:1, f * FT:(f + 1) * FT],
                     start=True, stop=True).then_inc(s_pe.h, 1)
            s_pe.n += 1

    @block.vector
    def _(v):
        v.wait_ge(s_pe.h, L["bcast"])
        for k in range(NC):
            ins = v.scalar_tensor_tensor(xh[:, k, :], xa[:, k, :], gcol[:, k:k + 1],
                                         pA[:], ALU.mult, ALU.mult)
        ins.then_inc(s_dv.h, 1)
        s_dv.n += 1
    assert s_dv.n == L["norm"]

    @block.gpsimd
    def _(g):
        for q in range(6):
            s = q % 2
            if q >= 2:
                if q - 2 < 3:
                    g.wait_ge(s_pe.h, L["h_grp"][(q - 2 + 1) * 4 * nft - 1])
                else:
                    g.wait_ge(s_pe.h, L["gate_grp"][(q - 5 + 1) * 4 * nft - 1])
            g.dma_start(winr[:, s, :, :],
                        winT.rearrange("(c p) o -> p c o", p=128)[:, :, q * 512:(q + 1) * 512]
                        ).then_inc(s_wc[s].h, 16)
            s_wc[s].n += 16

    @block.tensor
    def _(t):
        t.wait_ge(s_dv.h, L["norm"])
        for oc in range(NH):
            q, ol = divmod(oc, 4)
            if ol == 0:
                t.wait_ge(s_wc[q % 2].h, 16 * (q // 2 + 1))
            if oc >= 2:
                t.wait_ge(s_dv.h, L["conv_main"][oc - 2])
            for f in range(nft):
                j = oc * nft + f
                jj = j % 8
                if j >= 8:
                    t.wait_ge(s_dv.h, L["h_evac"][(oc - 2, f)])
                for k in range(NC):
                    mm = t.matmul(pb[jj][:], winr[:, q % 2, k, ol * 128:(ol + 1) * 128],
                                  xh[:, k, f * FT:(f + 1) * FT], start=(k == 0), stop=(k == NC - 1))
                mm.then_inc(s_pe.h, 1)
                s_pe.n += 1

    @block.vector
    def _(v):
        for oc in range(NH):
            sl = oc % 2
            for f in range(nft):
                j = oc * nft + f
                v.wait_ge(s_pe.h, L["h_grp"][j])
                v.tensor_copy(hring[:, sl, 4 + f * FT:4 + (f + 1) * FT], pb[j % 8][:]
                              ).then_inc(s_dv.h, 1)
                s_dv.n += 1
            v.wait_ge(s_dv.h, s_dv.n)
            v.tensor_copy(tailb[:, oc, :], hring[:, sl, TS - 3:TS]).then_inc(s_dv.h, 1)
            s_dv.n += 1
            v.wait_ge(s_dv.h, s_dv.n)
            v.tensor_copy(hp[:, oc, 3:6], hring[:, sl, 4:7]).then_inc(s_dv.h, 1)
            s_dv.n += 1
            v.wait_ge(s_dv.h, s_dv.n)
            v.scalar_tensor_tensor(convacc[0:128, 0:T - 3], hring[:, sl, 4:4 + T - 3],
                                   cwcol[:, oc, 0:1], zcol[:, 0:1].broadcast_to([128, T - 3]),
                                   ALU.mult, ALU.add).then_inc(s_dv.h, 1)
            s_dv.n += 1
            for kk in range(1, 4):
                v.wait_ge(s_dv.h, s_dv.n)
                v.scalar_tensor_tensor(convacc[0:128, 0:T - 3], hring[:, sl, 4 + kk:4 + kk + T - 3],
                                       cwcol[:, oc, kk:kk + 1], convacc[0:128, 0:T - 3],
                                       ALU.mult, ALU.add).then_inc(s_dv.h, 1)
                s_dv.n += 1
            v.wait_ge(s_dv.h, s_dv.n)
            v.tensor_scalar(hc[:, oc, 3:T], convacc[0:128, 0:T - 3], cbcol[:, oc:oc + 1],
                            None, ALU.add).then_inc(s_dv.h, 1)
            s_dv.n += 1
            assert s_dv.n == L["conv_main"][oc]

    @block.gpsimd
    def _(g):
        g.wait_ge(s_dv.h, L["conv_main"][NH - 1])
        g.dma_start(tail_in.ap().rearrange("(c p) k -> p c k", p=128), tailb[:]
                    ).then_inc(s_ag.h, 16)
        s_ag.n += 16
        g.wait_ge(s_ag.h, s_ag.n)
        g.collective_compute("AllGather", ALU.bypass, replica_groups=RG,
                             ins=[tail_in.ap().opt()], outs=[tail_out.ap().opt()]
                             ).then_inc(s_cc.h, 1)
        s_cc.n += 1
        g.wait_ge(s_cc.h, s_cc.n)
        g.dma_start(hp[:, :, 0:3],
                    tail_out.ap()[0:HID, 0:3].rearrange("(c p) k -> p c k", p=128)
                    ).then_inc(s_ag.h, 16)
        s_ag.n += 16
    tail_ag = s_ag.n
    cc_tail = s_cc.n

    @block.vector
    def _(v):
        v.wait_ge(s_ag.h, tail_ag)
        v.tensor_scalar(hp[:, :, 0:3], hp[:, :, 0:3], iscol[:, 0:1], None, ALU.mult
                        ).then_inc(s_dv.h, 1)
        s_dv.n += 1
        assert s_dv.n == L["tailmask"]
        for oc in range(NH):
            v.wait_ge(s_dv.h, s_dv.n)
            v.scalar_tensor_tensor(C[0:128, 0:3], hp[:, oc, 0:3], cwcol[:, oc, 0:1],
                                   zcol[:, 0:1].broadcast_to([128, 3]), ALU.mult, ALU.add
                                   ).then_inc(s_dv.h, 1)
            s_dv.n += 1
            for kk in range(1, 4):
                v.wait_ge(s_dv.h, s_dv.n)
                v.scalar_tensor_tensor(C[0:128, 0:3], hp[:, oc, kk:kk + 3], cwcol[:, oc, kk:kk + 1],
                                       C[0:128, 0:3], ALU.mult, ALU.add).then_inc(s_dv.h, 1)
                s_dv.n += 1
            v.wait_ge(s_dv.h, s_dv.n)
            v.tensor_scalar(hc[:, oc, 0:3], C[0:128, 0:3], cbcol[:, oc:oc + 1], None, ALU.add
                            ).then_inc(s_dv.h, 1)
            s_dv.n += 1
        assert s_dv.n == L["convfix"]

    @block.tensor
    def _(t):
        for oc in range(NH):
            q, ol = divmod(oc + NH, 4)
            if ol == 0:
                t.wait_ge(s_wc[q % 2].h, 16 * (q // 2 + 1))
            for f in range(nft):
                j = oc * nft + f
                jj = j % 8
                if j >= 8:
                    t.wait_ge(s_ac.h, L["gelu"][j - 8])
                else:
                    t.wait_ge(s_dv.h, L["h_evac"][(10 + j // 4, j % 4)])
                for k in range(NC):
                    mm = t.matmul(pb[jj][:], winr[:, q % 2, k, ol * 128:(ol + 1) * 128],
                                  xh[:, k, f * FT:(f + 1) * FT], start=(k == 0), stop=(k == NC - 1))
                mm.then_inc(s_pe.h, 1)
                s_pe.n += 1

    @block.scalar
    def _(sc):
        for j in range(NH * nft):
            oc, f = divmod(j, nft)
            jj = j % 8
            sc.wait_ge(s_pe.h, L["gate_grp"][j])
            sc.activation(gg[:, oc, f * FT:(f + 1) * FT], pb[jj][:], AF.Gelu
                          ).then_inc(s_ac.h, 1)
            s_ac.n += 1

    def emit_wg_load(g, fg):
        s = fg % 2
        if fg >= 2:
            g.wait_ge(s_pe.h, L["wg_I"][fg - 2])
        g.dma_start(wgr[:, s, :, :],
                    wgTd.rearrange("(c p) o -> p c o", p=128)[:, :, fg * 256:(fg + 1) * 256]
                    ).then_inc(s_wg2[s].h, 16)
        s_wg2[s].n += 16

    def emit_state_ag(g, fg):
        g.wait_ge(s_dv.h, Dd[(5, fg)])
        g.dma_start(st_in[fg].ap(), stcol[:, fg:fg + 1]).then_inc(s_ag.h, 16)
        s_ag.n += 16
        g.wait_ge(s_ag.h, s_ag.n)
        g.collective_compute("AllGather", ALU.bypass, replica_groups=RG,
                             ins=[st_in[fg].ap().opt()], outs=[st_out[fg].ap().opt()]
                             ).then_inc(s_cc.h, 1)
        s_cc.n += 1
        g.wait_ge(s_cc.h, s_cc.n)
        g.dma_start(strecv[:, fg:fg + 1], st_out[fg].ap()[0:128, :]).then_inc(s_ag.h, 16)
        s_ag.n += 16
        g.wait_ge(s_ag.h, s_ag.n)

    @block.gpsimd
    def _(g):
        emit_wg_load(g, 0)
        emit_wg_load(g, 1)
        for fg in range(NH):
            if fg + 2 < NH:
                emit_wg_load(g, fg + 2)
            emit_state_ag(g, fg)

    @block.tensor
    def _(t):
        t.wait_ge(s_dv.h, L["convfix"])
        for fg in range(NH):
            s = fg % 2
            t.wait_ge(s_wg2[s].h, 16 * (fg // 2 + 1))
            if fg == 0:
                t.wait_ge(s_ac.h, L["gelu"][NH * nft - 1])
            else:
                t.wait_ge(s_ac.h, L["a2"][fg - 1])
            for f in range(nft):
                for k in range(NH):
                    mm = t.matmul(pA[:, f * FT:(f + 1) * FT], wgr[:, s, k, 0:128],
                                  hc[:, k, f * FT:(f + 1) * FT], start=(k == 0), stop=(k == NH - 1))
                mm.then_inc(s_pe.h, 1)
                s_pe.n += 1
                for k in range(NH):
                    mm = t.matmul(pB[:, f * FT:(f + 1) * FT], wgr[:, s, k, 128:256],
                                  hc[:, k, f * FT:(f + 1) * FT], start=(k == 0), stop=(k == NH - 1))
                mm.then_inc(s_pe.h, 1)
                s_pe.n += 1
    assert s_pe.n == L["wout_base"]

    @block.scalar
    def _(sc):
        for fg in range(NH):
            sc.wait_ge(s_pe.h, L["wg_F"][fg])
            if fg >= 1:
                sc.wait_ge(s_dv.h, Dd[(6, fg - 1)])
            sc.activation(A[:], pA[:], AF.Sigmoid, bias=bgcol[:, fg:fg + 1]).then_inc(s_ac.h, 1)
            s_ac.n += 1
            sc.wait_ge(s_pe.h, L["wg_I"][fg])
            if fg >= 1:
                sc.wait_ge(s_dv.h, Dd[(4, fg - 1)])
            sc.activation(C[:], pB[:], AF.Sigmoid, bias=bgcol[:, NH + fg:NH + fg + 1]
                          ).then_inc(s_ac.h, 1)
            s_ac.n += 1
            sc.wait_ge(s_dv.h, Dd[(1, fg)])
            sc.activation(A[:], A[:], AF.Exp).then_inc(s_ac.h, 1)
            s_ac.n += 1
            if fg >= 1:
                sc.wait_ge(s_dv.h, Dd[(8, fg - 1)])
            sc.wait_ge(s_ac.h, s_ac.n)
            sc.activation(B[:], A[:], AF.Square).then_inc(s_ac.h, 1)
            s_ac.n += 1
            sc.wait_ge(s_ac.h, s_ac.n)
            sc.activation(B[:], B[:], AF.Sqrt, scale=-1.0, bias=epscol[:, 0:1]).then_inc(s_ac.h, 1)
            s_ac.n += 1

    @block.vector
    def _(v):
        def d9(fgx):
            v.wait_ge(s_ag.h, tail_ag + 16 * (2 * fgx + 2))
            v.tensor_scalar(strecv[:, fgx:fgx + 1], strecv[:, fgx:fgx + 1], iscol[:, 0:1],
                            None, ALU.mult).then_inc(s_dv.h, 1)
            s_dv.n += 1
            assert s_dv.n == D9[("a", fgx)]
            v.wait_ge(s_dv.h, s_dv.n)
            v.scalar_tensor_tensor(u_sl(fgx), Dv[:, fgx % 2, :], strecv[:, fgx:fgx + 1],
                                   u_sl(fgx), ALU.mult, ALU.add).then_inc(s_dv.h, 1)
            s_dv.n += 1
            assert s_dv.n == D9[("b", fgx)]

        for fg in range(NH):
            if fg >= 2:
                d9(fg - 2)
            v.wait_ge(s_ac.h, L["a1"][fg])
            v.tensor_scalar(A[:], A[:], spcol[:, fg:fg + 1], -8.0, ALU.mult, ALU.mult
                            ).then_inc(s_dv.h, 1)
            s_dv.n += 1
            assert s_dv.n == Dd[(1, fg)]
            v.wait_ge(s_ac.h, L["a5"][fg])
            v.tensor_tensor(C[:], B[:], C[:], ALU.mult).then_inc(s_dv.h, 1)
            s_dv.n += 1
            v.wait_ge(s_dv.h, s_dv.n)
            v.tensor_tensor(C[:], C[:], hc[:, fg, :], ALU.mult).then_inc(s_dv.h, 1)
            s_dv.n += 1
            v.wait_ge(s_dv.h, s_dv.n)
            v.tensor_tensor_scan(u_sl(fg), A[:], C[:], zcol[:, 0:1], ALU.mult, ALU.add
                                 ).then_inc(s_dv.h, 1)
            s_dv.n += 1
            assert s_dv.n == Dd[(4, fg)]
            v.wait_ge(s_dv.h, s_dv.n)
            v.tensor_copy(stcol[:, fg:fg + 1], u_sl(fg, T - 1, T)).then_inc(s_dv.h, 1)
            s_dv.n += 1
            v.wait_ge(s_dv.h, s_dv.n)
            v.tensor_tensor_scan(B[:], A[:], zcol[:, 0:1].broadcast_to([128, T]),
                                 onescol_f[:, 0:1], ALU.mult, ALU.add).then_inc(s_dv.h, 1)
            s_dv.n += 1
            v.wait_ge(s_dv.h, s_dv.n)
            v.tensor_tensor(u_sl(fg), gg[:, fg, :], u_sl(fg), ALU.mult).then_inc(s_dv.h, 1)
            s_dv.n += 1
            v.wait_ge(s_dv.h, s_dv.n)
            v.tensor_tensor(Dv[:, fg % 2, :], B[:], gg[:, fg, :], ALU.mult).then_inc(s_dv.h, 1)
            s_dv.n += 1
            assert s_dv.n == Dd[(8, fg)]
        d9(NH - 2)
        d9(NH - 1)
    assert s_dv.n == L["scan_end"]

    @block.gpsimd
    def _(g):
        for r in range(4):
            s = r % 2
            if r < 2:
                g.wait_ge(s_pe.h, L["wg_I"][10 + r])
            else:
                g.wait_ge(s_pe.h, L["wout_grp"][(r - 2 + 1) * 2 * nft - 1])
            g.dma_start(wgr[:, s, :, :],
                        woutTd.rearrange("(c p) o -> p c o", p=128)[:, :, r * 256:(r + 1) * 256]
                        ).then_inc(s_wg2[s].h, 16)
            s_wg2[s].n += 16

    def emit_xa_load(sync, i):
        sl = i % 2
        tgt = xar0 if sl == 0 else xar1
        if i == 0:
            sync.wait_ge(s_dv.h, L["norm"])
        if i == 1:
            sync.wait_ge(s_dv.h, L["scan_end"])
        if i >= 2:
            sync.wait_ge(s_so2[sl].h, 16 * (i // 2))
        sync.dma_start(tgt[0:128, 0:T],
                       xa_in.rearrange("(c p) t -> p c t", p=128)[:, i, :]
                       ).then_inc(s_xr2[sl].h, 16)
        s_xr2[sl].n += 16

    def emit_xa_store(sync, i):
        sl = i % 2
        tgt = xar0 if sl == 0 else xar1
        sync.wait_ge(s_dv.h, L["wout_add_i"][i])
        sync.dma_start(xa_out.rearrange("(c p) t -> p c t", p=128)[:, i, :],
                       tgt[0:128, 0:T]).then_inc(s_so2[sl].h, 16)
        s_so2[sl].n += 16

    @block.sync
    def _(sync):
        for i in range(NC):
            if i >= 2:
                emit_xa_store(sync, i - 2)
            emit_xa_load(sync, i)

    @block.tensor
    def _(t):
        t.wait_ge(s_dv.h, L["scan_end"])
        for i in range(NC):
            wq, wl = divmod(i, 2)
            t.wait_ge(s_wg2[wq % 2].h, 16 * (NH // 2) + 16 * (wq // 2 + 1))
            for f in range(nft):
                j = i * nft + f
                jj = j % 8
                if j >= 8:
                    t.wait_ge(s_dv.h, L["wout_add_i"][(j - 8) // nft])
                for k in range(NH):
                    mm = t.matmul(pb[jj][:], wgr[:, wq % 2, k, wl * 128:(wl + 1) * 128],
                                  u_sl(k, f * FT, (f + 1) * FT), start=(k == 0), stop=(k == NH - 1))
                mm.then_inc(s_pe.h, 1)
                s_pe.n += 1

    @block.vector
    def _(v):
        for i in range(NC):
            sl = i % 2
            tgt = xar0 if sl == 0 else xar1
            v.wait_ge(s_xr2[sl].h, 16 * (i // 2 + 1))
            for f in range(nft):
                j = i * nft + f
                v.wait_ge(s_pe.h, L["wout_grp"][j])
                ins = v.tensor_tensor(tgt[0:128, f * FT:(f + 1) * FT],
                                      tgt[0:128, f * FT:(f + 1) * FT], pb[j % 8][:], ALU.add)
            ins.then_inc(s_dv.h, 1)
            s_dv.n += 1
            assert s_dv.n == L["wout_add_i"][i]

    @block.sync
    def _(sync):
        emit_xa_store(sync, NC - 2)
        emit_xa_store(sync, NC - 1)
        for sl in range(2):
            sync.wait_ge(s_so2[sl].h, s_so2[sl].n)



def emit_mlp(b, D, T=T_CORE):
    nc = b.nc
    NC = 8
    GH = 2048
    NG = GH // 128
    nft = T // FT

    xa_in = D["xa"]
    wgt = D["wgt"]
    wst = D["wst"]
    gnc = D["gnc"]
    xa_out = D["xa_out"]

    xa = b.sb("xa", [128, NC, T])
    xg = b.sb("xg", [128, NC, T], BF16)
    u = b.sb("u", [128, NG, T], BF16)
    gncol = b.sb("gncol", [128, NC])
    onescol_bf = b.sb("onescol_bf", [128, 1], BF16)
    onesrow_f = b.sb("onesrow_f", [1, 128])
    srow = b.sb("srow", [1, T])
    gw_ring = b.sb("gw_ring", [128, 2, NC, 512], BF16)
    sw_ring = b.sb("sw_ring", [128, 2, NG, 256], BF16)
    gg_ring = b.sb("gg_ring", [128, 4, FT], BF16)

    p = [b.psum(f"p{i}") for i in range(4)]
    bc = b.psum("bc", (128, 2048))
    pb = p + [bc[:, f * FT:(f + 1) * FT] for f in range(4)]

    s_sy = b.sem("s_sy")
    s_gc2 = [b.sem(f"s_gc{i}") for i in range(2)]
    s_sc2 = [b.sem(f"s_sc{i}") for i in range(2)]
    s_pe = b.sem("s_pe")
    s_ac = b.sem("s_ac")
    s_dv = b.sem("s_dv")

    block = b.block()

    @block.sync
    def _(sync):
        sync.dma_start(xa[:, 0:4, :], xa_in.rearrange("(c p) t -> p c t", p=128)[:, 0:4, :]
                       ).then_inc(s_sy.h, 16)
        s_sy.n += 16
        sync.dma_start(gncol[:], gnc).then_inc(s_sy.h, 16)
        s_sy.n += 16

    @block.scalar
    def _(sc):
        sc.dma_start(xa[:, 4:8, :], xa_in.rearrange("(c p) t -> p c t", p=128)[:, 4:8, :]
                     ).then_inc(s_sy.h, 16)
        s_sy.n += 16
    ld_done = s_sy.n

    bcast_done_c = 8
    shrink_base_c = bcast_done_c + 2 * NG * nft

    @block.gpsimd
    def _(g):
        for q in range(8):
            s = q % 2
            if q >= 2:
                fg_done = 2 * (q - 2) + 2
                g.wait_ge(s_pe.h, bcast_done_c + fg_done * 2 * nft)
            g.dma_start(gw_ring[:, s, :, :],
                        wgt.rearrange("(c p) o -> p c o", p=128)[:, :, q * 512:(q + 1) * 512]
                        ).then_inc(s_gc2[s].h, 16)
            s_gc2[s].n += 16
        for r in range(4):
            s = r % 2
            if r >= 2:
                i_done = 2 * (r - 2) + 2
                g.wait_ge(s_pe.h, shrink_base_c + i_done * nft)
            g.dma_start(sw_ring[:, s, :, :],
                        wst.rearrange("(c p) o -> p c o", p=128)[:, :, r * 256:(r + 1) * 256]
                        ).then_inc(s_sc2[s].h, 16)
            s_sc2[s].n += 16

    @block.vector
    def _(v):
        v.memset(onescol_bf[:], 1.0)
        v.memset(onesrow_f[:], 1.0)
        v.wait_ge(s_sy.h, ld_done)
        v.tensor_copy(xg[:], xa[:]).then_inc(s_dv.h, 1)
        s_dv.n += 1
    cast_done = s_dv.n

    @block.scalar
    def _(sc):
        sc.wait_ge(s_dv.h, cast_done)
        sc.activation(u[:, 0:NC, :], xg[:], AF.Square).then_inc(s_ac.h, 1)
        s_ac.n += 1
    sq_done = s_ac.n

    @block.tensor
    def _(t):
        t.wait_ge(s_ac.h, sq_done)
        for f in range(nft):
            for k in range(NC):
                mm = t.matmul(p[f][0:1, :], onescol_bf[:], u[:, k, f * FT:(f + 1) * FT],
                              start=(k == 0), stop=(k == NC - 1))
            mm.then_inc(s_pe.h, 1)
            s_pe.n += 1
    stats_done = s_pe.n

    @block.vector
    def _(v):
        v.wait_ge(s_pe.h, stats_done)
        for f in range(nft):
            cp = v.tensor_copy(srow[0:1, f * FT:(f + 1) * FT], p[f][0:1, :])
        cp.then_inc(s_dv.h, 1)
        s_dv.n += 1
        v.wait_ge(s_dv.h, s_dv.n)
        v.reciprocal(srow[:], srow[:]).then_inc(s_dv.h, 1)
        s_dv.n += 1
    recip_done = s_dv.n

    @block.scalar
    def _(sc):
        sc.wait_ge(s_dv.h, recip_done)
        sc.activation(srow[:], srow[:], AF.Sqrt, scale=1024.0).then_inc(s_ac.h, 1)
        s_ac.n += 1
    srow_done = s_ac.n

    @block.tensor
    def _(t):
        t.wait_ge(s_ac.h, srow_done)
        for f in range(nft):
            t.matmul(bc[:, f * FT:(f + 1) * FT], onesrow_f[:], srow[0:1, f * FT:(f + 1) * FT],
                     start=True, stop=True).then_inc(s_pe.h, 1)
            s_pe.n += 1
    bcast_done = s_pe.n
    assert bcast_done == bcast_done_c

    @block.vector
    def _(v):
        v.wait_ge(s_pe.h, bcast_done)
        for k in range(NC):
            ins = v.scalar_tensor_tensor(xg[:, k, :], xa[:, k, :], gncol[:, k:k + 1],
                                         bc[:], ALU.mult, ALU.mult)
        ins.then_inc(s_dv.h, 1)
        s_dv.n += 1
    norm_done = s_dv.n

    @block.tensor
    def _(t):
        t.wait_ge(s_dv.h, norm_done)
        for fg in range(NG):
            q = fg // 2
            fl = fg % 2
            if fl == 0:
                t.wait_ge(s_gc2[q % 2].h, 16 * (q // 2 + 1))
            for f in range(nft):
                j = fg * nft + f
                jj = j % 4
                if j >= 4:
                    t.wait_ge(s_dv.h, norm_done + (j - 4) + 1)
                for k in range(NC):
                    mm = t.matmul(pb[2 * jj][:], gw_ring[:, q % 2, k, (2 * fl) * 128:(2 * fl + 1) * 128],
                                  xg[:, k, f * FT:(f + 1) * FT], start=(k == 0), stop=(k == NC - 1))
                mm.then_inc(s_pe.h, 1)
                s_pe.n += 1
                for k in range(NC):
                    mm = t.matmul(pb[2 * jj + 1][:], gw_ring[:, q % 2, k, (2 * fl + 1) * 128:(2 * fl + 2) * 128],
                                  xg[:, k, f * FT:(f + 1) * FT], start=(k == 0), stop=(k == NC - 1))
                mm.then_inc(s_pe.h, 1)
                s_pe.n += 1

    @block.scalar
    def _(sc):
        for j in range(NG * nft):
            jj = j % 4
            if j >= 4:
                sc.wait_ge(s_dv.h, norm_done + (j - 4) + 1)
            sc.wait_ge(s_pe.h, bcast_done + 2 * j + 1)
            sc.activation(gg_ring[:, jj, :], pb[2 * jj][:], AF.Gelu).then_inc(s_ac.h, 1)
            s_ac.n += 1

    @block.vector
    def _(v):
        for j in range(NG * nft):
            fg, f = divmod(j, nft)
            jj = j % 4
            v.wait_ge(s_ac.h, srow_done + j + 1)
            v.wait_ge(s_pe.h, bcast_done + 2 * j + 2)
            v.tensor_tensor(u[:, fg, f * FT:(f + 1) * FT], gg_ring[:, jj, :],
                            pb[2 * jj + 1][:], ALU.mult).then_inc(s_dv.h, 1)
            s_dv.n += 1
    grow_done = s_dv.n

    shrink_base = bcast_done + 2 * NG * nft
    assert shrink_base == shrink_base_c

    @block.tensor
    def _(t):
        for i in range(8):
            r = i // 2
            il = i % 2
            if il == 0:
                t.wait_ge(s_sc2[r % 2].h, 16 * (r // 2 + 1))
            for f in range(nft):
                j = i * nft + f
                jj = j % 8
                if j == 0:
                    t.wait_ge(s_dv.h, grow_done)
                if j >= 8:
                    t.wait_ge(s_dv.h, grow_done + (j - 8) + 1)
                for k in range(NG):
                    mm = t.matmul(pb[jj][:], sw_ring[:, r % 2, k, il * 128:(il + 1) * 128],
                                  u[:, k, f * FT:(f + 1) * FT], start=(k == 0), stop=(k == NG - 1))
                mm.then_inc(s_pe.h, 1)
                s_pe.n += 1

    @block.vector
    def _(v):
        for i in range(8):
            for f in range(nft):
                j = i * nft + f
                jj = j % 8
                v.wait_ge(s_pe.h, shrink_base + j + 1)
                v.tensor_tensor(xa[:, i, f * FT:(f + 1) * FT], xa[:, i, f * FT:(f + 1) * FT],
                                pb[jj][:], ALU.add).then_inc(s_dv.h, 1)
                s_dv.n += 1
    add_done = s_dv.n

    @block.sync
    def _(sync):
        sync.wait_ge(s_dv.h, add_done)
        sync.dma_start(xa_out.rearrange("(c p) t -> p c t", p=128), xa[:]).then_inc(s_sy.h, 16)
        s_sy.n += 16
        sync.wait_ge(s_sy.h, s_sy.n)





def emit_smqa(b, D, T=T_CORE):
    nc = b.nc
    nft = T // FT

    xa_in = D["xa"]
    wqTd = D["wqT"]
    wkTd = D["wkT"]
    wvTd = D["wvT"]
    woTd = D["woT"]
    gnc_d = D["gnc"]
    cos_d = D["cosT"]
    sin_d = D["sinT"]
    rot_d = D["rotP"]
    mL_d = D["mL"]
    mR_d = D["mR"]
    iscol_d = D["iscol"]
    xa_out = D["xa_out"]

    kh_in = nc.dram_tensor(b.pfx + "kh_in", [128, W], BF16)
    kh_out = nc.dram_tensor(b.pfx + "kh_out", [256, W], BF16)
    vh_in = nc.dram_tensor(b.pfx + "vh_in", [W, HD], BF16)
    vh_out = nc.dram_tensor(b.pfx + "vh_out", [2 * W, HD], BF16)

    xa = b.sb("xa", [128, NC, T])
    xn = b.sb("xn", [128, NC, T], BF16)
    qb = b.sb("qb", [128, QH, T], BF16)
    kext = b.sb("kext", [128, 3 * W], BF16)
    vext = b.sb("vext", [128, 24, HD], BF16)
    Pbuf = b.sb("Pbuf", [128, 12, FT], BF16)
    cosb = b.sb("cosb", [128, T], BF16)
    sinb = b.sb("sinb", [128, T], BF16)
    rotb = b.sb("rotb", [128, 128], BF16)
    mLb = b.sb("mLb", [128, 4, 512], BF16)
    mRb = b.sb("mRb", [128, 4, 512], BF16)
    tmp1 = b.sb("tmp1", [128, FT])
    tmp2 = b.sb("tmp2", [128, FT])
    wpc = b.sb("wpc", [128, 2, NC, 256], BF16)
    wkb = b.sb("wkb", [128, NC, HD], BF16)
    wvb = b.sb("wvb", [128, NC, HD], BF16)
    brec = b.sb("brec", [128, FT], BF16)
    lrow = b.sb("lrow", [1, FT])
    lrowb = b.sb("lrowb", [1, FT], BF16)
    gcol = b.sb("gcol", [128, NC])
    iscl = b.sb("iscl", [128, 1])
    onescol_bf = b.sb("onescol_bf", [128, 1], BF16)
    onesrow_f = b.sb("onesrow_f", [1, 128])
    onesrow_bf = b.sb("onesrow_bf", [1, 128], BF16)
    srow = b.sb("srow", [1, T])
    mcol = b.sb("mcol", [128, QH])
    mrow = b.sb("mrow", [1, 16])
    Obuf = xn

    pA = b.psum("pA", (128, 2048))
    pB = b.psum("pB", (128, 2048))
    pb = [pA[:, f * FT:(f + 1) * FT] for f in range(4)] + \
         [pB[:, f * FT:(f + 1) * FT] for f in range(4)]

    s_sy = b.sem("s_sy")
    s_tab = b.sem("s_tab")
    s_wp = [b.sem(f"s_wp{i}") for i in range(2)]
    s_cc = b.sem("s_cc")
    s_ag = b.sem("s_ag")
    s_pe = b.sem("s_pe")
    s_ac = b.sem("s_ac")
    s_dv = b.sem("s_dv")

    groups = [(w, h, qt) for w in (1, 0) for h in range(QH) for qt in range(2)]

    L = {"S": {}, "Lo": {}, "PV": {}, "rb": {}, "ev": {}, "msk": {}, "oev": {},
         "rc": {}, "brev": {}, "bev": {}}
    pe = 0
    pe += nft; L["stats"] = pe
    pe += nft; L["bcast"] = pe
    L["qproj"] = [pe + j + 1 for j in range(QH * nft)]
    pe += QH * nft
    pe += 1; L["kproj"] = pe
    L["vproj"] = [pe + i + 1 for i in range(16)]
    pe += 16
    L["rotq"] = [pe + j + 1 for j in range(QH * nft)]
    pe += QH * nft
    pe += 1; L["rotk"] = pe
    L["qsq"] = [pe + h + 1 for h in range(QH)]
    pe += QH
    pe += 1; L["ksqa"] = pe
    pe += 1; L["ksqb"] = pe
    pe += 1; L["mb"] = pe
    for g in groups:
        L["S"][g] = [pe + si + 1 for si in range(12)]
        pe += 12
        pe += 1; L["Lo"][g] = pe
        pe += 1; L["PV"][g] = pe
        pe += 1; L["rb"][g] = pe
    L["wout"] = [pe + j + 1 for j in range(NC * nft)]
    pe += NC * nft

    ac = 0
    ac += 1; L["sq"] = ac
    ac += 1; L["srow"] = ac
    ac += 1; L["qsqact"] = ac
    ac += 1; L["ksqact"] = ac
    ac += 1; L["msqrt"] = ac
    ac += 1; L["mneg"] = ac
    for g in groups:
        L["ev"][g] = [ac + si + 1 for si in range(12)]
        ac += 12

    dv = 0
    dv += 1; L["cast"] = dv
    dv += 1; L["srowcp"] = dv
    dv += 1; L["recipn"] = dv
    dv += 1; L["norm"] = dv
    L["qev"] = [dv + j + 1 for j in range(QH * nft)]
    dv += QH * nft
    dv += 1; L["kev"] = dv
    L["vev"] = [dv + i + 1 for i in range(16)]
    dv += 16
    L["qcmb"] = [dv + 3 * j + 3 for j in range(QH * nft)]
    dv += 3 * QH * nft
    L["kcmb"] = [dv + 3 * f + 3 for f in range(nft)]
    dv += 3 * nft
    L["qmax"] = [dv + h + 1 for h in range(QH)]
    dv += QH
    dv += 1; L["kmaxa"] = dv
    dv += 1; L["kmaxb"] = dv
    dv += 1; L["kmaxc"] = dv
    dv += 1; L["mmul"] = dv
    dv += 1; L["mcolev"] = dv
    for g in groups:
        w, h, qt = g
        nmask = sum(1 for _, mk in subtiles(qt) if mk is not None)
        nisc = (8 if qt == 0 else 4) if w == 0 else 0
        L["msk"][g] = dv + nmask + nisc
        dv += nmask + nisc
        dv += 1; L["rc"][g] = dv
        dv += 1; L["brev"][g] = dv
        dv += 1; L["bev"][g] = dv
        dv += 1; L["oev"][g] = dv
    L["wadd"] = [dv + j + 1 for j in range(NC * nft)]
    dv += NC * nft

    block = b.block()

    @block.sync
    def _(sync):
        for dst, src in [(xa[:, 0:4, :], xa_in.rearrange("(c p) t -> p c t", p=128)[:, 0:4, :]),
                         (gcol[:], gnc_d), (iscl[:], iscol_d)]:
            sync.dma_start(dst, src).then_inc(s_sy.h, 16)
            s_sy.n += 16

    @block.scalar
    def _(sc):
        sc.dma_start(xa[:, 4:8, :], xa_in.rearrange("(c p) t -> p c t", p=128)[:, 4:8, :]
                     ).then_inc(s_sy.h, 16)
        s_sy.n += 16
    ld_done = s_sy.n

    @block.gpsimd
    def _(g):
        g.dma_start(cosb[:], cos_d).then_inc(s_tab.h, 16)
        s_tab.n += 16
        g.dma_start(sinb[:], sin_d).then_inc(s_tab.h, 16)
        s_tab.n += 16
        g.dma_start(mLb[:], mL_d.rearrange("m p q -> p m q")).then_inc(s_tab.h, 16)
        s_tab.n += 16
        g.dma_start(mRb[:], mR_d.rearrange("m p q -> p m q")).then_inc(s_tab.h, 16)
        s_tab.n += 16
        g.dma_start(rotb[:], rot_d).then_inc(s_tab.h, 16)
        s_tab.n += 16
        g.dma_start(wkb[:], wkTd.rearrange("(c p) o -> p c o", p=128)).then_inc(s_wp[0].h, 16)
        s_wp[0].n += 16
        g.dma_start(wvb[:], wvTd.rearrange("(c p) o -> p c o", p=128)).then_inc(s_wp[1].h, 16)
        s_wp[1].n += 16
        for q in range(4):
            s = q % 2
            if q >= 2:
                g.wait_ge(s_pe.h, L["qproj"][(q - 2 + 1) * 2 * nft - 1])
            g.dma_start(wpc[:, s, :, :],
                        wqTd.rearrange("(c p) o -> p c o", p=128)[:, :, q * 256:(q + 1) * 256]
                        ).then_inc(s_wp[s].h, 16)
            s_wp[s].n += 16

    @block.vector
    def _(v):
        v.memset(onescol_bf[:], 1.0)
        v.memset(onesrow_f[:], 1.0)
        v.memset(onesrow_bf[:], 1.0)
        v.wait_ge(s_sy.h, ld_done)
        v.tensor_copy(xn[:], xa[:]).then_inc(s_dv.h, 1)
        s_dv.n += 1
    assert s_dv.n == L["cast"]

    @block.scalar
    def _(sc):
        sc.wait_ge(s_dv.h, L["cast"])
        sc.activation(qb[:], xn[:], AF.Square).then_inc(s_ac.h, 1)
        s_ac.n += 1

    @block.tensor
    def _(t):
        t.wait_ge(s_ac.h, L["sq"])
        for f in range(nft):
            for k in range(NC):
                mm = t.matmul(pb[f][0:1, :], onescol_bf[:], qb[:, k, f * FT:(f + 1) * FT],
                              start=(k == 0), stop=(k == NC - 1))
            mm.then_inc(s_pe.h, 1)
            s_pe.n += 1

    @block.vector
    def _(v):
        v.wait_ge(s_pe.h, L["stats"])
        for f in range(nft):
            cp = v.tensor_copy(srow[0:1, f * FT:(f + 1) * FT], pb[f][0:1, :])
        cp.then_inc(s_dv.h, 1)
        s_dv.n += 1
        v.wait_ge(s_dv.h, s_dv.n)
        v.reciprocal(srow[:], srow[:]).then_inc(s_dv.h, 1)
        s_dv.n += 1
    assert s_dv.n == L["recipn"]

    @block.scalar
    def _(sc):
        sc.wait_ge(s_dv.h, L["recipn"])
        sc.activation(srow[:], srow[:], AF.Sqrt, scale=1024.0).then_inc(s_ac.h, 1)
        s_ac.n += 1

    @block.tensor
    def _(t):
        t.wait_ge(s_ac.h, L["srow"])
        for f in range(nft):
            t.matmul(pA[:, f * FT:(f + 1) * FT], onesrow_f[:], srow[0:1, f * FT:(f + 1) * FT],
                     start=True, stop=True).then_inc(s_pe.h, 1)
            s_pe.n += 1

    @block.vector
    def _(v):
        v.wait_ge(s_pe.h, L["bcast"])
        for k in range(NC):
            ins = v.scalar_tensor_tensor(xn[:, k, :], xa[:, k, :], gcol[:, k:k + 1],
                                         pA[:], ALU.mult, ALU.mult)
        ins.then_inc(s_dv.h, 1)
        s_dv.n += 1
    assert s_dv.n == L["norm"]

    @block.tensor
    def _(t):
        t.wait_ge(s_dv.h, L["norm"])
        for hh in range(QH):
            q4, ol = divmod(hh, 2)
            if ol == 0:
                t.wait_ge(s_wp[q4 % 2].h, 32 if q4 < 2 else 48)
            for f in range(nft):
                j = hh * nft + f
                jj = j % 8
                if j >= 8:
                    t.wait_ge(s_dv.h, L["qev"][j - 8])
                for k in range(NC):
                    mm = t.matmul(pb[jj][:], wpc[:, q4 % 2, k, ol * 128:(ol + 1) * 128],
                                  xn[:, k, f * FT:(f + 1) * FT], start=(k == 0), stop=(k == NC - 1))
                mm.then_inc(s_pe.h, 1)
                s_pe.n += 1
        t.wait_ge(s_wp[0].h, 48)
        t.wait_ge(s_wp[1].h, 48)
        t.wait_ge(s_dv.h, L["qev"][QH * nft - 1])
        for f in range(nft):
            for k in range(NC):
                mm = t.matmul(pA[:, f * FT:(f + 1) * FT], wkb[:, k, :],
                              xn[:, k, f * FT:(f + 1) * FT], start=(k == 0), stop=(k == NC - 1))
        mm.then_inc(s_pe.h, 1)
        s_pe.n += 1
        assert s_pe.n == L["kproj"]
        for tc in range(16):
            bank = pB[:, (tc % 4) * FT:(tc % 4) * FT + HD]
            if tc >= 4:
                t.wait_ge(s_dv.h, L["vev"][tc - 4])
            for k in range(NC):
                mm = t.matmul(bank, xn[:, k, tc * 128:(tc + 1) * 128], wvb[:, k, :],
                              start=(k == 0), stop=(k == NC - 1))
            mm.then_inc(s_pe.h, 1)
            s_pe.n += 1

    @block.vector
    def _(v):
        for j in range(QH * nft):
            hh, f = divmod(j, nft)
            v.wait_ge(s_pe.h, L["qproj"][j])
            v.tensor_copy(qb[:, hh, f * FT:(f + 1) * FT], pb[j % 8][:]).then_inc(s_dv.h, 1)
            s_dv.n += 1
        v.wait_ge(s_pe.h, L["kproj"])
        v.tensor_copy(kext[:, W:3 * W], pA[:]).then_inc(s_dv.h, 1)
        s_dv.n += 1
        assert s_dv.n == L["kev"]
        for tc in range(16):
            v.wait_ge(s_pe.h, L["vproj"][tc])
            v.tensor_copy(vext[:, 8 + tc, :], pB[:, (tc % 4) * FT:(tc % 4) * FT + HD]
                          ).then_inc(s_dv.h, 1)
            s_dv.n += 1

    @block.tensor
    def _(t):
        t.wait_ge(s_tab.h, 80)
        for j in range(QH * nft):
            hh, f = divmod(j, nft)
            if j >= 8:
                t.wait_ge(s_dv.h, L["qcmb"][j - 8])
            else:
                t.wait_ge(s_dv.h, L["vev"][15])
            t.matmul(pb[j % 8][:], rotb[:], qb[:, hh, f * FT:(f + 1) * FT],
                     start=True, stop=True).then_inc(s_pe.h, 1)
            s_pe.n += 1
        t.wait_ge(s_dv.h, L["qcmb"][QH * nft - 1])
        for f in range(nft):
            mm = t.matmul(pb[f][:], rotb[:], kext[:, W + f * FT:W + (f + 1) * FT],
                          start=True, stop=True)
        mm.then_inc(s_pe.h, 1)
        s_pe.n += 1
        assert s_pe.n == L["rotk"]

    @block.vector
    def _(v):
        v.wait_ge(s_tab.h, 80)
        for j in range(QH * nft):
            hh, f = divmod(j, nft)
            fs = slice(f * FT, (f + 1) * FT)
            v.wait_ge(s_pe.h, L["rotq"][j])
            v.wait_ge(s_dv.h, s_dv.n)
            v.tensor_tensor(tmp1[:, 0:FT], qb[:, hh, fs], cosb[:, fs], ALU.mult
                            ).then_inc(s_dv.h, 1)
            s_dv.n += 1
            v.wait_ge(s_dv.h, s_dv.n)
            v.tensor_tensor(tmp2[:, 0:FT], sinb[:, fs], pb[j % 8][:], ALU.mult
                            ).then_inc(s_dv.h, 1)
            s_dv.n += 1
            v.wait_ge(s_dv.h, s_dv.n)
            v.tensor_tensor(qb[:, hh, fs], tmp1[:, 0:FT], tmp2[:, 0:FT], ALU.add
                            ).then_inc(s_dv.h, 1)
            s_dv.n += 1
        for f in range(nft):
            fs = slice(f * FT, (f + 1) * FT)
            ks = slice(W + f * FT, W + (f + 1) * FT)
            v.wait_ge(s_pe.h, L["rotk"])
            v.wait_ge(s_dv.h, s_dv.n)
            v.tensor_tensor(tmp1[:, 0:FT], kext[:, ks], cosb[:, fs], ALU.mult
                            ).then_inc(s_dv.h, 1)
            s_dv.n += 1
            v.wait_ge(s_dv.h, s_dv.n)
            v.tensor_tensor(tmp2[:, 0:FT], sinb[:, fs], pb[f][:], ALU.mult).then_inc(s_dv.h, 1)
            s_dv.n += 1
            v.wait_ge(s_dv.h, s_dv.n)
            v.tensor_tensor(kext[:, ks], tmp1[:, 0:FT], tmp2[:, 0:FT], ALU.add
                            ).then_inc(s_dv.h, 1)
            s_dv.n += 1
        assert s_dv.n == L["kcmb"][nft - 1]

    @block.gpsimd
    def _(g):
        g.wait_ge(s_dv.h, L["kcmb"][nft - 1])
        g.dma_start(kh_in.ap(), kext[:, 2 * W:3 * W]).then_inc(s_ag.h, 16)
        s_ag.n += 16
        g.wait_ge(s_ag.h, s_ag.n)
        g.dma_start(vh_in.ap().rearrange("(c p) h -> p c h", p=128), vext[:, 16:24, :]
                    ).then_inc(s_ag.h, 16)
        s_ag.n += 16
        g.wait_ge(s_ag.h, s_ag.n)
        g.collective_compute("AllGather", ALU.bypass, replica_groups=RG,
                             ins=[kh_in.ap().opt()], outs=[kh_out.ap().opt()]
                             ).then_inc(s_cc.h, 1)
        s_cc.n += 1
        g.collective_compute("AllGather", ALU.bypass, replica_groups=RG,
                             ins=[vh_in.ap().opt()], outs=[vh_out.ap().opt()]
                             ).then_inc(s_cc.h, 1)
        s_cc.n += 1
        g.wait_ge(s_cc.h, s_cc.n)
        g.dma_start(kext[:, 0:W], kh_out.ap()[0:128, :]).then_inc(s_ag.h, 16)
        s_ag.n += 16
        g.wait_ge(s_ag.h, s_ag.n)
        g.dma_start(vext[:, 0:8, :], vh_out.ap()[0:W, :].rearrange("(c p) h -> p c h", p=128)
                    ).then_inc(s_ag.h, 16)
        s_ag.n += 16
        g.wait_ge(s_ag.h, s_ag.n)
    ag_done = s_ag.n

    @block.scalar
    def _(sc):
        sc.wait_ge(s_dv.h, L["kcmb"][nft - 1])
        sc.activation(xn[:], qb[:], AF.Square).then_inc(s_ac.h, 1)
        s_ac.n += 1
        sc.wait_ge(s_ag.h, ag_done)
        sc.activation(Pbuf[:].rearrange("p c t -> p (c t)")[:, 0:3 * W], kext[:], AF.Square
                      ).then_inc(s_ac.h, 1)
        s_ac.n += 1
        sc.wait_ge(s_dv.h, L["mmul"])
        sc.activation(mrow[0:1, 0:QH], mrow[0:1, 0:QH], AF.Sqrt).then_inc(s_ac.h, 1)
        s_ac.n += 1
        sc.wait_ge(s_ac.h, s_ac.n)
        sc.mul(mrow[0:1, 0:QH], mrow[0:1, 0:QH], -ISQ).then_inc(s_ac.h, 1)
        s_ac.n += 1
        assert s_ac.n == L["mneg"]

    @block.tensor
    def _(t):
        t.wait_ge(s_ac.h, L["qsqact"])
        for hh in range(QH):
            if hh >= 1:
                t.wait_ge(s_dv.h, L["qmax"][hh - 1])
            for f in range(nft):
                mm = t.matmul(pA[0:1, f * FT:(f + 1) * FT], onescol_bf[:],
                              xn[:, hh, f * FT:(f + 1) * FT], start=True, stop=True)
            mm.then_inc(s_pe.h, 1)
            s_pe.n += 1
        t.wait_ge(s_ac.h, L["ksqact"])
        t.wait_ge(s_dv.h, L["qmax"][QH - 1])
        ksq = Pbuf[:].rearrange("p c t -> p (c t)")
        for f in range(4):
            mm = t.matmul(pB[0:1, f * FT:(f + 1) * FT], onescol_bf[:],
                          ksq[:, f * FT:(f + 1) * FT], start=True, stop=True)
        mm.then_inc(s_pe.h, 1)
        s_pe.n += 1
        assert s_pe.n == L["ksqa"]
        t.wait_ge(s_dv.h, L["kmaxa"])
        for f in range(2):
            mm = t.matmul(pB[0:1, f * FT:(f + 1) * FT], onescol_bf[:],
                          ksq[:, 2048 + f * FT:2048 + (f + 1) * FT], start=True, stop=True)
        mm.then_inc(s_pe.h, 1)
        s_pe.n += 1
        assert s_pe.n == L["ksqb"]
        t.wait_ge(s_ac.h, L["mneg"])
        t.matmul(pA[:, 0:QH], onesrow_f[:], mrow[0:1, 0:QH], start=True, stop=True
                 ).then_inc(s_pe.h, 1)
        s_pe.n += 1
        assert s_pe.n == L["mb"]

    @block.vector
    def _(v):
        for hh in range(QH):
            v.wait_ge(s_pe.h, L["qsq"][hh])
            v.reduce_max(mrow[0:1, hh:hh + 1], pA[0:1, :], axis=AX.X).then_inc(s_dv.h, 1)
            s_dv.n += 1
        v.wait_ge(s_pe.h, L["ksqa"])
        v.reduce_max(mrow[0:1, 8:9], pB[0:1, :], axis=AX.X).then_inc(s_dv.h, 1)
        s_dv.n += 1
        assert s_dv.n == L["kmaxa"]
        v.wait_ge(s_pe.h, L["ksqb"])
        v.reduce_max(mrow[0:1, 9:10], pB[0:1, 0:2 * FT], axis=AX.X).then_inc(s_dv.h, 1)
        s_dv.n += 1
        v.wait_ge(s_dv.h, s_dv.n)
        v.tensor_tensor(mrow[0:1, 8:9], mrow[0:1, 8:9], mrow[0:1, 9:10], ALU.max
                        ).then_inc(s_dv.h, 1)
        s_dv.n += 1
        assert s_dv.n == L["kmaxc"]
        v.wait_ge(s_dv.h, s_dv.n)
        v.tensor_scalar(mrow[0:1, 0:QH], mrow[0:1, 0:QH], mrow[0:1, 8:9], None, ALU.mult
                        ).then_inc(s_dv.h, 1)
        s_dv.n += 1
        assert s_dv.n == L["mmul"]
        v.wait_ge(s_pe.h, L["mb"])
        v.tensor_copy(mcol[:], pA[:, 0:QH]).then_inc(s_dv.h, 1)
        s_dv.n += 1
        assert s_dv.n == L["mcolev"]

    @block.tensor
    def _(t):
        for gi, g in enumerate(groups):
            w, h, qt = g
            st = subtiles(qt)
            if w == 0:
                t.wait_ge(s_ag.h, ag_done)
            for si, (ki, mk) in enumerate(st):
                bk = si % 6
                if si >= 6:
                    t.wait_ge(s_ac.h, L["ev"][g][si - 6])
                elif gi >= 1:
                    t.wait_ge(s_ac.h, L["ev"][groups[gi - 1]][6 + si])
                else:
                    t.wait_ge(s_dv.h, L["mcolev"])
                mm = t.matmul(pb[bk][:], kext[:, w * W + ki * 128:w * W + (ki + 1) * 128],
                              qb[:, h, w * W + qt * FT:w * W + (qt + 1) * FT],
                              start=True, stop=True)
                mm.then_inc(s_pe.h, 1)
                s_pe.n += 1
            assert s_pe.n == L["S"][g][-1]
            t.wait_ge(s_dv.h, L["msk"][g])
            if gi >= 1:
                t.wait_ge(s_dv.h, L["bev"][groups[gi - 1]])
            for si in range(12):
                mm = t.matmul(pb[6][0:1, :], onescol_bf[:], Pbuf[:, si, :],
                              start=(si == 0), stop=(si == 11))
            mm.then_inc(s_pe.h, 1)
            s_pe.n += 1
            assert s_pe.n == L["Lo"][g]
            if gi >= 1:
                t.wait_ge(s_dv.h, L["oev"][groups[gi - 1]])
            for si, (ki, mk) in enumerate(st):
                mm = t.matmul(pb[7][:], vext[:, w * 8 + ki, :], Pbuf[:, si, :],
                              start=(si == 0), stop=(si == 11))
            mm.then_inc(s_pe.h, 1)
            s_pe.n += 1
            assert s_pe.n == L["PV"][g]
            t.wait_ge(s_dv.h, L["brev"][g])
            t.matmul(pb[6][:], onesrow_bf[:], lrowb[0:1, :], start=True, stop=True
                     ).then_inc(s_pe.h, 1)
            s_pe.n += 1
            assert s_pe.n == L["rb"][g]

    @block.scalar
    def _(sc):
        for gi, g in enumerate(groups):
            w, h, qt = g
            for si in range(12):
                sc.wait_ge(s_pe.h, L["S"][g][si])
                if gi >= 1:
                    sc.wait_ge(s_pe.h, L["PV"][groups[gi - 1]])
                sc.activation(Pbuf[:, si, :], pb[si % 6][:], AF.Exp, scale=ISQ,
                              bias=mcol[:, h:h + 1]).then_inc(s_ac.h, 1)
                s_ac.n += 1
            assert s_ac.n == L["ev"][g][-1]

    @block.vector
    def _(v):
        v.wait_ge(s_tab.h, 80)
        for gi, g in enumerate(groups):
            w, h, qt = g
            st = subtiles(qt)
            for si, (ki, mk) in enumerate(st):
                left = (w * W + ki * 128) < W
                if mk is not None:
                    v.wait_ge(s_ac.h, L["ev"][g][si])
                    mt = mLb if mk[0] == "L" else mRb
                    v.tensor_tensor(Pbuf[:, si, :], Pbuf[:, si, :], mt[:, mk[1], :], ALU.mult
                                    ).then_inc(s_dv.h, 1)
                    s_dv.n += 1
                if w == 0 and left:
                    v.wait_ge(s_ac.h, L["ev"][g][si])
                    v.wait_ge(s_dv.h, s_dv.n)
                    v.tensor_scalar(Pbuf[:, si, :], Pbuf[:, si, :], iscl[:, 0:1], None,
                                    ALU.mult).then_inc(s_dv.h, 1)
                    s_dv.n += 1
            assert s_dv.n == L["msk"][g]
            v.wait_ge(s_pe.h, L["Lo"][g])
            v.reciprocal(lrow[0:1, :], pb[6][0:1, :]).then_inc(s_dv.h, 1)
            s_dv.n += 1
            v.wait_ge(s_dv.h, s_dv.n)
            v.tensor_copy(lrowb[0:1, :], lrow[0:1, :]).then_inc(s_dv.h, 1)
            s_dv.n += 1
            assert s_dv.n == L["brev"][g]
            v.wait_ge(s_pe.h, L["rb"][g])
            v.tensor_copy(brec[:], pb[6][:]).then_inc(s_dv.h, 1)
            s_dv.n += 1
            assert s_dv.n == L["bev"][g]
            v.wait_ge(s_pe.h, L["PV"][g])
            v.wait_ge(s_dv.h, s_dv.n)
            v.tensor_tensor(Obuf[:, h, w * W + qt * FT:w * W + (qt + 1) * FT],
                            pb[7][:], brec[:], ALU.mult).then_inc(s_dv.h, 1)
            s_dv.n += 1
            assert s_dv.n == L["oev"][g]

    @block.tensor
    def _(t):
        t.wait_ge(s_dv.h, L["oev"][groups[-1]])
        for i in range(NC):
            wq4, wl = divmod(i, 2)
            t.wait_ge(s_wp[wq4 % 2].h, 48 + 16 * (wq4 // 2 + 1))
            for f in range(nft):
                j = i * nft + f
                jj = j % 8
                if j >= 8:
                    t.wait_ge(s_dv.h, L["wadd"][j - 8])
                for k in range(NC):
                    mm = t.matmul(pb[jj][:], wpc[:, wq4 % 2, k, wl * 128:(wl + 1) * 128],
                                  Obuf[:, k, f * FT:(f + 1) * FT],
                                  start=(k == 0), stop=(k == NC - 1))
                mm.then_inc(s_pe.h, 1)
                s_pe.n += 1

    @block.gpsimd
    def _(g):
        for q in range(4):
            s = q % 2
            if q < 2:
                g.wait_ge(s_pe.h, L["qproj"][(q + 2 + 1) * 2 * nft - 1])
            else:
                g.wait_ge(s_pe.h, L["wout"][(q - 2 + 1) * 2 * nft - 1])
            g.dma_start(wpc[:, s, :, :],
                        woTd.rearrange("(c p) o -> p c o", p=128)[:, :, q * 256:(q + 1) * 256]
                        ).then_inc(s_wp[s].h, 16)
            s_wp[s].n += 16

    @block.vector
    def _(v):
        for i in range(NC):
            for f in range(nft):
                j = i * nft + f
                v.wait_ge(s_pe.h, L["wout"][j])
                v.tensor_tensor(xa[:, i, f * FT:(f + 1) * FT], xa[:, i, f * FT:(f + 1) * FT],
                                pb[j % 8][:], ALU.add).then_inc(s_dv.h, 1)
                s_dv.n += 1
        assert s_dv.n == L["wadd"][-1]

    @block.sync
    def _(sync):
        sync.wait_ge(s_dv.h, L["wadd"][-1])
        sync.dma_start(xa_out.rearrange("(c p) t -> p c t", p=128), xa[:]).then_inc(s_sy.h, 16)
        s_sy.n += 16
        sync.wait_ge(s_sy.h, s_sy.n)



def emit_moe(b, D, T=T_CORE):
    nc = b.nc
    ntc = T // 128

    xT_d = D["xT"]
    xnb_d = D["xnb"]
    xa_d = D["xa"]
    gwT_d = D["gwT"]
    w1_d = D["w1"]
    b1_d = D["b1c"]
    w2_d = D["w2"]
    b2_d = D["b2c"]
    sel_d = D["sel"]
    ident_d = D["identf"]
    rs_in = [nc.dram_tensor(b.pfx + f"rs_in{i}", [1024, TH], F32) for i in range(2)]
    rs_out = [nc.dram_tensor(b.pfx + f"rs_out{i}", [2048, TH], F32) for i in range(2)]

    ACC = b.sb("ACC", [128, NC, T])
    xnh = b.sb("xnh", [128, NC, TH], BF16)
    hid = b.sb("hid", [128, NF, TH], BF16)
    w1r = b.sb("w1r", [128, 2, NC, 512], BF16)
    w2r = b.sb("w2r", [128, 2, NF, 128], BF16)
    gwb = b.sb("gwb", [128, NC, E], BF16)
    selb = b.sb("selb", [8, E * 128])
    wTg = b.sb("wTg", [8, T])
    lg = b.sb("lg", [128, 16, E])
    p2s = b.sb("p2s", [128, E])
    lcol = b.sb("lcol", [128, 4])
    b1c = b.sb("b1cs", [128, E, NF])
    b2c = b.sb("b2cs", [128, E, 8])
    wbc = b.sb("wbc", [128, 2, FT])
    tmpv = b.sb("tmpv", [128, FT])
    xtr = b.sb("xtr", [128, 2, TH])
    identf = b.sb("idf", [128, 128])
    onescol_bf = b.sb("onescol_bf", [128, 1], BF16)

    pA = b.psum("pA", (128, 2048))
    pB = b.psum("pB", (128, 2048))
    pb = [pA[:, f * FT:(f + 1) * FT] for f in range(4)] + \
         [pB[:, f * FT:(f + 1) * FT] for f in range(4)]

    s_sy = b.sem("s_sy")
    s_xh = b.sem("s_xh")
    s_w1 = [b.sem(f"s_w1{i}") for i in range(2)]
    s_w2 = [b.sem(f"s_w2{i}") for i in range(2)]
    s_xr = [b.sem(f"s_xr{i}") for i in range(2)]
    s_pe = b.sem("s_pe")
    s_ac = b.sem("s_ac")
    s_dv = b.sem("s_dv")
    s_gp = b.sem("s_gp")
    s_rs = [b.sem(f"s_rs{i}") for i in range(2)]
    s_rsc = [b.sem(f"s_rsc{i}") for i in range(2)]
    b.rs_out = rs_out
    b.rs_cc = s_rsc

    L = {}
    pe = 0
    L["glog"] = [0] * ntc
    L["gtr"] = [0] * ntc
    for tc in range(ntc):
        pe += 1; L["glog"][tc] = pe
        if tc >= 1:
            pe += 1; L["gtr"][tc - 1] = pe
    pe += 1; L["gtr"][ntc - 1] = pe
    HG = {}
    YB = {}
    YG = {}
    for th in range(2):
        for e in range(E):
            for fc in range(NF):
                for tt in range(2):
                    pe += 1; HG[(th, e, fc, tt)] = pe
            for tt in range(2):
                pe += 1; YB[(th, e, tt)] = pe
            for i in range(8):
                for tt in range(2):
                    pe += 1; YG[(th, e, i, tt)] = pe
    L["pe_end"] = pe

    ac = 0
    GEX = {}
    for tc in range(ntc):
        ac += 1; GEX[tc] = ac
    REL = {}
    for th in range(2):
        for e in range(E):
            for fc in range(NF):
                for tt in range(2):
                    ac += 1; REL[(th, e, fc, tt)] = ac
    L["ac_end"] = ac

    dv = 0
    GEV = {}
    GSM = {}
    GTV = {}
    for tc in range(ntc):
        dv += 1; GEV[tc] = dv
        dv += 11; GSM[tc] = dv
        dv += 1; GTV[tc] = dv
    WBE = {}
    YEV = {}
    XAD = {}
    for th in range(2):
        for e in range(E):
            for tt in range(2):
                dv += 1; WBE[(th, e, tt)] = dv
            for i in range(8):
                for tt in range(2):
                    dv += 2; YEV[(th, e, i, tt)] = dv
        for kk in range(NC):
            dv += 1; XAD[th * NC + kk] = dv
    L["dv_end"] = dv

    block = b.block()

    @block.sync
    def _(sync):
        sync.dma_start(ACC[:, 0:4, :], xa_d.rearrange("(c p) t -> p c t", p=128)[:, 0:4, :]
                       ).then_inc(s_sy.h, 16)
        s_sy.n += 16
        sync.dma_start(b1c[:], b1_d).then_inc(s_sy.h, 16)
        s_sy.n += 16
        sync.dma_start(b2c[:], b2_d).then_inc(s_sy.h, 16)
        s_sy.n += 16
        sync.dma_start(selb[:], sel_d).then_inc(s_sy.h, 16)
        s_sy.n += 16
        sync.dma_start(identf[:], ident_d).then_inc(s_sy.h, 16)
        s_sy.n += 16

    @block.scalar
    def _(sc):
        sc.dma_start(ACC[:, 4:8, :], xa_d.rearrange("(c p) t -> p c t", p=128)[:, 4:8, :]
                     ).then_inc(s_sy.h, 16)
        s_sy.n += 16
    ld_done = s_sy.n

    @block.gpsimd
    def _(g):
        g.dma_start(gwb[:], gwT_d.rearrange("(c p) e -> p c e", p=128)).then_inc(s_gp.h, 16)
        s_gp.n += 16
        def xnh_load(m, hf):
            if m == 1:
                g.wait_ge(s_pe.h, L["glog"][7])
            elif m == 2:
                g.wait_ge(s_pe.h, L["glog"][15])
            elif m == 3:
                g.wait_ge(s_pe.h, YG[(0, E - 1, 7, 1)])
            g.dma_start(xnh[:], xnb_d.rearrange("(c p) t -> p c t", p=128)[:, :, hf * TH:(hf + 1) * TH]
                        ).then_inc(s_xh.h, 16)
            s_xh.n += 16
        xnh_load(0, 0)
        xnh_load(1, 1)
        xnh_load(2, 0)
        for th in range(2):
            if th == 1:
                xnh_load(3, 1)
            for e in range(E):
                for q in range(8):
                    s = q % 2
                    m1 = (th * E + e) * 8 + q
                    if m1 >= 2:
                        qp, ep, thp = (m1 - 2) % 8, ((m1 - 2) // 8) % E, (m1 - 2) // (8 * E)
                        g.wait_ge(s_pe.h, HG[(thp, ep, qp * 4 + 3, 1)])
                    g.dma_start(w1r[:, s, :, :],
                                w1_d[e].rearrange("(c p) f -> p c f", p=128)[:, :, q * 512:(q + 1) * 512]
                                ).then_inc(s_w1[s].h, 16)
                    s_w1[s].n += 16
                for i in range(8):
                    s = i % 2
                    m2 = (th * E + e) * 8 + i
                    if m2 >= 2:
                        ip, ep, thp = (m2 - 2) % 8, ((m2 - 2) // 8) % E, (m2 - 2) // (8 * E)
                        g.wait_ge(s_pe.h, YG[(thp, ep, ip, 1)])
                    g.dma_start(w2r[:, s, :, :],
                                w2_d[e].rearrange("(c p) h -> p c h", p=128)[:, :, i * 128:(i + 1) * 128]
                                ).then_inc(s_w2[s].h, 16)
                    s_w2[s].n += 16
        for th in range(2):
            g.wait_ge(s_rs[th].h, 64)
            g.collective_compute("AllGather", ALU.bypass, replica_groups=RG,
                                 ins=[rs_in[th].ap().opt()], outs=[rs_out[th].ap().opt()]
                                 ).then_inc(s_rsc[th].h, 1)
            s_rsc[th].n += 1

    def emit_logits(t, tc):
        hf = tc // 8
        tcl = tc % 8
        t.wait_ge(s_xh.h, 16 * (hf + 1))
        if tc >= 4:
            t.wait_ge(s_dv.h, GEV[tc - 4])
        bank = pb[tc % 4]
        col = (tc // 4) * 8
        for k in range(NC):
            mm = t.matmul(bank[:, col:col + 8], xnh[:, k, tcl * 128:(tcl + 1) * 128],
                          gwb[:, k, :], start=(k == 0), stop=(k == NC - 1))
        mm.then_inc(s_pe.h, 1)
        s_pe.n += 1
        assert s_pe.n == L["glog"][tc]

    def emit_gtr(t, tc):
        t.wait_ge(s_dv.h, GSM[tc])
        if tc >= 4:
            t.wait_ge(s_dv.h, GTV[tc - 4])
        t.transpose(pb[4 + tc % 4][0:8, 0:128], lg[:, tc, :], identf[:]).then_inc(s_pe.h, 1)
        s_pe.n += 1
        assert s_pe.n == L["gtr"][tc]

    @block.tensor
    def _(t):
        t.wait_ge(s_gp.h, 16)
        t.wait_ge(s_sy.h, ld_done)
        for tc in range(ntc):
            emit_logits(t, tc)
            if tc >= 1:
                emit_gtr(t, tc - 1)
        emit_gtr(t, ntc - 1)

    @block.scalar
    def _(sc):
        for tc in range(ntc):
            sc.wait_ge(s_dv.h, GSM[tc] - 9)
            sc.activation(lg[:, tc, :], lg[:, tc, :], AF.Exp, bias=lcol[:, 0:1]
                          ).then_inc(s_ac.h, 1)
            s_ac.n += 1
            assert s_ac.n == GEX[tc]

    @block.vector
    def _(v):
        v.memset(onescol_bf[:], 1.0)
        for tc in range(ntc):
            v.wait_ge(s_pe.h, L["glog"][tc])
            v.tensor_copy(lg[:, tc, :], pb[tc % 4][:, (tc // 4) * 8:(tc // 4) * 8 + 8]
                          ).then_inc(s_dv.h, 1)
            s_dv.n += 1
            assert s_dv.n == GEV[tc]
            v.wait_ge(s_dv.h, s_dv.n)
            v.reduce_max(lcol[:, 0:1], lg[:, tc, :], axis=AX.X).then_inc(s_dv.h, 1)
            s_dv.n += 1
            v.wait_ge(s_dv.h, s_dv.n)
            v.tensor_scalar(lcol[:, 0:1], lcol[:, 0:1], -1.0, None, ALU.mult).then_inc(s_dv.h, 1)
            s_dv.n += 1
            v.wait_ge(s_ac.h, GEX[tc])
            v.reduce_sum(lcol[:, 1:2], lg[:, tc, :], axis=AX.X).then_inc(s_dv.h, 1)
            s_dv.n += 1
            v.wait_ge(s_dv.h, s_dv.n)
            v.reciprocal(lcol[:, 1:2], lcol[:, 1:2]).then_inc(s_dv.h, 1)
            s_dv.n += 1
            v.wait_ge(s_dv.h, s_dv.n)
            v.tensor_scalar(lg[:, tc, :], lg[:, tc, :], lcol[:, 1:2], None, ALU.mult
                            ).then_inc(s_dv.h, 1)
            s_dv.n += 1
            v.wait_ge(s_dv.h, s_dv.n)
            v.reduce_max(lcol[:, 2:3], lg[:, tc, :], axis=AX.X).then_inc(s_dv.h, 1)
            s_dv.n += 1
            v.wait_ge(s_dv.h, s_dv.n)
            v.tensor_scalar(p2s[:], lg[:, tc, :], lcol[:, 2:3], None, ALU.is_ge
                            ).then_inc(s_dv.h, 1)
            s_dv.n += 1
            v.wait_ge(s_dv.h, s_dv.n)
            v.scalar_tensor_tensor(p2s[:], p2s[:], -BIG, lg[:, tc, :], ALU.mult, ALU.add
                                   ).then_inc(s_dv.h, 1)
            s_dv.n += 1
            v.wait_ge(s_dv.h, s_dv.n)
            v.reduce_max(lcol[:, 3:4], p2s[:], axis=AX.X).then_inc(s_dv.h, 1)
            s_dv.n += 1
            v.wait_ge(s_dv.h, s_dv.n)
            v.tensor_scalar(p2s[:], lg[:, tc, :], lcol[:, 3:4], None, ALU.is_ge
                            ).then_inc(s_dv.h, 1)
            s_dv.n += 1
            v.wait_ge(s_dv.h, s_dv.n)
            v.tensor_tensor(lg[:, tc, :], lg[:, tc, :], p2s[:], ALU.mult).then_inc(s_dv.h, 1)
            s_dv.n += 1
            assert s_dv.n == GSM[tc]
            v.wait_ge(s_pe.h, L["gtr"][tc])
            v.tensor_copy(wTg[0:8, tc * 128:(tc + 1) * 128],
                          pb[4 + tc % 4][0:8, 0:128]).then_inc(s_dv.h, 1)
            s_dv.n += 1
            assert s_dv.n == GTV[tc]

    @block.tensor
    def _(t):
        t.wait_ge(s_dv.h, GTV[ntc - 1])
        for th in range(2):
            t.wait_ge(s_xh.h, 16 * (th + 3))
            for e in range(E):
                for q in range(8):
                    s = q % 2
                    m1 = (th * E + e) * 8 + q
                    t.wait_ge(s_w1[s].h, 16 * (m1 // 2 + 1))
                    for icl in range(4):
                        fc = q * 4 + icl
                        for tt in range(2):
                            gidx = (fc * 2 + tt)
                            bk = gidx % 6
                            prev = gidx - 6
                            if prev >= 0:
                                t.wait_ge(s_ac.h, REL[(th, e, prev // 2, prev % 2)])
                            elif (th, e) != (0, 0):
                                pth, pe_ = (th, e - 1) if e > 0 else (th - 1, E - 1)
                                pgidx = 64 + prev
                                t.wait_ge(s_ac.h, REL[(pth, pe_, pgidx // 2, pgidx % 2)])
                                t.wait_ge(s_dv.h, YEV[(pth, pe_, 7, 1)])
                            for k in range(NC):
                                mm = t.matmul(pb[bk][:], w1r[:, s, k, icl * 128:(icl + 1) * 128],
                                              xnh[:, k, tt * FT:(tt + 1) * FT],
                                              start=(k == 0), stop=(k == NC - 1))
                            mm.then_inc(s_pe.h, 1)
                            s_pe.n += 1
                            assert s_pe.n == HG[(th, e, fc, tt)]
                for tt in range(2):
                    t.wait_ge(s_dv.h, GTV[ntc - 1] if (th, e, tt) == (0, 0, 0) else
                              WBE[(th, e, tt - 1)] if tt > 0 else
                              (WBE[(th, e - 1, 1)] if e > 0 else WBE[(th - 1, E - 1, 1)]))
                    t.matmul(pb[6][:], selb[:, e * 128:(e + 1) * 128],
                             wTg[0:8, th * TH + tt * FT:th * TH + (tt + 1) * FT],
                             start=True, stop=True).then_inc(s_pe.h, 1)
                    s_pe.n += 1
                    assert s_pe.n == YB[(th, e, tt)]
                for i in range(8):
                    s = i % 2
                    m2 = (th * E + e) * 8 + i
                    t.wait_ge(s_w2[s].h, 16 * (m2 // 2 + 1))
                    for tt in range(2):
                        ygi = i * 2 + tt
                        bk = ygi % 6
                        prev = ygi - 6
                        if prev >= 0:
                            t.wait_ge(s_dv.h, YEV[(th, e, prev // 2, prev % 2)])
                        else:
                            pg = 58 + bk
                            t.wait_ge(s_ac.h, REL[(th, e, pg // 2, pg % 2)])
                        t.wait_ge(s_ac.h, REL[(th, e, NF - 1, 1)])
                        for k in range(NF):
                            mm = t.matmul(pb[bk][:], w2r[:, s, k, :],
                                          hid[:, k, tt * FT:(tt + 1) * FT],
                                          start=(k == 0), stop=(k == NF - 1))
                        mm.then_inc(s_pe.h, 1)
                        s_pe.n += 1
                        assert s_pe.n == YG[(th, e, i, tt)]

    @block.scalar
    def _(sc):
        for th in range(2):
            for e in range(E):
                for fc in range(NF):
                    for tt in range(2):
                        gidx = fc * 2 + tt
                        bk = gidx % 6
                        sc.wait_ge(s_pe.h, HG[(th, e, fc, tt)])
                        if (th, e) != (0, 0) and fc == 0 and tt == 0:
                            pth, pe_ = (th, e - 1) if e > 0 else (th - 1, E - 1)
                            sc.wait_ge(s_pe.h, YG[(pth, pe_, 7, 1)])
                        sc.activation(hid[:, fc, tt * FT:(tt + 1) * FT], pb[bk][:], AF.Relu,
                                      bias=b1c[:, e, fc:fc + 1]).then_inc(s_ac.h, 1)
                        s_ac.n += 1
                        assert s_ac.n == REL[(th, e, fc, tt)]

    @block.vector
    def _(v):
        for th in range(2):
            for e in range(E):
                for tt in range(2):
                    v.wait_ge(s_pe.h, YB[(th, e, tt)])
                    v.tensor_copy(wbc[:, tt, :], pb[6][:]).then_inc(s_dv.h, 1)
                    s_dv.n += 1
                    assert s_dv.n == WBE[(th, e, tt)]
                for i in range(8):
                    for tt in range(2):
                        v.wait_ge(s_pe.h, YG[(th, e, i, tt)])
                        ygi = i * 2 + tt
                        bk = ygi % 6
                        v.wait_ge(s_dv.h, s_dv.n)
                        v.scalar_tensor_tensor(tmpv[:], pb[bk][:], b2c[:, e, i:i + 1],
                                               wbc[:, tt, :], ALU.add, ALU.mult).then_inc(s_dv.h, 1)
                        s_dv.n += 1
                        v.wait_ge(s_dv.h, s_dv.n)
                        v.tensor_tensor(ACC[:, i, th * TH + tt * FT:th * TH + (tt + 1) * FT],
                                        ACC[:, i, th * TH + tt * FT:th * TH + (tt + 1) * FT],
                                        tmpv[:], ALU.add).then_inc(s_dv.h, 1)
                        s_dv.n += 1
                        assert s_dv.n == YEV[(th, e, i, tt)]
            for kk in range(NC):
                k = th * NC + kk
                sl = k % 2
                v.wait_ge(s_xr[sl].h, 16 * (k // 2 + 1))
                v.tensor_tensor(ACC[:, kk, th * TH:(th + 1) * TH],
                                ACC[:, kk, th * TH:(th + 1) * TH], xtr[:, sl, :], ALU.add
                                ).then_inc(s_dv.h, 1)
                s_dv.n += 1
                assert s_dv.n == XAD[k]

    @block.sync
    def _(sync):
        def rs_store(th):
            sync.wait_ge(s_dv.h, XAD[th * NC + NC - 1])
            for s in range(2):
                for a in range(2):
                    sync.dma_start(
                        rs_in[th].ap().rearrange("(c p) t -> p c t", p=128
                            )[:, :, s * 512 + a * 256:s * 512 + (a + 1) * 256],
                        ACC[:, :, th * TH + (2 * a + s) * 256:th * TH + (2 * a + s + 1) * 256]
                    ).then_inc(s_rs[th].h, 16)
                    s_rs[th].n += 16
        for k in range(2 * NC):
            sl = k % 2
            th, ch = divmod(k, NC)
            if k >= 2:
                sync.wait_ge(s_dv.h, XAD[k - 2])
            sync.dma_start(xtr[:, sl, :],
                           xT_d.rearrange("(c p) t -> p c t", p=128)[:, ch, th * TH:(th + 1) * TH]
                           ).then_inc(s_xr[sl].h, 16)
            s_xr[sl].n += 16
            if k == NC - 1:
                rs_store(0)
        rs_store(1)




def emit_l1(b, D, T=T_CORE):
    nc = b.nc
    nft = T // FT

    iscol_d = D["iscol"]
    wq_d = D["wq"]
    wk_d = D["wk"]
    wv_d = D["wv"]
    bq_d = D["bqc"]
    bk_d = D["bkc"]
    wf_d = D["wf"]
    a1_d = D["a1"]
    a2_d = D["a2"]
    fb_d = D["fbc"]
    mk_d = D["mks"]
    id_d = D["identf"]
    bvrow_d = D["bvrow"]
    km_in = nc.dram_tensor(b.pfx + "km_in", [1, 16], F32)
    km_out = nc.dram_tensor(b.pfx + "km_out", [1, 16], F32)
    out_o = D["out"]

    k_in = nc.dram_tensor(b.pfx + "k_in", [1024, T], BF16)
    k_out = nc.dram_tensor(b.pfx + "k_out", [2048, T], BF16)
    v_in = nc.dram_tensor(b.pfx + "v_in", [T, 16 * 65], BF16)
    v_out = nc.dram_tensor(b.pfx + "v_out", [2 * T, 16 * 65], BF16)

    X = b.sb("X", [128, NC, T])
    xn = b.sb("xn", [128, NC, T], BF16)
    AQ = b.sb("AQ", [128, NC * T], BF16)
    AK = b.sb("AK", [128, NC * T], BF16)
    AV = b.sb("AV", [128, 16 * 16 * 65], BF16)
    wpc = b.sb("wpc", [128, 2, NC, 256], BF16)
    bvb = b.sb("bvb", [128, 1024], BF16)
    ttr = b.sb("ttr", [128, 2, FT])
    mrow = b.sb("mrow", [1, 3 * 16])
    mcol = b.sb("mcol", [128, 16])
    bqc = b.sb("bqcs", [128, 8])
    bkc = b.sb("bkcs", [128, 8])
    fbc = b.sb("fbcs", [128, 8])
    epsc = b.sb("epsc", [1, 1])
    onescol_bf = b.sb("onescol_bf", [128, 1], BF16)
    onesrow_f = b.sb("onesrow_f", [1, 128])
    onesrow_bf = b.sb("onesrow_bf", [1, 128], BF16)

    qb = AQ[:].rearrange("p (c t) -> p c t", c=NC)
    identf = bvb[:, 0:256].bitcast(F32)
    brec = ttr[:, 0, :].bitcast(BF16)[:, 0:FT]
    lrowb = ttr[:, 1, :].bitcast(BF16)[0:1, 0:FT]
    otr = AQ[:, 0:4096].bitcast(F32).rearrange("p (s t) -> p s t", s=2)
    h1 = AK[:].rearrange("p (c t) -> p c t", c=NC)
    kown = h1
    Pbuf = AK[:].rearrange("p (c t) -> p c t", c=32)
    h2 = AV[:, 0:NC * T].rearrange("p (c t) -> p c t", c=NC)
    vt65 = AV[:].rearrange("p (c d) -> p c d", c=16)
    khr = AV[:, 0:2 * 4096].rearrange("p (s t) -> p s t", s=2)
    vhr = AV[:, 2 * 4096:2 * 4096 + 2 * 32 * 65].rearrange("p (s c d) -> p s c d", s=2, c=32)
    mks = AV[:, 2 * 4096 + 2 * 32 * 65:2 * 4096 + 2 * 32 * 65 + 8 * 512].rearrange(
        "p (m q) -> p m q", m=8)
    AKf = AK[:].bitcast(F32)
    arow = AKf[0:1, 0:2048]
    brow = AKf[0:1, 2048:4096]
    bvrow = AKf[0:1, 4096:5120]
    lrow = ttr[0:1, 0, :]
    sqscr = xn[:].rearrange("p c t -> p (c t)")[:, 0:2048]
    O = xn
    xs = AQ[:].bitcast(F32)[:, 0:2048].rearrange("p (a b u) -> p a b u", a=2, b=2)
    iscl = b.sb("iscl", [128, 1])

    pA = b.psum("pA", (128, 2048))
    pB = b.psum("pB", (128, 2048))
    pb = [pA[:, f * FT:(f + 1) * FT] for f in range(4)] + \
         [pB[:, f * FT:(f + 1) * FT] for f in range(4)]

    s_sy = b.sem("s_sy")
    s_wp = [b.sem(f"s_wp{i}") for i in range(2)]
    s_kh = [b.sem(f"s_kh{i}") for i in range(2)]
    s_vh = [b.sem(f"s_vh{i}") for i in range(2)]
    s_cc = b.sem("s_cc")
    s_ag = b.sem("s_ag")
    s_is = b.sem("s_is")
    s_xs = [b.sem(f"s_xs{i}") for i in range(2)]
    s_sel = b.sem("s_sel")
    s_pe = b.sem("s_pe")
    s_ac = b.sem("s_ac")
    s_dv = b.sem("s_dv")
    s_gp = b.sem("s_gp")
    s_ot = [b.sem(f"s_ot{i}") for i in range(2)]

    groups = [(h, it) for h in range(NHD) for it in range(4)]
    nki = lambda it: 8 * it + 8

    L = {"S": {}, "PV": {}, "rb": {}, "ev": {}, "msk": {}, "oev": {}, "rc": {},
         "brv": {}, "bev": {}}
    pe = 0
    pe += nft; L["sum"] = pe
    pe += nft; L["sumsq"] = pe
    pe += 2 * nft; L["lnbc"] = pe
    pe += 1; L["bvb"] = pe
    FF = {}
    for st in range(3):
        for i in range(NC):
            for f in range(nft):
                pe += 1; FF[(st, i, f)] = pe
    PQ = {}
    PK = {}
    PV_ = {}
    for i in range(NC):
        for f in range(nft):
            pe += 1; PQ[(i, f)] = pe
    for i in range(NC):
        for f in range(nft):
            pe += 1; PK[(i, f)] = pe
    for hv in range(2):
        for tc in range(16):
            pe += 1; PV_[(tc, hv)] = pe
    QSQP = {}
    for c in range(NC):
        for par in range(2):
            pe += 1; QSQP[(c, par)] = pe
    L["qsq1"] = pe
    KSQP = {}
    for c in range(NC):
        for par in range(2):
            pe += 1; KSQP[(c, par)] = pe
    L["ksq1"] = pe
    pe += 1; L["mb"] = pe
    for g in groups:
        h, it = g
        L["S"][g] = [pe + s + 1 for s in range(nki(it))]
        pe += nki(it)
        pe += 1; L["PV"][g] = pe
        pe += 1; L["rb"][g] = pe
    L["pe_attn_end"] = pe
    TRO = {}
    for tc in range(16):
        pe += 1; TRO[tc] = pe
    L["pe_end"] = pe

    ac = 0
    ac += 1; L["sq"] = ac
    ac += 1; L["lnsqrt"] = ac
    FTN = {}
    for i in range(NC):
        for f in range(nft):
            ac += 1; FTN[(i, f)] = ac
    PQE = {}
    PKE = {}
    for i in range(NC):
        for f in range(nft):
            ac += 1; PQE[(i, f)] = ac
    for i in range(NC):
        for f in range(nft):
            ac += 1; PKE[(i, f)] = ac
    QSQA = []
    for c in range(NC):
        ac += 1; QSQA.append(ac)
    L["qsqa"] = ac
    KSQA = []
    for c in range(NC):
        ac += 1; KSQA.append(ac)
    L["ksqa"] = ac
    ac += 1; L["msq"] = ac
    ac += 1; L["msq2"] = ac
    for g in groups:
        h, it = g
        L["ev"][g] = [ac + s + 1 for s in range(nki(it))]
        ac += nki(it)
    L["ac_end"] = ac

    dv = 0
    dv += 1; L["cast"] = dv
    dv += 1; L["srowcp"] = dv
    dv += 2; L["rowm"] = dv
    dv += 1; L["recipn"] = dv
    dv += 1; L["recip2"] = dv
    dv += 1; L["mrN"] = dv
    LNA = {}
    for kk in range(NC):
        dv += 2; LNA[kk] = dv
    dv += 1; L["bve"] = dv
    FFE = {}
    for st in range(2):
        for i in range(NC):
            for f in range(nft):
                dv += 1; FFE[(st, i, f)] = dv
    FAD = {}
    for i in range(NC):
        for f in range(nft):
            dv += 1; FAD[(i, f)] = dv
    dv += 1; L["vones"] = dv
    PVE = {}
    for hv in range(2):
        for tc in range(16):
            dv += 1; PVE[(tc, hv)] = dv
    QMX = {}
    for h in range(NHD):
        dv += 1; QMX[h] = dv
    L["qmax"] = dv
    KMX = {}
    for h in range(NHD):
        dv += 1; KMX[h] = dv
    L["kmax1"] = dv
    dv += 1; L["kmax2"] = dv
    dv += 1; L["mmul"] = dv
    dv += 1; L["mcev"] = dv
    for g in groups:
        h, it = g
        nm = min(8, nki(it))
        L["msk"][g] = dv + nm
        dv += nm
        dv += 1; L["rc"][g] = dv
        dv += 1; L["brv"][g] = dv
        dv += 1; L["bev"][g] = dv
        dv += 1; L["oev"][g] = dv
    OAD = {}
    for i in range(NC):
        for f in range(nft):
            dv += 1; OAD[(i, f)] = dv
    TRE = {}
    for tc in range(16):
        dv += 1; TRE[tc] = dv
    L["dv_end"] = dv

    block = b.block()

    @block.sync
    def _(sync):
        sync.dma_start(iscl[:], iscol_d).then_inc(s_is.h, 16)
        s_is.n += 16
        for dst, src in [(bqc[:], bq_d), (bkc[:], bk_d),
                         (fbc[:], fb_d), (bvrow[:], bvrow_d)]:
            sync.dma_start(dst, src).then_inc(s_sy.h, 16)
            s_sy.n += 16
    ld_done = s_sy.n

    @block.sync
    def _(sync):
        for qq in range(2):
            sync.wait_ge(b.rs_cc[qq].h, 1)
            for c in range(NC):
                for r in range(2):
                    m = qq * 16 + c * 2 + r
                    sl = m % 2
                    if m >= 2:
                        sync.wait_ge(s_sel.h, 2 * (m - 1))
                    rows = b.rs_out[qq].ap()[r * 1024 + c * 128:r * 1024 + (c + 1) * 128, :]
                    sync.dma_start(xs[:, sl, 0, :], rows[:, 0:512]).then_inc(s_xs[sl].h, 16)
                    s_xs[sl].n += 16
                    sync.dma_start(xs[:, sl, 1, :], rows[:, 512:1024]).then_inc(s_xs[sl].h, 16)
                    s_xs[sl].n += 16


    wnames = [wf_d, a1_d, a2_d, wq_d, wk_d, wv_d]
    WPL = {}

    @block.gpsimd
    def _(g):
        cnt = [0, 0]
        for widx, wd in enumerate(wnames):
            for qp in range(4):
                m = widx * 4 + qp
                s = m % 2
                if m >= 2:
                    pm = m - 2
                    pwidx, pqp = pm // 4, pm % 4
                    if pwidx < 3:
                        L_prev = FF[(pwidx, pqp * 2 + 1, nft - 1)]
                    elif pwidx == 3:
                        L_prev = PQ[(pqp * 2 + 1, nft - 1)]
                    elif pwidx == 4:
                        L_prev = PK[(pqp * 2 + 1, nft - 1)]
                    else:
                        L_prev = PV_[(15, 0 if pqp < 2 else 1)]
                    g.wait_ge(s_pe.h, L_prev)
                g.dma_start(wpc[:, s, :, :],
                            wd.rearrange("(c p) o -> p c o", p=128)[:, :, qp * 256:(qp + 1) * 256]
                            ).then_inc(s_wp[s].h, 16)
                cnt[s] += 16
                s_wp[s].n += 16
                WPL[(widx, qp)] = cnt[s]

    @block.vector
    def _(v):
        v.memset(onescol_bf[:], 1.0)
        v.memset(onesrow_f[:], 1.0)
        v.memset(onesrow_bf[:], 1.0)
        v.memset(epsc[:], 1e-5)
        v.wait_ge(s_is.h, 16)
        for qq in range(2):
            for c in range(NC):
                for r in range(2):
                    m = qq * 16 + c * 2 + r
                    sl = m % 2
                    ft = 2 * r + qq
                    v.wait_ge(s_xs[sl].h, 32 * (m // 2 + 1))
                    v.tensor_tensor(xs[:, sl, 1, :], xs[:, sl, 1, :], xs[:, sl, 0, :],
                                    ALU.subtract).then_inc(s_sel.h, 1)
                    s_sel.n += 1
                    v.wait_ge(s_sel.h, s_sel.n)
                    v.scalar_tensor_tensor(X[:, c, ft * 512:(ft + 1) * 512], xs[:, sl, 1, :],
                                           iscl[:, 0:1], xs[:, sl, 0, :], ALU.mult, ALU.add
                                           ).then_inc(s_sel.h, 1)
                    s_sel.n += 1
        v.wait_ge(s_sy.h, ld_done)
        v.wait_ge(s_sel.h, s_sel.n)
        v.tensor_copy(xn[:], X[:]).then_inc(s_dv.h, 1)
        s_dv.n += 1
    assert s_dv.n == L["cast"]

    @block.scalar
    def _(sc):
        sc.wait_ge(s_dv.h, L["cast"])
        sc.activation(qb[:], xn[:], AF.Square).then_inc(s_ac.h, 1)
        s_ac.n += 1

    @block.tensor
    def _(t):
        t.wait_ge(s_dv.h, L["cast"])
        for f in range(nft):
            for k in range(NC):
                mm = t.matmul(pb[f][0:1, :], onescol_bf[:], xn[:, k, f * FT:(f + 1) * FT],
                              start=(k == 0), stop=(k == NC - 1))
            mm.then_inc(s_pe.h, 1)
            s_pe.n += 1
        t.wait_ge(s_ac.h, L["sq"])
        for f in range(nft):
            for k in range(NC):
                mm = t.matmul(pb[4 + f][0:1, :], onescol_bf[:], qb[:, k, f * FT:(f + 1) * FT],
                              start=(k == 0), stop=(k == NC - 1))
            mm.then_inc(s_pe.h, 1)
            s_pe.n += 1

    @block.vector
    def _(v):
        v.wait_ge(s_pe.h, L["sum"])
        for f in range(nft):
            cp = v.tensor_copy(arow[0:1, f * FT:(f + 1) * FT], pb[f][0:1, :])
        cp.then_inc(s_dv.h, 1)
        s_dv.n += 1
        assert s_dv.n == L["srowcp"]
        v.wait_ge(s_pe.h, L["sumsq"])
        v.wait_ge(s_dv.h, s_dv.n)
        v.tensor_scalar(arow[:], arow[:], 1.0 / 1024.0, None, ALU.mult).then_inc(s_dv.h, 1)
        s_dv.n += 1
        v.wait_ge(s_dv.h, s_dv.n)
        v.tensor_tensor(brow[:], arow[:], arow[:], ALU.mult).then_inc(s_dv.h, 1)
        s_dv.n += 1
        assert s_dv.n == L["rowm"]
        sscat = [pb[4 + f][0:1, :] for f in range(nft)]
        for f in range(nft):
            v.wait_ge(s_dv.h, s_dv.n)
            ins = v.scalar_tensor_tensor(brow[0:1, f * FT:(f + 1) * FT], sscat[f],
                                         1.0 / 1024.0, brow[0:1, f * FT:(f + 1) * FT],
                                         ALU.mult, ALU.subtract)
        ins.then_inc(s_dv.h, 1)
        s_dv.n += 1
        assert s_dv.n == L["recipn"]

    @block.scalar
    def _(sc):
        sc.wait_ge(s_dv.h, L["recipn"])
        sc.activation(brow[:], brow[:], AF.Sqrt, bias=epsc[0:1, 0:1]).then_inc(s_ac.h, 1)
        s_ac.n += 1
    assert True

    @block.vector
    def _(v):
        v.wait_ge(s_ac.h, L["lnsqrt"])
        v.reciprocal(brow[:], brow[:]).then_inc(s_dv.h, 1)
        s_dv.n += 1
        assert s_dv.n == L["recip2"]
        v.wait_ge(s_dv.h, s_dv.n)
        v.tensor_tensor(arow[:], arow[:], brow[:], ALU.mult).then_inc(s_dv.h, 1)
        s_dv.n += 1
        assert s_dv.n == L["mrN"]

    @block.tensor
    def _(t):
        t.wait_ge(s_dv.h, L["mrN"])
        for f in range(nft):
            t.matmul(pb[f][:], onesrow_f[:], brow[0:1, f * FT:(f + 1) * FT],
                     start=True, stop=True).then_inc(s_pe.h, 1)
            s_pe.n += 1
        for f in range(nft):
            t.matmul(pb[4 + f][:], onesrow_f[:], arow[0:1, f * FT:(f + 1) * FT],
                     start=True, stop=True).then_inc(s_pe.h, 1)
            s_pe.n += 1
        t.wait_ge(s_dv.h, LNA[NC - 1])
        for hvv in range(2):
            mm = t.matmul(pb[hvv][:], onesrow_f[:], bvrow[0:1, hvv * 512:(hvv + 1) * 512],
                          start=True, stop=True)
        mm.then_inc(s_pe.h, 1)
        s_pe.n += 1
        assert s_pe.n == L["bvb"]
    assert True

    @block.vector
    def _(v):
        v.wait_ge(s_pe.h, L["lnbc"])
        for kk in range(NC):
            v.wait_ge(s_dv.h, s_dv.n)
            v.tensor_tensor(xn[:, kk, :], X[:, kk, :], pA[:], ALU.mult).then_inc(s_dv.h, 1)
            s_dv.n += 1
            v.wait_ge(s_dv.h, s_dv.n)
            v.tensor_tensor(xn[:, kk, :], xn[:, kk, :], pB[:], ALU.subtract).then_inc(s_dv.h, 1)
            s_dv.n += 1
            assert s_dv.n == LNA[kk]
        v.wait_ge(s_pe.h, L["bvb"])
        v.tensor_copy(bvb[:, 0:512], pb[0][:])
        v.tensor_copy(bvb[:, 512:1024], pb[1][:]).then_inc(s_dv.h, 1)
        s_dv.n += 1
        assert s_dv.n == L["bve"]

    @block.tensor
    def _(t):
        for st in range(3):
            rhs_buf = [xn, h1, h2][st]
            for i in range(NC):
                qp = i // 2
                widx = st
                t.wait_ge(s_wp[(widx * 4 + qp) % 2].h, WPL[(widx, qp)])
                for f in range(nft):
                    j = (st * NC + i) * nft + f
                    jj = j % 8
                    if st == 0 and j < 8:
                        t.wait_ge(s_dv.h, L["bve"])
                    if st >= 1 and i == 0 and f == 0:
                        t.wait_ge(s_dv.h, FFE[(st - 1, NC - 1, nft - 1)])
                    prev = j - 8
                    if prev >= 0:
                        pst, pi = divmod(prev // nft, NC)
                        pf = prev % nft
                        if pst < 2:
                            t.wait_ge(s_dv.h, FFE[(pst, pi, pf)])
                        else:
                            t.wait_ge(s_ac.h, FTN[(pi, pf)])
                    for k in range(NC):
                        mm = t.matmul(pb[jj][:], wpc[:, (st * 4 + qp) % 2, k, (i % 2) * 128:(i % 2 + 1) * 128],
                                      rhs_buf[:, k, f * FT:(f + 1) * FT],
                                      start=(k == 0), stop=(k == NC - 1))
                    mm.then_inc(s_pe.h, 1)
                    s_pe.n += 1
                    assert s_pe.n == FF[(st, i, f)]

    @block.vector
    def _(v):
        for st in range(2):
            dst = [h1, h2][st]
            for i in range(NC):
                for f in range(nft):
                    j = (st * NC + i) * nft + f
                    v.wait_ge(s_pe.h, FF[(st, i, f)])
                    v.wait_ge(s_dv.h, s_dv.n)
                    v.tensor_copy(dst[:, i, f * FT:(f + 1) * FT], pb[j % 8][:]).then_inc(s_dv.h, 1)
                    s_dv.n += 1
                    assert s_dv.n == FFE[(st, i, f)]

    @block.scalar
    def _(sc):
        for i in range(NC):
            for f in range(nft):
                j = (2 * NC + i) * nft + f
                sc.wait_ge(s_pe.h, FF[(2, i, f)])
                jf = i * nft + f
                if jf >= 2:
                    pj = jf - 2
                    sc.wait_ge(s_dv.h, FAD[(pj // nft, pj % nft)])
                sc.activation(ttr[:, jf % 2, :], pb[j % 8][:], AF.Tanh, bias=fbc[:, i:i + 1]
                              ).then_inc(s_ac.h, 1)
                s_ac.n += 1
                assert s_ac.n == FTN[(i, f)]

    @block.vector
    def _(v):
        for i in range(NC):
            for f in range(nft):
                v.wait_ge(s_ac.h, FTN[(i, f)])
                v.wait_ge(s_dv.h, s_dv.n)
                jf = i * nft + f
                v.tensor_tensor(X[:, i, f * FT:(f + 1) * FT], X[:, i, f * FT:(f + 1) * FT],
                                ttr[:, jf % 2, :], ALU.add).then_inc(s_dv.h, 1)
                s_dv.n += 1
                assert s_dv.n == FAD[(i, f)]

    @block.tensor
    def _(t):
        for pi, (PX, widx) in enumerate([(PQ, 3), (PK, 4)]):
            for i in range(NC):
                qp = i // 2
                t.wait_ge(s_wp[(widx * 4 + qp) % 2].h, WPL[(widx, qp)])
                for f in range(nft):
                    j = i * nft + f
                    jj = j % 8
                    if pi == 0 and j < 8:
                        t.wait_ge(s_ac.h, FTN[(NC - 1, nft - 1)])
                        t.wait_ge(s_dv.h, FAD[(NC - 1, nft - 1)])
                    elif pi == 0:
                        t.wait_ge(s_ac.h, PQE[((j - 8) // nft, (j - 8) % nft)])
                    elif pi == 1 and j < 8:
                        t.wait_ge(s_ac.h, PQE[(NC - 1, nft - 1)])
                    else:
                        t.wait_ge(s_ac.h, PKE[((j - 8) // nft, (j - 8) % nft)])
                    for k in range(NC):
                        mm = t.matmul(pb[jj][:], wpc[:, (widx * 4 + qp) % 2, k, (i % 2) * 128:(i % 2 + 1) * 128],
                                      xn[:, k, f * FT:(f + 1) * FT], start=(k == 0), stop=(k == NC - 1))
                    mm.then_inc(s_pe.h, 1)
                    s_pe.n += 1
                    assert s_pe.n == (PQ if pi == 0 else PK)[(i, f)]
        widx = 5
        for hv in range(2):
            for tc in range(16):
                qpa, qpb_ = 2 * hv, 2 * hv + 1
                t.wait_ge(s_wp[(widx * 4 + qpa) % 2].h, WPL[(widx, qpa)])
                t.wait_ge(s_wp[(widx * 4 + qpb_) % 2].h, WPL[(widx, qpb_)])
                m = hv * 16 + tc
                bank = pb[m % 8]
                if m < 8:
                    t.wait_ge(s_ac.h, PKE[(NC - 1, nft - 1)])
                else:
                    pm = m - 8
                    t.wait_ge(s_dv.h, PVE[(pm % 16, pm // 16)])
                for k in range(NC):
                    t.matmul(bank[:, 0:256], xn[:, k, tc * 128:(tc + 1) * 128],
                             wpc[:, (widx * 4 + qpa) % 2, k, :],
                             start=(k == 0), stop=(k == NC - 1))
                for k in range(NC):
                    mm = t.matmul(bank[:, 256:512], xn[:, k, tc * 128:(tc + 1) * 128],
                                  wpc[:, (widx * 4 + qpb_) % 2, k, :],
                                  start=(k == 0), stop=(k == NC - 1))
                mm.then_inc(s_pe.h, 1)
                s_pe.n += 1
                assert s_pe.n == PV_[(tc, hv)]

    @block.scalar
    def _(sc):
        for pi, (PX, PXE, dst, bc) in enumerate([(PQ, PQE, qb, bqc), (PK, PKE, kown, bkc)]):
            for i in range(NC):
                for f in range(nft):
                    j = i * nft + f
                    sc.wait_ge(s_pe.h, PX[(i, f)])
                    sc.add(dst[:, i, f * FT:(f + 1) * FT], pb[j % 8][:],
                           bc[:, i:i + 1]).then_inc(s_ac.h, 1)
                    s_ac.n += 1
                    assert s_ac.n == PXE[(i, f)]

    @block.vector
    def _(v):
        v.wait_ge(s_pe.h, FF[(2, NC - 1, nft - 1)])
        v.memset(vt65[:], 1.0).then_inc(s_dv.h, 1)
        s_dv.n += 1
        assert s_dv.n == L["vones"]
        for hv in range(2):
            for tc in range(16):
                m = hv * 16 + tc
                v.wait_ge(s_pe.h, PV_[(tc, hv)])
                v.wait_ge(s_dv.h, s_dv.n)
                v.tensor_tensor(
                    vt65[:, tc, hv * 8 * 65:(hv * 8 + 8) * 65].rearrange("p (h d) -> p h d", h=8)[:, :, 0:64],
                    pb[m % 8][:].rearrange("p (h d) -> p h d", h=8),
                    bvb[:, hv * 512:(hv + 1) * 512].rearrange("p (h d) -> p h d", h=8),
                    ALU.add).then_inc(s_dv.h, 1)
                s_dv.n += 1
                assert s_dv.n == PVE[(tc, hv)]

    @block.scalar
    def _(sc):
        sc.wait_ge(s_ac.h, PQE[(NC - 1, nft - 1)])
        sc.wait_ge(s_dv.h, PVE[(15, 1)])
        for c in range(NC):
            if c >= 1:
                sc.wait_ge(s_pe.h, QSQP[(c - 1, 1)])
            sc.activation(sqscr, qb[:, c, :], AF.Square).then_inc(s_ac.h, 1)
            s_ac.n += 1
        assert s_ac.n == L["qsqa"]
        for c in range(NC):
            sc.wait_ge(s_pe.h, QSQP[(NC - 1, 1)] if c == 0 else KSQP[(c - 1, 1)])
            sc.activation(sqscr, kown[:, c, :], AF.Square).then_inc(s_ac.h, 1)
            s_ac.n += 1
        assert s_ac.n == L["ksqa"]
        sc.wait_ge(s_dv.h, L["mmul"])
        sc.activation(mrow[0:1, 0:16], mrow[0:1, 0:16], AF.Sqrt).then_inc(s_ac.h, 1)
        s_ac.n += 1
        sc.wait_ge(s_ac.h, s_ac.n)
        sc.mul(mrow[0:1, 0:16], mrow[0:1, 0:16], -0.125).then_inc(s_ac.h, 1)
        s_ac.n += 1
        assert s_ac.n == L["msq2"]

    @block.scalar
    def _(sc):
        for gi, g in enumerate(groups):
            h, it = g
            for si in range(nki(it)):
                sc.wait_ge(s_pe.h, L["S"][g][si])
                if gi >= 1:
                    sc.wait_ge(s_pe.h, L["PV"][groups[gi - 1]])
                sc.activation(Pbuf[:, si, :], pb[si % 6][:], AF.Exp, scale=0.125,
                              bias=mcol[:, h:h + 1]).then_inc(s_ac.h, 1)
                s_ac.n += 1
            assert s_ac.n == L["ev"][g][-1]

    @block.tensor
    def _(t):
        t.wait_ge(s_dv.h, PVE[(15, 1)])
        for c in range(NC):
            t.wait_ge(s_ac.h, QSQA[c])
            for par in range(2):
                hb = par * 64
                buf = pA if par == 0 else pB
                if c * 2 + par >= 2:
                    ph = c * 2 + par - 2
                    t.wait_ge(s_dv.h, QMX[ph])
                for f in range(nft):
                    mm = t.matmul(buf[0:1, f * FT:(f + 1) * FT], onescol_bf[hb:hb + 64, :],
                                  sqscr[hb:hb + 64, f * FT:(f + 1) * FT],
                                  start=True, stop=True)
                mm.then_inc(s_pe.h, 1)
                s_pe.n += 1
                assert s_pe.n == QSQP[(c, par)]
        for c in range(NC):
            t.wait_ge(s_ac.h, KSQA[c])
            for par in range(2):
                hb = par * 64
                buf = pA if par == 0 else pB
                if c * 2 + par >= 2:
                    t.wait_ge(s_dv.h, KMX[c * 2 + par - 2])
                else:
                    t.wait_ge(s_dv.h, QMX[NHD - 1 - (1 - par)])
                for f in range(nft):
                    mm = t.matmul(buf[0:1, f * FT:(f + 1) * FT], onescol_bf[hb:hb + 64, :],
                                  sqscr[hb:hb + 64, f * FT:(f + 1) * FT],
                                  start=True, stop=True)
                mm.then_inc(s_pe.h, 1)
                s_pe.n += 1
                assert s_pe.n == KSQP[(c, par)]
        t.wait_ge(s_ac.h, L["msq2"])
        t.matmul(pb[2][:, 0:16], onesrow_f[:], mrow[0:1, 0:16], start=True, stop=True
                 ).then_inc(s_pe.h, 1)
        s_pe.n += 1
        assert s_pe.n == L["mb"]

    @block.vector
    def _(v):
        for h in range(NHD):
            buf = pA if h % 2 == 0 else pB
            v.wait_ge(s_pe.h, QSQP[(h // 2, h % 2)])
            v.reduce_max(mrow[0:1, 16 + h:17 + h], buf[0:1, :], axis=AX.X).then_inc(s_dv.h, 1)
            s_dv.n += 1
            assert s_dv.n == QMX[h]
        for h in range(NHD):
            buf = pA if h % 2 == 0 else pB
            v.wait_ge(s_pe.h, KSQP[(h // 2, h % 2)])
            v.reduce_max(mrow[0:1, 32 + h:33 + h], buf[0:1, :], axis=AX.X).then_inc(s_dv.h, 1)
            s_dv.n += 1
            assert s_dv.n == KMX[h]
        v.wait_ge(s_ag.h, 16 * 5)
        v.tensor_copy(mrow[0:1, 0:16], mrow[0:1, 0:16]).then_inc(s_dv.h, 1)
        s_dv.n += 1
        assert s_dv.n == L["kmax2"]
        v.wait_ge(s_dv.h, s_dv.n)
        v.tensor_tensor(mrow[0:1, 0:16], mrow[0:1, 16:32], mrow[0:1, 0:16], ALU.mult
                        ).then_inc(s_dv.h, 1)
        s_dv.n += 1
        assert s_dv.n == L["mmul"]
        v.wait_ge(s_pe.h, L["mb"])
        v.tensor_copy(mcol[:], pb[2][:, 0:16]).then_inc(s_dv.h, 1)
        s_dv.n += 1
        assert s_dv.n == L["mcev"]

    @block.gpsimd
    def _(g):
        g.wait_ge(s_ac.h, PKE[(NC - 1, nft - 1)])
        g.dma_start(k_in.ap().rearrange("(c p) t -> p c t", p=128), kown[:]).then_inc(s_ag.h, 16)
        s_ag.n += 16
        g.wait_ge(s_ag.h, s_ag.n)
        g.collective_compute("AllGather", ALU.bypass, replica_groups=RG,
                             ins=[k_in.ap().opt()], outs=[k_out.ap().opt()]).then_inc(s_cc.h, 1)
        s_cc.n += 1
        g.wait_ge(s_dv.h, L["kmax1"])
        g.dma_start(km_in.ap(), mrow[0:1, 32:48]).then_inc(s_ag.h, 16)
        s_ag.n += 16
        g.wait_ge(s_ag.h, s_ag.n)
        g.collective_compute("AllReduce", ALU.max, replica_groups=RG,
                             ins=[km_in.ap().opt()], outs=[km_out.ap().opt()]).then_inc(s_cc.h, 1)
        s_cc.n += 1
        g.wait_ge(s_dv.h, PVE[(15, 1)])
        g.dma_start(v_in.ap().rearrange("(c p) d -> p c d", p=128), vt65[:]).then_inc(s_ag.h, 16)
        s_ag.n += 16
        g.wait_ge(s_ag.h, s_ag.n)
        g.collective_compute("AllGather", ALU.bypass, replica_groups=RG,
                             ins=[v_in.ap().opt()], outs=[v_out.ap().opt()]).then_inc(s_cc.h, 1)
        s_cc.n += 1
        g.dma_start(mks[:], mk_d.rearrange("m p q -> p m q")).then_inc(s_gp.h, 16)
        s_gp.n += 16
        g.wait_ge(s_cc.h, 2)
        g.dma_start(mrow[0:1, 0:16], km_out.ap()).then_inc(s_ag.h, 16)
        s_ag.n += 16
        g.wait_ge(s_ag.h, s_ag.n)
        g.dma_start(km_in.ap(), km_out.ap()).then_inc(s_ag.h, 16)
        s_ag.n += 16
        g.wait_ge(s_ag.h, s_ag.n)

    kh_done = {}
    vh_done = {}

    @block.sync
    def _(sync):
        def kh(h):
            sl = h % 2
            for o in range(2):
                hb = (h % 2) * 64
                sync.dma_start(
                    khr[hb:hb + 64, sl, :].rearrange("p (a t) -> p a t", t=256)[:, o::2, :],
                    k_out.ap()[o * 1024 + h * HD1:o * 1024 + (h + 1) * HD1, :].rearrange(
                        "p (a t) -> p a t", t=256)).then_inc(s_kh[sl].h, 16)
                s_kh[sl].n += 16
            kh_done[h] = 16 * (h // 2 + 1) * 2
        def vh(h):
            sl = h % 2
            for o in range(2):
                for ci in range(2):
                    sync.dma_start(
                        vhr[:, sl, 2 * o + ci::4, :],
                        v_out.ap()[o * 2048 + ci * 128:(o + 1) * 2048, h * 65:(h + 1) * 65
                                   ].rearrange("(a p) d -> p a d", p=128)[:, 0::2, :]
                        ).then_inc(s_vh[sl].h, 16)
                    s_vh[sl].n += 16
            vh_done[h] = 16 * (h // 2 + 1) * 4
        sync.wait_ge(s_cc.h, 1)
        sync.wait_ge(s_ag.h, 48)
        kh(0)
        kh(1)
        sync.wait_ge(s_cc.h, 3)
        vh(0)
        vh(1)
        for h in range(2, NHD):
            sync.wait_ge(s_pe.h, L["rb"][(h - 2, 3)])
            kh(h)
            vh(h)

    @block.tensor
    def _(t):
        t.wait_ge(s_dv.h, L["mcev"])
        for gi, g in enumerate(groups):
            h, it = g
            sl = h % 2
            if it == 0:
                t.wait_ge(s_kh[sl].h, kh_done[h])
                t.wait_ge(s_vh[sl].h, vh_done[h])
            for si in range(nki(it)):
                bk = si % 6
                if si >= 6:
                    t.wait_ge(s_ac.h, L["ev"][g][si - 6])
                elif gi >= 1:
                    pg = groups[gi - 1]
                    pn = nki(pg[1])
                    t.wait_ge(s_ac.h, L["ev"][pg][pn - 6 + si] if pn >= 6 - si else 0)
                hb = (h % 2) * 64
                mm = t.matmul(pb[bk][:], khr[hb:hb + 64, sl, si * 128:(si + 1) * 128],
                              qb[hb:hb + 64, h // 2, it * FT:(it + 1) * FT],
                              start=True, stop=True)
                mm.then_inc(s_pe.h, 1)
                s_pe.n += 1
            assert s_pe.n == L["S"][g][-1]
            t.wait_ge(s_dv.h, L["msk"][g])
            if gi >= 1:
                t.wait_ge(s_dv.h, L["oev"][groups[gi - 1]])
            for si in range(nki(it)):
                mm = t.matmul(pb[6][0:65, :], vhr[:, sl, si, :], Pbuf[:, si, :],
                              start=(si == 0), stop=(si == nki(it) - 1))
            mm.then_inc(s_pe.h, 1)
            s_pe.n += 1
            assert s_pe.n == L["PV"][g]
            t.wait_ge(s_dv.h, L["brv"][g])
            t.matmul(pb[7][:], onesrow_bf[:], lrowb[0:1, :], start=True, stop=True
                     ).then_inc(s_pe.h, 1)
            s_pe.n += 1
            assert s_pe.n == L["rb"][g]

    @block.vector
    def _(v):
        v.wait_ge(s_gp.h, 16)
        for gi, g in enumerate(groups):
            h, it = g
            nm = 0
            for si in range(nki(it)):
                r0 = si - 8 * it
                if 0 <= r0 < 8:
                    v.wait_ge(s_ac.h, L["ev"][g][si])
                    v.wait_ge(s_dv.h, s_dv.n)
                    v.tensor_tensor(Pbuf[:, si, :], Pbuf[:, si, :], mks[:, r0, :], ALU.mult
                                    ).then_inc(s_dv.h, 1)
                    s_dv.n += 1
                    nm += 1
            assert s_dv.n == L["msk"][g]
            v.wait_ge(s_pe.h, L["PV"][g])
            v.reciprocal(lrow[0:1, :], pb[6][64:65, :]).then_inc(s_dv.h, 1)
            s_dv.n += 1
            assert s_dv.n == L["rc"][g]
            v.wait_ge(s_dv.h, s_dv.n)
            v.tensor_copy(lrowb[0:1, :], lrow[0:1, :]).then_inc(s_dv.h, 1)
            s_dv.n += 1
            assert s_dv.n == L["brv"][g]
            v.wait_ge(s_pe.h, L["rb"][g])
            v.tensor_copy(brec[:], pb[7][:]).then_inc(s_dv.h, 1)
            s_dv.n += 1
            assert s_dv.n == L["bev"][g]
            v.wait_ge(s_dv.h, s_dv.n)
            v.tensor_tensor(O[(h % 2) * 64:(h % 2) * 64 + 64, h // 2, it * FT:(it + 1) * FT],
                            pb[6][0:64, :], brec[0:64, :], ALU.mult).then_inc(s_dv.h, 1)
            s_dv.n += 1
            assert s_dv.n == L["oev"][g]
        for i in range(NC):
            for f in range(nft):
                v.wait_ge(s_dv.h, s_dv.n)
                v.tensor_tensor(X[:, i, f * FT:(f + 1) * FT], X[:, i, f * FT:(f + 1) * FT],
                                O[:, i, f * FT:(f + 1) * FT], ALU.add).then_inc(s_dv.h, 1)
                s_dv.n += 1
                assert s_dv.n == OAD[(i, f)]
        for tc in range(16):
            v.wait_ge(s_pe.h, TRO[tc])
            v.tensor_copy(otr[:, tc % 2, :], (pA if tc % 2 == 0 else pB)[:, 0:1024]
                          ).then_inc(s_dv.h, 1)
            s_dv.n += 1
            assert s_dv.n == TRE[tc]

    @block.tensor
    def _(t):
        t.wait_ge(s_sy.h, ld_done + 16)
        for tc in range(16):
            if tc >= 2:
                t.wait_ge(s_ot[tc % 2].h, 16 * (tc // 2))
            if tc < 2:
                t.wait_ge(s_dv.h, OAD[(NC - 1, nft - 1)])
            for k in range(NC):
                mm = t.transpose((pA if tc % 2 == 0 else pB)[:, k * 128:(k + 1) * 128],
                                 X[:, k, tc * 128:(tc + 1) * 128], identf[:])
            mm.then_inc(s_pe.h, 1)
            s_pe.n += 1
            assert s_pe.n == TRO[tc]

    @block.sync
    def _(sync):
        sync.wait_ge(s_dv.h, OAD[(NC - 1, nft - 1)])
        sync.dma_start(identf[:], id_d).then_inc(s_sy.h, 16)
        s_sy.n += 16
        for tc in range(16):
            sync.wait_ge(s_dv.h, TRE[tc])
            sync.dma_start(out_o[tc * 128:(tc + 1) * 128, :], otr[:, tc % 2, :]
                           ).then_inc(s_ot[tc % 2].h, 16)
            s_ot[tc % 2].n += 16
        for sl in range(2):
            sync.wait_ge(s_ot[sl].h, s_ot[sl].n)







def emit_reshard(b, D, T=T_CORE):
    nc = b.nc
    x1 = D["x1"]
    dst = D["l1x"]
    iscol_d = D["iscol"]
    rs_in = nc.dram_tensor(b.pfx + "rs_in", [1024, T], F32)
    rs_out = nc.dram_tensor(b.pfx + "rs_out", [2048, T], F32)

    xs = b.sb("xs", [128, 2, 1024])
    xsel = b.sb("xsel", [128, 2, 1024])
    iscl = b.sb("iscl", [128, 1])
    s_sy = b.sem("s_sy")
    s_ag = b.sem("s_ag")
    s_cc = b.sem("s_cc")
    s_dv = b.sem("s_dv")
    s_st2 = [b.sem(f"s_st{i}") for i in range(2)]
    s_vv = b.sem("s_vv")
    block = b.block()

    @block.gpsimd
    def _(g):
        for s in range(2):
            g.dma_start(rs_in.ap().rearrange("d (a u) -> d a u", u=256)[:, s::2, :].rearrange("d a u -> d (a u)")
                        if False else
                        rs_in.ap()[:, s * 1024:(s + 1) * 1024].rearrange("d (a u) -> d a u", u=256),
                        x1.rearrange("d (a u) -> d a u", u=256)[:, s::2, :]).then_inc(s_ag.h, 16)
            s_ag.n += 16
            g.wait_ge(s_ag.h, s_ag.n)
        g.collective_compute("AllGather", ALU.bypass, replica_groups=RG,
                             ins=[rs_in.ap().opt()], outs=[rs_out.ap().opt()]).then_inc(s_cc.h, 1)
        s_cc.n += 1
        g.wait_ge(s_cc.h, s_cc.n)

    @block.sync
    def _(sync):
        sync.dma_start(iscl[:], iscol_d).then_inc(s_sy.h, 16)
        s_sy.n += 16
        n = 0
        for c in range(8):
            for r in range(2):
                m = c * 2 + r
                if m >= 1:
                    sync.wait_ge(s_dv.h, m)
                if m >= 2:
                    sync.wait_ge(s_st2[m % 2].h, 16 * ((m - 2) // 2 + 1))
                sync.wait_ge(s_cc.h, 1)
                sync.dma_start(xs[:, 0, :], rs_out.ap()[r * 1024 + c * 128:r * 1024 + (c + 1) * 128, 0:1024]
                               ).then_inc(s_sy.h, 16)
                s_sy.n += 16
                sync.dma_start(xs[:, 1, :], rs_out.ap()[r * 1024 + c * 128:r * 1024 + (c + 1) * 128, 1024:2048]
                               ).then_inc(s_sy.h, 16)
                s_sy.n += 16

    @block.vector
    def _(v):
        for c in range(8):
            for r in range(2):
                m = c * 2 + r
                v.wait_ge(s_sy.h, 16 + 32 * (m + 1))
                if m >= 2:
                    v.wait_ge(s_st2[m % 2].h, 16 * ((m - 2) // 2 + 1))
                v.tensor_tensor(xsel[:, m % 2, :], xs[:, 1, :], xs[:, 0, :], ALU.subtract).then_inc(s_vv.h, 1)
                s_vv.n += 1
                v.wait_ge(s_vv.h, s_vv.n)
                v.scalar_tensor_tensor(xsel[:, m % 2, :],
                                       xsel[:, m % 2, :], iscl[:, 0:1], xs[:, 0, :],
                                       ALU.mult, ALU.add).then_inc(s_dv.h, 1)
                s_dv.n += 1

    @block.scalar
    def _(sc):
        for c in range(8):
            for r in range(2):
                m = c * 2 + r
                sc.wait_ge(s_dv.h, m + 1)
                sc.dma_start(dst.rearrange("(cc p) t -> p cc t", p=128)[:, c, r * 1024:(r + 1) * 1024],
                             xsel[:, m % 2, :]).then_inc(s_st2[m % 2].h, 16)
                s_st2[m % 2].n += 16
        for sl in range(2):
            sc.wait_ge(s_st2[sl].h, s_st2[sl].n)




def build_fused():
    b = Builder("mega")
    nc = b.nc
    F = F32
    T = T_CORE
    EI = lambda n, s, dt=F32: nc.dram_tensor(n, s, dt, kind="ExternalInput").ap()
    INT = lambda n, s, dt=F32: nc.dram_tensor(n, s, dt).ap()

    io = {}
    io["x_in"] = EI("x_in", [T, 1024])
    io["identf"] = EI("identf", [128, 128])
    io["iscol"] = EI("iscol", [128, 1])
    for n, s in [("winT", [1024, 3072]), ("wgT", [1536, 3072]), ("woutT", [1536, 1024]),
                 ("cwc", [128, 12, 4]), ("cbc", [128, 12]), ("bgc", [128, 24]),
                 ("fbc_h", [128, 12]), ("gnc_h", [128, 8])]:
        io[n] = EI(n, s)
    for n, s in [("wgt1", [1024, 4096]), ("wst1", [2048, 1024]), ("gnc_m1", [128, 8]),
                 ("wgt2", [1024, 4096]), ("wst2", [2048, 1024]), ("gnc_m2", [128, 8])]:
        io[n] = EI(n, s)
    for n, s in [("wqT", [1024, 1024]), ("wkT", [1024, 128]), ("wvT", [1024, 128]),
                 ("woT", [1024, 1024]), ("gnc_sq", [128, 8]), ("cosT", [128, T]),
                 ("sinT", [128, T]), ("rotP", [128, 128]), ("mL", [4, 128, 512]),
                 ("mR", [4, 128, 512])]:
        io[n] = EI(n, s)
    for n, s in [("gwT", [1024, 8]), ("w1", [8, 1024, 4096]), ("b1c", [128, 8, 32]),
                 ("w2", [8, 4096, 1024]), ("b2c", [128, 8, 8]), ("sel", [8, 1024])]:
        io[n] = EI(n, s)
    for n, s in [("wq", [1024, 1024]), ("wk", [1024, 1024]), ("wv", [1024, 1024]),
                 ("bqc", [128, 8]), ("bkc", [128, 8]), ("wf", [1024, 1024]),
                 ("a1", [1024, 1024]), ("a2", [1024, 1024]), ("fbc_l", [128, 8]),
                 ("mks", [8, 128, 512]), ("bvrow", [1, 1024])]:
        io[n] = EI(n, s)
    out_o = nc.dram_tensor("out", [T, 1024], F32, kind="ExternalOutput").ap()

    c_xT = INT("c_xT", [1024, T])
    c_xa = INT("c_xa", [1024, T])
    c_xnb = INT("c_xnb", [1024, T], BF16)

    b.begin_phase("p0_")
    emit_prep(b, {"x_in": io["x_in"], "identf": io["identf"],
                  "xT": c_xT, "xa0": c_xa, "xnb": c_xnb})
    b.end_phase()

    b.begin_phase("p1_")
    emit_hawk(b, {"xa": c_xa, "winT": io["winT"], "wgT": io["wgT"], "woutT": io["woutT"],
                  "cwc": io["cwc"], "cbc": io["cbc"], "bgc": io["bgc"], "fbc": io["fbc_h"],
                  "gnc": io["gnc_h"], "iscol": io["iscol"], "xa_out": c_xa})
    b.end_phase()

    b.begin_phase("p2_")
    emit_mlp(b, {"xa": c_xa, "wgt": io["wgt1"], "wst": io["wst1"], "gnc": io["gnc_m1"],
                 "xa_out": c_xa})
    b.end_phase()

    b.begin_phase("p3_")
    emit_smqa(b, {"xa": c_xa, "wqT": io["wqT"], "wkT": io["wkT"], "wvT": io["wvT"],
                  "woT": io["woT"], "gnc": io["gnc_sq"], "cosT": io["cosT"],
                  "sinT": io["sinT"], "rotP": io["rotP"], "mL": io["mL"], "mR": io["mR"],
                  "iscol": io["iscol"], "xa_out": c_xa})
    b.end_phase()

    b.begin_phase("p4_")
    emit_mlp(b, {"xa": c_xa, "wgt": io["wgt2"], "wst": io["wst2"], "gnc": io["gnc_m2"],
                 "xa_out": c_xa})
    b.end_phase()

    b.begin_phase("p5_")
    emit_moe(b, {"xT": c_xT, "xnb": c_xnb, "xa": c_xa, "gwT": io["gwT"], "w1": io["w1"],
                 "b1c": io["b1c"], "w2": io["w2"], "b2c": io["b2c"], "sel": io["sel"],
                 "identf": io["identf"]})
    b.end_phase()

    b.begin_phase("p7_")
    emit_l1(b, {"iscol": io["iscol"], "wq": io["wq"], "wk": io["wk"], "wv": io["wv"],
                "bqc": io["bqc"], "bkc": io["bkc"], "wf": io["wf"], "a1": io["a1"],
                "a2": io["a2"], "fbc": io["fbc_l"], "mks": io["mks"],
                "identf": io["identf"], "bvrow": io["bvrow"], "out": out_o})
    b.end_phase()

    b.done()
    return b.nc




_PROGS = {}


def _get(name, builder):
    if name not in _PROGS:
        _PROGS[name] = builder()
    return _PROGS[name]


def kernel(**inp):
    f32 = np.float32
    x = np.asarray(inp["L0_attn_x"], f32)
    B, T2, D = x.shape
    cores = list(range(8))
    identf = np.eye(128, dtype=f32)

    winT, wgT, woutT, cwc, cbc, bgc, fbc = host_hawk_weights(
        np.asarray(inp["L0_attn_hawk_Win"], f32), np.asarray(inp["L0_attn_hawk_Wg"], f32),
        np.asarray(inp["L0_attn_hawk_Wout"], f32), np.asarray(inp["L0_attn_hawk_cw"], f32),
        np.asarray(inp["L0_attn_hawk_cb"], f32), np.asarray(inp["L0_attn_hawk_bg"], f32),
        np.asarray(inp["L0_attn_hawk_fb"], f32))
    gnc_hawk = host_col(np.asarray(inp["L0_attn_gn_hawk"], f32))
    wgt1, wst1 = host_mlp_weights(np.asarray(inp["L0_attn_g1_Wgrow"], f32),
                                  np.asarray(inp["L0_attn_g1_Wshrink"], f32))
    gnc_m1 = host_col(np.asarray(inp["L0_attn_gn_hgmlp"], f32))
    wqT, wkT, wvT, woT = host_smqa_weights(np.asarray(inp["L0_attn_smqa_Wq"], f32),
                                           np.asarray(inp["L0_attn_smqa_Wkv"], f32),
                                           np.asarray(inp["L0_attn_smqa_Wout"], f32))
    mL, mR = host_masks()
    rotP = host_rot_mat()
    gnc_sq = host_col(np.asarray(inp["L0_attn_gn_smqa"], f32))
    cos0, sin0 = host_rotary(0)
    cos1, sin1 = host_rotary(2048)
    wgt2, wst2 = host_mlp_weights(np.asarray(inp["L0_attn_g2_Wgrow"], f32),
                                  np.asarray(inp["L0_attn_g2_Wshrink"], f32))
    gnc_m2 = host_col(np.asarray(inp["L0_attn_gn_sgmlp"], f32))
    gwT, b1c, b2c, sel = host_moe_weights(np.asarray(inp["L0_ffn_gate_w"], f32),
                                          np.asarray(inp["L0_ffn_b1"], f32),
                                          np.asarray(inp["L0_ffn_b2"], f32))
    w1 = np.ascontiguousarray(np.asarray(inp["L0_ffn_w1"], f32))
    w2 = np.ascontiguousarray(np.asarray(inp["L0_ffn_w2"], f32))
    wq, bqc, wk, bkc, wv, _bvc, wf, a1, a2, fbcol = host_l1_weights(
        np.asarray(inp["L1_attn_Wq"], f32), np.asarray(inp["L1_attn_bq"], f32),
        np.asarray(inp["L1_attn_Wk"], f32), np.asarray(inp["L1_attn_bk"], f32),
        np.asarray(inp["L1_attn_Wv"], f32), np.asarray(inp["L1_attn_bv"], f32),
        np.asarray(inp["L1_ffn_weight"], f32), np.asarray(inp["L1_ffn_aux_weight1"], f32),
        np.asarray(inp["L1_ffn_aux_weight2"], f32), np.asarray(inp["L1_ffn_bias"], f32))
    bvrow = np.ascontiguousarray(np.asarray(inp["L1_attn_bv"], f32).reshape(1, 1024))
    mks = {j: host_l1_masks(j) for j in range(2)}

    nc = _get("mega", build_fused)
    ins = []
    for c in cores:
        b2, hh = c // 2, c % 2
        ins.append(dict(
            x_in=np.ascontiguousarray(x[b2, hh * 2048:(hh + 1) * 2048]),
            identf=identf, iscol=np.full((128, 1), float(hh), f32),
            winT=winT, wgT=wgT, woutT=woutT, cwc=cwc, cbc=cbc, bgc=bgc,
            fbc_h=fbc, gnc_h=gnc_hawk,
            wgt1=wgt1, wst1=wst1, gnc_m1=gnc_m1,
            wgt2=wgt2, wst2=wst2, gnc_m2=gnc_m2,
            wqT=wqT, wkT=wkT, wvT=wvT, woT=woT, gnc_sq=gnc_sq,
            cosT=cos0 if hh == 0 else cos1, sinT=sin0 if hh == 0 else sin1,
            rotP=rotP, mL=mL, mR=mR,
            gwT=gwT, w1=w1, b1c=b1c, w2=w2, b2c=b2c, sel=sel,
            wq=wq, wk=wk, wv=wv, bqc=bqc, bkc=bkc, wf=wf, a1=a1, a2=a2,
            fbc_l=fbcol, mks=mks[hh], bvrow=bvrow))
    res = run_bass_kernel_spmd(nc, ins, core_ids=cores).results

    toks = {j: host_interleave_tokens(j) for j in range(2)}
    out = np.empty((B, T2, D), f32)
    for c in cores:
        b2, j = c // 2, c % 2
        out[b2, toks[j]] = res[c]["out"]
    return out
